# version 11
# baseline (speedup 1.0000x reference)
import math
import numpy as np
import concourse.bass as bass
import concourse.mybir as mybir
from concourse.bass_utils import run_bass_kernel_spmd

F32 = mybir.dt.float32
BF16 = mybir.dt.bfloat16
I32 = mybir.dt.int32
AX = mybir.AxisListType.X
AF = mybir.ActivationFunctionType
OP = mybir.AluOpType

NCORES = 8
T = 2048
D = 1024
DQK = 256
E = 2048
FF = 4096
G = 64
NT = T // 128
ND = D // 128
NE = E // 128
NQ = DQK // 128
NF = FF // 128
EPS = 1e-5
PI = math.pi
INV2PI = 1.0 / (2.0 * PI)
PI_LO = float(np.nextafter(np.float32(np.pi), np.float32(0.0)))
LORA_SCALING = 2.0


def build_nc(dbg=False):
    nc = bass.Bass("TRN2", target_bir_lowering=False)

    dr = lambda name, shape, dt, kind: nc.dram_tensor(name, shape, dt, kind=kind).ap()
    x_h = dr("x", [T, D], F32, "ExternalInput")
    wqkT = dr("wqkT", [D, DQK], F32, "ExternalInput")
    wvT = dr("wvT", [D, E], F32, "ExternalInput")
    wgT = dr("wgT", [D, E], F32, "ExternalInput")
    woT = dr("woT", [E, D], F32, "ExternalInput")
    wfcT = dr("wfcT", [D, FF], F32, "ExternalInput")
    wpjT = dr("wpjT", [FF, D], F32, "ExternalInput")
    afcT_h = dr("afcT", [D, 16], F32, "ExternalInput")
    bfcT_h = dr("bfcT", [16, FF], F32, "ExternalInput")
    apjT_h = dr("apjT", [FF, 16], F32, "ExternalInput")
    bpjT_h = dr("bpjT", [16, D], F32, "ExternalInput")
    bqk_h = dr("bqk", [128, NQ], F32, "ExternalInput")
    bv_h = dr("bv", [1, E], F32, "ExternalInput")
    bg_h = dr("bg", [128, NE], F32, "ExternalInput")
    bo_h = dr("bo", [1, D], F32, "ExternalInput")
    bfc_h = dr("bfc", [128, NF], F32, "ExternalInput")
    bpj_h = dr("bpj", [1, D], F32, "ExternalInput")
    ln1w_h = dr("ln1w", [128, ND], F32, "ExternalInput")
    ln1b_h = dr("ln1b", [128, ND], F32, "ExternalInput")
    ln2w_h = dr("ln2wr", [128, D], F32, "ExternalInput")
    ln2b_h = dr("ln2br", [128, D], F32, "ExternalInput")
    gam_h = dr("gam", [128, 8], F32, "ExternalInput")
    bet_h = dr("bet", [128, 8], F32, "ExternalInput")
    pos_h = dr("posemb", [65, 1], F32, "ExternalInput")
    oneh_h = dr("onehot", [65, G * G], BF16, "ExternalInput")
    ident_h = dr("identb", [128, 128], BF16, "ExternalInput")
    identf_h = dr("identf", [128, 128], F32, "ExternalInput")
    ones_h = dr("onesrow", [1, 128], BF16, "ExternalInput")
    onesf_h = dr("onesrowf", [1, 128], F32, "ExternalInput")
    out_h = dr("out", [T, D], F32, "ExternalOutput")
    sc_fc = nc.dram_tensor("scr_fc", [D, FF], BF16, kind="Internal").ap()
    scr1 = nc.dram_tensor("scr_1", [1, 1], F32, kind="Internal").ap()
    sc_pj = nc.dram_tensor("scr_pj", [FF, D], BF16, kind="Internal").ap()
    x2_scr = nc.dram_tensor("scr_x2", [T, D], BF16, kind="Internal").ap()

    dbg_t = {}
    if dbg:
        for nm, shape in (("d_qs", [128, 8]), ("d_qs2", [128, 8]),
                          ("d_mxp", [128, 32]),
                          ("d_y", [128, ND * T]), ("d_qk", [128, NQ * T]),
                          ("d_v", [128, NT * E]), ("d_lkv", [128, 2 * E]),
                          ("d_x2", [T, D]), ("d_att", [128, 512]),
                          ("d_cos", [128, T]), ("d_sin", [128, T]),
                          ("d_wfc", [D, FF])):
            dbg_t[nm] = dr(nm, shape, F32, "ExternalOutput")

    al = nc.alloc_sbuf_tensor
    GAMT = al("GAMT", [128, 8], F32)
    BETT = al("BETT", [128, 8], F32)
    BQK = al("BQK", [128, NQ], F32)
    BGC = al("BGC", [128, NE], F32)
    BFCC = al("BFCC", [128, NF], F32)
    LN1W = al("LN1W", [128, ND], F32)
    LN1B = al("LN1B", [128, ND], F32)
    LN2W = al("LN2W", [128, D], BF16)
    LN2B = al("LN2B", [128, D], BF16)
    BRW = al("BRW", [1, E], BF16)
    POSE = al("POSE", [65, 1], BF16)
    IDENT = al("IDENT", [128, 128], BF16)
    IDENTF = al("IDENTF", [128, 128], F32)
    ONES = al("ONES", [1, 128], BF16)
    ZROW = al("ZROW", [1, 128], BF16)
    ONESF = al("ONESF", [1, 128], F32)
    BIAS64 = al("BIAS64", [64, 64], F32)
    EPSC = al("EPSC", [128, 1], F32)
    ST6 = al("ST6", [128, 12], F32)
    MV = al("MV", [128, 2], F32)
    SA = al("SA", [128, 8], F32)
    MXP = al("MXP", [128, 32], F32)
    QS = al("QS", [128, 8], F32)
    QS2 = al("QS2", [128, 8], F32)
    COS = al("COS", [128, T], BF16)
    SIN = al("SIN", [128, T], BF16)
    XT = al("XT", [128, 2 * D], F32)
    FSCR = al("FSCR", [128, 1024], F32)
    FSC2 = al("FSC2", [128, 1024], F32)
    ISCR = al("ISCR", [128, 1024], I32)
    AV = al("AV", [128, NT * E], BF16)
    AY = al("AY", [128, ND * T], BF16)
    AGT = al("AGT", [128, NE * 512], BF16)
    AZQ = al("AZQ", [128, 4096], BF16)
    AMSC = al("AMSC", [128, 2048], BF16)
    QK = al("QK", [128, NQ * T], BF16)
    LINKV = al("LINKV", [128, 2 * E], BF16)
    AW = al("AW", [128, 4096], BF16)
    WSLB = al("WSLB", [128, 2048], BF16)
    LKT = al("LKT", [128, NT * DQK], BF16)
    QQ = al("QQ", [128, 1024], BF16)
    QKK = al("QKK", [128, 1024], BF16)
    LQ = al("LQ", [128, 1024], BF16)
    GATEE = al("GATEE", [128, 1024], BF16)
    ATT = al("ATT", [128, 512], BF16)

    ONEH = AV[0:65, 0:G * G]
    SQT = FSCR[:, 0:256]
    M1 = AMSC[:, 0:512]
    M2 = AMSC[:, 512:1024]
    UB0 = AMSC[:, 1024:1536]
    UB1 = AMSC[:, 1536:2048]
    Z2 = AZQ[:, 0:1024]
    T2E = AZQ[:, 1024:2048]
    OLN = AZQ[:, 2048:3072]
    BPJT = QQ[0:16, 0:1024]
    APJT = QKK[:, 0:512]
    AFCT = QKK[:, 512:640]
    T1 = LQ[0:16, 0:1024]
    T2L = GATEE[0:16, 0:1024]
    BFCTA = COS[0:16, :]
    BFCTB = SIN[0:16, :]

    PBIG = nc.alloc_psum_tensor("PBIG", [128, 2048], F32)
    PA = nc.alloc_psum_tensor("PA", [128, 1024], F32)
    PC = nc.alloc_psum_tensor("PC", [128, 512], F32)
    PD = nc.alloc_psum_tensor("PD", [128, 512], F32)
    PCD = [PC, PD]

    ops = []
    EMIT = {k: (lambda fn, k=k: ops.append((k, fn)))
            for k in ("tensor", "vector", "scalar", "sync", "gpsimd")}

    class Ctr:
        def __init__(self, h):
            self.h = h
            self.n = 0

    from contextlib import ExitStack
    stack = ExitStack()
    sem = {k: Ctr(stack.enter_context(nc.semaphore(k)))
           for k in ("ld", "gp", "gpc", "te", "ve", "se", "st", "qld",
                     "xs", "wgl", "wol", "xrl", "wvl", "wfl", "wpl",
                     "x2l", "qst", "lps", "b64", "qsc")}
    hw = {e: {} for e in EMIT}

    last_dma = {e: None for e in EMIT}

    def wait(engine, ctr, val):
        if val is None or val <= 0:
            return
        if val > hw[engine].get(id(ctr), 0):
            hw[engine][id(ctr)] = val
            last_dma[engine] = None
            EMIT[engine](lambda e, h=ctr.h, v=val: e.wait_ge(h, v))

    def waits(engine, deps):
        for d in deps:
            if d is None:
                continue
            if isinstance(d, list):
                waits(engine, d)
            else:
                wait(engine, d[0], d[1])

    def last(*tokens):
        out = {}
        for t in tokens:
            if t is None:
                continue
            if isinstance(t, list):
                for u in t:
                    if u is not None and out.get(id(u[0]), (None, 0))[1] < u[1]:
                        out[id(u[0])] = u
                continue
            if out.get(id(t[0]), (None, 0))[1] < t[1]:
                out[id(t[0])] = t
        return list(out.values())

    def dma(engine, out_ap, in_ap, deps=(), ctr_name=None):
        waits(engine, deps)
        if last_dma[engine] is not None:
            wait(engine, *last_dma[engine])
        c = sem[ctr_name] if ctr_name else (
            sem["ld"] if engine == "sync" else sem["gp"])
        wait(engine, c, c.n)
        c.n += 16
        n = c.n
        EMIT[engine](lambda e, o=out_ap, i=in_ap, h=c.h:
                     e.dma_start(out=o, in_=i).then_inc(h, 16))
        last_dma[engine] = (c, n)
        return (c, n)

    def mm(lhsT, rhs, out, start, stop, inc=False, deps=()):
        waits("tensor", deps)
        if inc:
            sem["te"].n += 1
            n = sem["te"].n
            EMIT["tensor"](lambda e, a=lhsT, b=rhs, o=out, s1=start, s2=stop,
                           h=sem["te"].h:
                           e.matmul(o, a, b, start=s1, stop=s2).then_inc(h, 1))
            return (sem["te"], n)
        EMIT["tensor"](lambda e, a=lhsT, b=rhs, o=out, s1=start, s2=stop:
                       e.matmul(o, a, b, start=s1, stop=s2))
        return None

    def act(out_ap, in_ap, func, bias=0.0, scale=1.0, deps=()):
        waits("scalar", deps)
        sem["se"].n += 1
        n = sem["se"].n
        EMIT["scalar"](lambda e, o=out_ap, i=in_ap, f=func, b=bias, s=scale,
                       h=sem["se"].h:
                       e.activation(out=o, in_=i, func=f, bias=b, scale=s)
                       .then_inc(h, 1))
        return (sem["se"], n)

    def vop(emitfn, deps=()):
        waits("vector", deps)
        sem["ve"].n += 1
        n = sem["ve"].n
        EMIT["vector"](lambda e, f=emitfn, h=sem["ve"].h: f(e).then_inc(h, 1))
        return (sem["ve"], n)

    def gop(emitfn, deps=()):
        waits("gpsimd", deps)
        sem["gpc"].n += 1
        n = sem["gpc"].n
        EMIT["gpsimd"](lambda e, f=emitfn, h=sem["gpc"].h: f(e).then_inc(h, 1))
        return (sem["gpc"], n)

    def flush_mm(region, deps=(), m=128):
        return mm(ZROW[:, 0:m], ONES[:, :], region, False, True, inc=True,
                  deps=deps)

    ld_gam = dma("sync", GAMT[:, :], gam_h)
    ld_bet = dma("sync", BETT[:, :], bet_h)
    ld_bqk = dma("sync", BQK[:, :], bqk_h)
    ld_bg = dma("sync", BGC[:, :], bg_h)
    ld_bfc = dma("sync", BFCC[:, :], bfc_h)
    ld_l1w = dma("sync", LN1W[:, :], ln1w_h)
    ld_l1b = dma("sync", LN1B[:, :], ln1b_h)
    ld_onh = dma("sync", ONEH, oneh_h)
    ld_id = dma("sync", IDENT[:, :], ident_h)
    ld_on = dma("sync", ONES[:, :], ones_h)
    ld_onf = dma("sync", ONESF[:, :], onesf_h)
    ld_l2w = dma("gpsimd", LN2W[:, :], ln2w_h)
    ld_l2b = dma("gpsimd", LN2B[:, :], ln2b_h)
    ld_bvr = dma("gpsimd", BRW[:, :], bv_h)
    ld_pos = dma("gpsimd", POSE[:, :], pos_h)
    pre_ld = (sem["ld"], sem["ld"].n)
    ld_gam = ld_bet = ld_bqk = ld_bg = ld_bfc = pre_ld
    ld_l1w = ld_l1b = ld_onh = ld_id = ld_on = ld_onf = pre_ld
    pre_gp = (sem["gp"], sem["gp"].n)
    ld_l2w = ld_l2b = ld_bvr = ld_pos = pre_gp
    iot1 = gop(lambda e: e.iota(FSCR[0:1, 0:128], [[1, 128]],
                                channel_multiplier=0,
                                allow_small_or_imprecise_dtypes=True))
    ms_att = gop(lambda e: e.memset(ATT[:, :], 0.0))
    ms_zr = gop(lambda e: e.memset(ZROW[:, :], 0.0))
    ms_eps = gop(lambda e: e.memset(EPSC[:, :], EPS))
    a_invf = act(IDENTF[0:1, 0:128], FSCR[0:1, 0:128], AF.Exp,
                 scale=-math.log(10000.0) / 128.0, deps=[iot1])

    def sincos_half2(hh, shift, dst_half, pdeps):
        posr = FSC2[0:1, 0:1024]
        pio = gop(lambda e, hh=hh: e.iota(posr, [[1, 1024]], base=hh * 1024,
                                          channel_multiplier=0,
                                          allow_small_or_imprecise_dtypes=True),
                  deps=list(pdeps))
        mmdeps = last(a_invf, pio) + list(pdeps)
        for j in range(2):
            ttok = mm(IDENTF[0:1, 0:128], posr[:, j * 512:(j + 1) * 512],
                      PA[:, j * 512:(j + 1) * 512], True, True, inc=(j == 1),
                      deps=mmdeps if j == 0 else ())
        ts = vop(lambda e: e.tensor_scalar(
            out=FSCR[:, 0:1024], in0=PA[:, :], scalar1=INV2PI, scalar2=shift,
            op0=OP.mult, op1=OP.add), deps=last(ttok))
        k1 = vop(lambda e: e.tensor_single_scalar(
            out=ISCR[:, 0:1024], in_=FSCR[:, 0:1024], scalar=1.0,
            op=OP.mult), deps=[ts])
        c2 = vop(lambda e: e.scalar_tensor_tensor(
            out=XT[:, 0:1024], in0=FSCR[:, 0:1024], scalar=1.0,
            in1=ISCR[:, 0:1024], op0=OP.mult, op1=OP.subtract), deps=[k1])
        km = vop(lambda e: e.scalar_tensor_tensor(
            out=XT[:, 1024:2048], in0=XT[:, 0:1024], scalar=-0.5,
            in1=ISCR[:, 0:1024], op0=OP.is_lt, op1=OP.subtract), deps=[c2])
        kf = vop(lambda e: e.scalar_tensor_tensor(
            out=XT[:, 0:1024], in0=XT[:, 0:1024], scalar=0.5,
            in1=XT[:, 1024:2048], op0=OP.is_gt, op1=OP.subtract), deps=[km])
        dd = vop(lambda e: e.scalar_tensor_tensor(
            out=FSCR[:, 0:1024], in0=XT[:, 0:1024], scalar=-1.0,
            in1=FSCR[:, 0:1024], op0=OP.mult, op1=OP.add), deps=[kf])
        r1 = vop(lambda e: e.tensor_scalar(
            out=FSCR[:, 0:1024], in0=FSCR[:, 0:1024], scalar1=2.0 * PI,
            scalar2=PI_LO, op0=OP.mult, op1=OP.min), deps=[dd])
        r2 = vop(lambda e: e.tensor_single_scalar(
            out=FSCR[:, 0:1024], in_=FSCR[:, 0:1024], scalar=-PI_LO,
            op=OP.max), deps=[r1])
        return act(dst_half, FSCR[:, 0:1024], AF.Sin, deps=[r2]), ttok

    tbl_toks = []
    mm_last = None
    for hh in range(2):
        prev = [tbl_toks[-1]] if tbl_toks else []
        tk, mm_last = sincos_half2(hh, 0.0,
                                   SIN[:, hh * 1024:(hh + 1) * 1024], prev)
        tbl_toks.append(tk)
        tk, mm_last = sincos_half2(hh, 0.25,
                                   COS[:, hh * 1024:(hh + 1) * 1024],
                                   [tbl_toks[-1]])
        tbl_toks.append(tk)
    tables_done = tbl_toks[-1]
    s_sin = tables_done
    s_cos = tables_done

    ev_prev = tables_done
    b64_dmas = []
    for j in range(8):
        tok = mm(POSE[:, :], ONEH[:, j * 512:(j + 1) * 512], PC[0:1, :],
                 True, True, inc=True,
                 deps=last(ld_pos, ld_onh, ev_prev,
                           b64_dmas[-1] if b64_dmas else None))
        ev_prev = vop(lambda e: e.tensor_copy(
            out=FSCR[0:1, 0:512], in_=PC[0:1, :]), deps=[tok])
        b64_dmas.append(dma("sync", BIAS64[8 * j:8 * (j + 1), :],
                            FSCR[0:1, 0:512], deps=[ev_prev],
                            ctr_name="b64"))
    ld_b64 = last(*b64_dmas)
    pc_free = ev_prev
    pd_free = None

    def stats1024(src, deps):
        s1 = vop(lambda e: e.bn_stats(out=ST6[:, 0:6], in_=src[:, 0:512]),
                 deps=deps)
        s2 = vop(lambda e: e.bn_stats(out=ST6[:, 6:12], in_=src[:, 512:1024]),
                 deps=[s1])
        return vop(lambda e: e.bn_aggr(out=MV[:, :], in_=ST6[:, 0:12]),
                   deps=[s2])

    def ln_single(vcol, mcol, acol, bcol, deps):
        w2 = vop(lambda e, v=vcol: e.tensor_scalar(
            out=SA[:, 2:3], in0=v, scalar1=1.0, scalar2=EPS,
            op0=OP.mult, op1=OP.add), deps=deps)
        w2r = vop(lambda e: e.reciprocal(out=SA[:, 3:4], in_=SA[:, 2:3]),
                  deps=[w2])
        aa = act(acol, SA[:, 3:4], AF.Sqrt, deps=[w2r])
        return vop(lambda e, m=mcol, a=acol, b=bcol: e.scalar_tensor_tensor(
            out=b, in0=m, scalar=-1.0, in1=a, op0=OP.mult, op1=OP.mult),
            deps=[aa])

    def ln_double(vcol, mcol, acol, bcol, deps):
        w2 = vop(lambda e, v=vcol: e.tensor_scalar(
            out=SA[:, 2:3], in0=v, scalar1=1.0, scalar2=EPS,
            op0=OP.mult, op1=OP.add), deps=deps)
        w2r = vop(lambda e: e.reciprocal(out=SA[:, 3:4], in_=SA[:, 2:3]),
                  deps=[w2])
        v2 = vop(lambda e, v=vcol: e.tensor_tensor(
            out=SA[:, 4:5], in0=v, in1=SA[:, 3:4], op=OP.mult), deps=[w2r])
        w4 = vop(lambda e: e.tensor_scalar(
            out=SA[:, 5:6], in0=SA[:, 4:5], scalar1=1.0, scalar2=EPS,
            op0=OP.mult, op1=OP.add), deps=[v2])
        w4r = vop(lambda e: e.reciprocal(out=SA[:, 6:7], in_=SA[:, 5:6]),
                  deps=[w4])
        pr = vop(lambda e: e.tensor_tensor(
            out=SA[:, 7:8], in0=SA[:, 3:4], in1=SA[:, 6:7], op=OP.mult),
            deps=[w4r])
        aa = act(acol, SA[:, 7:8], AF.Sqrt, deps=[pr])
        return vop(lambda e, m=mcol, a=acol, b=bcol: e.scalar_tensor_tensor(
            out=b, in0=m, scalar=-1.0, in1=a, op0=OP.mult, op1=OP.mult),
            deps=[aa])

    ev_tok = None
    zg_rd = [None] * 4
    prev_z = None
    for g in range(4):
        z_toks = []
        for tt in range(4):
            t = g * 4 + tt
            buf = tt % 2
            xs = XT[:, buf * D:(buf + 1) * D]
            xdep = [prev_z] if t >= 2 else [tables_done, ld_b64]
            xld = dma("sync", xs, x_h[t * 128:(t + 1) * 128, :], deps=xdep,
                      ctr_name="xs")
            ag = stats1024(xs, [xld])
            bb = ln_double(MV[:, 1:2], MV[:, 0:1], SA[:, 0:1], SA[:, 1:2],
                           [ag])
            ztok = vop(lambda e, xs=xs, tt=tt: e.tensor_scalar(
                out=AZQ[:, tt * D:(tt + 1) * D], in0=xs,
                scalar1=SA[:, 0:1], scalar2=SA[:, 1:2],
                op0=OP.mult, op1=OP.add),
                deps=last(bb, zg_rd[g - 1] if g >= 1 else None))
            z_toks.append(ztok)
            prev_z = ztok
        for c in range(ND):
            q = c % 4
            reg = PBIG[:, q * 512:(q + 1) * 512]
            deps = last(*(z_toks + [ev_tok, ld_id]))
            for tt in range(4):
                tok = mm(AZQ[:, tt * D + c * 128: tt * D + (c + 1) * 128],
                         IDENT[:, :], reg[:, tt * 128:(tt + 1) * 128],
                         True, True, inc=(tt == 3),
                         deps=deps if tt == 0 else ())
            ev_tok = act(AY[:, c * T + g * 512: c * T + (g + 1) * 512],
                         reg, AF.Identity, bias=LN1B[:, c:c + 1],
                         scale=LN1W[:, c:c + 1],
                         deps=last(tok, ld_l1w, ld_l1b))
            if c == ND - 1:
                zg_rd[g] = tok
    y_done = ev_tok
    Y = AY

    pcd_free = [pc_free, pd_free]
    aw_free = [None, None]
    qk_evs = []
    for qc in range(NQ):
        wb = qc % 2
        wsl = AW[:, wb * 1024:(wb + 1) * 1024]
        wqk_ld = dma("gpsimd", wsl.rearrange("p (c k) -> p c k", c=ND),
                     wqkT[:, qc * 128:(qc + 1) * 128]
                     .rearrange("(c p) k -> p c k", p=128),
                     deps=last(aw_free[wb]), ctr_name="wvl")
        for tc in range(4):
            pi = (qc * 4 + tc) % 2
            ps = PCD[pi]
            deps = last(wqk_ld, y_done, pcd_free[pi])
            for c in range(ND):
                tok = mm(wsl[:, c * 128:(c + 1) * 128],
                         Y[:, c * T + tc * 512: c * T + (tc + 1) * 512],
                         ps[:, :], c == 0, c == ND - 1, inc=(c == ND - 1),
                         deps=deps if c == 0 else ())
            ev = act(QK[:, qc * T + tc * 512: qc * T + (tc + 1) * 512],
                     ps[:, :], AF.Silu, bias=BQK[:, qc:qc + 1],
                     deps=last(tok, ld_bqk))
            pcd_free[pi] = ev
            qk_evs.append(ev)
        aw_free[wb] = tok
    qk_done = last(*qk_evs)
    V = AV

    v_evs = []
    for oc in range(4):
        wsl = AW[:, :]
        wdep = last(aw_free[0], aw_free[1]) if oc == 0 else [v_evs[-1]]
        wv_ld = dma("gpsimd", wsl.rearrange("p (c k) -> p c k", c=ND),
                    wvT[:, oc * 512:(oc + 1) * 512]
                    .rearrange("(c p) k -> p c k", p=128),
                    deps=wdep, ctr_name="wvl")
        for t in range(NT):
            pi = t % 2
            ps = PCD[pi]
            deps = last(wv_ld, y_done, pcd_free[pi], ld_on, ld_bvr)
            for c in range(ND):
                mm(Y[:, c * T + t * 128: c * T + (t + 1) * 128],
                   AW[:, c * 512:(c + 1) * 512],
                   ps[:, :], c == 0, False, deps=deps if c == 0 else ())
            tok = mm(ONES[:, :], BRW[:, oc * 512:(oc + 1) * 512], ps[:, :],
                     False, True, inc=True)
            ev = act(V[:, t * E + oc * 512: t * E + (oc + 1) * 512],
                     ps[:, :], AF.Silu, deps=[tok])
            pcd_free[pi] = ev
            v_evs.append(ev)
    aw_free = [v_evs[-1], v_evs[-1]]
    v_done = last(*v_evs)

    def head_rope(hidx, tok_off, dst, d0_off, d1_off, deps):
        n = 512
        a0 = act(UB0, QK[:, tok_off:tok_off + n], AF.Identity,
                 bias=BETT[:, hidx:hidx + 1], scale=GAMT[:, hidx:hidx + 1],
                 deps=deps)
        a1 = act(UB1, QK[:, T + tok_off:T + tok_off + n], AF.Identity,
                 bias=BETT[:, 4 + hidx:4 + hidx + 1],
                 scale=GAMT[:, 4 + hidx:4 + hidx + 1], deps=deps)
        cs = COS[:, tok_off:tok_off + n]
        sn = SIN[:, tok_off:tok_off + n]
        m1 = vop(lambda e: e.tensor_mul(out=M1, in0=UB0, in1=cs),
                 deps=last(a0, s_cos))
        m2 = vop(lambda e: e.tensor_mul(out=M2, in0=UB1, in1=sn),
                 deps=last(a1, s_sin))
        o1 = vop(lambda e: e.tensor_sub(
            out=dst[:, d0_off:d0_off + n], in0=M1, in1=M2), deps=[m1, m2])
        m3 = vop(lambda e: e.tensor_mul(out=M1, in0=UB1, in1=cs), deps=[o1])
        m4 = vop(lambda e: e.tensor_mul(out=M2, in0=UB0, in1=sn), deps=[o1])
        o2 = vop(lambda e: e.tensor_add(
            out=dst[:, d1_off:d1_off + n], in0=M1, in1=M2), deps=[m3, m4])
        return (last(a0, a1), last(o1, o2))

    LKF = AZQ
    lk_toks = []
    for cc in range(4):
        dep = last(qk_done, ld_gam, ld_bet,
                   zg_rd[3] if cc == 0 else None,
                   lk_toks[-1] if lk_toks else None)
        _, o = head_rope(3, cc * 512, LKF, cc * 512, 2048 + cc * 512, dep)
        lk_toks.append(o)
    lk_done = last(*lk_toks)
    lkt_evs = []
    ev = None
    for c2 in range(2):
        for tg in range(4):
            deps = last(lk_done, ev, pcd_free[0])
            for tt in range(4):
                t = tg * 4 + tt
                tok = mm(LKF[:, c2 * 2048 + t * 128: c2 * 2048 + (t + 1) * 128],
                         IDENT[:, :], PC[:, tt * 128:(tt + 1) * 128],
                         True, True, inc=(tt == 3),
                         deps=deps if tt == 0 else ())
            ev = vop(lambda e, c2=c2, tg=tg: e.tensor_copy(
                out=LKT[:, :].rearrange("p (t k) -> p t k", t=NT)
                [:, tg * 4:(tg + 1) * 4, c2 * 128:(c2 + 1) * 128],
                in_=PC[:, :].rearrange("p (t k) -> p t k", t=4)),
                deps=[tok])
            lkt_evs.append(ev)
    lkt_done = last(*lkt_evs)
    pcd_free[0] = lkt_evs[-1]

    regions8 = [PBIG[:, 0:512], PBIG[:, 512:1024], PBIG[:, 1024:1536],
                PBIG[:, 1536:2048], PA[:, 0:512], PA[:, 512:1024],
                PC[:, :], PD[:, :]]
    lkv_evs = []
    for m in range(2):
        for n4 in range(4):
            r = regions8[m * 4 + n4]
            deps = last(lkt_done, v_done, pcd_free[0], pcd_free[1], ev_tok)
            for t in range(NT):
                tok = mm(LKT[:, t * DQK + m * 128: t * DQK + (m + 1) * 128],
                         V[:, t * E + n4 * 512: t * E + (n4 + 1) * 512],
                         r, t == 0, t == NT - 1, inc=(t == NT - 1),
                         deps=deps if t == 0 else ())
            lkv_evs.append(act(
                LINKV[:, m * E + n4 * 512: m * E + (n4 + 1) * 512], r,
                AF.Identity, scale=1.0 / T, deps=[tok]))
    lkv_done = last(*lkv_evs)

    ld_bor = dma("gpsimd", BRW[:, 0:1024], bo_h, deps=last(v_done),
                 ctr_name="lps")
    XB0 = LKT[:, 0:1024]
    XB1 = LKT[:, 1024:2048]
    X2S0 = LKT[:, 2048:3072]
    X2S1 = LKT[:, 3072:4096]

    x2_sts = [None] * NT
    pa_free = [lkv_evs[4], lkv_evs[5]]
    pcd_free = [lkv_evs[6], lkv_evs[7]]
    pbig_free = [last(lkv_evs[0], lkv_evs[1]), last(lkv_evs[2], lkv_evs[3])]
    gt_free = v_evs[-1]
    gatee_prev = [None, None]
    wo_prev = [None, None]
    z2_rd = lkt_done
    epi_z2 = lkt_done
    epi_ol = lkt_done
    xb_prev = [lkv_done, lkv_done]
    x2s_prev = [None, None]

    for C in range(4):
        base = C * 512
        d0 = last(qk_done, ms_att, lkv_done, z2_rd)
        aq, hq = head_rope(0, base, QQ, 0, 512, d0)
        ak, hk = head_rope(2, base, QKK, 0, 512, [hq])
        al_, hl = head_rope(1, base, LQ, 0, 512, [hk])
        sdeps = last(hq, hk, pcd_free[0])
        for p in range(4):
            for c2 in range(2):
                mm(QKK[:, c2 * 512 + p * 128: c2 * 512 + (p + 1) * 128],
                   QQ[:, c2 * 512 + p * 128: c2 * 512 + (p + 1) * 128],
                   PC[:, p * 128:(p + 1) * 128], c2 == 0,
                   c2 == 1 and p != 3,
                   deps=sdeps if (p == 0 and c2 == 0) else ())
        tok = flush_mm(PC[:, 384:512], deps=last(ms_zr, ld_on))
        pc3 = PC[:, :].rearrange("p (c k) -> p c k", c=4)
        sq3 = SQT[:, :].rearrange("p (c k) -> p c k", c=4)
        at3 = ATT[:, :].rearrange("p (c k) -> p c k", c=4)
        b3 = BIAS64[0:64, :].unsqueeze(1).broadcast_to([64, 4, 64])
        sq_t = vop(lambda e: e.scalar_tensor_tensor(
            out=sq3[0:64, :, :], in0=pc3[0:64, :, 0:64], scalar=1.0 / G,
            in1=b3, op0=OP.mult, op1=OP.add), deps=last(tok, ld_b64))
        sq_b = vop(lambda e: e.scalar_tensor_tensor(
            out=sq3[64:128, :, :], in0=pc3[64:128, :, 64:128],
            scalar=1.0 / G, in1=b3, op0=OP.mult, op1=OP.add), deps=[sq_t])
        at_t = vop(lambda e: e.scalar_tensor_tensor(
            out=at3[0:64, :, 0:64], in0=sq3[0:64, :, :], scalar=0.0,
            in1=sq3[0:64, :, :], op0=OP.max, op1=OP.mult), deps=[sq_t])
        at_b = vop(lambda e: e.scalar_tensor_tensor(
            out=at3[64:128, :, 64:128], in0=sq3[64:128, :, :], scalar=0.0,
            in1=sq3[64:128, :, :], op0=OP.max, op1=OP.mult), deps=[sq_b])
        att_done = last(at_t, at_b)
        pcd_free[0] = at_b
        if dbg and C == 0:
            dma("gpsimd", dbg_t["d_att"], ATT[:, :], deps=[att_done],
                ctr_name="lps")

        gt_toks = []
        for Ei in range(NE):
            wb = Ei % 2
            prev_wg = gt_toks[-2] if len(gt_toks) >= 2 else gt_free
            wg_ld = dma("gpsimd", AW[:, wb * 1024:(wb + 1) * 1024]
                        .rearrange("p (c k) -> p c k", c=ND),
                        wgT[:, Ei * 128:(Ei + 1) * 128]
                        .rearrange("(c p) k -> p c k", p=128),
                        deps=last(prev_wg), ctr_name="wgl")
            pi = Ei % 2
            gdeps = last(wg_ld, pcd_free[pi], y_done)
            for c in range(ND):
                tok = mm(AW[:, wb * 1024 + c * 128: wb * 1024 + (c + 1) * 128],
                         Y[:, c * T + base: c * T + base + 512],
                         PCD[pi][:, :], c == 0, c == ND - 1,
                         inc=(c == ND - 1), deps=gdeps if c == 0 else ())
            gprev = gt_toks[-2] if len(gt_toks) >= 2 else gatee_prev[pi]
            gev = act(GATEE[:, 0:512] if pi == 0 else GATEE[:, 512:1024],
                      PCD[pi][:, :], AF.Silu, bias=BGC[:, Ei:Ei + 1],
                      deps=last(tok, ld_bg, gprev))
            pcd_free[pi] = gev
            ph = Ei % 2
            adeps = last(pa_free[ph], lkv_done, hl, att_done, v_done)
            for m in range(2):
                mm(LINKV[:, m * E + Ei * 128: m * E + (Ei + 1) * 128],
                   LQ[:, m * 512:(m + 1) * 512],
                   PA[:, ph * 512:(ph + 1) * 512], m == 0, False,
                   deps=adeps if m == 0 else ())
            for p in range(4):
                mm(V[:, (4 * C + p) * E + Ei * 128:
                     (4 * C + p) * E + (Ei + 1) * 128],
                   ATT[:, p * 128:(p + 1) * 128],
                   PA[:, ph * 512 + p * 128: ph * 512 + (p + 1) * 128],
                   False, False)
            tok2 = flush_mm(PA[:, ph * 512: ph * 512 + 128])
            gt = vop(lambda e, Ei=Ei, ph=ph, pi=pi: e.tensor_tensor(
                out=AGT[:, Ei * 512:(Ei + 1) * 512],
                in0=PA[:, ph * 512:(ph + 1) * 512],
                in1=GATEE[:, pi * 512:(pi + 1) * 512], op=OP.mult),
                deps=last(tok2, gev))
            pa_free[ph] = gt
            gt_toks.append(gt)
        gts = last(*gt_toks)

        for half in range(2):
            ts_pair = (2 * half, 2 * half + 1)
            regs = (PBIG[:, 0:1024], PBIG[:, 1024:2048])
            hd = last(gts, pbig_free[0], pbig_free[1])
            for Ei in range(NE):
                wb = Ei % 2
                wo_ld = dma("gpsimd", WSLB[:, wb * 1024:(wb + 1) * 1024],
                            woT[Ei * 128:(Ei + 1) * 128, :],
                            deps=last(wo_prev[wb]), ctr_name="wol")
                wdeps = last(wo_ld, gts) + (hd if Ei == 0 else [])
                for k in range(2):
                    ts_i = ts_pair[k]
                    for nh in range(2):
                        tok = mm(AGT[:, Ei * 512 + ts_i * 128:
                                     Ei * 512 + (ts_i + 1) * 128],
                                 WSLB[:, wb * 1024 + nh * 512:
                                      wb * 1024 + (nh + 1) * 512],
                                 regs[k][:, nh * 512:(nh + 1) * 512],
                                 Ei == 0, False, inc=(k == 1 and nh == 1),
                                 deps=wdeps if (k == 0 and nh == 0) else ())
                wo_prev[wb] = tok
            wo_last = []
            for k in range(2):
                for nh in range(2):
                    tok = mm(ONES[:, :], BRW[:, nh * 512:(nh + 1) * 512],
                             regs[k][:, nh * 512:(nh + 1) * 512], False, True,
                             inc=(nh == 1), deps=last(ld_bor, ld_on))
                wo_last.append(tok)
            for k in range(2):
                ts_i = ts_pair[k]
                t_abs = 4 * C + ts_i
                reg = regs[k]
                ag = stats1024(reg, [wo_last[k]])
                bb = ln_single(MV[:, 1:2], MV[:, 0:1], SA[:, 0:1],
                               SA[:, 1:2], [ag])
                z2t = act(Z2, reg, AF.Identity, bias=SA[:, 1:2], scale=SA[:, 0:1],
                          deps=last(bb, epi_z2))
                t2a = vop(lambda e: e.tensor_mul(out=T2E, in0=Z2,
                                                 in1=LN2W[:, :]),
                          deps=last(z2t, ld_l2w))
                epi_z2 = t2a
                t2b = vop(lambda e: e.tensor_add(out=T2E, in0=T2E,
                                                 in1=LN2B[:, :]),
                          deps=last(t2a, ld_l2b))
                ag2 = stats1024(T2E, [t2b])
                bb2 = ln_single(MV[:, 1:2], MV[:, 0:1], SA[:, 0:1],
                                SA[:, 1:2], [ag2])
                olnt = act(OLN, T2E, AF.Identity, bias=SA[:, 1:2],
                           scale=SA[:, 0:1], deps=last(bb2, epi_ol))
                xb = XB0 if t_abs % 2 == 0 else XB1
                x2s = X2S0 if t_abs % 2 == 0 else X2S1
                par = t_abs % 2
                xre = dma("gpsimd", xb, x_h[t_abs * 128:(t_abs + 1) * 128, :],
                          deps=last(xb_prev[par]), ctr_name="xrl")
                fin = vop(lambda e, xb=xb, x2s=x2s: e.tensor_add(
                    out=x2s, in0=xb, in1=OLN),
                    deps=last(olnt, xre, x2s_prev[par]))
                epi_ol = fin
                xb_prev[par] = fin
                stok = dma("sync", x2_scr[t_abs * 128:(t_abs + 1) * 128, :],
                           x2s, deps=[fin], ctr_name="st")
                x2s_prev[par] = stok
                x2_sts[t_abs] = stok
                pbig_free[k] = fin
        gt_free = wo_last[-1]
        z2_rd = gts

    x2_done = last(*[x2_sts[t] for t in range(NT)])
    if dbg:
        for nm, src in (("d_y", Y), ("d_qk", QK), ("d_v", V),
                        ("d_lkv", LINKV), ("d_cos", COS), ("d_sin", SIN)):
            dma("gpsimd", dbg_t[nm], src[:, :],
                deps=last(y_done, qk_done, v_done, lkv_done, x2_done),
                ctr_name="lps")
        dma("gpsimd", dbg_t["d_x2"], x2_scr, deps=last(x2_done),
            ctr_name="lps")

    def quantize(w_hbm, cols, scr, qsl, pc_dep, start_deps):
        w4 = w_hbm.rearrange("(c p) (j k) -> p c j k", p=128, k=1024)
        s4 = scr.rearrange("(c p) (j k) -> p c j k", p=128, k=1024)
        ncw = cols // 1024
        mx_toks = []
        prev = None
        for i in range(32):
            b = i % 2
            stg = XT[:, b * 1024:(b + 1) * 1024]
            c, j = divmod(i, ncw)
            ldt = dma("sync", stg, w4[:, c, j, :],
                      deps=last(prev) + (list(start_deps) if i == 0 else []),
                      ctr_name="qld")
            mx = vop(lambda e, stg=stg, i=i: e.tensor_reduce(
                out=MXP[:, i:i + 1], in_=stg, axis=AX, op=OP.max,
                apply_absolute_value=True), deps=[ldt])
            prev = mx
            mx_toks.append(mx)
        m1 = vop(lambda e, q=qsl: e.tensor_reduce(
            out=q[:, 0:1], in_=MXP[:, 0:32], axis=AX, op=OP.max),
            deps=last(*mx_toks))
        d1 = dma("sync", FSCR[0:1, 0:128], qsl[:, 0:1], deps=[m1],
                 ctr_name="qsc")
        m2 = vop(lambda e, q=qsl: e.tensor_reduce(
            out=q[0:1, 6:7], in_=FSCR[0:1, 0:128], axis=AX, op=OP.max),
            deps=[d1])
        d2a = dma("sync", scr1, qsl[0:1, 6:7], deps=[m2], ctr_name="qsc")
        d2 = dma("sync", qsl[:, 0:1], scr1.broadcast_to([128, 1]),
                 deps=[d2a], ctr_name="qsc")
        bmax = d2
        rr = vop(lambda e, q=qsl: e.reciprocal(out=q[:, 1:2], in_=q[:, 0:1]),
                 deps=[bmax])
        sinv = vop(lambda e, q=qsl: e.tensor_single_scalar(
            out=q[:, 2:3], in_=q[:, 1:2], scalar=127.0, op=OP.mult),
            deps=[rr])
        s_ = vop(lambda e, q=qsl: e.tensor_single_scalar(
            out=q[:, 3:4], in_=q[:, 0:1], scalar=1.0 / 127.0, op=OP.mult),
            deps=[bmax])
        s2 = vop(lambda e, q=qsl: e.tensor_tensor(
            out=q[:, 4:5], in0=q[:, 3:4], in1=q[:, 3:4], op=OP.mult),
            deps=[s_])
        sinv2 = vop(lambda e, q=qsl: e.tensor_single_scalar(
            out=q[:, 5:6], in_=q[:, 2:3], scalar=LORA_SCALING, op=OP.mult),
            deps=[sinv])
        st_prev = [None, None]
        qf_prev = [None, None]
        stt = None
        for i in range(32):
            b = i % 2
            stg = XT[:, b * 1024:(b + 1) * 1024]
            wf = AGT[:, b * 1024:(b + 1) * 1024]
            c, j = divmod(i, ncw)
            ldt = dma("sync", stg, w4[:, c, j, :],
                      deps=last(st_prev[b]) if st_prev[b] else last(*mx_toks),
                      ctr_name="qld")
            q1 = vop(lambda e, stg=stg, q=qsl: e.tensor_single_scalar(
                out=ISCR[:, :], in_=stg, scalar=q[:, 2:3], op=OP.mult),
                deps=last(ldt, sinv))
            cc = vop(lambda e, stg=stg, q=qsl: e.scalar_tensor_tensor(
                out=FSCR[:, :], in0=stg, scalar=q[:, 2:3], in1=ISCR[:, :],
                op0=OP.mult, op1=OP.subtract), deps=[q1])
            st_prev[b] = cc
            qm = vop(lambda e: e.scalar_tensor_tensor(
                out=FSC2[:, :], in0=FSCR[:, :], scalar=-0.5, in1=ISCR[:, :],
                op0=OP.is_lt, op1=OP.subtract), deps=[cc])
            qf = vop(lambda e, wf=wf: e.scalar_tensor_tensor(
                out=wf, in0=FSCR[:, :], scalar=0.5, in1=FSC2[:, :],
                op0=OP.is_gt, op1=OP.subtract),
                deps=last(qm, qf_prev[b]))
            stt = dma("sync", s4[:, c, j, :], wf, deps=[qf],
                      ctr_name="qst")
            qf_prev[b] = stt
        return dict(sinv=sinv, s=s_, s2=s2, sinv2=sinv2, done=stt, bmax=bmax)

    q_fc = quantize(wfcT, FF, sc_fc, QS, pcd_free[0], last(gt_free))
    q_pj = quantize(wpjT, D, sc_pj, QS2, q_fc["bmax"], [q_fc["done"]])
    ld_bpr = dma("sync", XT[0:1, 0:1024], bpj_h, deps=last(q_pj["done"]),
                 ctr_name="qsc")
    bpr_tok = act(BRW[:, 1024:2048], XT[0:1, 0:1024], AF.Identity,
                  scale=QS2[0:1, 2:3], deps=last(ld_bpr, q_pj["sinv"],
                                                 gt_free))
    if dbg:
        dma("gpsimd", dbg_t["d_wfc"], sc_fc, deps=[q_fc["done"]],
            ctr_name="lps")
        dma("sync", dbg_t["d_qs"], QS[:, :],
            deps=last(q_fc["sinv2"], q_fc["s2"]), ctr_name="b64")
        dma("sync", dbg_t["d_qs2"], QS2[:, :],
            deps=last(q_pj["sinv2"], q_pj["s2"]), ctr_name="b64")
        dma("sync", dbg_t["d_mxp"], MXP[:, :], deps=last(q_fc["sinv2"]),
            ctr_name="b64")

    lkv_rd_done = gts
    late_dep = last(hl, att_done, gts, bpr_tok)
    ld_afc = dma("gpsimd", AFCT.rearrange("p (c k) -> p c k", c=ND),
                 afcT_h.rearrange("(c p) k -> p c k", p=128), deps=late_dep,
                 ctr_name="lps")
    ld_apj = dma("gpsimd", APJT.rearrange("p (c k) -> p c k", c=NF),
                 apjT_h.rearrange("(c p) k -> p c k", p=128), deps=late_dep,
                 ctr_name="lps")
    ld_bpjT = dma("gpsimd", BPJT, bpjT_h, deps=late_dep, ctr_name="lps")
    ld_bfcA = dma("gpsimd", BFCTA, bfcT_h[:, 0:2048], deps=late_dep,
                  ctr_name="lps")
    ld_bfcB = dma("gpsimd", BFCTB, bfcT_h[:, 2048:4096], deps=late_dep,
                  ctr_name="lps")

    Y2 = AY
    H = AV
    regs3 = [PBIG[:, 0:1024], PBIG[:, 1024:2048], PA[:, :]]
    reg_free = [last(pbig_free[0]), last(pbig_free[1]),
                last(pa_free[0], pa_free[1])]
    pcq_free = q_pj["bmax"]
    wfc_prev = [None, None]
    wpj_prev = [None, None]
    sfc4 = sc_fc.rearrange("(c p) (o k) -> p c o k", p=128, k=128)
    spj3 = sc_pj.rearrange("(c p) k -> p c k", p=128)
    out_sts = []
    azq_rd = epi_ol
    y_rd = gts
    v_rd = gts
    x2ld_prev = [None, None]
    x2ld2_prev = [None, None]
    outt_prev = [None, None]
    oln_prev = bpr_tok

    for Cf in range(2):
        tb = Cf * 8
        ztoks_all = []
        ev = None
        for g2 in range(2):
            z_toks = []
            for tt in range(4):
                t_abs = tb + g2 * 4 + tt
                par = tt % 2
                xbuf = LKT[:, par * 1024:(par + 1) * 1024]
                xld = dma("sync", xbuf,
                          x2_scr[t_abs * 128:(t_abs + 1) * 128, :],
                          deps=last(x2_sts[t_abs], x2ld_prev[par]),
                          ctr_name="x2l")
                ag = stats1024(xbuf, [xld])
                bb = ln_single(MV[:, 1:2], MV[:, 0:1], SA[:, 0:1],
                               SA[:, 1:2], [ag])
                ztok = vop(lambda e, xbuf=xbuf, tt=tt: e.tensor_scalar(
                    out=AZQ[:, tt * D:(tt + 1) * D], in0=xbuf,
                    scalar1=SA[:, 0:1], scalar2=SA[:, 1:2],
                    op0=OP.mult, op1=OP.add),
                    deps=last(bb, azq_rd))
                x2ld_prev[par] = ztok
                z_toks.append(ztok)
            for c in range(ND):
                q = c % 4
                reg = PBIG[:, q * 512:(q + 1) * 512]
                deps = last(*(z_toks + [ev, ld_id] + reg_free[q // 2]))
                for tt in range(4):
                    tok = mm(AZQ[:, tt * D + c * 128: tt * D + (c + 1) * 128],
                             IDENT[:, :], reg[:, tt * 128:(tt + 1) * 128],
                             True, True, inc=(tt == 3),
                             deps=deps if tt == 0 else ())
                ev = vop(lambda e, c=c, g2=g2, reg=reg: e.tensor_copy(
                    out=Y2[:, c * 1024 + g2 * 512:
                           c * 1024 + (g2 + 1) * 512],
                    in_=reg), deps=[tok])
                reg_free[q // 2] = [ev]
            azq_rd = tok
            ztoks_all += z_toks
        y2_done = ev

        t1toks = []
        for h2 in range(2):
            deps = last(y2_done, ld_afc, pcq_free, hl)
            for c in range(ND):
                tok = mm(AFCT[:, c * 16:(c + 1) * 16],
                         Y2[:, c * 1024 + h2 * 512: c * 1024 + (h2 + 1) * 512],
                         PC[0:16, :], c == 0, c == ND - 1,
                         inc=(c == ND - 1), deps=deps if c == 0 else ())
            t1toks.append(act(T1[0:16, h2 * 512:(h2 + 1) * 512], PC[0:16, :],
                              AF.Identity, scale=QS[0:16, 5:6],
                              deps=last(tok, q_fc["sinv2"])))
            pcq_free = t1toks[-1]
        t1_done = last(*t1toks)
        h_toks = []
        for o1 in range(NF):
            wb = o1 % 2
            wfc_ld = dma("sync", AW[:, wb * 1024:(wb + 1) * 1024],
                         sfc4[:, :, o1, :],
                         deps=last(wfc_prev[wb], q_fc["done"], gt_free),
                         ctr_name="wfl")
            ri = o1 % 3
            reg = regs3[ri]
            deps = last(wfc_ld, y2_done, t1_done, ld_bfcA, ld_bfcB,
                        *reg_free[ri])
            for c in range(ND):
                for nh in range(2):
                    mm(AW[:, wb * 1024 + c * 128: wb * 1024 + (c + 1) * 128],
                       Y2[:, c * 1024 + nh * 512: c * 1024 + (nh + 1) * 512],
                       reg[:, nh * 512:(nh + 1) * 512], c == 0, False,
                       deps=deps if (c == 0 and nh == 0) else ())
            bsrc = BFCTA if o1 < 16 else BFCTB
            bof = (o1 % 16) * 128
            for nh in range(2):
                tok = mm(bsrc[:, bof:bof + 128],
                         T1[0:16, nh * 512:(nh + 1) * 512],
                         reg[:, nh * 512:(nh + 1) * 512], False, True,
                         inc=(nh == 1))
            hev = act(H[:, o1 * 1024:(o1 + 1) * 1024], reg, AF.Gelu,
                      bias=BFCC[:, o1:o1 + 1], scale=QS[:, 3:4],
                      deps=last(tok, q_fc["s"], ld_bfc, v_rd))
            h_toks.append(hev)
            reg_free[ri] = [hev]
            wfc_prev[wb] = hev
        h_done = last(*h_toks)
        t2toks = []
        for h2 in range(2):
            deps = last(h_done, ld_apj, pcq_free)
            for fc in range(NF):
                tok = mm(APJT[:, fc * 16:(fc + 1) * 16],
                         H[:, fc * 1024 + h2 * 512:
                           fc * 1024 + (h2 + 1) * 512],
                         PC[0:16, :], fc == 0, fc == NF - 1,
                         inc=(fc == NF - 1), deps=deps if fc == 0 else ())
            t2toks.append(act(T2L[0:16, h2 * 512:(h2 + 1) * 512], PC[0:16, :],
                              AF.Identity, scale=QS2[0:16, 5:6],
                              deps=last(tok, q_pj["sinv2"])))
            pcq_free = t2toks[-1]
        t2_done = last(*t2toks)
        for half in range(4):
            ts_pair = (2 * half, 2 * half + 1)
            regs = (PBIG[:, 0:1024], PBIG[:, 1024:2048])
            hd = last(*(reg_free[0] + reg_free[1]))
            for fc in range(NF):
                wb = fc % 2
                wpj_ld = dma("sync", WSLB[:, wb * 1024:(wb + 1) * 1024],
                             spj3[:, fc, :],
                             deps=last(wpj_prev[wb], q_pj["done"]),
                             ctr_name="wpl")
                wdeps = last(wpj_ld, h_done, t2_done) + \
                    (hd if fc == 0 else [])
                for k in range(2):
                    ts_i = ts_pair[k]
                    for nh in range(2):
                        tok = mm(H[:, fc * 1024 + ts_i * 128:
                                   fc * 1024 + (ts_i + 1) * 128],
                                 WSLB[:, wb * 1024 + nh * 512:
                                      wb * 1024 + (nh + 1) * 512],
                                 regs[k][:, nh * 512:(nh + 1) * 512],
                                 fc == 0, False, inc=(k == 1 and nh == 1),
                                 deps=wdeps if (k == 0 and nh == 0) else ())
                wpj_prev[wb] = tok
            f2_last = []
            for k in range(2):
                ts_i = ts_pair[k]
                for nh in range(2):
                    mm(T2L[0:16, ts_i * 128:(ts_i + 1) * 128],
                       BPJT[0:16, nh * 512:(nh + 1) * 512],
                       regs[k][:, nh * 512:(nh + 1) * 512], False, False,
                       deps=last(ld_bpjT, t2_done))
                for nh in range(2):
                    tok = mm(ONES[:, :],
                             BRW[:, 1024 + nh * 512:1024 + (nh + 1) * 512],
                             regs[k][:, nh * 512:(nh + 1) * 512], False, True,
                             inc=(nh == 1), deps=last(bpr_tok))
                f2_last.append(tok)
            for k in range(2):
                ts_i = ts_pair[k]
                t_abs = tb + ts_i
                par = t_abs % 2
                reg = regs[k]
                ag = stats1024(reg, [f2_last[k]])
                vp = vop(lambda e: e.scalar_tensor_tensor(
                    out=SA[:, 2:3], in0=MV[:, 1:2], scalar=QS2[:, 4:5],
                    in1=EPSC[:, :], op0=OP.mult, op1=OP.add),
                    deps=last(ag, ms_eps, q_pj["s2"]))
                w2r = vop(lambda e: e.reciprocal(out=SA[:, 3:4],
                                                 in_=SA[:, 2:3]), deps=[vp])
                aa0 = act(SA[:, 4:5], SA[:, 3:4], AF.Sqrt, deps=[w2r])
                aa = vop(lambda e: e.tensor_tensor(
                    out=SA[:, 0:1], in0=SA[:, 4:5], in1=QS2[:, 3:4],
                    op=OP.mult), deps=[aa0])
                bb = vop(lambda e: e.scalar_tensor_tensor(
                    out=SA[:, 1:2], in0=MV[:, 0:1], scalar=-1.0,
                    in1=SA[:, 0:1], op0=OP.mult, op1=OP.mult), deps=[aa])
                olnt = act(OLN, reg, AF.Identity, bias=SA[:, 1:2],
                           scale=SA[:, 0:1], deps=last(bb, oln_prev))
                x2b = LKT[:, (2 + par) * 1024:(3 + par) * 1024]
                x2l = dma("sync", x2b,
                          x2_scr[t_abs * 128:(t_abs + 1) * 128, :],
                          deps=last(x2_sts[t_abs], x2ld2_prev[par]),
                          ctr_name="x2l")
                ob2 = FSCR[:, :] if par == 0 else FSC2[:, :]
                fin = vop(lambda e, x2b=x2b, ob2=ob2: e.tensor_add(
                    out=ob2, in0=x2b, in1=OLN),
                    deps=last(olnt, x2l, outt_prev[par]))
                oln_prev = fin
                x2ld2_prev[par] = fin
                stok = dma("sync", out_h[t_abs * 128:(t_abs + 1) * 128, :],
                           ob2, deps=[fin], ctr_name="st")
                outt_prev[par] = stok
                out_sts.append(stok)
                reg_free[k] = [fin]
        azq_rd = oln_prev

    waits("sync", last(*out_sts))

    with nc.Block() as block:
        @block.sync
        def _(eng):
            for nm, fn in ops:
                if nm == "sync":
                    fn(eng)

        @block.gpsimd
        def _(eng):
            for nm, fn in ops:
                if nm == "gpsimd":
                    fn(eng)

        @block.tensor
        def _(eng):
            for nm, fn in ops:
                if nm == "tensor":
                    fn(eng)

        @block.vector
        def _(eng):
            for nm, fn in ops:
                if nm == "vector":
                    fn(eng)

        @block.scalar
        def _(eng):
            for nm, fn in ops:
                if nm == "scalar":
                    fn(eng)

    stack.close()
    return nc, dbg_t


def _prep_inputs(inputs):
    import ml_dtypes
    bf = ml_dtypes.bfloat16
    f32 = np.float32
    ct = lambda a: np.ascontiguousarray(np.asarray(a), dtype=f32)

    common = {
        "wqkT": ct(np.asarray(inputs["L0_attn_Wqk"]).T),
        "wvT": ct(np.asarray(inputs["L0_attn_Wv"]).T),
        "wgT": ct(np.asarray(inputs["L0_attn_Wg"]).T),
        "woT": ct(np.asarray(inputs["L0_attn_Wo"]).T),
        "wfcT": ct(np.asarray(inputs["L0_ffn_W_fc"]).T),
        "wpjT": ct(np.asarray(inputs["L0_ffn_W_proj"]).T),
        "afcT": ct(np.asarray(inputs["L0_ffn_A_fc"]).T),
        "bfcT": ct(np.asarray(inputs["L0_ffn_B_fc"]).T),
        "apjT": ct(np.asarray(inputs["L0_ffn_A_proj"]).T),
        "bpjT": ct(np.asarray(inputs["L0_ffn_B_proj"]).T),
        "bqk": ct(np.asarray(inputs["L0_attn_bqk"]).reshape(NQ, 128).T),
        "bv": ct(np.asarray(inputs["L0_attn_bv"]).reshape(1, E)),
        "bg": ct(np.asarray(inputs["L0_attn_bg"]).reshape(NE, 128).T),
        "bo": ct(np.asarray(inputs["L0_attn_bo"]).reshape(1, D)),
        "bfc": ct(np.asarray(inputs["L0_ffn_b_fc"]).reshape(NF, 128).T),
        "bpj": ct(np.asarray(inputs["L0_ffn_b_proj"]).reshape(1, D)),
        "ln1w": ct(np.asarray(inputs["L0_attn_ln1_w"]).reshape(ND, 128).T),
        "ln1b": ct(np.asarray(inputs["L0_attn_ln1_b"]).reshape(ND, 128).T),
        "ln2wr": ct(np.broadcast_to(np.asarray(inputs["L0_attn_ln2_w"]),
                                    (128, D))),
        "ln2br": ct(np.broadcast_to(np.asarray(inputs["L0_attn_ln2_b"]),
                                    (128, D))),
        "gam": ct(np.asarray(inputs["L0_attn_gamma"]).reshape(4, 2, 128)
                  .transpose(2, 1, 0).reshape(128, 8)),
        "bet": ct(np.asarray(inputs["L0_attn_beta"]).reshape(4, 2, 128)
                  .transpose(2, 1, 0).reshape(128, 8)),
        "posemb": ct(np.asarray(inputs["L0_attn_pos_emb"]).reshape(65, 1)),
    }
    idx = np.arange(G)
    dist = np.clip(np.abs(idx[:, None] - idx[None, :]), 0, 32)
    onehot = np.zeros((65, G * G), dtype=bf)
    onehot[dist.reshape(-1), np.arange(G * G)] = 1
    common["onehot"] = onehot
    common["identb"] = np.eye(128, dtype=bf)
    common["identf"] = np.eye(128, dtype=f32)
    common["onesrow"] = np.ones((1, 128), dtype=bf)
    common["onesrowf"] = np.ones((1, 128), dtype=f32)
    x = np.ascontiguousarray(np.asarray(inputs["L0_attn_x"]), dtype=f32)
    return [dict(common, x=x[i]) for i in range(NCORES)]


def kernel(**inputs):
    in_maps = _prep_inputs(inputs)
    nc, _ = build_nc(dbg=False)
    res = run_bass_kernel_spmd(nc, in_maps, core_ids=list(range(NCORES)))
    out = np.stack([np.asarray(r["out"]) for r in res.results], axis=0)
    return out.astype(np.float32)

# version 12
# speedup vs baseline: 1.2310x; 1.2310x over previous
import math
import numpy as np
import concourse.bass as bass
import concourse.mybir as mybir
from concourse.bass_utils import run_bass_kernel_spmd

F32 = mybir.dt.float32
BF16 = mybir.dt.bfloat16
I32 = mybir.dt.int32
AX = mybir.AxisListType.X
AF = mybir.ActivationFunctionType
OP = mybir.AluOpType

NCORES = 8
T = 2048
D = 1024
DQK = 256
E = 2048
FF = 4096
G = 64
NT = T // 128
ND = D // 128
NE = E // 128
NQ = DQK // 128
NF = FF // 128
EPS = 1e-5
PI = math.pi
INV2PI = 1.0 / (2.0 * PI)
PI_LO = float(np.nextafter(np.float32(np.pi), np.float32(0.0)))
LORA_SCALING = 2.0


def build_nc(dbg=False):
    nc = bass.Bass("TRN2", target_bir_lowering=False)

    dr = lambda name, shape, dt, kind: nc.dram_tensor(name, shape, dt, kind=kind).ap()
    x_h = dr("x", [T, D], F32, "ExternalInput")
    wqkT = dr("wqkT", [D, DQK], F32, "ExternalInput")
    wvT = dr("wvT", [D, E], F32, "ExternalInput")
    wgT = dr("wgT", [D, E], F32, "ExternalInput")
    woT = dr("woT", [E, D], F32, "ExternalInput")
    wfcT = dr("wfcT", [D, FF], F32, "ExternalInput")
    wpjT = dr("wpjT", [FF, D], F32, "ExternalInput")
    afcT_h = dr("afcT", [D, 16], F32, "ExternalInput")
    bfcT_h = dr("bfcT", [16, FF], F32, "ExternalInput")
    apjT_h = dr("apjT", [FF, 16], F32, "ExternalInput")
    bpjT_h = dr("bpjT", [16, D], F32, "ExternalInput")
    bqk_h = dr("bqk", [128, NQ], F32, "ExternalInput")
    bv_h = dr("bv", [1, E], F32, "ExternalInput")
    bg_h = dr("bg", [128, NE], F32, "ExternalInput")
    bo_h = dr("bo", [1, D], F32, "ExternalInput")
    bfc_h = dr("bfc", [128, NF], F32, "ExternalInput")
    bpj_h = dr("bpj", [1, D], F32, "ExternalInput")
    ln1w_h = dr("ln1w", [128, ND], F32, "ExternalInput")
    ln1b_h = dr("ln1b", [128, ND], F32, "ExternalInput")
    ln2w_h = dr("ln2wr", [128, D], F32, "ExternalInput")
    ln2b_h = dr("ln2br", [128, D], F32, "ExternalInput")
    gam_h = dr("gam", [128, 8], F32, "ExternalInput")
    bet_h = dr("bet", [128, 8], F32, "ExternalInput")
    pos_h = dr("posemb", [65, 1], F32, "ExternalInput")
    oneh_h = dr("onehot", [65, G * G], BF16, "ExternalInput")
    ident_h = dr("identb", [128, 128], BF16, "ExternalInput")
    identf_h = dr("identf", [128, 128], F32, "ExternalInput")
    ones_h = dr("onesrow", [1, 128], BF16, "ExternalInput")
    onesf_h = dr("onesrowf", [1, 128], F32, "ExternalInput")
    out_h = dr("out", [T, D], F32, "ExternalOutput")
    sc_fc = nc.dram_tensor("scr_fc", [D, FF], BF16, kind="Internal").ap()
    scr1 = nc.dram_tensor("scr_1", [1, 1], F32, kind="Internal").ap()
    sc_pj = nc.dram_tensor("scr_pj", [FF, D], BF16, kind="Internal").ap()
    x2_scr = nc.dram_tensor("scr_x2", [T, D], BF16, kind="Internal").ap()

    dbg_t = {}
    if dbg:
        for nm, shape in (("d_qs", [128, 8]), ("d_qs2", [128, 8]),
                          ("d_mxp", [128, 32]),
                          ("d_y", [128, ND * T]), ("d_qk", [128, NQ * T]),
                          ("d_v", [128, NT * E]), ("d_lkv", [128, 2 * E]),
                          ("d_x2", [T, D]), ("d_att", [128, 512]),
                          ("d_cos", [128, T]), ("d_sin", [128, T]),
                          ("d_wfc", [D, FF])):
            dbg_t[nm] = dr(nm, shape, F32, "ExternalOutput")

    al = nc.alloc_sbuf_tensor
    GAMT = al("GAMT", [128, 8], F32)
    BETT = al("BETT", [128, 8], F32)
    BQK = al("BQK", [128, NQ], F32)
    BGC = al("BGC", [128, NE], F32)
    BFCC = al("BFCC", [128, NF], F32)
    LN1W = al("LN1W", [128, ND], F32)
    LN1B = al("LN1B", [128, ND], F32)
    LN2W = al("LN2W", [128, D], BF16)
    LN2B = al("LN2B", [128, D], BF16)
    BRW = al("BRW", [1, E], BF16)
    POSE = al("POSE", [65, 1], BF16)
    IDENT = al("IDENT", [128, 128], BF16)
    IDENTF = al("IDENTF", [128, 128], F32)
    ONES = al("ONES", [1, 128], BF16)
    ZROW = al("ZROW", [1, 128], BF16)
    ONESF = al("ONESF", [1, 128], F32)
    BIAS64 = al("BIAS64", [64, 64], F32)
    EPSC = al("EPSC", [128, 1], F32)
    ST6 = al("ST6", [128, 12], F32)
    MV = al("MV", [128, 2], F32)
    SA = al("SA", [128, 8], F32)
    MXP = al("MXP", [128, 32], F32)
    QS = al("QS", [128, 8], F32)
    QS2 = al("QS2", [128, 8], F32)
    COS = al("COS", [128, T], BF16)
    SIN = al("SIN", [128, T], BF16)
    XT = al("XT", [128, 2 * D], F32)
    FSCR = al("FSCR", [128, 1024], F32)
    FSC2 = al("FSC2", [128, 1024], F32)
    ISCR = al("ISCR", [128, 1024], I32)
    AV = al("AV", [128, NT * E], BF16)
    AY = al("AY", [128, ND * T], BF16)
    AGT = al("AGT", [128, NE * 512], BF16)
    AZQ = al("AZQ", [128, 4096], BF16)
    AMSC = al("AMSC", [128, 2048], BF16)
    QK = al("QK", [128, NQ * T], BF16)
    LINKV = al("LINKV", [128, 2 * E], BF16)
    AW = al("AW", [128, 4096], BF16)
    WSLB = al("WSLB", [128, 2048], BF16)
    LKT = al("LKT", [128, NT * DQK], BF16)
    QQ = al("QQ", [128, 1024], BF16)
    QKK = al("QKK", [128, 1024], BF16)
    LQ = al("LQ", [128, 1024], BF16)
    GATEE = al("GATEE", [128, 1024], BF16)
    ATT = al("ATT", [128, 512], BF16)

    ONEH = AV[0:65, 0:G * G]
    SQT = FSCR[:, 0:256]
    M1 = AMSC[:, 0:512]
    M2 = AMSC[:, 512:1024]
    UB0 = AMSC[:, 1024:1536]
    UB1 = AMSC[:, 1536:2048]
    Z2 = AZQ[:, 0:1024]
    T2E = AZQ[:, 1024:2048]
    OLN = AZQ[:, 2048:3072]
    BPJT = QQ[0:16, 0:1024]
    APJT = QKK[:, 0:512]
    AFCT = QKK[:, 512:640]
    T1 = LQ[0:16, 0:1024]
    T2L = GATEE[0:16, 0:1024]
    BFCTA = COS[0:16, :]
    BFCTB = SIN[0:16, :]

    PBIG = nc.alloc_psum_tensor("PBIG", [128, 2048], F32)
    PA = nc.alloc_psum_tensor("PA", [128, 1024], F32)
    PC = nc.alloc_psum_tensor("PC", [128, 512], F32)
    PD = nc.alloc_psum_tensor("PD", [128, 512], F32)
    PCD = [PC, PD]

    ops = []
    EMIT = {k: (lambda fn, k=k: ops.append((k, fn)))
            for k in ("tensor", "vector", "scalar", "sync", "gpsimd")}

    class Ctr:
        def __init__(self, h):
            self.h = h
            self.n = 0

    from contextlib import ExitStack
    stack = ExitStack()
    sem = {k: Ctr(stack.enter_context(nc.semaphore(k)))
           for k in ("ld", "gp", "gpc", "te", "ve", "se", "st",
                     "lps", "b64", "qsc",
                     "qld0", "qld1", "xs0", "xs1", "wgl0", "wgl1",
                     "wol0", "wol1", "xrl0", "xrl1", "wvl0", "wvl1",
                     "wfl0", "wfl1", "wpl0", "wpl1", "x2l0", "x2l1",
                     "qst0", "qst1")}
    pipetog = {}
    hw = {e: {} for e in EMIT}

    last_dma = {e: None for e in EMIT}

    def wait(engine, ctr, val):
        if val is None or val <= 0:
            return
        if val > hw[engine].get(id(ctr), 0):
            hw[engine][id(ctr)] = val
            last_dma[engine] = None
            EMIT[engine](lambda e, h=ctr.h, v=val: e.wait_ge(h, v))

    def waits(engine, deps):
        for d in deps:
            if d is None:
                continue
            if isinstance(d, list):
                waits(engine, d)
            else:
                wait(engine, d[0], d[1])

    def last(*tokens):
        out = {}
        for t in tokens:
            if t is None:
                continue
            if isinstance(t, list):
                for u in t:
                    if u is not None and out.get(id(u[0]), (None, 0))[1] < u[1]:
                        out[id(u[0])] = u
                continue
            if out.get(id(t[0]), (None, 0))[1] < t[1]:
                out[id(t[0])] = t
        return list(out.values())

    def dma(engine, out_ap, in_ap, deps=(), ctr_name=None):
        waits(engine, deps)
        if ctr_name is not None and (ctr_name + "0") in sem:
            t = pipetog.get(ctr_name, 0)
            pipetog[ctr_name] = 1 - t
            c = sem[ctr_name + str(t)]
        else:
            c = sem[ctr_name] if ctr_name else (
                sem["ld"] if engine == "sync" else sem["gp"])
        wait(engine, c, c.n)
        c.n += 16
        n = c.n
        EMIT[engine](lambda e, o=out_ap, i=in_ap, h=c.h:
                     e.dma_start(out=o, in_=i).then_inc(h, 16))
        return (c, n)

    def mm(lhsT, rhs, out, start, stop, inc=False, deps=()):
        waits("tensor", deps)
        if inc:
            sem["te"].n += 1
            n = sem["te"].n
            EMIT["tensor"](lambda e, a=lhsT, b=rhs, o=out, s1=start, s2=stop,
                           h=sem["te"].h:
                           e.matmul(o, a, b, start=s1, stop=s2).then_inc(h, 1))
            return (sem["te"], n)
        EMIT["tensor"](lambda e, a=lhsT, b=rhs, o=out, s1=start, s2=stop:
                       e.matmul(o, a, b, start=s1, stop=s2))
        return None

    def act(out_ap, in_ap, func, bias=0.0, scale=1.0, deps=()):
        waits("scalar", deps)
        sem["se"].n += 1
        n = sem["se"].n
        EMIT["scalar"](lambda e, o=out_ap, i=in_ap, f=func, b=bias, s=scale,
                       h=sem["se"].h:
                       e.activation(out=o, in_=i, func=f, bias=b, scale=s)
                       .then_inc(h, 1))
        return (sem["se"], n)

    def vop(emitfn, deps=()):
        waits("vector", deps)
        sem["ve"].n += 1
        n = sem["ve"].n
        EMIT["vector"](lambda e, f=emitfn, h=sem["ve"].h: f(e).then_inc(h, 1))
        return (sem["ve"], n)

    def gop(emitfn, deps=()):
        waits("gpsimd", deps)
        sem["gpc"].n += 1
        n = sem["gpc"].n
        EMIT["gpsimd"](lambda e, f=emitfn, h=sem["gpc"].h: f(e).then_inc(h, 1))
        return (sem["gpc"], n)

    def flush_mm(region, deps=(), m=128):
        return mm(ZROW[:, 0:m], ONES[:, :], region, False, True, inc=True,
                  deps=deps)

    ld_gam = dma("sync", GAMT[:, :], gam_h)
    ld_bet = dma("sync", BETT[:, :], bet_h)
    ld_bqk = dma("sync", BQK[:, :], bqk_h)
    ld_bg = dma("sync", BGC[:, :], bg_h)
    ld_bfc = dma("sync", BFCC[:, :], bfc_h)
    ld_l1w = dma("sync", LN1W[:, :], ln1w_h)
    ld_l1b = dma("sync", LN1B[:, :], ln1b_h)
    ld_onh = dma("sync", ONEH, oneh_h)
    ld_id = dma("sync", IDENT[:, :], ident_h)
    ld_on = dma("sync", ONES[:, :], ones_h)
    ld_onf = dma("sync", ONESF[:, :], onesf_h)
    ld_l2w = dma("gpsimd", LN2W[:, :], ln2w_h)
    ld_l2b = dma("gpsimd", LN2B[:, :], ln2b_h)
    ld_bvr = dma("gpsimd", BRW[:, :], bv_h)
    ld_pos = dma("gpsimd", POSE[:, :], pos_h)
    pre_ld = (sem["ld"], sem["ld"].n)
    ld_gam = ld_bet = ld_bqk = ld_bg = ld_bfc = pre_ld
    ld_l1w = ld_l1b = ld_onh = ld_id = ld_on = ld_onf = pre_ld
    pre_gp = (sem["gp"], sem["gp"].n)
    ld_l2w = ld_l2b = ld_bvr = ld_pos = pre_gp
    iot1 = gop(lambda e: e.iota(FSCR[0:1, 0:128], [[1, 128]],
                                channel_multiplier=0,
                                allow_small_or_imprecise_dtypes=True))
    ms_att = gop(lambda e: e.memset(ATT[:, :], 0.0))
    ms_zr = gop(lambda e: e.memset(ZROW[:, :], 0.0))
    ms_eps = gop(lambda e: e.memset(EPSC[:, :], EPS))
    a_invf = act(IDENTF[0:1, 0:128], FSCR[0:1, 0:128], AF.Exp,
                 scale=-math.log(10000.0) / 128.0, deps=[iot1])

    def sincos_half2(hh, shift, dst_half, pdeps):
        posr = FSC2[0:1, 0:1024]
        pio = gop(lambda e, hh=hh: e.iota(posr, [[1, 1024]], base=hh * 1024,
                                          channel_multiplier=0,
                                          allow_small_or_imprecise_dtypes=True),
                  deps=list(pdeps))
        mmdeps = last(a_invf, pio) + list(pdeps)
        for j in range(2):
            ttok = mm(IDENTF[0:1, 0:128], posr[:, j * 512:(j + 1) * 512],
                      PA[:, j * 512:(j + 1) * 512], True, True, inc=(j == 1),
                      deps=mmdeps if j == 0 else ())
        ts = vop(lambda e: e.tensor_scalar(
            out=FSCR[:, 0:1024], in0=PA[:, :], scalar1=INV2PI, scalar2=shift,
            op0=OP.mult, op1=OP.add), deps=last(ttok))
        k1 = vop(lambda e: e.tensor_single_scalar(
            out=ISCR[:, 0:1024], in_=FSCR[:, 0:1024], scalar=1.0,
            op=OP.mult), deps=[ts])
        c2 = vop(lambda e: e.scalar_tensor_tensor(
            out=XT[:, 0:1024], in0=FSCR[:, 0:1024], scalar=1.0,
            in1=ISCR[:, 0:1024], op0=OP.mult, op1=OP.subtract), deps=[k1])
        km = vop(lambda e: e.scalar_tensor_tensor(
            out=XT[:, 1024:2048], in0=XT[:, 0:1024], scalar=-0.5,
            in1=ISCR[:, 0:1024], op0=OP.is_lt, op1=OP.subtract), deps=[c2])
        kf = vop(lambda e: e.scalar_tensor_tensor(
            out=XT[:, 0:1024], in0=XT[:, 0:1024], scalar=0.5,
            in1=XT[:, 1024:2048], op0=OP.is_gt, op1=OP.subtract), deps=[km])
        dd = vop(lambda e: e.scalar_tensor_tensor(
            out=FSCR[:, 0:1024], in0=XT[:, 0:1024], scalar=-1.0,
            in1=FSCR[:, 0:1024], op0=OP.mult, op1=OP.add), deps=[kf])
        r1 = vop(lambda e: e.tensor_scalar(
            out=FSCR[:, 0:1024], in0=FSCR[:, 0:1024], scalar1=2.0 * PI,
            scalar2=PI_LO, op0=OP.mult, op1=OP.min), deps=[dd])
        r2 = vop(lambda e: e.tensor_single_scalar(
            out=FSCR[:, 0:1024], in_=FSCR[:, 0:1024], scalar=-PI_LO,
            op=OP.max), deps=[r1])
        return act(dst_half, FSCR[:, 0:1024], AF.Sin, deps=[r2]), ttok

    tbl_toks = []
    mm_last = None
    for hh in range(2):
        prev = [tbl_toks[-1]] if tbl_toks else []
        tk, mm_last = sincos_half2(hh, 0.0,
                                   SIN[:, hh * 1024:(hh + 1) * 1024], prev)
        tbl_toks.append(tk)
        tk, mm_last = sincos_half2(hh, 0.25,
                                   COS[:, hh * 1024:(hh + 1) * 1024],
                                   [tbl_toks[-1]])
        tbl_toks.append(tk)
    tables_done = tbl_toks[-1]
    s_sin = tables_done
    s_cos = tables_done

    ev_prev = tables_done
    b64_dmas = []
    for j in range(8):
        tok = mm(POSE[:, :], ONEH[:, j * 512:(j + 1) * 512], PC[0:1, :],
                 True, True, inc=True,
                 deps=last(ld_pos, ld_onh, ev_prev,
                           b64_dmas[-1] if b64_dmas else None))
        ev_prev = vop(lambda e: e.tensor_copy(
            out=FSCR[0:1, 0:512], in_=PC[0:1, :]), deps=[tok])
        b64_dmas.append(dma("sync", BIAS64[8 * j:8 * (j + 1), :],
                            FSCR[0:1, 0:512], deps=[ev_prev],
                            ctr_name="b64"))
    ld_b64 = last(*b64_dmas)
    pc_free = ev_prev
    pd_free = None

    def stats1024(src, deps):
        s1 = vop(lambda e: e.bn_stats(out=ST6[:, 0:6], in_=src[:, 0:512]),
                 deps=deps)
        s2 = vop(lambda e: e.bn_stats(out=ST6[:, 6:12], in_=src[:, 512:1024]),
                 deps=[s1])
        return vop(lambda e: e.bn_aggr(out=MV[:, :], in_=ST6[:, 0:12]),
                   deps=[s2])

    def ln_single(vcol, mcol, acol, bcol, deps):
        w2 = vop(lambda e, v=vcol: e.tensor_scalar(
            out=SA[:, 2:3], in0=v, scalar1=1.0, scalar2=EPS,
            op0=OP.mult, op1=OP.add), deps=deps)
        w2r = vop(lambda e: e.reciprocal(out=SA[:, 3:4], in_=SA[:, 2:3]),
                  deps=[w2])
        aa = act(acol, SA[:, 3:4], AF.Sqrt, deps=[w2r])
        return vop(lambda e, m=mcol, a=acol, b=bcol: e.scalar_tensor_tensor(
            out=b, in0=m, scalar=-1.0, in1=a, op0=OP.mult, op1=OP.mult),
            deps=[aa])

    def ln_double(vcol, mcol, acol, bcol, deps):
        w2 = vop(lambda e, v=vcol: e.tensor_scalar(
            out=SA[:, 2:3], in0=v, scalar1=1.0, scalar2=EPS,
            op0=OP.mult, op1=OP.add), deps=deps)
        w2r = vop(lambda e: e.reciprocal(out=SA[:, 3:4], in_=SA[:, 2:3]),
                  deps=[w2])
        v2 = vop(lambda e, v=vcol: e.tensor_tensor(
            out=SA[:, 4:5], in0=v, in1=SA[:, 3:4], op=OP.mult), deps=[w2r])
        w4 = vop(lambda e: e.tensor_scalar(
            out=SA[:, 5:6], in0=SA[:, 4:5], scalar1=1.0, scalar2=EPS,
            op0=OP.mult, op1=OP.add), deps=[v2])
        w4r = vop(lambda e: e.reciprocal(out=SA[:, 6:7], in_=SA[:, 5:6]),
                  deps=[w4])
        pr = vop(lambda e: e.tensor_tensor(
            out=SA[:, 7:8], in0=SA[:, 3:4], in1=SA[:, 6:7], op=OP.mult),
            deps=[w4r])
        aa = act(acol, SA[:, 7:8], AF.Sqrt, deps=[pr])
        return vop(lambda e, m=mcol, a=acol, b=bcol: e.scalar_tensor_tensor(
            out=b, in0=m, scalar=-1.0, in1=a, op0=OP.mult, op1=OP.mult),
            deps=[aa])

    ev_tok = None
    zg_rd = [None] * 4
    prev_z = None
    for g in range(4):
        z_toks = []
        for tt in range(4):
            t = g * 4 + tt
            buf = tt % 2
            xs = XT[:, buf * D:(buf + 1) * D]
            xdep = [prev_z] if t >= 2 else [tables_done, ld_b64]
            xld = dma("sync", xs, x_h[t * 128:(t + 1) * 128, :], deps=xdep,
                      ctr_name="xs")
            ag = stats1024(xs, [xld])
            bb = ln_double(MV[:, 1:2], MV[:, 0:1], SA[:, 0:1], SA[:, 1:2],
                           [ag])
            ztok = vop(lambda e, xs=xs, tt=tt: e.tensor_scalar(
                out=AZQ[:, tt * D:(tt + 1) * D], in0=xs,
                scalar1=SA[:, 0:1], scalar2=SA[:, 1:2],
                op0=OP.mult, op1=OP.add),
                deps=last(bb, zg_rd[g - 1] if g >= 1 else None))
            z_toks.append(ztok)
            prev_z = ztok
        for c in range(ND):
            q = c % 4
            reg = PBIG[:, q * 512:(q + 1) * 512]
            deps = last(*(z_toks + [ev_tok, ld_id]))
            for tt in range(4):
                tok = mm(AZQ[:, tt * D + c * 128: tt * D + (c + 1) * 128],
                         IDENT[:, :], reg[:, tt * 128:(tt + 1) * 128],
                         True, True, inc=(tt == 3),
                         deps=deps if tt == 0 else ())
            ev_tok = act(AY[:, c * T + g * 512: c * T + (g + 1) * 512],
                         reg, AF.Identity, bias=LN1B[:, c:c + 1],
                         scale=LN1W[:, c:c + 1],
                         deps=last(tok, ld_l1w, ld_l1b))
            if c == ND - 1:
                zg_rd[g] = tok
    y_done = ev_tok
    Y = AY

    pcd_free = [pc_free, pd_free]
    aw_free = [None, None]
    qk_evs = []
    for qc in range(NQ):
        wb = qc % 2
        wsl = AW[:, wb * 1024:(wb + 1) * 1024]
        wqk_ld = dma("gpsimd", wsl.rearrange("p (c k) -> p c k", c=ND),
                     wqkT[:, qc * 128:(qc + 1) * 128]
                     .rearrange("(c p) k -> p c k", p=128),
                     deps=last(aw_free[wb]), ctr_name="wvl")
        for tc in range(4):
            pi = (qc * 4 + tc) % 2
            ps = PCD[pi]
            deps = last(wqk_ld, y_done, pcd_free[pi])
            for c in range(ND):
                tok = mm(wsl[:, c * 128:(c + 1) * 128],
                         Y[:, c * T + tc * 512: c * T + (tc + 1) * 512],
                         ps[:, :], c == 0, c == ND - 1, inc=(c == ND - 1),
                         deps=deps if c == 0 else ())
            ev = act(QK[:, qc * T + tc * 512: qc * T + (tc + 1) * 512],
                     ps[:, :], AF.Silu, bias=BQK[:, qc:qc + 1],
                     deps=last(tok, ld_bqk))
            pcd_free[pi] = ev
            qk_evs.append(ev)
        aw_free[wb] = tok
    qk_done = last(*qk_evs)
    V = AV

    v_evs = []
    V3 = V[:, :].rearrange("p (t k) -> p t k", t=NT)
    for oc in range(8):
        wb = oc % 2
        wsl = AW[:, wb * 2048:(wb + 1) * 2048]
        wdep = last(aw_free[wb]) if oc < 2 else [v_evs[8 * (oc - 2) + 7]]
        wv_ld = dma("gpsimd", wsl.rearrange("p (c k) -> p c k", c=ND),
                    wvT[:, oc * 256:(oc + 1) * 256]
                    .rearrange("(c p) k -> p c k", p=128),
                    deps=wdep, ctr_name="wvl")
        for tg in range(8):
            pi = tg % 2
            ps = PCD[pi]
            deps = last(wv_ld, y_done, pcd_free[pi], ld_on, ld_bvr)
            for tt in range(2):
                t = 2 * tg + tt
                for c in range(ND):
                    mm(Y[:, c * T + t * 128: c * T + (t + 1) * 128],
                       wsl[:, c * 256:(c + 1) * 256],
                       ps[:, tt * 256:(tt + 1) * 256], c == 0, False,
                       deps=deps if (c == 0 and tt == 0) else ())
                tok = mm(ONES[:, :], BRW[:, oc * 256:(oc + 1) * 256],
                         ps[:, tt * 256:(tt + 1) * 256], False, True,
                         inc=(tt == 1))
            ev = act(V3[:, 2 * tg:2 * tg + 2, oc * 256:(oc + 1) * 256],
                     ps[:, :].rearrange("p (t k) -> p t k", t=2),
                     AF.Silu, deps=[tok])
            pcd_free[pi] = ev
            v_evs.append(ev)
    aw_free = [v_evs[-1], v_evs[-2]]
    v_done = last(*v_evs)

    def head_rope(hidx, tok_off, dst, d0_off, d1_off, deps):
        n = 512
        a0 = act(UB0, QK[:, tok_off:tok_off + n], AF.Identity,
                 bias=BETT[:, hidx:hidx + 1], scale=GAMT[:, hidx:hidx + 1],
                 deps=deps)
        a1 = act(UB1, QK[:, T + tok_off:T + tok_off + n], AF.Identity,
                 bias=BETT[:, 4 + hidx:4 + hidx + 1],
                 scale=GAMT[:, 4 + hidx:4 + hidx + 1], deps=deps)
        cs = COS[:, tok_off:tok_off + n]
        sn = SIN[:, tok_off:tok_off + n]
        m1 = vop(lambda e: e.tensor_mul(out=M1, in0=UB0, in1=cs),
                 deps=last(a0, s_cos))
        m2 = vop(lambda e: e.tensor_mul(out=M2, in0=UB1, in1=sn),
                 deps=last(a1, s_sin))
        o1 = vop(lambda e: e.tensor_sub(
            out=dst[:, d0_off:d0_off + n], in0=M1, in1=M2), deps=[m1, m2])
        m3 = vop(lambda e: e.tensor_mul(out=M1, in0=UB1, in1=cs), deps=[o1])
        m4 = vop(lambda e: e.tensor_mul(out=M2, in0=UB0, in1=sn), deps=[o1])
        o2 = vop(lambda e: e.tensor_add(
            out=dst[:, d1_off:d1_off + n], in0=M1, in1=M2), deps=[m3, m4])
        return (last(a0, a1), last(o1, o2))

    LKF = AZQ
    lk_toks = []
    for cc in range(4):
        dep = last(qk_done, ld_gam, ld_bet,
                   zg_rd[3] if cc == 0 else None,
                   lk_toks[-1] if lk_toks else None)
        _, o = head_rope(3, cc * 512, LKF, cc * 512, 2048 + cc * 512, dep)
        lk_toks.append(o)
    lk_done = last(*lk_toks)
    lkt_evs = []
    ev = None
    for c2 in range(2):
        for tg in range(4):
            deps = last(lk_done, ev, pcd_free[0])
            for tt in range(4):
                t = tg * 4 + tt
                tok = mm(LKF[:, c2 * 2048 + t * 128: c2 * 2048 + (t + 1) * 128],
                         IDENT[:, :], PC[:, tt * 128:(tt + 1) * 128],
                         True, True, inc=(tt == 3),
                         deps=deps if tt == 0 else ())
            ev = vop(lambda e, c2=c2, tg=tg: e.tensor_copy(
                out=LKT[:, :].rearrange("p (t k) -> p t k", t=NT)
                [:, tg * 4:(tg + 1) * 4, c2 * 128:(c2 + 1) * 128],
                in_=PC[:, :].rearrange("p (t k) -> p t k", t=4)),
                deps=[tok])
            lkt_evs.append(ev)
    lkt_done = last(*lkt_evs)
    pcd_free[0] = lkt_evs[-1]

    regions8 = [PBIG[:, 0:512], PBIG[:, 512:1024], PBIG[:, 1024:1536],
                PBIG[:, 1536:2048], PA[:, 0:512], PA[:, 512:1024],
                PC[:, :], PD[:, :]]
    lkv_evs = []
    for m in range(2):
        for n4 in range(4):
            r = regions8[m * 4 + n4]
            deps = last(lkt_done, v_done, pcd_free[0], pcd_free[1], ev_tok)
            for t in range(NT):
                tok = mm(LKT[:, t * DQK + m * 128: t * DQK + (m + 1) * 128],
                         V[:, t * E + n4 * 512: t * E + (n4 + 1) * 512],
                         r, t == 0, t == NT - 1, inc=(t == NT - 1),
                         deps=deps if t == 0 else ())
            lkv_evs.append(act(
                LINKV[:, m * E + n4 * 512: m * E + (n4 + 1) * 512], r,
                AF.Identity, scale=1.0 / T, deps=[tok]))
    lkv_done = last(*lkv_evs)

    ld_bor = dma("gpsimd", BRW[:, 0:1024], bo_h, deps=last(v_done),
                 ctr_name="lps")
    XB0 = LKT[:, 0:1024]
    XB1 = LKT[:, 1024:2048]
    X2S0 = LKT[:, 2048:3072]
    X2S1 = LKT[:, 3072:4096]

    x2_sts = [None] * NT
    pa_free = [lkv_evs[4], lkv_evs[5]]
    pcd_free = [lkv_evs[6], lkv_evs[7]]
    pbig_free = [last(lkv_evs[0], lkv_evs[1]), last(lkv_evs[2], lkv_evs[3])]
    gt_free = v_evs[-1]
    gatee_prev = [None, None]
    wo_prev = [None, None]
    z2_rd = lkt_done
    epi_z2 = lkt_done
    epi_ol = lkt_done
    xb_prev = [lkv_done, lkv_done]
    x2s_prev = [None, None]

    for C in range(4):
        base = C * 512
        d0 = last(qk_done, ms_att, lkv_done, z2_rd)
        aq, hq = head_rope(0, base, QQ, 0, 512, d0)
        ak, hk = head_rope(2, base, QKK, 0, 512, [hq])
        al_, hl = head_rope(1, base, LQ, 0, 512, [hk])
        sdeps = last(hq, hk, pcd_free[0])
        for p in range(4):
            for c2 in range(2):
                mm(QKK[:, c2 * 512 + p * 128: c2 * 512 + (p + 1) * 128],
                   QQ[:, c2 * 512 + p * 128: c2 * 512 + (p + 1) * 128],
                   PC[:, p * 128:(p + 1) * 128], c2 == 0,
                   c2 == 1 and p != 3,
                   deps=sdeps if (p == 0 and c2 == 0) else ())
        tok = flush_mm(PC[:, 384:512], deps=last(ms_zr, ld_on))
        pc3 = PC[:, :].rearrange("p (c k) -> p c k", c=4)
        sq3 = SQT[:, :].rearrange("p (c k) -> p c k", c=4)
        at3 = ATT[:, :].rearrange("p (c k) -> p c k", c=4)
        b3 = BIAS64[0:64, :].unsqueeze(1).broadcast_to([64, 4, 64])
        sq_t = vop(lambda e: e.scalar_tensor_tensor(
            out=sq3[0:64, :, :], in0=pc3[0:64, :, 0:64], scalar=1.0 / G,
            in1=b3, op0=OP.mult, op1=OP.add), deps=last(tok, ld_b64))
        sq_b = vop(lambda e: e.scalar_tensor_tensor(
            out=sq3[64:128, :, :], in0=pc3[64:128, :, 64:128],
            scalar=1.0 / G, in1=b3, op0=OP.mult, op1=OP.add), deps=[sq_t])
        at_t = vop(lambda e: e.scalar_tensor_tensor(
            out=at3[0:64, :, 0:64], in0=sq3[0:64, :, :], scalar=0.0,
            in1=sq3[0:64, :, :], op0=OP.max, op1=OP.mult), deps=[sq_t])
        at_b = vop(lambda e: e.scalar_tensor_tensor(
            out=at3[64:128, :, 64:128], in0=sq3[64:128, :, :], scalar=0.0,
            in1=sq3[64:128, :, :], op0=OP.max, op1=OP.mult), deps=[sq_b])
        att_done = last(at_t, at_b)
        pcd_free[0] = at_b
        if dbg and C == 0:
            dma("gpsimd", dbg_t["d_att"], ATT[:, :], deps=[att_done],
                ctr_name="lps")

        gt_toks = []
        for Ei in range(NE):
            wb = Ei % 2
            prev_wg = gt_toks[-2] if len(gt_toks) >= 2 else gt_free
            wg_ld = dma("gpsimd", AW[:, wb * 1024:(wb + 1) * 1024]
                        .rearrange("p (c k) -> p c k", c=ND),
                        wgT[:, Ei * 128:(Ei + 1) * 128]
                        .rearrange("(c p) k -> p c k", p=128),
                        deps=last(prev_wg), ctr_name="wgl")
            pi = Ei % 2
            gdeps = last(wg_ld, pcd_free[pi], y_done)
            for c in range(ND):
                tok = mm(AW[:, wb * 1024 + c * 128: wb * 1024 + (c + 1) * 128],
                         Y[:, c * T + base: c * T + base + 512],
                         PCD[pi][:, :], c == 0, c == ND - 1,
                         inc=(c == ND - 1), deps=gdeps if c == 0 else ())
            gprev = gt_toks[-2] if len(gt_toks) >= 2 else gatee_prev[pi]
            gev = act(GATEE[:, 0:512] if pi == 0 else GATEE[:, 512:1024],
                      PCD[pi][:, :], AF.Silu, bias=BGC[:, Ei:Ei + 1],
                      deps=last(tok, ld_bg, gprev))
            pcd_free[pi] = gev
            ph = Ei % 2
            adeps = last(pa_free[ph], lkv_done, hl, att_done, v_done)
            for m in range(2):
                mm(LINKV[:, m * E + Ei * 128: m * E + (Ei + 1) * 128],
                   LQ[:, m * 512:(m + 1) * 512],
                   PA[:, ph * 512:(ph + 1) * 512], m == 0, False,
                   deps=adeps if m == 0 else ())
            for p in range(4):
                mm(V[:, (4 * C + p) * E + Ei * 128:
                     (4 * C + p) * E + (Ei + 1) * 128],
                   ATT[:, p * 128:(p + 1) * 128],
                   PA[:, ph * 512 + p * 128: ph * 512 + (p + 1) * 128],
                   False, False)
            tok2 = flush_mm(PA[:, ph * 512: ph * 512 + 128])
            gt = vop(lambda e, Ei=Ei, ph=ph, pi=pi: e.tensor_tensor(
                out=AGT[:, Ei * 512:(Ei + 1) * 512],
                in0=PA[:, ph * 512:(ph + 1) * 512],
                in1=GATEE[:, pi * 512:(pi + 1) * 512], op=OP.mult),
                deps=last(tok2, gev))
            pa_free[ph] = gt
            gt_toks.append(gt)
        gts = last(*gt_toks)

        regs4 = (PBIG[:, 0:1024], PBIG[:, 1024:2048], PA[:, :], None)
        hd = last(gts, pbig_free[0], pbig_free[1], pa_free[0], pa_free[1],
                  pcd_free[0], pcd_free[1])
        for Ei in range(NE):
            wb = Ei % 2
            wo_ld = dma("gpsimd", WSLB[:, wb * 1024:(wb + 1) * 1024],
                        woT[Ei * 128:(Ei + 1) * 128, :],
                        deps=last(wo_prev[wb]), ctr_name="wol")
            wdeps = last(wo_ld, gts) + (hd if Ei == 0 else [])
            for k in range(4):
                for nh in range(2):
                    dst = (PCD[nh][:, :] if k == 3 else
                           regs4[k][:, nh * 512:(nh + 1) * 512])
                    tok = mm(AGT[:, Ei * 512 + k * 128:
                                 Ei * 512 + (k + 1) * 128],
                             WSLB[:, wb * 1024 + nh * 512:
                                  wb * 1024 + (nh + 1) * 512],
                             dst, Ei == 0, False,
                             inc=(k == 3 and nh == 1),
                             deps=wdeps if (k == 0 and nh == 0) else ())
            wo_prev[wb] = tok
        wo_last = []
        for k in range(4):
            for nh in range(2):
                dst = (PCD[nh][:, :] if k == 3 else
                       regs4[k][:, nh * 512:(nh + 1) * 512])
                tok = mm(ONES[:, :], BRW[:, nh * 512:(nh + 1) * 512],
                         dst, False, True, inc=(nh == 1),
                         deps=last(ld_bor, ld_on))
            wo_last.append(tok)
        for k in range(4):
            t_abs = 4 * C + k
            if k == 3:
                s1 = vop(lambda e: e.bn_stats(out=ST6[:, 0:6], in_=PC[:, :]),
                         deps=[wo_last[k]])
                s2 = vop(lambda e: e.bn_stats(out=ST6[:, 6:12], in_=PD[:, :]),
                         deps=[s1])
                ag = vop(lambda e: e.bn_aggr(out=MV[:, :], in_=ST6[:, 0:12]),
                         deps=[s2])
            else:
                ag = stats1024(regs4[k], [wo_last[k]])
            bb = ln_single(MV[:, 1:2], MV[:, 0:1], SA[:, 0:1],
                           SA[:, 1:2], [ag])
            if k == 3:
                z2t_a = act(Z2[:, 0:512], PC[:, :], AF.Identity,
                            bias=SA[:, 1:2], scale=SA[:, 0:1],
                            deps=last(bb, epi_z2))
                z2t = act(Z2[:, 512:1024], PD[:, :], AF.Identity,
                          bias=SA[:, 1:2], scale=SA[:, 0:1], deps=[z2t_a])
                pcd_free = [z2t, z2t]
            else:
                z2t = act(Z2, regs4[k], AF.Identity, bias=SA[:, 1:2],
                          scale=SA[:, 0:1], deps=last(bb, epi_z2))
            t2a = vop(lambda e: e.tensor_mul(out=T2E, in0=Z2,
                                             in1=LN2W[:, :]),
                      deps=last(z2t, ld_l2w))
            epi_z2 = t2a
            t2b = vop(lambda e: e.tensor_add(out=T2E, in0=T2E,
                                             in1=LN2B[:, :]),
                      deps=last(t2a, ld_l2b))
            ag2 = stats1024(T2E, [t2b])
            bb2 = ln_single(MV[:, 1:2], MV[:, 0:1], SA[:, 0:1],
                            SA[:, 1:2], [ag2])
            olnt = act(OLN, T2E, AF.Identity, bias=SA[:, 1:2],
                       scale=SA[:, 0:1], deps=last(bb2, epi_ol))
            par = t_abs % 2
            xb = XB0 if par == 0 else XB1
            x2s = X2S0 if par == 0 else X2S1
            xre = dma("gpsimd", xb, x_h[t_abs * 128:(t_abs + 1) * 128, :],
                      deps=last(xb_prev[par]), ctr_name="xrl")
            fin = vop(lambda e, xb=xb, x2s=x2s: e.tensor_add(
                out=x2s, in0=xb, in1=OLN),
                deps=last(olnt, xre, x2s_prev[par]))
            epi_ol = fin
            xb_prev[par] = fin
            stok = dma("sync", x2_scr[t_abs * 128:(t_abs + 1) * 128, :],
                       x2s, deps=[fin], ctr_name="st")
            x2s_prev[par] = stok
            x2_sts[t_abs] = stok
            if k < 2:
                pbig_free[k] = fin
            elif k == 2:
                pa_free = [fin, fin]
        gt_free = wo_last[-1]
        z2_rd = gts

    x2_done = last(*[x2_sts[t] for t in range(NT)])
    if dbg:
        for nm, src in (("d_y", Y), ("d_qk", QK), ("d_v", V),
                        ("d_lkv", LINKV), ("d_cos", COS), ("d_sin", SIN)):
            dma("gpsimd", dbg_t[nm], src[:, :],
                deps=last(y_done, qk_done, v_done, lkv_done, x2_done),
                ctr_name="lps")
        dma("gpsimd", dbg_t["d_x2"], x2_scr, deps=last(x2_done),
            ctr_name="lps")

    def quantize(w_hbm, cols, scr, qsl, pc_dep, start_deps):
        w4 = w_hbm.rearrange("(c p) (j k) -> p c j k", p=128, k=1024)
        s4 = scr.rearrange("(c p) (j k) -> p c j k", p=128, k=1024)
        ncw = cols // 1024
        mx_toks = []
        prev = None
        for i in range(32):
            b = i % 2
            stg = XT[:, b * 1024:(b + 1) * 1024]
            c, j = divmod(i, ncw)
            ldt = dma("sync", stg, w4[:, c, j, :],
                      deps=last(prev) + (list(start_deps) if i == 0 else []),
                      ctr_name="qld")
            mx = vop(lambda e, stg=stg, i=i: e.tensor_reduce(
                out=MXP[:, i:i + 1], in_=stg, axis=AX, op=OP.max,
                apply_absolute_value=True), deps=[ldt])
            prev = mx
            mx_toks.append(mx)
        m1 = vop(lambda e, q=qsl: e.tensor_reduce(
            out=q[:, 0:1], in_=MXP[:, 0:32], axis=AX, op=OP.max),
            deps=last(*mx_toks))
        d1 = dma("sync", FSCR[0:1, 0:128], qsl[:, 0:1], deps=[m1],
                 ctr_name="qsc")
        m2 = vop(lambda e, q=qsl: e.tensor_reduce(
            out=q[0:1, 6:7], in_=FSCR[0:1, 0:128], axis=AX, op=OP.max),
            deps=[d1])
        d2a = dma("sync", scr1, qsl[0:1, 6:7], deps=[m2], ctr_name="qsc")
        d2 = dma("sync", qsl[:, 0:1], scr1.broadcast_to([128, 1]),
                 deps=[d2a], ctr_name="qsc")
        bmax = d2
        rr = vop(lambda e, q=qsl: e.reciprocal(out=q[:, 1:2], in_=q[:, 0:1]),
                 deps=[bmax])
        sinv = vop(lambda e, q=qsl: e.tensor_single_scalar(
            out=q[:, 2:3], in_=q[:, 1:2], scalar=127.0, op=OP.mult),
            deps=[rr])
        s_ = vop(lambda e, q=qsl: e.tensor_single_scalar(
            out=q[:, 3:4], in_=q[:, 0:1], scalar=1.0 / 127.0, op=OP.mult),
            deps=[bmax])
        s2 = vop(lambda e, q=qsl: e.tensor_tensor(
            out=q[:, 4:5], in0=q[:, 3:4], in1=q[:, 3:4], op=OP.mult),
            deps=[s_])
        sinv2 = vop(lambda e, q=qsl: e.tensor_single_scalar(
            out=q[:, 5:6], in_=q[:, 2:3], scalar=LORA_SCALING, op=OP.mult),
            deps=[sinv])
        st_prev = [None, None]
        qf_prev = [None, None]
        stt = None
        for i in range(32):
            b = i % 2
            stg = XT[:, b * 1024:(b + 1) * 1024]
            wf = AGT[:, b * 1024:(b + 1) * 1024]
            c, j = divmod(i, ncw)
            ldt = dma("sync", stg, w4[:, c, j, :],
                      deps=last(st_prev[b]) if st_prev[b] else last(*mx_toks),
                      ctr_name="qld")
            q1 = vop(lambda e, stg=stg, q=qsl: e.tensor_single_scalar(
                out=ISCR[:, :], in_=stg, scalar=q[:, 2:3], op=OP.mult),
                deps=last(ldt, sinv))
            cc = vop(lambda e, stg=stg, q=qsl: e.scalar_tensor_tensor(
                out=FSCR[:, :], in0=stg, scalar=q[:, 2:3], in1=ISCR[:, :],
                op0=OP.mult, op1=OP.subtract), deps=[q1])
            st_prev[b] = cc
            qm = vop(lambda e: e.scalar_tensor_tensor(
                out=FSC2[:, :], in0=FSCR[:, :], scalar=-0.5, in1=ISCR[:, :],
                op0=OP.is_lt, op1=OP.subtract), deps=[cc])
            qf = vop(lambda e, wf=wf: e.scalar_tensor_tensor(
                out=wf, in0=FSCR[:, :], scalar=0.5, in1=FSC2[:, :],
                op0=OP.is_gt, op1=OP.subtract),
                deps=last(qm, qf_prev[b]))
            stt = dma("sync", s4[:, c, j, :], wf, deps=[qf],
                      ctr_name="qst")
            qf_prev[b] = stt
        return dict(sinv=sinv, s=s_, s2=s2, sinv2=sinv2, done=stt, bmax=bmax)

    q_fc = quantize(wfcT, FF, sc_fc, QS, pcd_free[0], last(gt_free))
    q_pj = quantize(wpjT, D, sc_pj, QS2, q_fc["bmax"], [q_fc["done"]])
    ld_bpr = dma("sync", XT[0:1, 0:1024], bpj_h, deps=last(q_pj["done"]),
                 ctr_name="qsc")
    bpr_tok = act(BRW[:, 1024:2048], XT[0:1, 0:1024], AF.Identity,
                  scale=QS2[0:1, 2:3], deps=last(ld_bpr, q_pj["sinv"],
                                                 gt_free))
    if dbg:
        dma("gpsimd", dbg_t["d_wfc"], sc_fc, deps=[q_fc["done"]],
            ctr_name="lps")
        dma("sync", dbg_t["d_qs"], QS[:, :],
            deps=last(q_fc["sinv2"], q_fc["s2"]), ctr_name="b64")
        dma("sync", dbg_t["d_qs2"], QS2[:, :],
            deps=last(q_pj["sinv2"], q_pj["s2"]), ctr_name="b64")
        dma("sync", dbg_t["d_mxp"], MXP[:, :], deps=last(q_fc["sinv2"]),
            ctr_name="b64")

    lkv_rd_done = gts
    late_dep = last(hl, att_done, gts, bpr_tok)
    ld_afc = dma("gpsimd", AFCT.rearrange("p (c k) -> p c k", c=ND),
                 afcT_h.rearrange("(c p) k -> p c k", p=128), deps=late_dep,
                 ctr_name="lps")
    ld_apj = dma("gpsimd", APJT.rearrange("p (c k) -> p c k", c=NF),
                 apjT_h.rearrange("(c p) k -> p c k", p=128), deps=late_dep,
                 ctr_name="lps")
    ld_bpjT = dma("gpsimd", BPJT, bpjT_h, deps=late_dep, ctr_name="lps")
    ld_bfcA = dma("gpsimd", BFCTA, bfcT_h[:, 0:2048], deps=late_dep,
                  ctr_name="lps")
    ld_bfcB = dma("gpsimd", BFCTB, bfcT_h[:, 2048:4096], deps=late_dep,
                  ctr_name="lps")

    Y2 = AY
    H = AV
    regs3 = [PBIG[:, 0:1024], PBIG[:, 1024:2048], PA[:, :]]
    reg_free = [last(pbig_free[0]), last(pbig_free[1]),
                last(pa_free[0], pa_free[1])]
    pcq_free = q_pj["bmax"]
    wfc_prev = [None, None]
    wpj_prev = [None, None]
    sfc4 = sc_fc.rearrange("(c p) (o k) -> p c o k", p=128, k=128)
    spj3 = sc_pj.rearrange("(c p) k -> p c k", p=128)
    out_sts = []
    azq_rd = epi_ol
    y_rd = gts
    v_rd = gts
    x2ld_prev = [None, None]
    x2ld2_prev = [None, None]
    outt_prev = [None, None]
    oln_prev = bpr_tok

    for Cf in range(2):
        tb = Cf * 8
        ztoks_all = []
        ev = None
        for g2 in range(2):
            z_toks = []
            for tt in range(4):
                t_abs = tb + g2 * 4 + tt
                par = tt % 2
                xbuf = LKT[:, par * 1024:(par + 1) * 1024]
                xld = dma("sync", xbuf,
                          x2_scr[t_abs * 128:(t_abs + 1) * 128, :],
                          deps=last(x2_sts[t_abs], x2ld_prev[par]),
                          ctr_name="x2l")
                ag = stats1024(xbuf, [xld])
                bb = ln_single(MV[:, 1:2], MV[:, 0:1], SA[:, 0:1],
                               SA[:, 1:2], [ag])
                ztok = vop(lambda e, xbuf=xbuf, tt=tt: e.tensor_scalar(
                    out=AZQ[:, tt * D:(tt + 1) * D], in0=xbuf,
                    scalar1=SA[:, 0:1], scalar2=SA[:, 1:2],
                    op0=OP.mult, op1=OP.add),
                    deps=last(bb, azq_rd))
                x2ld_prev[par] = ztok
                z_toks.append(ztok)
            for c in range(ND):
                q = c % 4
                reg = PBIG[:, q * 512:(q + 1) * 512]
                deps = last(*(z_toks + [ev, ld_id] + reg_free[q // 2]))
                for tt in range(4):
                    tok = mm(AZQ[:, tt * D + c * 128: tt * D + (c + 1) * 128],
                             IDENT[:, :], reg[:, tt * 128:(tt + 1) * 128],
                             True, True, inc=(tt == 3),
                             deps=deps if tt == 0 else ())
                ev = vop(lambda e, c=c, g2=g2, reg=reg: e.tensor_copy(
                    out=Y2[:, c * 1024 + g2 * 512:
                           c * 1024 + (g2 + 1) * 512],
                    in_=reg), deps=[tok])
                reg_free[q // 2] = [ev]
            azq_rd = tok
            ztoks_all += z_toks
        y2_done = ev

        t1toks = []
        for h2 in range(2):
            deps = last(y2_done, ld_afc, pcq_free, hl)
            for c in range(ND):
                tok = mm(AFCT[:, c * 16:(c + 1) * 16],
                         Y2[:, c * 1024 + h2 * 512: c * 1024 + (h2 + 1) * 512],
                         PC[0:16, :], c == 0, c == ND - 1,
                         inc=(c == ND - 1), deps=deps if c == 0 else ())
            t1toks.append(act(T1[0:16, h2 * 512:(h2 + 1) * 512], PC[0:16, :],
                              AF.Identity, scale=QS[0:16, 5:6],
                              deps=last(tok, q_fc["sinv2"])))
            pcq_free = t1toks[-1]
        t1_done = last(*t1toks)
        h_toks = []
        for o1 in range(NF):
            wb = o1 % 2
            wfc_ld = dma("sync", AW[:, wb * 1024:(wb + 1) * 1024],
                         sfc4[:, :, o1, :],
                         deps=last(wfc_prev[wb], q_fc["done"], gt_free),
                         ctr_name="wfl")
            ri = o1 % 3
            reg = regs3[ri]
            deps = last(wfc_ld, y2_done, t1_done, ld_bfcA, ld_bfcB,
                        *reg_free[ri])
            for c in range(ND):
                for nh in range(2):
                    mm(AW[:, wb * 1024 + c * 128: wb * 1024 + (c + 1) * 128],
                       Y2[:, c * 1024 + nh * 512: c * 1024 + (nh + 1) * 512],
                       reg[:, nh * 512:(nh + 1) * 512], c == 0, False,
                       deps=deps if (c == 0 and nh == 0) else ())
            bsrc = BFCTA if o1 < 16 else BFCTB
            bof = (o1 % 16) * 128
            for nh in range(2):
                tok = mm(bsrc[:, bof:bof + 128],
                         T1[0:16, nh * 512:(nh + 1) * 512],
                         reg[:, nh * 512:(nh + 1) * 512], False, True,
                         inc=(nh == 1))
            hev = act(H[:, o1 * 1024:(o1 + 1) * 1024], reg, AF.Gelu,
                      bias=BFCC[:, o1:o1 + 1], scale=QS[:, 3:4],
                      deps=last(tok, q_fc["s"], ld_bfc, v_rd))
            h_toks.append(hev)
            reg_free[ri] = [hev]
            wfc_prev[wb] = hev
        h_done = last(*h_toks)
        t2toks = []
        for h2 in range(2):
            deps = last(h_done, ld_apj, pcq_free)
            for fc in range(NF):
                tok = mm(APJT[:, fc * 16:(fc + 1) * 16],
                         H[:, fc * 1024 + h2 * 512:
                           fc * 1024 + (h2 + 1) * 512],
                         PC[0:16, :], fc == 0, fc == NF - 1,
                         inc=(fc == NF - 1), deps=deps if fc == 0 else ())
            t2toks.append(act(T2L[0:16, h2 * 512:(h2 + 1) * 512], PC[0:16, :],
                              AF.Identity, scale=QS2[0:16, 5:6],
                              deps=last(tok, q_pj["sinv2"])))
            pcq_free = t2toks[-1]
        t2_done = last(*t2toks)
        for half in range(4):
            ts_pair = (2 * half, 2 * half + 1)
            regs = (PBIG[:, 0:1024], PBIG[:, 1024:2048])
            hd = last(*(reg_free[0] + reg_free[1]))
            for fc in range(NF):
                wb = fc % 2
                wpj_ld = dma("sync", WSLB[:, wb * 1024:(wb + 1) * 1024],
                             spj3[:, fc, :],
                             deps=last(wpj_prev[wb], q_pj["done"]),
                             ctr_name="wpl")
                wdeps = last(wpj_ld, h_done, t2_done) + \
                    (hd if fc == 0 else [])
                for k in range(2):
                    ts_i = ts_pair[k]
                    for nh in range(2):
                        tok = mm(H[:, fc * 1024 + ts_i * 128:
                                   fc * 1024 + (ts_i + 1) * 128],
                                 WSLB[:, wb * 1024 + nh * 512:
                                      wb * 1024 + (nh + 1) * 512],
                                 regs[k][:, nh * 512:(nh + 1) * 512],
                                 fc == 0, False, inc=(k == 1 and nh == 1),
                                 deps=wdeps if (k == 0 and nh == 0) else ())
                wpj_prev[wb] = tok
            f2_last = []
            for k in range(2):
                ts_i = ts_pair[k]
                for nh in range(2):
                    mm(T2L[0:16, ts_i * 128:(ts_i + 1) * 128],
                       BPJT[0:16, nh * 512:(nh + 1) * 512],
                       regs[k][:, nh * 512:(nh + 1) * 512], False, False,
                       deps=last(ld_bpjT, t2_done))
                for nh in range(2):
                    tok = mm(ONES[:, :],
                             BRW[:, 1024 + nh * 512:1024 + (nh + 1) * 512],
                             regs[k][:, nh * 512:(nh + 1) * 512], False, True,
                             inc=(nh == 1), deps=last(bpr_tok))
                f2_last.append(tok)
            for k in range(2):
                ts_i = ts_pair[k]
                t_abs = tb + ts_i
                par = t_abs % 2
                reg = regs[k]
                ag = stats1024(reg, [f2_last[k]])
                vp = vop(lambda e: e.scalar_tensor_tensor(
                    out=SA[:, 2:3], in0=MV[:, 1:2], scalar=QS2[:, 4:5],
                    in1=EPSC[:, :], op0=OP.mult, op1=OP.add),
                    deps=last(ag, ms_eps, q_pj["s2"]))
                w2r = vop(lambda e: e.reciprocal(out=SA[:, 3:4],
                                                 in_=SA[:, 2:3]), deps=[vp])
                aa0 = act(SA[:, 4:5], SA[:, 3:4], AF.Sqrt, deps=[w2r])
                aa = vop(lambda e: e.tensor_tensor(
                    out=SA[:, 0:1], in0=SA[:, 4:5], in1=QS2[:, 3:4],
                    op=OP.mult), deps=[aa0])
                bb = vop(lambda e: e.scalar_tensor_tensor(
                    out=SA[:, 1:2], in0=MV[:, 0:1], scalar=-1.0,
                    in1=SA[:, 0:1], op0=OP.mult, op1=OP.mult), deps=[aa])
                olnt = act(OLN, reg, AF.Identity, bias=SA[:, 1:2],
                           scale=SA[:, 0:1], deps=last(bb, oln_prev))
                x2b = LKT[:, (2 + par) * 1024:(3 + par) * 1024]
                x2l = dma("sync", x2b,
                          x2_scr[t_abs * 128:(t_abs + 1) * 128, :],
                          deps=last(x2_sts[t_abs], x2ld2_prev[par]),
                          ctr_name="x2l")
                ob2 = FSCR[:, :] if par == 0 else FSC2[:, :]
                fin = vop(lambda e, x2b=x2b, ob2=ob2: e.tensor_add(
                    out=ob2, in0=x2b, in1=OLN),
                    deps=last(olnt, x2l, outt_prev[par]))
                oln_prev = fin
                x2ld2_prev[par] = fin
                stok = dma("sync", out_h[t_abs * 128:(t_abs + 1) * 128, :],
                           ob2, deps=[fin], ctr_name="st")
                outt_prev[par] = stok
                out_sts.append(stok)
                reg_free[k] = [fin]
        azq_rd = oln_prev

    waits("sync", last(*out_sts))

    with nc.Block() as block:
        @block.sync
        def _(eng):
            for nm, fn in ops:
                if nm == "sync":
                    fn(eng)

        @block.gpsimd
        def _(eng):
            for nm, fn in ops:
                if nm == "gpsimd":
                    fn(eng)

        @block.tensor
        def _(eng):
            for nm, fn in ops:
                if nm == "tensor":
                    fn(eng)

        @block.vector
        def _(eng):
            for nm, fn in ops:
                if nm == "vector":
                    fn(eng)

        @block.scalar
        def _(eng):
            for nm, fn in ops:
                if nm == "scalar":
                    fn(eng)

    stack.close()
    return nc, dbg_t


def _prep_inputs(inputs):
    import ml_dtypes
    bf = ml_dtypes.bfloat16
    f32 = np.float32
    ct = lambda a: np.ascontiguousarray(np.asarray(a), dtype=f32)

    common = {
        "wqkT": ct(np.asarray(inputs["L0_attn_Wqk"]).T),
        "wvT": ct(np.asarray(inputs["L0_attn_Wv"]).T),
        "wgT": ct(np.asarray(inputs["L0_attn_Wg"]).T),
        "woT": ct(np.asarray(inputs["L0_attn_Wo"]).T),
        "wfcT": ct(np.asarray(inputs["L0_ffn_W_fc"]).T),
        "wpjT": ct(np.asarray(inputs["L0_ffn_W_proj"]).T),
        "afcT": ct(np.asarray(inputs["L0_ffn_A_fc"]).T),
        "bfcT": ct(np.asarray(inputs["L0_ffn_B_fc"]).T),
        "apjT": ct(np.asarray(inputs["L0_ffn_A_proj"]).T),
        "bpjT": ct(np.asarray(inputs["L0_ffn_B_proj"]).T),
        "bqk": ct(np.asarray(inputs["L0_attn_bqk"]).reshape(NQ, 128).T),
        "bv": ct(np.asarray(inputs["L0_attn_bv"]).reshape(1, E)),
        "bg": ct(np.asarray(inputs["L0_attn_bg"]).reshape(NE, 128).T),
        "bo": ct(np.asarray(inputs["L0_attn_bo"]).reshape(1, D)),
        "bfc": ct(np.asarray(inputs["L0_ffn_b_fc"]).reshape(NF, 128).T),
        "bpj": ct(np.asarray(inputs["L0_ffn_b_proj"]).reshape(1, D)),
        "ln1w": ct(np.asarray(inputs["L0_attn_ln1_w"]).reshape(ND, 128).T),
        "ln1b": ct(np.asarray(inputs["L0_attn_ln1_b"]).reshape(ND, 128).T),
        "ln2wr": ct(np.broadcast_to(np.asarray(inputs["L0_attn_ln2_w"]),
                                    (128, D))),
        "ln2br": ct(np.broadcast_to(np.asarray(inputs["L0_attn_ln2_b"]),
                                    (128, D))),
        "gam": ct(np.asarray(inputs["L0_attn_gamma"]).reshape(4, 2, 128)
                  .transpose(2, 1, 0).reshape(128, 8)),
        "bet": ct(np.asarray(inputs["L0_attn_beta"]).reshape(4, 2, 128)
                  .transpose(2, 1, 0).reshape(128, 8)),
        "posemb": ct(np.asarray(inputs["L0_attn_pos_emb"]).reshape(65, 1)),
    }
    idx = np.arange(G)
    dist = np.clip(np.abs(idx[:, None] - idx[None, :]), 0, 32)
    onehot = np.zeros((65, G * G), dtype=bf)
    onehot[dist.reshape(-1), np.arange(G * G)] = 1
    common["onehot"] = onehot
    common["identb"] = np.eye(128, dtype=bf)
    common["identf"] = np.eye(128, dtype=f32)
    common["onesrow"] = np.ones((1, 128), dtype=bf)
    common["onesrowf"] = np.ones((1, 128), dtype=f32)
    x = np.ascontiguousarray(np.asarray(inputs["L0_attn_x"]), dtype=f32)
    return [dict(common, x=x[i]) for i in range(NCORES)]


def kernel(**inputs):
    in_maps = _prep_inputs(inputs)
    nc, _ = build_nc(dbg=False)
    res = run_bass_kernel_spmd(nc, in_maps, core_ids=list(range(NCORES)))
    out = np.stack([np.asarray(r["out"]) for r in res.results], axis=0)
    return out.astype(np.float32)

# version 14
# speedup vs baseline: 1.2806x; 1.0402x over previous
import math
import numpy as np
import concourse.bass as bass
import concourse.mybir as mybir
from concourse.bass_utils import run_bass_kernel_spmd

F32 = mybir.dt.float32
BF16 = mybir.dt.bfloat16
I32 = mybir.dt.int32
AX = mybir.AxisListType.X
AF = mybir.ActivationFunctionType
OP = mybir.AluOpType

NCORES = 8
T = 2048
D = 1024
DQK = 256
E = 2048
FF = 4096
G = 64
NT = T // 128
ND = D // 128
NE = E // 128
NQ = DQK // 128
NF = FF // 128
EPS = 1e-5
PI = math.pi
INV2PI = 1.0 / (2.0 * PI)
PI_LO = float(np.nextafter(np.float32(np.pi), np.float32(0.0)))
LORA_SCALING = 2.0


def build_nc(dbg=False):
    nc = bass.Bass("TRN2", target_bir_lowering=False)

    dr = lambda name, shape, dt, kind: nc.dram_tensor(name, shape, dt, kind=kind).ap()
    x_h = dr("x", [T, D], F32, "ExternalInput")
    wqkT = dr("wqkT", [D, DQK], F32, "ExternalInput")
    wvT = dr("wvT", [D, E], F32, "ExternalInput")
    wgT = dr("wgT", [D, E], F32, "ExternalInput")
    woT = dr("woT", [E, D], F32, "ExternalInput")
    wfcT = dr("wfcT", [D, FF], F32, "ExternalInput")
    wpjT = dr("wpjT", [FF, D], F32, "ExternalInput")
    afcT_h = dr("afcT", [D, 16], F32, "ExternalInput")
    bfcT_h = dr("bfcT", [16, FF], F32, "ExternalInput")
    apjT_h = dr("apjT", [FF, 16], F32, "ExternalInput")
    bpjT_h = dr("bpjT", [16, D], F32, "ExternalInput")
    bqk_h = dr("bqk", [128, NQ], F32, "ExternalInput")
    bv_h = dr("bv", [1, E], F32, "ExternalInput")
    bg_h = dr("bg", [128, NE], F32, "ExternalInput")
    bo_h = dr("bo", [1, D], F32, "ExternalInput")
    bfc_h = dr("bfc", [128, NF], F32, "ExternalInput")
    bpj_h = dr("bpj", [1, D], F32, "ExternalInput")
    ln1w_h = dr("ln1w", [128, ND], F32, "ExternalInput")
    ln1b_h = dr("ln1b", [128, ND], F32, "ExternalInput")
    ln2w_h = dr("ln2wr", [128, D], F32, "ExternalInput")
    ln2b_h = dr("ln2br", [128, D], F32, "ExternalInput")
    gam_h = dr("gam", [128, 8], F32, "ExternalInput")
    bet_h = dr("bet", [128, 8], F32, "ExternalInput")
    pos_h = dr("posemb", [65, 1], F32, "ExternalInput")
    oneh_h = dr("onehot", [65, G * G], BF16, "ExternalInput")
    ident_h = dr("identb", [128, 128], BF16, "ExternalInput")
    identf_h = dr("identf", [128, 128], F32, "ExternalInput")
    ones_h = dr("onesrow", [1, 128], BF16, "ExternalInput")
    onesf_h = dr("onesrowf", [1, 128], F32, "ExternalInput")
    out_h = dr("out", [T, D], F32, "ExternalOutput")
    sc_fc = nc.dram_tensor("scr_fc", [D, FF], BF16, kind="Internal").ap()
    scr1 = nc.dram_tensor("scr_1", [1, 1], F32, kind="Internal").ap()
    sc_pj = nc.dram_tensor("scr_pj", [FF, D], BF16, kind="Internal").ap()
    x2_scr = nc.dram_tensor("scr_x2", [T, D], BF16, kind="Internal").ap()

    dbg_t = {}
    if dbg:
        for nm, shape in (("d_qs", [128, 8]), ("d_qs2", [128, 8]),
                          ("d_mxp", [128, 32]),
                          ("d_y", [128, ND * T]), ("d_qk", [128, NQ * T]),
                          ("d_v", [128, NT * E]), ("d_lkv", [128, 2 * E]),
                          ("d_x2", [T, D]), ("d_att", [128, 512]),
                          ("d_cos", [128, T]), ("d_sin", [128, T]),
                          ("d_wfc", [D, FF])):
            dbg_t[nm] = dr(nm, shape, F32, "ExternalOutput")

    al = nc.alloc_sbuf_tensor
    GAMT = al("GAMT", [128, 8], F32)
    BETT = al("BETT", [128, 8], F32)
    BQK = al("BQK", [128, NQ], F32)
    BGC = al("BGC", [128, NE], F32)
    BFCC = al("BFCC", [128, NF], F32)
    LN1W = al("LN1W", [128, ND], F32)
    LN1B = al("LN1B", [128, ND], F32)
    LN2W = al("LN2W", [128, D], BF16)
    LN2B = al("LN2B", [128, D], BF16)
    BRW = al("BRW", [1, E], BF16)
    POSE = al("POSE", [65, 1], BF16)
    IDENT = al("IDENT", [128, 128], BF16)
    IDENTF = al("IDENTF", [128, 128], F32)
    ONES = al("ONES", [1, 128], BF16)
    ZROW = al("ZROW", [1, 128], BF16)
    ONESF = al("ONESF", [1, 128], F32)
    BIAS64 = al("BIAS64", [64, 64], F32)
    EPSC = al("EPSC", [128, 1], F32)
    ST6 = al("ST6", [128, 12], F32)
    MV = al("MV", [128, 2], F32)
    SA = al("SA", [128, 8], F32)
    MXP = al("MXP", [128, 32], F32)
    QS = al("QS", [128, 8], F32)
    QS2 = al("QS2", [128, 8], F32)
    COS = al("COS", [128, T], BF16)
    SIN = al("SIN", [128, T], BF16)
    XT = al("XT", [128, 2 * D], F32)
    FSCR = al("FSCR", [128, 1024], F32)
    FSC2 = al("FSC2", [128, 1024], F32)
    ISCR = al("ISCR", [128, 1024], I32)
    AV = al("AV", [128, NT * E], BF16)
    AY = al("AY", [128, ND * T], BF16)
    AGT = al("AGT", [128, NE * 512], BF16)
    AZQ = al("AZQ", [128, 4096], BF16)
    AMSC = al("AMSC", [128, 2048], BF16)
    QK = al("QK", [128, NQ * T], BF16)
    LINKV = al("LINKV", [128, 2 * E], BF16)
    AW = al("AW", [128, 4096], BF16)
    WSLB = al("WSLB", [128, 2048], BF16)
    LKT = al("LKT", [128, NT * DQK], BF16)
    QQ = al("QQ", [128, 1024], BF16)
    QKK = al("QKK", [128, 1024], BF16)
    LQ = al("LQ", [128, 1024], BF16)
    GATEE = al("GATEE", [128, 1024], BF16)
    ATT = al("ATT", [128, 512], BF16)

    ONEH = AV[0:65, 0:G * G]
    SQT = FSCR[:, 0:256]
    M1 = AMSC[:, 0:512]
    M2 = AMSC[:, 512:1024]
    UB0 = AMSC[:, 1024:1536]
    UB1 = AMSC[:, 1536:2048]
    Z2 = AZQ[:, 0:1024]
    T2E = AZQ[:, 1024:2048]
    OLN = AZQ[:, 2048:3072]
    BPJT = QQ[0:16, 0:1024]
    APJT = QKK[:, 0:512]
    AFCT = QKK[:, 512:640]
    T1 = LQ[0:16, 0:1024]
    T2L = GATEE[0:16, 0:1024]
    BFCTA = COS[0:16, :]
    BFCTB = SIN[0:16, :]

    PBIG = nc.alloc_psum_tensor("PBIG", [128, 2048], F32)
    PA = nc.alloc_psum_tensor("PA", [128, 1024], F32)
    PC = nc.alloc_psum_tensor("PC", [128, 512], F32)
    PD = nc.alloc_psum_tensor("PD", [128, 512], F32)
    PCD = [PC, PD]

    ops = []
    EMIT = {k: (lambda fn, k=k: ops.append((k, fn)))
            for k in ("tensor", "vector", "scalar", "sync", "gpsimd")}

    class Ctr:
        def __init__(self, h):
            self.h = h
            self.n = 0

    from contextlib import ExitStack
    stack = ExitStack()
    sem = {k: Ctr(stack.enter_context(nc.semaphore(k)))
           for k in ("ld", "gp", "gpc", "te", "ve", "se", "st",
                     "lps", "b64", "qsc",
                     "qld0", "qld1", "xs0", "xs1", "wgl0", "wgl1",
                     "wol0", "wol1", "xrl0", "xrl1", "wvl0", "wvl1",
                     "wfl0", "wfl1", "wpl0", "wpl1", "x2l0", "x2l1",
                     "qst0", "qst1")}
    pipetog = {}
    hw = {e: {} for e in EMIT}

    last_dma = {e: None for e in EMIT}

    def wait(engine, ctr, val):
        if val is None or val <= 0:
            return
        if val > hw[engine].get(id(ctr), 0):
            hw[engine][id(ctr)] = val
            last_dma[engine] = None
            EMIT[engine](lambda e, h=ctr.h, v=val: e.wait_ge(h, v))

    def waits(engine, deps):
        for d in deps:
            if d is None:
                continue
            if isinstance(d, list):
                waits(engine, d)
            else:
                wait(engine, d[0], d[1])

    def last(*tokens):
        out = {}
        for t in tokens:
            if t is None:
                continue
            if isinstance(t, list):
                for u in t:
                    if u is not None and out.get(id(u[0]), (None, 0))[1] < u[1]:
                        out[id(u[0])] = u
                continue
            if out.get(id(t[0]), (None, 0))[1] < t[1]:
                out[id(t[0])] = t
        return list(out.values())

    def dma(engine, out_ap, in_ap, deps=(), ctr_name=None):
        waits(engine, deps)
        if ctr_name is not None and (ctr_name + "0") in sem:
            t = pipetog.get(ctr_name, 0)
            pipetog[ctr_name] = 1 - t
            c = sem[ctr_name + str(t)]
        else:
            c = sem[ctr_name] if ctr_name else (
                sem["ld"] if engine == "sync" else sem["gp"])
        wait(engine, c, c.n)
        c.n += 16
        n = c.n
        EMIT[engine](lambda e, o=out_ap, i=in_ap, h=c.h:
                     e.dma_start(out=o, in_=i).then_inc(h, 16))
        return (c, n)

    def mm(lhsT, rhs, out, start, stop, inc=False, deps=()):
        waits("tensor", deps)
        if inc:
            sem["te"].n += 1
            n = sem["te"].n
            EMIT["tensor"](lambda e, a=lhsT, b=rhs, o=out, s1=start, s2=stop,
                           h=sem["te"].h:
                           e.matmul(o, a, b, start=s1, stop=s2).then_inc(h, 1))
            return (sem["te"], n)
        EMIT["tensor"](lambda e, a=lhsT, b=rhs, o=out, s1=start, s2=stop:
                       e.matmul(o, a, b, start=s1, stop=s2))
        return None

    def act(out_ap, in_ap, func, bias=0.0, scale=1.0, deps=()):
        waits("scalar", deps)
        sem["se"].n += 1
        n = sem["se"].n
        EMIT["scalar"](lambda e, o=out_ap, i=in_ap, f=func, b=bias, s=scale,
                       h=sem["se"].h:
                       e.activation(out=o, in_=i, func=f, bias=b, scale=s)
                       .then_inc(h, 1))
        return (sem["se"], n)

    def vop(emitfn, deps=()):
        waits("vector", deps)
        sem["ve"].n += 1
        n = sem["ve"].n
        EMIT["vector"](lambda e, f=emitfn, h=sem["ve"].h: f(e).then_inc(h, 1))
        return (sem["ve"], n)

    def gop(emitfn, deps=()):
        waits("gpsimd", deps)
        sem["gpc"].n += 1
        n = sem["gpc"].n
        EMIT["gpsimd"](lambda e, f=emitfn, h=sem["gpc"].h: f(e).then_inc(h, 1))
        return (sem["gpc"], n)

    def flush_mm(region, deps=(), m=128):
        return mm(ZROW[:, 0:m], ONES[:, :], region, False, True, inc=True,
                  deps=deps)

    ld_gam = dma("sync", GAMT[:, :], gam_h)
    ld_bet = dma("sync", BETT[:, :], bet_h)
    ld_bqk = dma("sync", BQK[:, :], bqk_h)
    ld_bg = dma("sync", BGC[:, :], bg_h)
    ld_bfc = dma("sync", BFCC[:, :], bfc_h)
    ld_l1w = dma("sync", LN1W[:, :], ln1w_h)
    ld_l1b = dma("sync", LN1B[:, :], ln1b_h)
    ld_onh = dma("sync", ONEH, oneh_h)
    ld_id = dma("sync", IDENT[:, :], ident_h)
    ld_on = dma("sync", ONES[:, :], ones_h)
    ld_onf = dma("sync", ONESF[:, :], onesf_h)
    ld_l2w = dma("gpsimd", LN2W[:, :], ln2w_h)
    ld_l2b = dma("gpsimd", LN2B[:, :], ln2b_h)
    ld_bvr = dma("gpsimd", BRW[:, :], bv_h)
    ld_pos = dma("gpsimd", POSE[:, :], pos_h)
    pre_ld = (sem["ld"], sem["ld"].n)
    ld_gam = ld_bet = ld_bqk = ld_bg = ld_bfc = pre_ld
    ld_l1w = ld_l1b = ld_onh = ld_id = ld_on = ld_onf = pre_ld
    pre_gp = (sem["gp"], sem["gp"].n)
    ld_l2w = ld_l2b = ld_bvr = ld_pos = pre_gp
    iot1 = gop(lambda e: e.iota(FSCR[0:1, 0:128], [[1, 128]],
                                channel_multiplier=0,
                                allow_small_or_imprecise_dtypes=True))
    ms_att = gop(lambda e: e.memset(ATT[:, :], 0.0))
    ms_zr = gop(lambda e: e.memset(ZROW[:, :], 0.0))
    ms_eps = gop(lambda e: e.memset(EPSC[:, :], EPS))
    a_invf = act(IDENTF[0:1, 0:128], FSCR[0:1, 0:128], AF.Exp,
                 scale=-math.log(10000.0) / 128.0, deps=[iot1])

    def sincos_half2(hh, shift, dst_half, pdeps):
        posr = FSC2[0:1, 0:1024]
        pio = gop(lambda e, hh=hh: e.iota(posr, [[1, 1024]], base=hh * 1024,
                                          channel_multiplier=0,
                                          allow_small_or_imprecise_dtypes=True),
                  deps=list(pdeps))
        mmdeps = last(a_invf, pio) + list(pdeps)
        for j in range(2):
            ttok = mm(IDENTF[0:1, 0:128], posr[:, j * 512:(j + 1) * 512],
                      PA[:, j * 512:(j + 1) * 512], True, True, inc=(j == 1),
                      deps=mmdeps if j == 0 else ())
        ts = vop(lambda e: e.tensor_scalar(
            out=FSCR[:, 0:1024], in0=PA[:, :], scalar1=INV2PI, scalar2=shift,
            op0=OP.mult, op1=OP.add), deps=last(ttok))
        k1 = vop(lambda e: e.tensor_single_scalar(
            out=ISCR[:, 0:1024], in_=FSCR[:, 0:1024], scalar=1.0,
            op=OP.mult), deps=[ts])
        c2 = vop(lambda e: e.scalar_tensor_tensor(
            out=XT[:, 0:1024], in0=FSCR[:, 0:1024], scalar=1.0,
            in1=ISCR[:, 0:1024], op0=OP.mult, op1=OP.subtract), deps=[k1])
        km = vop(lambda e: e.scalar_tensor_tensor(
            out=XT[:, 1024:2048], in0=XT[:, 0:1024], scalar=-0.5,
            in1=ISCR[:, 0:1024], op0=OP.is_lt, op1=OP.subtract), deps=[c2])
        kf = vop(lambda e: e.scalar_tensor_tensor(
            out=XT[:, 0:1024], in0=XT[:, 0:1024], scalar=0.5,
            in1=XT[:, 1024:2048], op0=OP.is_gt, op1=OP.subtract), deps=[km])
        dd = vop(lambda e: e.scalar_tensor_tensor(
            out=FSCR[:, 0:1024], in0=XT[:, 0:1024], scalar=-1.0,
            in1=FSCR[:, 0:1024], op0=OP.mult, op1=OP.add), deps=[kf])
        r1 = vop(lambda e: e.tensor_scalar(
            out=FSCR[:, 0:1024], in0=FSCR[:, 0:1024], scalar1=2.0 * PI,
            scalar2=PI_LO, op0=OP.mult, op1=OP.min), deps=[dd])
        r2 = vop(lambda e: e.tensor_single_scalar(
            out=FSCR[:, 0:1024], in_=FSCR[:, 0:1024], scalar=-PI_LO,
            op=OP.max), deps=[r1])
        return act(dst_half, FSCR[:, 0:1024], AF.Sin, deps=[r2]), ttok

    tbl_toks = []
    mm_last = None
    for hh in range(2):
        prev = [tbl_toks[-1]] if tbl_toks else []
        tk, mm_last = sincos_half2(hh, 0.0,
                                   SIN[:, hh * 1024:(hh + 1) * 1024], prev)
        tbl_toks.append(tk)
        tk, mm_last = sincos_half2(hh, 0.25,
                                   COS[:, hh * 1024:(hh + 1) * 1024],
                                   [tbl_toks[-1]])
        tbl_toks.append(tk)
    tables_done = tbl_toks[-1]
    s_sin = tables_done
    s_cos = tables_done

    ev_prev = tables_done
    b64_dmas = []
    for j in range(8):
        tok = mm(POSE[:, :], ONEH[:, j * 512:(j + 1) * 512], PC[0:1, :],
                 True, True, inc=True,
                 deps=last(ld_pos, ld_onh, ev_prev,
                           b64_dmas[-1] if b64_dmas else None))
        ev_prev = vop(lambda e: e.tensor_copy(
            out=FSCR[0:1, 0:512], in_=PC[0:1, :]), deps=[tok])
        b64_dmas.append(dma("sync", BIAS64[8 * j:8 * (j + 1), :],
                            FSCR[0:1, 0:512], deps=[ev_prev],
                            ctr_name="b64"))
    ld_b64 = last(*b64_dmas)
    pc_free = ev_prev
    pd_free = None

    def stats1024(src, deps):
        s1 = vop(lambda e: e.bn_stats(out=ST6[:, 0:6], in_=src[:, 0:512]),
                 deps=deps)
        s2 = vop(lambda e: e.bn_stats(out=ST6[:, 6:12], in_=src[:, 512:1024]),
                 deps=[s1])
        return vop(lambda e: e.bn_aggr(out=MV[:, :], in_=ST6[:, 0:12]),
                   deps=[s2])

    def ln_single(vcol, mcol, acol, bcol, deps):
        w2 = vop(lambda e, v=vcol: e.tensor_scalar(
            out=SA[:, 2:3], in0=v, scalar1=1.0, scalar2=EPS,
            op0=OP.mult, op1=OP.add), deps=deps)
        w2r = vop(lambda e: e.reciprocal(out=SA[:, 3:4], in_=SA[:, 2:3]),
                  deps=[w2])
        aa = act(acol, SA[:, 3:4], AF.Sqrt, deps=[w2r])
        return vop(lambda e, m=mcol, a=acol, b=bcol: e.scalar_tensor_tensor(
            out=b, in0=m, scalar=-1.0, in1=a, op0=OP.mult, op1=OP.mult),
            deps=[aa])

    def ln_double(vcol, mcol, acol, bcol, deps):
        w2 = vop(lambda e, v=vcol: e.tensor_scalar(
            out=SA[:, 2:3], in0=v, scalar1=1.0, scalar2=EPS,
            op0=OP.mult, op1=OP.add), deps=deps)
        w2r = vop(lambda e: e.reciprocal(out=SA[:, 3:4], in_=SA[:, 2:3]),
                  deps=[w2])
        v2 = vop(lambda e, v=vcol: e.tensor_tensor(
            out=SA[:, 4:5], in0=v, in1=SA[:, 3:4], op=OP.mult), deps=[w2r])
        w4 = vop(lambda e: e.tensor_scalar(
            out=SA[:, 5:6], in0=SA[:, 4:5], scalar1=1.0, scalar2=EPS,
            op0=OP.mult, op1=OP.add), deps=[v2])
        w4r = vop(lambda e: e.reciprocal(out=SA[:, 6:7], in_=SA[:, 5:6]),
                  deps=[w4])
        pr = vop(lambda e: e.tensor_tensor(
            out=SA[:, 7:8], in0=SA[:, 3:4], in1=SA[:, 6:7], op=OP.mult),
            deps=[w4r])
        aa = act(acol, SA[:, 7:8], AF.Sqrt, deps=[pr])
        return vop(lambda e, m=mcol, a=acol, b=bcol: e.scalar_tensor_tensor(
            out=b, in0=m, scalar=-1.0, in1=a, op0=OP.mult, op1=OP.mult),
            deps=[aa])

    ev_tok = None
    zg_rd = [None] * 4
    prev_z = None
    for g in range(4):
        z_toks = []
        for tt in range(4):
            t = g * 4 + tt
            buf = tt % 2
            xs = XT[:, buf * D:(buf + 1) * D]
            xdep = [prev_z] if t >= 2 else [tables_done, ld_b64]
            xld = dma("sync", xs, x_h[t * 128:(t + 1) * 128, :], deps=xdep,
                      ctr_name="xs")
            ag = stats1024(xs, [xld])
            bb = ln_double(MV[:, 1:2], MV[:, 0:1], SA[:, 0:1], SA[:, 1:2],
                           [ag])
            ztok = vop(lambda e, xs=xs, tt=tt: e.tensor_scalar(
                out=AZQ[:, tt * D:(tt + 1) * D], in0=xs,
                scalar1=SA[:, 0:1], scalar2=SA[:, 1:2],
                op0=OP.mult, op1=OP.add),
                deps=last(bb, zg_rd[g - 1] if g >= 1 else None))
            z_toks.append(ztok)
            prev_z = ztok
        for c in range(ND):
            q = c % 4
            reg = PBIG[:, q * 512:(q + 1) * 512]
            deps = last(*(z_toks + [ev_tok, ld_id]))
            for tt in range(4):
                tok = mm(AZQ[:, tt * D + c * 128: tt * D + (c + 1) * 128],
                         IDENT[:, :], reg[:, tt * 128:(tt + 1) * 128],
                         True, True, inc=(tt == 3),
                         deps=deps if tt == 0 else ())
            ev_tok = act(AY[:, c * T + g * 512: c * T + (g + 1) * 512],
                         reg, AF.Identity, bias=LN1B[:, c:c + 1],
                         scale=LN1W[:, c:c + 1],
                         deps=last(tok, ld_l1w, ld_l1b))
            if c == ND - 1:
                zg_rd[g] = tok
    y_done = ev_tok
    Y = AY

    pcd_free = [pc_free, pd_free]
    aw_free = [None, None]
    qk_evs = []
    for qc in range(NQ):
        wb = qc % 2
        wsl = AW[:, wb * 1024:(wb + 1) * 1024]
        wqk_ld = dma("gpsimd", wsl.rearrange("p (c k) -> p c k", c=ND),
                     wqkT[:, qc * 128:(qc + 1) * 128]
                     .rearrange("(c p) k -> p c k", p=128),
                     deps=last(aw_free[wb]), ctr_name="wvl")
        for tc in range(4):
            pi = (qc * 4 + tc) % 2
            ps = PCD[pi]
            deps = last(wqk_ld, y_done, pcd_free[pi])
            for c in range(ND):
                tok = mm(wsl[:, c * 128:(c + 1) * 128],
                         Y[:, c * T + tc * 512: c * T + (tc + 1) * 512],
                         ps[:, :], c == 0, c == ND - 1, inc=(c == ND - 1),
                         deps=deps if c == 0 else ())
            ev = act(QK[:, qc * T + tc * 512: qc * T + (tc + 1) * 512],
                     ps[:, :], AF.Silu, bias=BQK[:, qc:qc + 1],
                     deps=last(tok, ld_bqk))
            pcd_free[pi] = ev
            qk_evs.append(ev)
        aw_free[wb] = tok
    qk_done = last(*qk_evs)
    V = AV

    def quant_max(w_hbm, cols, qsl, start_deps, xt_dep):
        w_mx = min(2048, cols)
        npc = (D * FF) // (128 * w_mx)
        w2 = w_hbm.rearrange("(c p) (j k) -> p c j k", p=128, k=w_mx)
        ncw2 = cols // w_mx
        mx_toks = []
        prev = xt_dep
        for i in range(npc):
            stg = XT[:, 0:w_mx]
            c, j = divmod(i, ncw2)
            ldt = dma("sync", stg, w2[:, c, j, :],
                      deps=last(prev) + (list(start_deps) if i == 0 else []),
                      ctr_name="qld")
            mx = vop(lambda e, stg=stg, i=i: e.tensor_reduce(
                out=MXP[:, i:i + 1], in_=stg, axis=AX, op=OP.max,
                apply_absolute_value=True), deps=[ldt])
            prev = mx
            mx_toks.append(mx)
        m1 = vop(lambda e, q=qsl, n=npc: e.tensor_reduce(
            out=q[:, 0:1], in_=MXP[:, 0:n], axis=AX, op=OP.max),
            deps=last(*mx_toks))
        d1 = dma("sync", FSCR[0:1, 0:128], qsl[:, 0:1], deps=[m1],
                 ctr_name="qsc")
        m2 = vop(lambda e, q=qsl: e.tensor_reduce(
            out=q[0:1, 6:7], in_=FSCR[0:1, 0:128], axis=AX, op=OP.max),
            deps=[d1])
        d2a = dma("sync", scr1, qsl[0:1, 6:7], deps=[m2], ctr_name="qsc")
        d2 = dma("sync", qsl[:, 0:1], scr1.broadcast_to([128, 1]),
                 deps=[d2a], ctr_name="qsc")
        rr = vop(lambda e, q=qsl: e.reciprocal(out=q[:, 1:2], in_=q[:, 0:1]),
                 deps=[d2])
        sinv = vop(lambda e, q=qsl: e.tensor_single_scalar(
            out=q[:, 2:3], in_=q[:, 1:2], scalar=127.0, op=OP.mult),
            deps=[rr])
        s_ = vop(lambda e, q=qsl: e.tensor_single_scalar(
            out=q[:, 3:4], in_=q[:, 0:1], scalar=1.0 / 127.0, op=OP.mult),
            deps=[d2])
        s2 = vop(lambda e, q=qsl: e.tensor_tensor(
            out=q[:, 4:5], in0=q[:, 3:4], in1=q[:, 3:4], op=OP.mult),
            deps=[s_])
        sinv2 = vop(lambda e, q=qsl: e.tensor_single_scalar(
            out=q[:, 5:6], in_=q[:, 2:3], scalar=LORA_SCALING, op=OP.mult),
            deps=[sinv])
        return dict(sinv=sinv, s=s_, s2=s2, sinv2=sinv2,
                    st_prev=[last(*mx_toks), last(*mx_toks)],
                    qf_prev=[None, None], stores=[],
                    w4=w_hbm.rearrange("(c p) (j k) -> p c j k", p=128,
                                       k=1024),
                    ncw=cols // 1024, qsl=qsl)

    def quant_pieces(st, rng, wf_fn, extra_deps=()):
        scr = st["scr"]
        for i in rng:
            b = i % 2
            stg = XT[:, b * 1024:(b + 1) * 1024]
            wf = wf_fn(b)
            c, j = divmod(i, st["ncw"])
            ldt = dma("sync", stg, st["w4"][:, c, j, :],
                      deps=last(st["st_prev"][b]), ctr_name="qld")
            q1 = vop(lambda e, stg=stg, q=st["qsl"]: e.tensor_single_scalar(
                out=ISCR[:, :], in_=stg, scalar=q[:, 2:3], op=OP.mult),
                deps=last(ldt, st["sinv"]))
            cc = vop(lambda e, stg=stg, q=st["qsl"]: e.scalar_tensor_tensor(
                out=FSCR[:, :], in0=stg, scalar=q[:, 2:3], in1=ISCR[:, :],
                op0=OP.mult, op1=OP.subtract), deps=[q1])
            st["st_prev"][b] = cc
            qm = vop(lambda e: e.scalar_tensor_tensor(
                out=FSC2[:, :], in0=FSCR[:, :], scalar=-0.5, in1=ISCR[:, :],
                op0=OP.is_lt, op1=OP.subtract), deps=[cc])
            qf = vop(lambda e, wf=wf: e.scalar_tensor_tensor(
                out=wf, in0=FSCR[:, :], scalar=0.5, in1=FSC2[:, :],
                op0=OP.is_gt, op1=OP.subtract),
                deps=last(qm, st["qf_prev"][b]) + list(extra_deps))
            stt = dma("sync", st["s4"][:, c, j, :], wf, deps=[qf],
                      ctr_name="qst")
            st["qf_prev"][b] = stt
            st["stores"].append(stt)

    q_fc = quant_max(wfcT, FF, QS, last(y_done, tables_done, ld_b64),
                     last(prev_z, tables_done))
    q_fc["scr"] = None
    q_fc["s4"] = sc_fc.rearrange("(c p) (j k) -> p c j k", p=128, k=1024)
    quant_pieces(q_fc, range(32), lambda b: AGT[:, b * 1024:(b + 1) * 1024])
    q_fc["done"] = last(*q_fc["stores"])

    v_evs = []
    V3 = V[:, :].rearrange("p (t k) -> p t k", t=NT)
    for oc in range(8):
        wb = oc % 2
        wsl = AW[:, wb * 2048:(wb + 1) * 2048]
        wdep = last(aw_free[wb]) if oc < 2 else [v_evs[8 * (oc - 2) + 7]]
        wv_ld = dma("gpsimd", wsl.rearrange("p (c k) -> p c k", c=ND),
                    wvT[:, oc * 256:(oc + 1) * 256]
                    .rearrange("(c p) k -> p c k", p=128),
                    deps=wdep, ctr_name="wvl")
        for tg in range(8):
            pi = tg % 2
            ps = PCD[pi]
            deps = last(wv_ld, y_done, pcd_free[pi], ld_on, ld_bvr)
            for tt in range(2):
                t = 2 * tg + tt
                for c in range(ND):
                    mm(Y[:, c * T + t * 128: c * T + (t + 1) * 128],
                       wsl[:, c * 256:(c + 1) * 256],
                       ps[:, tt * 256:(tt + 1) * 256], c == 0, False,
                       deps=deps if (c == 0 and tt == 0) else ())
                tok = mm(ONES[:, :], BRW[:, oc * 256:(oc + 1) * 256],
                         ps[:, tt * 256:(tt + 1) * 256], False, True,
                         inc=(tt == 1))
            ev = act(V3[:, 2 * tg:2 * tg + 2, oc * 256:(oc + 1) * 256],
                     ps[:, :].rearrange("p (t k) -> p t k", t=2),
                     AF.Silu, deps=[tok])
            pcd_free[pi] = ev
            v_evs.append(ev)
    aw_free = [v_evs[-1], v_evs[-2]]
    v_done = last(*v_evs)

    def head_rope(hidx, tok_off, dst, d0_off, d1_off, deps):
        n = 512
        a0 = act(UB0, QK[:, tok_off:tok_off + n], AF.Identity,
                 bias=BETT[:, hidx:hidx + 1], scale=GAMT[:, hidx:hidx + 1],
                 deps=deps)
        a1 = act(UB1, QK[:, T + tok_off:T + tok_off + n], AF.Identity,
                 bias=BETT[:, 4 + hidx:4 + hidx + 1],
                 scale=GAMT[:, 4 + hidx:4 + hidx + 1], deps=deps)
        cs = COS[:, tok_off:tok_off + n]
        sn = SIN[:, tok_off:tok_off + n]
        m1 = vop(lambda e: e.tensor_mul(out=M1, in0=UB0, in1=cs),
                 deps=last(a0, s_cos))
        m2 = vop(lambda e: e.tensor_mul(out=M2, in0=UB1, in1=sn),
                 deps=last(a1, s_sin))
        o1 = vop(lambda e: e.tensor_sub(
            out=dst[:, d0_off:d0_off + n], in0=M1, in1=M2), deps=[m1, m2])
        m3 = vop(lambda e: e.tensor_mul(out=M1, in0=UB1, in1=cs), deps=[o1])
        m4 = vop(lambda e: e.tensor_mul(out=M2, in0=UB0, in1=sn), deps=[o1])
        o2 = vop(lambda e: e.tensor_add(
            out=dst[:, d1_off:d1_off + n], in0=M1, in1=M2), deps=[m3, m4])
        return (last(a0, a1), last(o1, o2))

    LKF = AZQ
    lk_toks = []
    for cc in range(4):
        dep = last(qk_done, ld_gam, ld_bet,
                   zg_rd[3] if cc == 0 else None,
                   lk_toks[-1] if lk_toks else None)
        _, o = head_rope(3, cc * 512, LKF, cc * 512, 2048 + cc * 512, dep)
        lk_toks.append(o)
    lk_done = last(*lk_toks)
    lkt_evs = []
    ev = None
    for c2 in range(2):
        for tg in range(4):
            deps = last(lk_done, ev, pcd_free[0])
            for tt in range(4):
                t = tg * 4 + tt
                tok = mm(LKF[:, c2 * 2048 + t * 128: c2 * 2048 + (t + 1) * 128],
                         IDENT[:, :], PC[:, tt * 128:(tt + 1) * 128],
                         True, True, inc=(tt == 3),
                         deps=deps if tt == 0 else ())
            ev = vop(lambda e, c2=c2, tg=tg: e.tensor_copy(
                out=LKT[:, :].rearrange("p (t k) -> p t k", t=NT)
                [:, tg * 4:(tg + 1) * 4, c2 * 128:(c2 + 1) * 128],
                in_=PC[:, :].rearrange("p (t k) -> p t k", t=4)),
                deps=[tok])
            lkt_evs.append(ev)
    lkt_done = last(*lkt_evs)
    pcd_free[0] = lkt_evs[-1]

    q_pj = quant_max(wpjT, D, QS2, last(q_fc["done"]),
                     last(q_fc["st_prev"][0], q_fc["st_prev"][1]))
    q_pj["scr"] = None
    q_pj["s4"] = sc_pj.rearrange("(c p) (j k) -> p c j k", p=128, k=1024)

    regions8 = [PBIG[:, 0:512], PBIG[:, 512:1024], PBIG[:, 1024:1536],
                PBIG[:, 1536:2048], PA[:, 0:512], PA[:, 512:1024],
                PC[:, :], PD[:, :]]
    lkv_evs = []
    for m in range(2):
        for n4 in range(4):
            r = regions8[m * 4 + n4]
            deps = last(lkt_done, v_done, pcd_free[0], pcd_free[1], ev_tok)
            for t in range(NT):
                tok = mm(LKT[:, t * DQK + m * 128: t * DQK + (m + 1) * 128],
                         V[:, t * E + n4 * 512: t * E + (n4 + 1) * 512],
                         r, t == 0, t == NT - 1, inc=(t == NT - 1),
                         deps=deps if t == 0 else ())
            lkv_evs.append(act(
                LINKV[:, m * E + n4 * 512: m * E + (n4 + 1) * 512], r,
                AF.Identity, scale=1.0 / T, deps=[tok]))
    lkv_done = last(*lkv_evs)

    ld_bor = dma("gpsimd", BRW[:, 0:1024], bo_h, deps=last(v_done),
                 ctr_name="lps")
    XB0 = LKT[:, 0:1024]
    XB1 = LKT[:, 1024:2048]
    X2S0 = LKT[:, 2048:3072]
    X2S1 = LKT[:, 3072:4096]

    x2_sts = [None] * NT
    pa_free = [lkv_evs[4], lkv_evs[5]]
    pcd_free = [lkv_evs[6], lkv_evs[7]]
    pbig_free = [last(lkv_evs[0], lkv_evs[1]), last(lkv_evs[2], lkv_evs[3])]
    gt_free = v_evs[-1]
    gatee_prev = [None, None]
    wo_prev = [None, None]
    z2_rd = lkt_done
    epi_z2 = lkt_done
    epi_ol = lkt_done
    xb_prev = [lkv_done, lkv_done]
    x2s_prev = [None, None]

    pj_batch_dep = None
    pj_done_batches = [range(0, 11), range(11, 22), range(22, 32), range(0)]
    for C in range(4):
        base = C * 512
        d0 = last(qk_done, ms_att, lkv_done, z2_rd, pj_batch_dep)
        aq, hq = head_rope(0, base, QQ, 0, 512, d0)
        ak, hk = head_rope(2, base, QKK, 0, 512, [hq])
        al_, hl = head_rope(1, base, LQ, 0, 512, [hk])
        sdeps = last(hq, hk, pcd_free[0])
        for p in range(4):
            for c2 in range(2):
                mm(QKK[:, c2 * 512 + p * 128: c2 * 512 + (p + 1) * 128],
                   QQ[:, c2 * 512 + p * 128: c2 * 512 + (p + 1) * 128],
                   PC[:, p * 128:(p + 1) * 128], c2 == 0,
                   c2 == 1 and p != 3,
                   deps=sdeps if (p == 0 and c2 == 0) else ())
        tok = flush_mm(PC[:, 384:512], deps=last(ms_zr, ld_on))
        pc3 = PC[:, :].rearrange("p (c k) -> p c k", c=4)
        sq3 = SQT[:, :].rearrange("p (c k) -> p c k", c=4)
        at3 = ATT[:, :].rearrange("p (c k) -> p c k", c=4)
        b3 = BIAS64[0:64, :].unsqueeze(1).broadcast_to([64, 4, 64])
        sq_t = vop(lambda e: e.scalar_tensor_tensor(
            out=sq3[0:64, :, :], in0=pc3[0:64, :, 0:64], scalar=1.0 / G,
            in1=b3, op0=OP.mult, op1=OP.add), deps=last(tok, ld_b64))
        sq_b = vop(lambda e: e.scalar_tensor_tensor(
            out=sq3[64:128, :, :], in0=pc3[64:128, :, 64:128],
            scalar=1.0 / G, in1=b3, op0=OP.mult, op1=OP.add), deps=[sq_t])
        at_t = vop(lambda e: e.scalar_tensor_tensor(
            out=at3[0:64, :, 0:64], in0=sq3[0:64, :, :], scalar=0.0,
            in1=sq3[0:64, :, :], op0=OP.max, op1=OP.mult), deps=[sq_t])
        at_b = vop(lambda e: e.scalar_tensor_tensor(
            out=at3[64:128, :, 64:128], in0=sq3[64:128, :, :], scalar=0.0,
            in1=sq3[64:128, :, :], op0=OP.max, op1=OP.mult), deps=[sq_b])
        att_done = last(at_t, at_b)
        pcd_free[0] = at_b
        if dbg and C == 0:
            dma("gpsimd", dbg_t["d_att"], ATT[:, :], deps=[att_done],
                ctr_name="lps")

        gt_toks = []
        for Ei in range(NE):
            wb = Ei % 2
            prev_wg = gt_toks[-2] if len(gt_toks) >= 2 else gt_free
            wg_ld = dma("gpsimd", AW[:, wb * 1024:(wb + 1) * 1024]
                        .rearrange("p (c k) -> p c k", c=ND),
                        wgT[:, Ei * 128:(Ei + 1) * 128]
                        .rearrange("(c p) k -> p c k", p=128),
                        deps=last(prev_wg), ctr_name="wgl")
            pi = Ei % 2
            gdeps = last(wg_ld, pcd_free[pi], y_done)
            for c in range(ND):
                tok = mm(AW[:, wb * 1024 + c * 128: wb * 1024 + (c + 1) * 128],
                         Y[:, c * T + base: c * T + base + 512],
                         PCD[pi][:, :], c == 0, c == ND - 1,
                         inc=(c == ND - 1), deps=gdeps if c == 0 else ())
            gprev = gt_toks[-2] if len(gt_toks) >= 2 else gatee_prev[pi]
            agt_dep = (q_fc["qf_prev"][Ei // 2]
                       if (C == 0 and Ei < 4) else None)
            gev = act(GATEE[:, 0:512] if pi == 0 else GATEE[:, 512:1024],
                      PCD[pi][:, :], AF.Silu, bias=BGC[:, Ei:Ei + 1],
                      deps=last(tok, ld_bg, gprev))
            pcd_free[pi] = gev
            ph = Ei % 2
            adeps = last(pa_free[ph], lkv_done, hl, att_done, v_done)
            for m in range(2):
                mm(LINKV[:, m * E + Ei * 128: m * E + (Ei + 1) * 128],
                   LQ[:, m * 512:(m + 1) * 512],
                   PA[:, ph * 512:(ph + 1) * 512], m == 0, False,
                   deps=adeps if m == 0 else ())
            for p in range(4):
                mm(V[:, (4 * C + p) * E + Ei * 128:
                     (4 * C + p) * E + (Ei + 1) * 128],
                   ATT[:, p * 128:(p + 1) * 128],
                   PA[:, ph * 512 + p * 128: ph * 512 + (p + 1) * 128],
                   False, False)
            tok2 = flush_mm(PA[:, ph * 512: ph * 512 + 128])
            gt = vop(lambda e, Ei=Ei, ph=ph, pi=pi: e.tensor_tensor(
                out=AGT[:, Ei * 512:(Ei + 1) * 512],
                in0=PA[:, ph * 512:(ph + 1) * 512],
                in1=GATEE[:, pi * 512:(pi + 1) * 512], op=OP.mult),
                deps=last(tok2, gev, agt_dep))
            pa_free[ph] = gt
            gt_toks.append(gt)
        gts = last(*gt_toks)

        regs4 = (PBIG[:, 0:1024], PBIG[:, 1024:2048], PA[:, :], None)
        hd = last(gts, pbig_free[0], pbig_free[1], pa_free[0], pa_free[1],
                  pcd_free[0], pcd_free[1])
        for Ei in range(NE):
            wb = Ei % 2
            wo_ld = dma("gpsimd", WSLB[:, wb * 1024:(wb + 1) * 1024],
                        woT[Ei * 128:(Ei + 1) * 128, :],
                        deps=last(wo_prev[wb]), ctr_name="wol")
            wdeps = last(wo_ld, gts) + (hd if Ei == 0 else [])
            for k in range(4):
                for nh in range(2):
                    dst = (PCD[nh][:, :] if k == 3 else
                           regs4[k][:, nh * 512:(nh + 1) * 512])
                    tok = mm(AGT[:, Ei * 512 + k * 128:
                                 Ei * 512 + (k + 1) * 128],
                             WSLB[:, wb * 1024 + nh * 512:
                                  wb * 1024 + (nh + 1) * 512],
                             dst, Ei == 0, False,
                             inc=(k == 3 and nh == 1),
                             deps=wdeps if (k == 0 and nh == 0) else ())
            wo_prev[wb] = tok
        wo_last = []
        for k in range(4):
            for nh in range(2):
                dst = (PCD[nh][:, :] if k == 3 else
                       regs4[k][:, nh * 512:(nh + 1) * 512])
                tok = mm(ONES[:, :], BRW[:, nh * 512:(nh + 1) * 512],
                         dst, False, True, inc=(nh == 1),
                         deps=last(ld_bor, ld_on))
            wo_last.append(tok)
        for k in range(4):
            t_abs = 4 * C + k
            if k == 3:
                s1 = vop(lambda e: e.bn_stats(out=ST6[:, 0:6], in_=PC[:, :]),
                         deps=[wo_last[k]])
                s2 = vop(lambda e: e.bn_stats(out=ST6[:, 6:12], in_=PD[:, :]),
                         deps=[s1])
                ag = vop(lambda e: e.bn_aggr(out=MV[:, :], in_=ST6[:, 0:12]),
                         deps=[s2])
            else:
                ag = stats1024(regs4[k], [wo_last[k]])
            bb = ln_single(MV[:, 1:2], MV[:, 0:1], SA[:, 0:1],
                           SA[:, 1:2], [ag])
            if k == 3:
                z2t_a = act(Z2[:, 0:512], PC[:, :], AF.Identity,
                            bias=SA[:, 1:2], scale=SA[:, 0:1],
                            deps=last(bb, epi_z2))
                z2t = act(Z2[:, 512:1024], PD[:, :], AF.Identity,
                          bias=SA[:, 1:2], scale=SA[:, 0:1], deps=[z2t_a])
                pcd_free = [z2t, z2t]
            else:
                z2t = act(Z2, regs4[k], AF.Identity, bias=SA[:, 1:2],
                          scale=SA[:, 0:1], deps=last(bb, epi_z2))
            t2a = vop(lambda e: e.tensor_mul(out=T2E, in0=Z2,
                                             in1=LN2W[:, :]),
                      deps=last(z2t, ld_l2w))
            epi_z2 = t2a
            t2b = vop(lambda e: e.tensor_add(out=T2E, in0=T2E,
                                             in1=LN2B[:, :]),
                      deps=last(t2a, ld_l2b))
            ag2 = stats1024(T2E, [t2b])
            bb2 = ln_single(MV[:, 1:2], MV[:, 0:1], SA[:, 0:1],
                            SA[:, 1:2], [ag2])
            olnt = act(OLN, T2E, AF.Identity, bias=SA[:, 1:2],
                       scale=SA[:, 0:1], deps=last(bb2, epi_ol))
            par = t_abs % 2
            xb = XB0 if par == 0 else XB1
            x2s = X2S0 if par == 0 else X2S1
            xre = dma("gpsimd", xb, x_h[t_abs * 128:(t_abs + 1) * 128, :],
                      deps=last(xb_prev[par]), ctr_name="xrl")
            fin = vop(lambda e, xb=xb, x2s=x2s: e.tensor_add(
                out=x2s, in0=xb, in1=OLN),
                deps=last(olnt, xre, x2s_prev[par]))
            epi_ol = fin
            xb_prev[par] = fin
            stok = dma("sync", x2_scr[t_abs * 128:(t_abs + 1) * 128, :],
                       x2s, deps=[fin], ctr_name="st")
            x2s_prev[par] = stok
            x2_sts[t_abs] = stok
            if k < 2:
                pbig_free[k] = fin
            elif k == 2:
                pa_free = [fin, fin]
        gt_free = wo_last[-1]
        z2_rd = gts
        n0 = len(q_pj["stores"])
        quant_pieces(q_pj, pj_done_batches[C],
                     lambda b: AMSC[:, b * 1024:(b + 1) * 1024],
                     extra_deps=last(hl))
        if len(q_pj["stores"]) > n0:
            pj_batch_dep = last(*q_pj["stores"][n0:])

    q_pj["done"] = last(*q_pj["stores"])
    x2_done = last(*[x2_sts[t] for t in range(NT)])
    if dbg:
        for nm, src in (("d_y", Y), ("d_qk", QK), ("d_v", V),
                        ("d_lkv", LINKV), ("d_cos", COS), ("d_sin", SIN)):
            dma("gpsimd", dbg_t[nm], src[:, :],
                deps=last(y_done, qk_done, v_done, lkv_done, x2_done),
                ctr_name="lps")
        dma("gpsimd", dbg_t["d_x2"], x2_scr, deps=last(x2_done),
            ctr_name="lps")

    ld_bpr = dma("sync", XT[0:1, 0:1024], bpj_h,
                 deps=last(q_pj["st_prev"][0], q_pj["st_prev"][1]),
                 ctr_name="qsc")
    bpr_tok = act(BRW[:, 1024:2048], XT[0:1, 0:1024], AF.Identity,
                  scale=QS2[0:1, 2:3], deps=last(ld_bpr, q_pj["sinv"],
                                                 v_done))
    if dbg:
        dma("gpsimd", dbg_t["d_wfc"], sc_fc, deps=[q_fc["done"]],
            ctr_name="lps")
        dma("sync", dbg_t["d_qs"], QS[:, :],
            deps=last(q_fc["sinv2"], q_fc["s2"]), ctr_name="b64")
        dma("sync", dbg_t["d_qs2"], QS2[:, :],
            deps=last(q_pj["sinv2"], q_pj["s2"]), ctr_name="b64")
        dma("sync", dbg_t["d_mxp"], MXP[:, :], deps=last(q_fc["sinv2"]),
            ctr_name="b64")

    lkv_rd_done = gts
    late_dep = last(hl, att_done, gts, bpr_tok)
    ld_afc = dma("gpsimd", AFCT.rearrange("p (c k) -> p c k", c=ND),
                 afcT_h.rearrange("(c p) k -> p c k", p=128), deps=late_dep,
                 ctr_name="lps")
    ld_apj = dma("gpsimd", APJT.rearrange("p (c k) -> p c k", c=NF),
                 apjT_h.rearrange("(c p) k -> p c k", p=128), deps=late_dep,
                 ctr_name="lps")
    ld_bpjT = dma("gpsimd", BPJT, bpjT_h, deps=late_dep, ctr_name="lps")
    ld_bfcA = dma("gpsimd", BFCTA, bfcT_h[:, 0:2048], deps=late_dep,
                  ctr_name="lps")
    ld_bfcB = dma("gpsimd", BFCTB, bfcT_h[:, 2048:4096], deps=late_dep,
                  ctr_name="lps")

    Y2 = AY
    H = AV
    regs3 = [PBIG[:, 0:1024], PBIG[:, 1024:2048], PA[:, :]]
    reg_free = [last(pbig_free[0]), last(pbig_free[1]),
                last(pa_free[0], pa_free[1])]
    pcq_free = last(pcd_free[0], pcd_free[1])
    wfc_prev = [None, None]
    wpj_prev = [None, None]
    sfc4 = sc_fc.rearrange("(c p) (o k) -> p c o k", p=128, k=128)
    spj3 = sc_pj.rearrange("(c p) k -> p c k", p=128)
    out_sts = []
    azq_rd = epi_ol
    y_rd = gts
    v_rd = gts
    x2ld_prev = [None, None]
    x2ld2_prev = [None, None]
    outt_prev = [None, None]
    oln_prev = bpr_tok

    for Cf in range(2):
        tb = Cf * 8
        ztoks_all = []
        ev = None
        for g2 in range(2):
            z_toks = []
            for tt in range(4):
                t_abs = tb + g2 * 4 + tt
                par = tt % 2
                xbuf = LKT[:, par * 1024:(par + 1) * 1024]
                xld = dma("sync", xbuf,
                          x2_scr[t_abs * 128:(t_abs + 1) * 128, :],
                          deps=last(x2_sts[t_abs], x2ld_prev[par]),
                          ctr_name="x2l")
                ag = stats1024(xbuf, [xld])
                bb = ln_single(MV[:, 1:2], MV[:, 0:1], SA[:, 0:1],
                               SA[:, 1:2], [ag])
                ztok = vop(lambda e, xbuf=xbuf, tt=tt: e.tensor_scalar(
                    out=AZQ[:, tt * D:(tt + 1) * D], in0=xbuf,
                    scalar1=SA[:, 0:1], scalar2=SA[:, 1:2],
                    op0=OP.mult, op1=OP.add),
                    deps=last(bb, azq_rd))
                x2ld_prev[par] = ztok
                z_toks.append(ztok)
            for c in range(ND):
                q = c % 4
                reg = PBIG[:, q * 512:(q + 1) * 512]
                deps = last(*(z_toks + [ev, ld_id] + reg_free[q // 2]))
                for tt in range(4):
                    tok = mm(AZQ[:, tt * D + c * 128: tt * D + (c + 1) * 128],
                             IDENT[:, :], reg[:, tt * 128:(tt + 1) * 128],
                             True, True, inc=(tt == 3),
                             deps=deps if tt == 0 else ())
                ev = vop(lambda e, c=c, g2=g2, reg=reg: e.tensor_copy(
                    out=Y2[:, c * 1024 + g2 * 512:
                           c * 1024 + (g2 + 1) * 512],
                    in_=reg), deps=[tok])
                reg_free[q // 2] = [ev]
            azq_rd = tok
            ztoks_all += z_toks
        y2_done = ev

        t1toks = []
        for h2 in range(2):
            deps = last(y2_done, ld_afc, pcq_free, hl)
            for c in range(ND):
                tok = mm(AFCT[:, c * 16:(c + 1) * 16],
                         Y2[:, c * 1024 + h2 * 512: c * 1024 + (h2 + 1) * 512],
                         PC[0:16, :], c == 0, c == ND - 1,
                         inc=(c == ND - 1), deps=deps if c == 0 else ())
            t1toks.append(act(T1[0:16, h2 * 512:(h2 + 1) * 512], PC[0:16, :],
                              AF.Identity, scale=QS[0:16, 5:6],
                              deps=last(tok, q_fc["sinv2"])))
            pcq_free = t1toks[-1]
        t1_done = last(*t1toks)
        h_toks = []
        for o1 in range(NF):
            wb = o1 % 2
            wfc_ld = dma("sync", AW[:, wb * 1024:(wb + 1) * 1024],
                         sfc4[:, :, o1, :],
                         deps=last(wfc_prev[wb], q_fc["done"], gt_free),
                         ctr_name="wfl")
            ri = o1 % 3
            reg = regs3[ri]
            deps = last(wfc_ld, y2_done, t1_done, ld_bfcA, ld_bfcB,
                        *reg_free[ri])
            for c in range(ND):
                for nh in range(2):
                    mm(AW[:, wb * 1024 + c * 128: wb * 1024 + (c + 1) * 128],
                       Y2[:, c * 1024 + nh * 512: c * 1024 + (nh + 1) * 512],
                       reg[:, nh * 512:(nh + 1) * 512], c == 0, False,
                       deps=deps if (c == 0 and nh == 0) else ())
            bsrc = BFCTA if o1 < 16 else BFCTB
            bof = (o1 % 16) * 128
            for nh in range(2):
                tok = mm(bsrc[:, bof:bof + 128],
                         T1[0:16, nh * 512:(nh + 1) * 512],
                         reg[:, nh * 512:(nh + 1) * 512], False, True,
                         inc=(nh == 1))
            hev = act(H[:, o1 * 1024:(o1 + 1) * 1024], reg, AF.Gelu,
                      bias=BFCC[:, o1:o1 + 1], scale=QS[:, 3:4],
                      deps=last(tok, q_fc["s"], ld_bfc, v_rd))
            h_toks.append(hev)
            reg_free[ri] = [hev]
            wfc_prev[wb] = hev
        h_done = last(*h_toks)
        t2toks = []
        for h2 in range(2):
            deps = last(h_done, ld_apj, pcq_free)
            for fc in range(NF):
                tok = mm(APJT[:, fc * 16:(fc + 1) * 16],
                         H[:, fc * 1024 + h2 * 512:
                           fc * 1024 + (h2 + 1) * 512],
                         PC[0:16, :], fc == 0, fc == NF - 1,
                         inc=(fc == NF - 1), deps=deps if fc == 0 else ())
            t2toks.append(act(T2L[0:16, h2 * 512:(h2 + 1) * 512], PC[0:16, :],
                              AF.Identity, scale=QS2[0:16, 5:6],
                              deps=last(tok, q_pj["sinv2"])))
            pcq_free = t2toks[-1]
        t2_done = last(*t2toks)
        for half in range(4):
            ts_pair = (2 * half, 2 * half + 1)
            regs = (PBIG[:, 0:1024], PBIG[:, 1024:2048])
            hd = last(*(reg_free[0] + reg_free[1]))
            for fc in range(NF):
                wb = fc % 2
                wpj_ld = dma("sync", WSLB[:, wb * 1024:(wb + 1) * 1024],
                             spj3[:, fc, :],
                             deps=last(wpj_prev[wb], q_pj["done"]),
                             ctr_name="wpl")
                wdeps = last(wpj_ld, h_done, t2_done) + \
                    (hd if fc == 0 else [])
                for k in range(2):
                    ts_i = ts_pair[k]
                    for nh in range(2):
                        tok = mm(H[:, fc * 1024 + ts_i * 128:
                                   fc * 1024 + (ts_i + 1) * 128],
                                 WSLB[:, wb * 1024 + nh * 512:
                                      wb * 1024 + (nh + 1) * 512],
                                 regs[k][:, nh * 512:(nh + 1) * 512],
                                 fc == 0, False, inc=(k == 1 and nh == 1),
                                 deps=wdeps if (k == 0 and nh == 0) else ())
                wpj_prev[wb] = tok
            f2_last = []
            for k in range(2):
                ts_i = ts_pair[k]
                for nh in range(2):
                    mm(T2L[0:16, ts_i * 128:(ts_i + 1) * 128],
                       BPJT[0:16, nh * 512:(nh + 1) * 512],
                       regs[k][:, nh * 512:(nh + 1) * 512], False, False,
                       deps=last(ld_bpjT, t2_done))
                for nh in range(2):
                    tok = mm(ONES[:, :],
                             BRW[:, 1024 + nh * 512:1024 + (nh + 1) * 512],
                             regs[k][:, nh * 512:(nh + 1) * 512], False, True,
                             inc=(nh == 1), deps=last(bpr_tok))
                f2_last.append(tok)
            for k in range(2):
                ts_i = ts_pair[k]
                t_abs = tb + ts_i
                par = t_abs % 2
                reg = regs[k]
                ag = stats1024(reg, [f2_last[k]])
                vp = vop(lambda e: e.scalar_tensor_tensor(
                    out=SA[:, 2:3], in0=MV[:, 1:2], scalar=QS2[:, 4:5],
                    in1=EPSC[:, :], op0=OP.mult, op1=OP.add),
                    deps=last(ag, ms_eps, q_pj["s2"]))
                w2r = vop(lambda e: e.reciprocal(out=SA[:, 3:4],
                                                 in_=SA[:, 2:3]), deps=[vp])
                aa0 = act(SA[:, 4:5], SA[:, 3:4], AF.Sqrt, deps=[w2r])
                aa = vop(lambda e: e.tensor_tensor(
                    out=SA[:, 0:1], in0=SA[:, 4:5], in1=QS2[:, 3:4],
                    op=OP.mult), deps=[aa0])
                bb = vop(lambda e: e.scalar_tensor_tensor(
                    out=SA[:, 1:2], in0=MV[:, 0:1], scalar=-1.0,
                    in1=SA[:, 0:1], op0=OP.mult, op1=OP.mult), deps=[aa])
                olnt = act(OLN, reg, AF.Identity, bias=SA[:, 1:2],
                           scale=SA[:, 0:1], deps=last(bb, oln_prev))
                x2b = LKT[:, (2 + par) * 1024:(3 + par) * 1024]
                x2l = dma("sync", x2b,
                          x2_scr[t_abs * 128:(t_abs + 1) * 128, :],
                          deps=last(x2_sts[t_abs], x2ld2_prev[par]),
                          ctr_name="x2l")
                ob2 = FSCR[:, :] if par == 0 else FSC2[:, :]
                fin = vop(lambda e, x2b=x2b, ob2=ob2: e.tensor_add(
                    out=ob2, in0=x2b, in1=OLN),
                    deps=last(olnt, x2l, outt_prev[par]))
                oln_prev = fin
                x2ld2_prev[par] = fin
                stok = dma("sync", out_h[t_abs * 128:(t_abs + 1) * 128, :],
                           ob2, deps=[fin], ctr_name="st")
                outt_prev[par] = stok
                out_sts.append(stok)
                reg_free[k] = [fin]
        azq_rd = oln_prev

    waits("sync", last(*out_sts))

    with nc.Block() as block:
        @block.sync
        def _(eng):
            for nm, fn in ops:
                if nm == "sync":
                    fn(eng)

        @block.gpsimd
        def _(eng):
            for nm, fn in ops:
                if nm == "gpsimd":
                    fn(eng)

        @block.tensor
        def _(eng):
            for nm, fn in ops:
                if nm == "tensor":
                    fn(eng)

        @block.vector
        def _(eng):
            for nm, fn in ops:
                if nm == "vector":
                    fn(eng)

        @block.scalar
        def _(eng):
            for nm, fn in ops:
                if nm == "scalar":
                    fn(eng)

    stack.close()
    return nc, dbg_t


def _prep_inputs(inputs):
    import ml_dtypes
    bf = ml_dtypes.bfloat16
    f32 = np.float32
    ct = lambda a: np.ascontiguousarray(np.asarray(a), dtype=f32)

    common = {
        "wqkT": ct(np.asarray(inputs["L0_attn_Wqk"]).T),
        "wvT": ct(np.asarray(inputs["L0_attn_Wv"]).T),
        "wgT": ct(np.asarray(inputs["L0_attn_Wg"]).T),
        "woT": ct(np.asarray(inputs["L0_attn_Wo"]).T),
        "wfcT": ct(np.asarray(inputs["L0_ffn_W_fc"]).T),
        "wpjT": ct(np.asarray(inputs["L0_ffn_W_proj"]).T),
        "afcT": ct(np.asarray(inputs["L0_ffn_A_fc"]).T),
        "bfcT": ct(np.asarray(inputs["L0_ffn_B_fc"]).T),
        "apjT": ct(np.asarray(inputs["L0_ffn_A_proj"]).T),
        "bpjT": ct(np.asarray(inputs["L0_ffn_B_proj"]).T),
        "bqk": ct(np.asarray(inputs["L0_attn_bqk"]).reshape(NQ, 128).T),
        "bv": ct(np.asarray(inputs["L0_attn_bv"]).reshape(1, E)),
        "bg": ct(np.asarray(inputs["L0_attn_bg"]).reshape(NE, 128).T),
        "bo": ct(np.asarray(inputs["L0_attn_bo"]).reshape(1, D)),
        "bfc": ct(np.asarray(inputs["L0_ffn_b_fc"]).reshape(NF, 128).T),
        "bpj": ct(np.asarray(inputs["L0_ffn_b_proj"]).reshape(1, D)),
        "ln1w": ct(np.asarray(inputs["L0_attn_ln1_w"]).reshape(ND, 128).T),
        "ln1b": ct(np.asarray(inputs["L0_attn_ln1_b"]).reshape(ND, 128).T),
        "ln2wr": ct(np.broadcast_to(np.asarray(inputs["L0_attn_ln2_w"]),
                                    (128, D))),
        "ln2br": ct(np.broadcast_to(np.asarray(inputs["L0_attn_ln2_b"]),
                                    (128, D))),
        "gam": ct(np.asarray(inputs["L0_attn_gamma"]).reshape(4, 2, 128)
                  .transpose(2, 1, 0).reshape(128, 8)),
        "bet": ct(np.asarray(inputs["L0_attn_beta"]).reshape(4, 2, 128)
                  .transpose(2, 1, 0).reshape(128, 8)),
        "posemb": ct(np.asarray(inputs["L0_attn_pos_emb"]).reshape(65, 1)),
    }
    idx = np.arange(G)
    dist = np.clip(np.abs(idx[:, None] - idx[None, :]), 0, 32)
    onehot = np.zeros((65, G * G), dtype=bf)
    onehot[dist.reshape(-1), np.arange(G * G)] = 1
    common["onehot"] = onehot
    common["identb"] = np.eye(128, dtype=bf)
    common["identf"] = np.eye(128, dtype=f32)
    common["onesrow"] = np.ones((1, 128), dtype=bf)
    common["onesrowf"] = np.ones((1, 128), dtype=f32)
    x = np.ascontiguousarray(np.asarray(inputs["L0_attn_x"]), dtype=f32)
    return [dict(common, x=x[i]) for i in range(NCORES)]


def kernel(**inputs):
    in_maps = _prep_inputs(inputs)
    nc, _ = build_nc(dbg=False)
    res = run_bass_kernel_spmd(nc, in_maps, core_ids=list(range(NCORES)))
    out = np.stack([np.asarray(r["out"]) for r in res.results], axis=0)
    return out.astype(np.float32)

# version 15
# speedup vs baseline: 1.4487x; 1.1313x over previous
import math
import numpy as np
import concourse.bass as bass
import concourse.mybir as mybir
from concourse.bass_utils import run_bass_kernel_spmd

F32 = mybir.dt.float32
BF16 = mybir.dt.bfloat16
I32 = mybir.dt.int32
I8 = mybir.dt.int8
AX = mybir.AxisListType.X
AF = mybir.ActivationFunctionType
OP = mybir.AluOpType

NCORES = 8
T = 2048
D = 1024
DQK = 256
E = 2048
FF = 4096
G = 64
NT = T // 128
ND = D // 128
NE = E // 128
NQ = DQK // 128
NF = FF // 128
EPS = 1e-5
PI = math.pi
INV2PI = 1.0 / (2.0 * PI)
PI_LO = float(np.nextafter(np.float32(np.pi), np.float32(0.0)))
LORA_SCALING = 2.0


def build_nc(dbg=False):
    nc = bass.Bass("TRN2", target_bir_lowering=False)

    dr = lambda name, shape, dt, kind: nc.dram_tensor(name, shape, dt, kind=kind).ap()
    x_h = dr("x", [T, D], F32, "ExternalInput")
    wqkT = dr("wqkT", [D, DQK], F32, "ExternalInput")
    wvT = dr("wvT", [D, E], F32, "ExternalInput")
    wgT = dr("wgT", [D, E], F32, "ExternalInput")
    woT = dr("woT", [E, D], F32, "ExternalInput")
    wfcT = dr("wfcT", [D, FF], F32, "ExternalInput")
    wpjT = dr("wpjT", [FF, D], F32, "ExternalInput")
    afcT_h = dr("afcT", [D, 16], F32, "ExternalInput")
    bfcT_h = dr("bfcT", [16, FF], F32, "ExternalInput")
    apjT_h = dr("apjT", [FF, 16], F32, "ExternalInput")
    bpjT_h = dr("bpjT", [16, D], F32, "ExternalInput")
    bqk_h = dr("bqk", [128, NQ], F32, "ExternalInput")
    bv_h = dr("bv", [1, E], F32, "ExternalInput")
    bg_h = dr("bg", [128, NE], F32, "ExternalInput")
    bo_h = dr("bo", [1, D], F32, "ExternalInput")
    bfc_h = dr("bfc", [128, NF], F32, "ExternalInput")
    bpj_h = dr("bpj", [1, D], F32, "ExternalInput")
    ln1w_h = dr("ln1w", [128, ND], F32, "ExternalInput")
    ln1b_h = dr("ln1b", [128, ND], F32, "ExternalInput")
    ln2w_h = dr("ln2wr", [128, D], F32, "ExternalInput")
    ln2b_h = dr("ln2br", [128, D], F32, "ExternalInput")
    gam_h = dr("gam", [128, 8], F32, "ExternalInput")
    bet_h = dr("bet", [128, 8], F32, "ExternalInput")
    pos_h = dr("posemb", [65, 1], F32, "ExternalInput")
    oneh_h = dr("onehot", [65, G * G], BF16, "ExternalInput")
    ident_h = dr("identb", [128, 128], BF16, "ExternalInput")
    identf_h = dr("identf", [128, 128], F32, "ExternalInput")
    ones_h = dr("onesrow", [1, 128], BF16, "ExternalInput")
    onesf_h = dr("onesrowf", [1, 128], F32, "ExternalInput")
    out_h = dr("out", [T, D], F32, "ExternalOutput")
    sc_fc = nc.dram_tensor("scr_fc", [D, FF], I8, kind="Internal").ap()
    scr1 = nc.dram_tensor("scr_1", [1, 1], F32, kind="Internal").ap()
    sc_pj = nc.dram_tensor("scr_pj", [FF, D], I8, kind="Internal").ap()
    x2_scr = nc.dram_tensor("scr_x2", [T, D], BF16, kind="Internal").ap()

    dbg_t = {}
    if dbg:
        for nm, shape in (("d_qs", [128, 8]), ("d_qs2", [128, 8]),
                          ("d_mxp", [128, 32]),
                          ("d_y", [128, ND * T]), ("d_qk", [128, NQ * T]),
                          ("d_v", [128, NT * E]), ("d_lkv", [128, 2 * E]),
                          ("d_x2", [T, D]), ("d_att", [128, 512]),
                          ("d_cos", [128, T]), ("d_sin", [128, T]),
                          ("d_wfc", [D, FF])):
            dbg_t[nm] = dr(nm, shape, F32, "ExternalOutput")

    al = nc.alloc_sbuf_tensor
    GAMT = al("GAMT", [128, 8], F32)
    BETT = al("BETT", [128, 8], F32)
    BQK = al("BQK", [128, NQ], F32)
    BGC = al("BGC", [128, NE], F32)
    BFCC = al("BFCC", [128, NF], F32)
    LN1W = al("LN1W", [128, ND], F32)
    LN1B = al("LN1B", [128, ND], F32)
    LN2W = al("LN2W", [128, D], BF16)
    LN2B = al("LN2B", [128, D], BF16)
    BRW = al("BRW", [1, E], BF16)
    POSE = al("POSE", [65, 1], BF16)
    IDENT = al("IDENT", [128, 128], BF16)
    IDENTF = al("IDENTF", [128, 128], F32)
    ONES = al("ONES", [1, 128], BF16)
    ZROW = al("ZROW", [1, 128], BF16)
    ONESF = al("ONESF", [1, 128], F32)
    BIAS64 = al("BIAS64", [64, 64], F32)
    EPSC = al("EPSC", [128, 1], F32)
    ST6 = al("ST6", [128, 12], F32)
    MV = al("MV", [128, 2], F32)
    SA = al("SA", [128, 8], F32)
    MXP = al("MXP", [128, 32], F32)
    QS = al("QS", [128, 8], F32)
    QS2 = al("QS2", [128, 8], F32)
    COS = al("COS", [128, T], BF16)
    SIN = al("SIN", [128, T], BF16)
    XT = al("XT", [128, 2 * D], F32)
    FSCR = al("FSCR", [128, 1024], F32)
    FSC2 = al("FSC2", [128, 1024], F32)
    ISCR = al("ISCR", [128, 1024], I32)
    AV = al("AV", [128, NT * E], BF16)
    AY = al("AY", [128, ND * T], BF16)
    AGT = al("AGT", [128, NE * 512], BF16)
    AZQ = al("AZQ", [128, 4096], BF16)
    AMSC = al("AMSC", [128, 2048], BF16)
    QK = al("QK", [128, NQ * T], BF16)
    LINKV = al("LINKV", [128, 2 * E], BF16)
    AW = al("AW", [128, 4096], BF16)
    WSLB = al("WSLB", [128, 2048], BF16)
    LKT = al("LKT", [128, NT * DQK], BF16)
    QQ = al("QQ", [128, 1024], BF16)
    QKK = al("QKK", [128, 1024], BF16)
    LQ = al("LQ", [128, 1024], BF16)
    GATEE = al("GATEE", [128, 1024], BF16)
    ATT = al("ATT", [128, 512], BF16)

    ONEH = AV[0:65, 0:G * G]
    SQT = FSCR[:, 0:256]
    M1 = AMSC[:, 0:512]
    M2 = AMSC[:, 512:1024]
    UB0 = AMSC[:, 1024:1536]
    UB1 = AMSC[:, 1536:2048]
    Z2 = AZQ[:, 0:1024]
    T2E = AZQ[:, 1024:2048]
    OLN = AZQ[:, 2048:3072]
    BPJT = QQ[0:16, 0:1024]
    APJT = QKK[:, 0:512]
    AFCT = QKK[:, 512:640]
    T1 = LQ[0:16, 0:1024]
    T2L = GATEE[0:16, 0:1024]
    BFCTA = COS[0:16, :]
    BFCTB = SIN[0:16, :]
    AGTI8 = AGT.bitcast(I8)
    AMSCI8 = AMSC.bitcast(I8)

    PBIG = nc.alloc_psum_tensor("PBIG", [128, 2048], F32)
    PA = nc.alloc_psum_tensor("PA", [128, 1024], F32)
    PC = nc.alloc_psum_tensor("PC", [128, 512], F32)
    PD = nc.alloc_psum_tensor("PD", [128, 512], F32)
    PCD = [PC, PD]

    ops = []
    EMIT = {k: (lambda fn, k=k: ops.append((k, fn)))
            for k in ("tensor", "vector", "scalar", "sync", "gpsimd")}

    class Ctr:
        def __init__(self, h):
            self.h = h
            self.n = 0

    from contextlib import ExitStack
    stack = ExitStack()
    sem = {k: Ctr(stack.enter_context(nc.semaphore(k)))
           for k in ("ld", "gp", "gpc", "te", "ve", "se", "st",
                     "lps", "b64", "qsc",
                     "qld0", "qld1", "xs0", "xs1", "wgl0", "wgl1",
                     "wol0", "wol1", "xrl0", "xrl1", "wvl0", "wvl1",
                     "wfl0", "wfl1", "wpl0", "wpl1", "x2l0", "x2l1",
                     "qst0", "qst1")}
    pipetog = {}
    hw = {e: {} for e in EMIT}

    last_dma = {e: None for e in EMIT}

    def wait(engine, ctr, val):
        if val is None or val <= 0:
            return
        if val > hw[engine].get(id(ctr), 0):
            hw[engine][id(ctr)] = val
            last_dma[engine] = None
            EMIT[engine](lambda e, h=ctr.h, v=val: e.wait_ge(h, v))

    def waits(engine, deps):
        for d in deps:
            if d is None:
                continue
            if isinstance(d, list):
                waits(engine, d)
            else:
                wait(engine, d[0], d[1])

    def last(*tokens):
        out = {}
        for t in tokens:
            if t is None:
                continue
            if isinstance(t, list):
                for u in t:
                    if u is not None and out.get(id(u[0]), (None, 0))[1] < u[1]:
                        out[id(u[0])] = u
                continue
            if out.get(id(t[0]), (None, 0))[1] < t[1]:
                out[id(t[0])] = t
        return list(out.values())

    def dma(engine, out_ap, in_ap, deps=(), ctr_name=None):
        waits(engine, deps)
        if ctr_name is not None and (ctr_name + "0") in sem:
            t = pipetog.get(ctr_name, 0)
            pipetog[ctr_name] = 1 - t
            c = sem[ctr_name + str(t)]
        else:
            c = sem[ctr_name] if ctr_name else (
                sem["ld"] if engine == "sync" else sem["gp"])
        wait(engine, c, c.n)
        c.n += 16
        n = c.n
        EMIT[engine](lambda e, o=out_ap, i=in_ap, h=c.h:
                     e.dma_start(out=o, in_=i).then_inc(h, 16))
        return (c, n)

    def mm(lhsT, rhs, out, start, stop, inc=False, deps=()):
        waits("tensor", deps)
        if inc:
            sem["te"].n += 1
            n = sem["te"].n
            EMIT["tensor"](lambda e, a=lhsT, b=rhs, o=out, s1=start, s2=stop,
                           h=sem["te"].h:
                           e.matmul(o, a, b, start=s1, stop=s2).then_inc(h, 1))
            return (sem["te"], n)
        EMIT["tensor"](lambda e, a=lhsT, b=rhs, o=out, s1=start, s2=stop:
                       e.matmul(o, a, b, start=s1, stop=s2))
        return None

    def act(out_ap, in_ap, func, bias=0.0, scale=1.0, deps=()):
        waits("scalar", deps)
        sem["se"].n += 1
        n = sem["se"].n
        EMIT["scalar"](lambda e, o=out_ap, i=in_ap, f=func, b=bias, s=scale,
                       h=sem["se"].h:
                       e.activation(out=o, in_=i, func=f, bias=b, scale=s)
                       .then_inc(h, 1))
        return (sem["se"], n)

    def vop(emitfn, deps=()):
        waits("vector", deps)
        sem["ve"].n += 1
        n = sem["ve"].n
        EMIT["vector"](lambda e, f=emitfn, h=sem["ve"].h: f(e).then_inc(h, 1))
        return (sem["ve"], n)

    def gop(emitfn, deps=()):
        waits("gpsimd", deps)
        sem["gpc"].n += 1
        n = sem["gpc"].n
        EMIT["gpsimd"](lambda e, f=emitfn, h=sem["gpc"].h: f(e).then_inc(h, 1))
        return (sem["gpc"], n)

    def flush_mm(region, deps=(), m=128):
        return mm(ZROW[:, 0:m], ONES[:, :], region, False, True, inc=True,
                  deps=deps)

    ld_gam = dma("sync", GAMT[:, :], gam_h)
    ld_bet = dma("sync", BETT[:, :], bet_h)
    ld_bqk = dma("sync", BQK[:, :], bqk_h)
    ld_bg = dma("sync", BGC[:, :], bg_h)
    ld_bfc = dma("sync", BFCC[:, :], bfc_h)
    ld_l1w = dma("sync", LN1W[:, :], ln1w_h)
    ld_l1b = dma("sync", LN1B[:, :], ln1b_h)
    ld_onh = dma("sync", ONEH, oneh_h)
    ld_id = dma("sync", IDENT[:, :], ident_h)
    ld_on = dma("sync", ONES[:, :], ones_h)
    ld_onf = dma("sync", ONESF[:, :], onesf_h)
    ld_l2w = dma("gpsimd", LN2W[:, :], ln2w_h)
    ld_l2b = dma("gpsimd", LN2B[:, :], ln2b_h)
    ld_bvr = dma("gpsimd", BRW[:, :], bv_h)
    ld_pos = dma("gpsimd", POSE[:, :], pos_h)
    pre_ld = (sem["ld"], sem["ld"].n)
    ld_gam = ld_bet = ld_bqk = ld_bg = ld_bfc = pre_ld
    ld_l1w = ld_l1b = ld_onh = ld_id = ld_on = ld_onf = pre_ld
    pre_gp = (sem["gp"], sem["gp"].n)
    ld_l2w = ld_l2b = ld_bvr = ld_pos = pre_gp
    iot1 = gop(lambda e: e.iota(FSCR[0:1, 0:128], [[1, 128]],
                                channel_multiplier=0,
                                allow_small_or_imprecise_dtypes=True))
    ms_att = gop(lambda e: e.memset(ATT[:, :], 0.0))
    ms_zr = gop(lambda e: e.memset(ZROW[:, :], 0.0))
    ms_eps = gop(lambda e: e.memset(EPSC[:, :], EPS))
    a_invf = act(IDENTF[0:1, 0:128], FSCR[0:1, 0:128], AF.Exp,
                 scale=-math.log(10000.0) / 128.0, deps=[iot1])

    def sincos_half2(hh, shift, dst_half, pdeps):
        posr = FSC2[0:1, 0:1024]
        pio = gop(lambda e, hh=hh: e.iota(posr, [[1, 1024]], base=hh * 1024,
                                          channel_multiplier=0,
                                          allow_small_or_imprecise_dtypes=True),
                  deps=list(pdeps))
        mmdeps = last(a_invf, pio) + list(pdeps)
        for j in range(2):
            ttok = mm(IDENTF[0:1, 0:128], posr[:, j * 512:(j + 1) * 512],
                      PA[:, j * 512:(j + 1) * 512], True, True, inc=(j == 1),
                      deps=mmdeps if j == 0 else ())
        ts = vop(lambda e: e.tensor_scalar(
            out=FSCR[:, 0:1024], in0=PA[:, :], scalar1=INV2PI, scalar2=shift,
            op0=OP.mult, op1=OP.add), deps=last(ttok))
        k1 = vop(lambda e: e.tensor_single_scalar(
            out=ISCR[:, 0:1024], in_=FSCR[:, 0:1024], scalar=1.0,
            op=OP.mult), deps=[ts])
        c2 = vop(lambda e: e.scalar_tensor_tensor(
            out=XT[:, 0:1024], in0=FSCR[:, 0:1024], scalar=1.0,
            in1=ISCR[:, 0:1024], op0=OP.mult, op1=OP.subtract), deps=[k1])
        km = vop(lambda e: e.scalar_tensor_tensor(
            out=XT[:, 1024:2048], in0=XT[:, 0:1024], scalar=-0.5,
            in1=ISCR[:, 0:1024], op0=OP.is_lt, op1=OP.subtract), deps=[c2])
        kf = vop(lambda e: e.scalar_tensor_tensor(
            out=XT[:, 0:1024], in0=XT[:, 0:1024], scalar=0.5,
            in1=XT[:, 1024:2048], op0=OP.is_gt, op1=OP.subtract), deps=[km])
        dd = vop(lambda e: e.scalar_tensor_tensor(
            out=FSCR[:, 0:1024], in0=XT[:, 0:1024], scalar=-1.0,
            in1=FSCR[:, 0:1024], op0=OP.mult, op1=OP.add), deps=[kf])
        r1 = vop(lambda e: e.tensor_scalar(
            out=FSCR[:, 0:1024], in0=FSCR[:, 0:1024], scalar1=2.0 * PI,
            scalar2=PI_LO, op0=OP.mult, op1=OP.min), deps=[dd])
        r2 = vop(lambda e: e.tensor_single_scalar(
            out=FSCR[:, 0:1024], in_=FSCR[:, 0:1024], scalar=-PI_LO,
            op=OP.max), deps=[r1])
        return act(dst_half, FSCR[:, 0:1024], AF.Sin, deps=[r2]), ttok

    tbl_toks = []
    mm_last = None
    for hh in range(2):
        prev = [tbl_toks[-1]] if tbl_toks else []
        tk, mm_last = sincos_half2(hh, 0.0,
                                   SIN[:, hh * 1024:(hh + 1) * 1024], prev)
        tbl_toks.append(tk)
        tk, mm_last = sincos_half2(hh, 0.25,
                                   COS[:, hh * 1024:(hh + 1) * 1024],
                                   [tbl_toks[-1]])
        tbl_toks.append(tk)
    tables_done = tbl_toks[-1]
    s_sin = tables_done
    s_cos = tables_done

    ev_prev = tables_done
    b64_dmas = []
    for j in range(8):
        tok = mm(POSE[:, :], ONEH[:, j * 512:(j + 1) * 512], PC[0:1, :],
                 True, True, inc=True,
                 deps=last(ld_pos, ld_onh, ev_prev,
                           b64_dmas[-1] if b64_dmas else None))
        ev_prev = vop(lambda e: e.tensor_copy(
            out=FSCR[0:1, 0:512], in_=PC[0:1, :]), deps=[tok])
        b64_dmas.append(dma("sync", BIAS64[8 * j:8 * (j + 1), :],
                            FSCR[0:1, 0:512], deps=[ev_prev],
                            ctr_name="b64"))
    ld_b64 = last(*b64_dmas)
    pc_free = ev_prev
    pd_free = None

    def stats1024(src, deps):
        s1 = vop(lambda e: e.bn_stats(out=ST6[:, 0:6], in_=src[:, 0:512]),
                 deps=deps)
        s2 = vop(lambda e: e.bn_stats(out=ST6[:, 6:12], in_=src[:, 512:1024]),
                 deps=[s1])
        return vop(lambda e: e.bn_aggr(out=MV[:, :], in_=ST6[:, 0:12]),
                   deps=[s2])

    def ln_single(vcol, mcol, acol, bcol, deps):
        w2 = vop(lambda e, v=vcol: e.tensor_scalar(
            out=SA[:, 2:3], in0=v, scalar1=1.0, scalar2=EPS,
            op0=OP.mult, op1=OP.add), deps=deps)
        w2r = vop(lambda e: e.reciprocal(out=SA[:, 3:4], in_=SA[:, 2:3]),
                  deps=[w2])
        aa = act(acol, SA[:, 3:4], AF.Sqrt, deps=[w2r])
        return vop(lambda e, m=mcol, a=acol, b=bcol: e.scalar_tensor_tensor(
            out=b, in0=m, scalar=-1.0, in1=a, op0=OP.mult, op1=OP.mult),
            deps=[aa])

    def ln_double(vcol, mcol, acol, bcol, deps):
        w2 = vop(lambda e, v=vcol: e.tensor_scalar(
            out=SA[:, 2:3], in0=v, scalar1=1.0, scalar2=EPS,
            op0=OP.mult, op1=OP.add), deps=deps)
        w2r = vop(lambda e: e.reciprocal(out=SA[:, 3:4], in_=SA[:, 2:3]),
                  deps=[w2])
        v2 = vop(lambda e, v=vcol: e.tensor_tensor(
            out=SA[:, 4:5], in0=v, in1=SA[:, 3:4], op=OP.mult), deps=[w2r])
        w4 = vop(lambda e: e.tensor_scalar(
            out=SA[:, 5:6], in0=SA[:, 4:5], scalar1=1.0, scalar2=EPS,
            op0=OP.mult, op1=OP.add), deps=[v2])
        w4r = vop(lambda e: e.reciprocal(out=SA[:, 6:7], in_=SA[:, 5:6]),
                  deps=[w4])
        pr = vop(lambda e: e.tensor_tensor(
            out=SA[:, 7:8], in0=SA[:, 3:4], in1=SA[:, 6:7], op=OP.mult),
            deps=[w4r])
        aa = act(acol, SA[:, 7:8], AF.Sqrt, deps=[pr])
        return vop(lambda e, m=mcol, a=acol, b=bcol: e.scalar_tensor_tensor(
            out=b, in0=m, scalar=-1.0, in1=a, op0=OP.mult, op1=OP.mult),
            deps=[aa])

    ev_tok = None
    zg_rd = [None] * 4
    prev_z = None
    for g in range(4):
        z_toks = []
        for tt in range(4):
            t = g * 4 + tt
            buf = tt % 2
            xs = XT[:, buf * D:(buf + 1) * D]
            xdep = [prev_z] if t >= 2 else [tables_done, ld_b64]
            xld = dma("sync", xs, x_h[t * 128:(t + 1) * 128, :], deps=xdep,
                      ctr_name="xs")
            ag = stats1024(xs, [xld])
            bb = ln_double(MV[:, 1:2], MV[:, 0:1], SA[:, 0:1], SA[:, 1:2],
                           [ag])
            ztok = vop(lambda e, xs=xs, tt=tt: e.tensor_scalar(
                out=AZQ[:, tt * D:(tt + 1) * D], in0=xs,
                scalar1=SA[:, 0:1], scalar2=SA[:, 1:2],
                op0=OP.mult, op1=OP.add),
                deps=last(bb, zg_rd[g - 1] if g >= 1 else None))
            z_toks.append(ztok)
            prev_z = ztok
        for c in range(ND):
            q = c % 4
            reg = PBIG[:, q * 512:(q + 1) * 512]
            deps = last(*(z_toks + [ev_tok, ld_id]))
            for tt in range(4):
                tok = mm(AZQ[:, tt * D + c * 128: tt * D + (c + 1) * 128],
                         IDENT[:, :], reg[:, tt * 128:(tt + 1) * 128],
                         True, True, inc=(tt == 3),
                         deps=deps if tt == 0 else ())
            ev_tok = act(AY[:, c * T + g * 512: c * T + (g + 1) * 512],
                         reg, AF.Identity, bias=LN1B[:, c:c + 1],
                         scale=LN1W[:, c:c + 1],
                         deps=last(tok, ld_l1w, ld_l1b))
            if c == ND - 1:
                zg_rd[g] = tok
    y_done = ev_tok
    Y = AY

    pcd_free = [pc_free, pd_free]
    aw_free = [None, None]
    qk_evs = []
    for qc in range(NQ):
        wb = qc % 2
        wsl = AW[:, wb * 1024:(wb + 1) * 1024]
        wqk_ld = dma("gpsimd", wsl.rearrange("p (c k) -> p c k", c=ND),
                     wqkT[:, qc * 128:(qc + 1) * 128]
                     .rearrange("(c p) k -> p c k", p=128),
                     deps=last(aw_free[wb]), ctr_name="wvl")
        for tc in range(4):
            pi = (qc * 4 + tc) % 2
            ps = PCD[pi]
            deps = last(wqk_ld, y_done, pcd_free[pi])
            for c in range(ND):
                tok = mm(wsl[:, c * 128:(c + 1) * 128],
                         Y[:, c * T + tc * 512: c * T + (tc + 1) * 512],
                         ps[:, :], c == 0, c == ND - 1, inc=(c == ND - 1),
                         deps=deps if c == 0 else ())
            ev = act(QK[:, qc * T + tc * 512: qc * T + (tc + 1) * 512],
                     ps[:, :], AF.Silu, bias=BQK[:, qc:qc + 1],
                     deps=last(tok, ld_bqk))
            pcd_free[pi] = ev
            qk_evs.append(ev)
        aw_free[wb] = tok
    qk_done = last(*qk_evs)
    V = AV

    def quant_max(w_hbm, cols, qsl, start_deps, xt_dep):
        w_mx = min(2048, cols)
        npc = (D * FF) // (128 * w_mx)
        w2 = w_hbm.rearrange("(c p) (j k) -> p c j k", p=128, k=w_mx)
        ncw2 = cols // w_mx
        mx_toks = []
        prev = xt_dep
        for i in range(npc):
            stg = XT[:, 0:w_mx]
            c, j = divmod(i, ncw2)
            ldt = dma("sync", stg, w2[:, c, j, :],
                      deps=last(prev) + (list(start_deps) if i == 0 else []),
                      ctr_name="qld")
            mx = vop(lambda e, stg=stg, i=i: e.tensor_reduce(
                out=MXP[:, i:i + 1], in_=stg, axis=AX, op=OP.max,
                apply_absolute_value=True), deps=[ldt])
            prev = mx
            mx_toks.append(mx)
        m1 = vop(lambda e, q=qsl, n=npc: e.tensor_reduce(
            out=q[:, 0:1], in_=MXP[:, 0:n], axis=AX, op=OP.max),
            deps=last(*mx_toks))
        d1 = dma("sync", FSCR[0:1, 0:128], qsl[:, 0:1], deps=[m1],
                 ctr_name="qsc")
        m2 = vop(lambda e, q=qsl: e.tensor_reduce(
            out=q[0:1, 6:7], in_=FSCR[0:1, 0:128], axis=AX, op=OP.max),
            deps=[d1])
        d2a = dma("sync", scr1, qsl[0:1, 6:7], deps=[m2], ctr_name="qsc")
        d2 = dma("sync", qsl[:, 0:1], scr1.broadcast_to([128, 1]),
                 deps=[d2a], ctr_name="qsc")
        rr = vop(lambda e, q=qsl: e.reciprocal(out=q[:, 1:2], in_=q[:, 0:1]),
                 deps=[d2])
        sinv = vop(lambda e, q=qsl: e.tensor_single_scalar(
            out=q[:, 2:3], in_=q[:, 1:2], scalar=127.0, op=OP.mult),
            deps=[rr])
        s_ = vop(lambda e, q=qsl: e.tensor_single_scalar(
            out=q[:, 3:4], in_=q[:, 0:1], scalar=1.0 / 127.0, op=OP.mult),
            deps=[d2])
        s2 = vop(lambda e, q=qsl: e.tensor_tensor(
            out=q[:, 4:5], in0=q[:, 3:4], in1=q[:, 3:4], op=OP.mult),
            deps=[s_])
        sinv2 = vop(lambda e, q=qsl: e.tensor_single_scalar(
            out=q[:, 5:6], in_=q[:, 2:3], scalar=LORA_SCALING, op=OP.mult),
            deps=[sinv])
        return dict(sinv=sinv, s=s_, s2=s2, sinv2=sinv2,
                    st_prev=[last(*mx_toks), last(*mx_toks)],
                    qf_prev=[None, None], stores=[],
                    w4=w_hbm.rearrange("(c p) (j k) -> p c j k", p=128,
                                       k=1024),
                    ncw=cols // 1024, qsl=qsl)

    def quant_pieces(st, rng, wf_fn, extra_deps=()):
        scr = st["scr"]
        for i in rng:
            b = i % 2
            stg = XT[:, b * 1024:(b + 1) * 1024]
            wf = wf_fn(b)
            c, j = divmod(i, st["ncw"])
            ldt = dma("sync", stg, st["w4"][:, c, j, :],
                      deps=last(st["st_prev"][b]), ctr_name="qld")
            q1 = vop(lambda e, stg=stg, q=st["qsl"]: e.tensor_single_scalar(
                out=ISCR[:, :], in_=stg, scalar=q[:, 2:3], op=OP.mult),
                deps=last(ldt, st["sinv"]))
            cc = vop(lambda e, stg=stg, q=st["qsl"]: e.scalar_tensor_tensor(
                out=FSCR[:, :], in0=stg, scalar=q[:, 2:3], in1=ISCR[:, :],
                op0=OP.mult, op1=OP.subtract), deps=[q1])
            st["st_prev"][b] = cc
            qm = vop(lambda e: e.scalar_tensor_tensor(
                out=FSC2[:, :], in0=FSCR[:, :], scalar=-0.5, in1=ISCR[:, :],
                op0=OP.is_lt, op1=OP.subtract), deps=[cc])
            qf = vop(lambda e, wf=wf: e.scalar_tensor_tensor(
                out=wf, in0=FSCR[:, :], scalar=0.5, in1=FSC2[:, :],
                op0=OP.is_gt, op1=OP.subtract),
                deps=last(qm, st["qf_prev"][b]) + list(extra_deps))
            stt = dma("sync", st["s4"][:, c, j, :], wf, deps=[qf],
                      ctr_name="qst")
            st["qf_prev"][b] = stt
            st["stores"].append(stt)

    q_fc = quant_max(wfcT, FF, QS, last(y_done, tables_done, ld_b64),
                     last(prev_z, tables_done))
    q_fc["scr"] = None
    q_fc["s4"] = sc_fc.rearrange("(c p) (j k) -> p c j k", p=128, k=1024)
    quant_pieces(q_fc, range(32), lambda b: AGTI8[:, b * 1024:(b + 1) * 1024])
    q_fc["done"] = last(*q_fc["stores"])

    v_evs = []
    V3 = V[:, :].rearrange("p (t k) -> p t k", t=NT)
    for oc in range(8):
        wb = oc % 2
        wsl = AW[:, wb * 2048:(wb + 1) * 2048]
        wdep = last(aw_free[wb]) if oc < 2 else [v_evs[8 * (oc - 2) + 7]]
        wv_ld = dma("gpsimd", wsl.rearrange("p (c k) -> p c k", c=ND),
                    wvT[:, oc * 256:(oc + 1) * 256]
                    .rearrange("(c p) k -> p c k", p=128),
                    deps=wdep, ctr_name="wvl")
        for tg in range(8):
            pi = tg % 2
            ps = PCD[pi]
            deps = last(wv_ld, y_done, pcd_free[pi], ld_on, ld_bvr)
            for tt in range(2):
                t = 2 * tg + tt
                for c in range(ND):
                    mm(Y[:, c * T + t * 128: c * T + (t + 1) * 128],
                       wsl[:, c * 256:(c + 1) * 256],
                       ps[:, tt * 256:(tt + 1) * 256], c == 0, False,
                       deps=deps if (c == 0 and tt == 0) else ())
                tok = mm(ONES[:, :], BRW[:, oc * 256:(oc + 1) * 256],
                         ps[:, tt * 256:(tt + 1) * 256], False, True,
                         inc=(tt == 1))
            ev = act(V3[:, 2 * tg:2 * tg + 2, oc * 256:(oc + 1) * 256],
                     ps[:, :].rearrange("p (t k) -> p t k", t=2),
                     AF.Silu, deps=[tok])
            pcd_free[pi] = ev
            v_evs.append(ev)
    aw_free = [v_evs[-1], v_evs[-2]]
    v_done = last(*v_evs)

    def head_rope(hidx, tok_off, dst, d0_off, d1_off, deps):
        n = 512
        a0 = act(UB0, QK[:, tok_off:tok_off + n], AF.Identity,
                 bias=BETT[:, hidx:hidx + 1], scale=GAMT[:, hidx:hidx + 1],
                 deps=deps)
        a1 = act(UB1, QK[:, T + tok_off:T + tok_off + n], AF.Identity,
                 bias=BETT[:, 4 + hidx:4 + hidx + 1],
                 scale=GAMT[:, 4 + hidx:4 + hidx + 1], deps=deps)
        cs = COS[:, tok_off:tok_off + n]
        sn = SIN[:, tok_off:tok_off + n]
        m1 = vop(lambda e: e.tensor_mul(out=M1, in0=UB0, in1=cs),
                 deps=last(a0, s_cos))
        m2 = vop(lambda e: e.tensor_mul(out=M2, in0=UB1, in1=sn),
                 deps=last(a1, s_sin))
        o1 = vop(lambda e: e.tensor_sub(
            out=dst[:, d0_off:d0_off + n], in0=M1, in1=M2), deps=[m1, m2])
        m3 = vop(lambda e: e.tensor_mul(out=M1, in0=UB1, in1=cs), deps=[o1])
        m4 = vop(lambda e: e.tensor_mul(out=M2, in0=UB0, in1=sn), deps=[o1])
        o2 = vop(lambda e: e.tensor_add(
            out=dst[:, d1_off:d1_off + n], in0=M1, in1=M2), deps=[m3, m4])
        return (last(a0, a1), last(o1, o2))

    LKF = AZQ
    lk_toks = []
    for cc in range(4):
        dep = last(qk_done, ld_gam, ld_bet,
                   zg_rd[3] if cc == 0 else None,
                   lk_toks[-1] if lk_toks else None)
        _, o = head_rope(3, cc * 512, LKF, cc * 512, 2048 + cc * 512, dep)
        lk_toks.append(o)
    lk_done = last(*lk_toks)
    lkt_evs = []
    ev = None
    for c2 in range(2):
        for tg in range(4):
            deps = last(lk_done, ev, pcd_free[0])
            for tt in range(4):
                t = tg * 4 + tt
                tok = mm(LKF[:, c2 * 2048 + t * 128: c2 * 2048 + (t + 1) * 128],
                         IDENT[:, :], PC[:, tt * 128:(tt + 1) * 128],
                         True, True, inc=(tt == 3),
                         deps=deps if tt == 0 else ())
            ev = vop(lambda e, c2=c2, tg=tg: e.tensor_copy(
                out=LKT[:, :].rearrange("p (t k) -> p t k", t=NT)
                [:, tg * 4:(tg + 1) * 4, c2 * 128:(c2 + 1) * 128],
                in_=PC[:, :].rearrange("p (t k) -> p t k", t=4)),
                deps=[tok])
            lkt_evs.append(ev)
    lkt_done = last(*lkt_evs)
    pcd_free[0] = lkt_evs[-1]

    q_pj = quant_max(wpjT, D, QS2, last(q_fc["done"]),
                     last(q_fc["st_prev"][0], q_fc["st_prev"][1]))
    q_pj["scr"] = None
    q_pj["s4"] = sc_pj.rearrange("(c p) (j k) -> p c j k", p=128, k=1024)

    regions8 = [PBIG[:, 0:512], PBIG[:, 512:1024], PBIG[:, 1024:1536],
                PBIG[:, 1536:2048], PA[:, 0:512], PA[:, 512:1024],
                PC[:, :], PD[:, :]]
    lkv_evs = []
    for m in range(2):
        for n4 in range(4):
            r = regions8[m * 4 + n4]
            deps = last(lkt_done, v_done, pcd_free[0], pcd_free[1], ev_tok)
            for t in range(NT):
                tok = mm(LKT[:, t * DQK + m * 128: t * DQK + (m + 1) * 128],
                         V[:, t * E + n4 * 512: t * E + (n4 + 1) * 512],
                         r, t == 0, t == NT - 1, inc=(t == NT - 1),
                         deps=deps if t == 0 else ())
            lkv_evs.append(act(
                LINKV[:, m * E + n4 * 512: m * E + (n4 + 1) * 512], r,
                AF.Identity, scale=1.0 / T, deps=[tok]))
    lkv_done = last(*lkv_evs)

    ld_bor = dma("gpsimd", BRW[:, 0:1024], bo_h, deps=last(v_done),
                 ctr_name="lps")
    XB0 = LKT[:, 0:1024]
    XB1 = LKT[:, 1024:2048]
    X2S0 = LKT[:, 2048:3072]
    X2S1 = LKT[:, 3072:4096]

    x2_sts = [None] * NT
    pa_free = [lkv_evs[4], lkv_evs[5]]
    pcd_free = [lkv_evs[6], lkv_evs[7]]
    pbig_free = [last(lkv_evs[0], lkv_evs[1]), last(lkv_evs[2], lkv_evs[3])]
    gt_free = v_evs[-1]
    gatee_prev = [None, None]
    wo_prev = [None, None]
    z2_rd = lkt_done
    epi_z2 = lkt_done
    epi_ol = lkt_done
    xb_prev = [lkv_done, lkv_done]
    x2s_prev = [None, None]

    pj_batch_dep = None
    pj_done_batches = [range(0, 11), range(11, 22), range(22, 32), range(0)]
    for C in range(4):
        base = C * 512
        d0 = last(qk_done, ms_att, lkv_done, z2_rd, pj_batch_dep)
        aq, hq = head_rope(0, base, QQ, 0, 512, d0)
        ak, hk = head_rope(2, base, QKK, 0, 512, [hq])
        al_, hl = head_rope(1, base, LQ, 0, 512, [hk])
        sdeps = last(hq, hk, pcd_free[0])
        for p in range(4):
            for c2 in range(2):
                mm(QKK[:, c2 * 512 + p * 128: c2 * 512 + (p + 1) * 128],
                   QQ[:, c2 * 512 + p * 128: c2 * 512 + (p + 1) * 128],
                   PC[:, p * 128:(p + 1) * 128], c2 == 0,
                   c2 == 1 and p != 3,
                   deps=sdeps if (p == 0 and c2 == 0) else ())
        tok = flush_mm(PC[:, 384:512], deps=last(ms_zr, ld_on))
        pc3 = PC[:, :].rearrange("p (c k) -> p c k", c=4)
        sq3 = SQT[:, :].rearrange("p (c k) -> p c k", c=4)
        at3 = ATT[:, :].rearrange("p (c k) -> p c k", c=4)
        b3 = BIAS64[0:64, :].unsqueeze(1).broadcast_to([64, 4, 64])
        sq_t = vop(lambda e: e.scalar_tensor_tensor(
            out=sq3[0:64, :, :], in0=pc3[0:64, :, 0:64], scalar=1.0 / G,
            in1=b3, op0=OP.mult, op1=OP.add), deps=last(tok, ld_b64))
        sq_b = vop(lambda e: e.scalar_tensor_tensor(
            out=sq3[64:128, :, :], in0=pc3[64:128, :, 64:128],
            scalar=1.0 / G, in1=b3, op0=OP.mult, op1=OP.add), deps=[sq_t])
        at_t = vop(lambda e: e.scalar_tensor_tensor(
            out=at3[0:64, :, 0:64], in0=sq3[0:64, :, :], scalar=0.0,
            in1=sq3[0:64, :, :], op0=OP.max, op1=OP.mult), deps=[sq_t])
        at_b = vop(lambda e: e.scalar_tensor_tensor(
            out=at3[64:128, :, 64:128], in0=sq3[64:128, :, :], scalar=0.0,
            in1=sq3[64:128, :, :], op0=OP.max, op1=OP.mult), deps=[sq_b])
        att_done = last(at_t, at_b)
        pcd_free[0] = at_b
        if dbg and C == 0:
            dma("gpsimd", dbg_t["d_att"], ATT[:, :], deps=[att_done],
                ctr_name="lps")

        gt_toks = []
        for Ei in range(NE):
            wb = Ei % 2
            prev_wg = gt_toks[-2] if len(gt_toks) >= 2 else gt_free
            wg_ld = dma("gpsimd", AW[:, wb * 1024:(wb + 1) * 1024]
                        .rearrange("p (c k) -> p c k", c=ND),
                        wgT[:, Ei * 128:(Ei + 1) * 128]
                        .rearrange("(c p) k -> p c k", p=128),
                        deps=last(prev_wg), ctr_name="wgl")
            pi = Ei % 2
            gdeps = last(wg_ld, pcd_free[pi], y_done)
            for c in range(ND):
                tok = mm(AW[:, wb * 1024 + c * 128: wb * 1024 + (c + 1) * 128],
                         Y[:, c * T + base: c * T + base + 512],
                         PCD[pi][:, :], c == 0, c == ND - 1,
                         inc=(c == ND - 1), deps=gdeps if c == 0 else ())
            gprev = gt_toks[-2] if len(gt_toks) >= 2 else gatee_prev[pi]
            agt_dep = (q_fc["qf_prev"][Ei]
                       if (C == 0 and Ei < 2) else None)
            gev = act(GATEE[:, 0:512] if pi == 0 else GATEE[:, 512:1024],
                      PCD[pi][:, :], AF.Silu, bias=BGC[:, Ei:Ei + 1],
                      deps=last(tok, ld_bg, gprev))
            pcd_free[pi] = gev
            ph = Ei % 2
            adeps = last(pa_free[ph], lkv_done, hl, att_done, v_done)
            for m in range(2):
                mm(LINKV[:, m * E + Ei * 128: m * E + (Ei + 1) * 128],
                   LQ[:, m * 512:(m + 1) * 512],
                   PA[:, ph * 512:(ph + 1) * 512], m == 0, False,
                   deps=adeps if m == 0 else ())
            for p in range(4):
                mm(V[:, (4 * C + p) * E + Ei * 128:
                     (4 * C + p) * E + (Ei + 1) * 128],
                   ATT[:, p * 128:(p + 1) * 128],
                   PA[:, ph * 512 + p * 128: ph * 512 + (p + 1) * 128],
                   False, False)
            tok2 = flush_mm(PA[:, ph * 512: ph * 512 + 128])
            gt = vop(lambda e, Ei=Ei, ph=ph, pi=pi: e.tensor_tensor(
                out=AGT[:, Ei * 512:(Ei + 1) * 512],
                in0=PA[:, ph * 512:(ph + 1) * 512],
                in1=GATEE[:, pi * 512:(pi + 1) * 512], op=OP.mult),
                deps=last(tok2, gev, agt_dep))
            pa_free[ph] = gt
            gt_toks.append(gt)
        gts = last(*gt_toks)

        regs4 = (PBIG[:, 0:1024], PBIG[:, 1024:2048], PA[:, :], None)
        hd = last(gts, pbig_free[0], pbig_free[1], pa_free[0], pa_free[1],
                  pcd_free[0], pcd_free[1])
        for Ei in range(NE):
            wb = Ei % 2
            wo_ld = dma("gpsimd", WSLB[:, wb * 1024:(wb + 1) * 1024],
                        woT[Ei * 128:(Ei + 1) * 128, :],
                        deps=last(wo_prev[wb]), ctr_name="wol")
            wdeps = last(wo_ld, gts) + (hd if Ei == 0 else [])
            for k in range(4):
                for nh in range(2):
                    dst = (PCD[nh][:, :] if k == 3 else
                           regs4[k][:, nh * 512:(nh + 1) * 512])
                    tok = mm(AGT[:, Ei * 512 + k * 128:
                                 Ei * 512 + (k + 1) * 128],
                             WSLB[:, wb * 1024 + nh * 512:
                                  wb * 1024 + (nh + 1) * 512],
                             dst, Ei == 0, False,
                             inc=(k == 3 and nh == 1),
                             deps=wdeps if (k == 0 and nh == 0) else ())
            wo_prev[wb] = tok
        wo_last = []
        for k in range(4):
            for nh in range(2):
                dst = (PCD[nh][:, :] if k == 3 else
                       regs4[k][:, nh * 512:(nh + 1) * 512])
                tok = mm(ONES[:, :], BRW[:, nh * 512:(nh + 1) * 512],
                         dst, False, True, inc=(nh == 1),
                         deps=last(ld_bor, ld_on))
            wo_last.append(tok)
        for k in range(4):
            t_abs = 4 * C + k
            if k == 3:
                s1 = vop(lambda e: e.bn_stats(out=ST6[:, 0:6], in_=PC[:, :]),
                         deps=[wo_last[k]])
                s2 = vop(lambda e: e.bn_stats(out=ST6[:, 6:12], in_=PD[:, :]),
                         deps=[s1])
                ag = vop(lambda e: e.bn_aggr(out=MV[:, :], in_=ST6[:, 0:12]),
                         deps=[s2])
            else:
                ag = stats1024(regs4[k], [wo_last[k]])
            bb = ln_single(MV[:, 1:2], MV[:, 0:1], SA[:, 0:1],
                           SA[:, 1:2], [ag])
            if k == 3:
                z2t_a = act(Z2[:, 0:512], PC[:, :], AF.Identity,
                            bias=SA[:, 1:2], scale=SA[:, 0:1],
                            deps=last(bb, epi_z2))
                z2t = act(Z2[:, 512:1024], PD[:, :], AF.Identity,
                          bias=SA[:, 1:2], scale=SA[:, 0:1], deps=[z2t_a])
                pcd_free = [z2t, z2t]
            else:
                z2t = act(Z2, regs4[k], AF.Identity, bias=SA[:, 1:2],
                          scale=SA[:, 0:1], deps=last(bb, epi_z2))
            t2a = vop(lambda e: e.tensor_mul(out=T2E, in0=Z2,
                                             in1=LN2W[:, :]),
                      deps=last(z2t, ld_l2w))
            epi_z2 = t2a
            t2b = vop(lambda e: e.tensor_add(out=T2E, in0=T2E,
                                             in1=LN2B[:, :]),
                      deps=last(t2a, ld_l2b))
            ag2 = stats1024(T2E, [t2b])
            bb2 = ln_single(MV[:, 1:2], MV[:, 0:1], SA[:, 0:1],
                            SA[:, 1:2], [ag2])
            olnt = act(OLN, T2E, AF.Identity, bias=SA[:, 1:2],
                       scale=SA[:, 0:1], deps=last(bb2, epi_ol))
            par = t_abs % 2
            xb = XB0 if par == 0 else XB1
            x2s = X2S0 if par == 0 else X2S1
            xre = dma("gpsimd", xb, x_h[t_abs * 128:(t_abs + 1) * 128, :],
                      deps=last(xb_prev[par]), ctr_name="xrl")
            fin = vop(lambda e, xb=xb, x2s=x2s: e.tensor_add(
                out=x2s, in0=xb, in1=OLN),
                deps=last(olnt, xre, x2s_prev[par]))
            epi_ol = fin
            xb_prev[par] = fin
            stok = dma("sync", x2_scr[t_abs * 128:(t_abs + 1) * 128, :],
                       x2s, deps=[fin], ctr_name="st")
            x2s_prev[par] = stok
            x2_sts[t_abs] = stok
            if k < 2:
                pbig_free[k] = fin
            elif k == 2:
                pa_free = [fin, fin]
        gt_free = wo_last[-1]
        z2_rd = gts
        n0 = len(q_pj["stores"])
        quant_pieces(q_pj, pj_done_batches[C],
                     lambda b: AMSCI8[:, b * 1024:(b + 1) * 1024],
                     extra_deps=last(hl))
        if len(q_pj["stores"]) > n0:
            pj_batch_dep = last(*q_pj["stores"][n0:])

    q_pj["done"] = last(*q_pj["stores"])
    x2_done = last(*[x2_sts[t] for t in range(NT)])
    if dbg:
        for nm, src in (("d_y", Y), ("d_qk", QK), ("d_v", V),
                        ("d_lkv", LINKV), ("d_cos", COS), ("d_sin", SIN)):
            dma("gpsimd", dbg_t[nm], src[:, :],
                deps=last(y_done, qk_done, v_done, lkv_done, x2_done),
                ctr_name="lps")
        dma("gpsimd", dbg_t["d_x2"], x2_scr, deps=last(x2_done),
            ctr_name="lps")

    ld_bpr = dma("sync", XT[0:1, 0:1024], bpj_h,
                 deps=last(q_pj["st_prev"][0], q_pj["st_prev"][1]),
                 ctr_name="qsc")
    bpr_tok = act(BRW[:, 1024:2048], XT[0:1, 0:1024], AF.Identity,
                  scale=QS2[0:1, 2:3], deps=last(ld_bpr, q_pj["sinv"],
                                                 v_done))
    if dbg:
        dma("gpsimd", dbg_t["d_wfc"], sc_fc, deps=[q_fc["done"]],
            ctr_name="lps")
        dma("sync", dbg_t["d_qs"], QS[:, :],
            deps=last(q_fc["sinv2"], q_fc["s2"]), ctr_name="b64")
        dma("sync", dbg_t["d_qs2"], QS2[:, :],
            deps=last(q_pj["sinv2"], q_pj["s2"]), ctr_name="b64")
        dma("sync", dbg_t["d_mxp"], MXP[:, :], deps=last(q_fc["sinv2"]),
            ctr_name="b64")

    lkv_rd_done = gts
    late_dep = last(hl, att_done, gts, bpr_tok)
    ld_afc = dma("gpsimd", AFCT.rearrange("p (c k) -> p c k", c=ND),
                 afcT_h.rearrange("(c p) k -> p c k", p=128), deps=late_dep,
                 ctr_name="lps")
    ld_apj = dma("gpsimd", APJT.rearrange("p (c k) -> p c k", c=NF),
                 apjT_h.rearrange("(c p) k -> p c k", p=128), deps=late_dep,
                 ctr_name="lps")
    ld_bpjT = dma("gpsimd", BPJT, bpjT_h, deps=late_dep, ctr_name="lps")
    ld_bfcA = dma("gpsimd", BFCTA, bfcT_h[:, 0:2048], deps=late_dep,
                  ctr_name="lps")
    ld_bfcB = dma("gpsimd", BFCTB, bfcT_h[:, 2048:4096], deps=late_dep,
                  ctr_name="lps")

    Y2 = AY
    H = AV
    regs3 = [PBIG[:, 0:1024], PBIG[:, 1024:2048], PA[:, :]]
    reg_free = [last(pbig_free[0]), last(pbig_free[1]),
                last(pa_free[0], pa_free[1])]
    pcq_free = last(pcd_free[0], pcd_free[1])
    wfc_prev = [None, None]
    wpj_prev = [None, None]
    wfc_cst = [None, None]
    wpj_cst = [None, None]
    sfc4 = sc_fc.rearrange("(c p) (o k) -> p c o k", p=128, k=128)
    spj3 = sc_pj.rearrange("(c p) k -> p c k", p=128)
    out_sts = []
    azq_rd = epi_ol
    y_rd = gts
    v_rd = gts
    x2ld_prev = [None, None]
    x2ld2_prev = [None, None]
    outt_prev = [None, None]
    oln_prev = bpr_tok

    for Cf in range(2):
        tb = Cf * 8
        ztoks_all = []
        ev = None
        for g2 in range(2):
            z_toks = []
            for tt in range(4):
                t_abs = tb + g2 * 4 + tt
                par = tt % 2
                xbuf = LKT[:, par * 1024:(par + 1) * 1024]
                xld = dma("sync", xbuf,
                          x2_scr[t_abs * 128:(t_abs + 1) * 128, :],
                          deps=last(x2_sts[t_abs], x2ld_prev[par]),
                          ctr_name="x2l")
                ag = stats1024(xbuf, [xld])
                bb = ln_single(MV[:, 1:2], MV[:, 0:1], SA[:, 0:1],
                               SA[:, 1:2], [ag])
                ztok = vop(lambda e, xbuf=xbuf, tt=tt: e.tensor_scalar(
                    out=AZQ[:, tt * D:(tt + 1) * D], in0=xbuf,
                    scalar1=SA[:, 0:1], scalar2=SA[:, 1:2],
                    op0=OP.mult, op1=OP.add),
                    deps=last(bb, azq_rd))
                x2ld_prev[par] = ztok
                z_toks.append(ztok)
            for c in range(ND):
                q = c % 4
                reg = PBIG[:, q * 512:(q + 1) * 512]
                deps = last(*(z_toks + [ev, ld_id] + reg_free[q // 2]))
                for tt in range(4):
                    tok = mm(AZQ[:, tt * D + c * 128: tt * D + (c + 1) * 128],
                             IDENT[:, :], reg[:, tt * 128:(tt + 1) * 128],
                             True, True, inc=(tt == 3),
                             deps=deps if tt == 0 else ())
                ev = vop(lambda e, c=c, g2=g2, reg=reg: e.tensor_copy(
                    out=Y2[:, c * 1024 + g2 * 512:
                           c * 1024 + (g2 + 1) * 512],
                    in_=reg), deps=[tok])
                reg_free[q // 2] = [ev]
            azq_rd = tok
            ztoks_all += z_toks
        y2_done = ev

        t1toks = []
        for h2 in range(2):
            deps = last(y2_done, ld_afc, pcq_free, hl)
            for c in range(ND):
                tok = mm(AFCT[:, c * 16:(c + 1) * 16],
                         Y2[:, c * 1024 + h2 * 512: c * 1024 + (h2 + 1) * 512],
                         PC[0:16, :], c == 0, c == ND - 1,
                         inc=(c == ND - 1), deps=deps if c == 0 else ())
            t1toks.append(act(T1[0:16, h2 * 512:(h2 + 1) * 512], PC[0:16, :],
                              AF.Identity, scale=QS[0:16, 5:6],
                              deps=last(tok, q_fc["sinv2"])))
            pcq_free = t1toks[-1]
        t1_done = last(*t1toks)
        h_toks = []
        for o1 in range(NF):
            wb = o1 % 2
            stg8 = AMSCI8[:, wb * 1024:(wb + 1) * 1024]
            ld8 = dma("sync", stg8, sfc4[:, :, o1, :],
                      deps=last(wfc_cst[wb], q_fc["done"], gt_free,
                                q_pj["qf_prev"][wb]),
                      ctr_name="wfl")
            cst = vop(lambda e, stg8=stg8, wb=wb: e.tensor_copy(
                out=AW[:, wb * 1024:(wb + 1) * 1024], in_=stg8),
                deps=last(ld8, wfc_prev[wb]))
            wfc_cst[wb] = cst
            ri = o1 % 3
            reg = regs3[ri]
            deps = last(cst, y2_done, t1_done, ld_bfcA, ld_bfcB,
                        *reg_free[ri])
            for c in range(ND):
                for nh in range(2):
                    mm(AW[:, wb * 1024 + c * 128: wb * 1024 + (c + 1) * 128],
                       Y2[:, c * 1024 + nh * 512: c * 1024 + (nh + 1) * 512],
                       reg[:, nh * 512:(nh + 1) * 512], c == 0, False,
                       deps=deps if (c == 0 and nh == 0) else ())
            bsrc = BFCTA if o1 < 16 else BFCTB
            bof = (o1 % 16) * 128
            for nh in range(2):
                tok = mm(bsrc[:, bof:bof + 128],
                         T1[0:16, nh * 512:(nh + 1) * 512],
                         reg[:, nh * 512:(nh + 1) * 512], False, True,
                         inc=(nh == 1))
            hev = act(H[:, o1 * 1024:(o1 + 1) * 1024], reg, AF.Gelu,
                      bias=BFCC[:, o1:o1 + 1], scale=QS[:, 3:4],
                      deps=last(tok, q_fc["s"], ld_bfc, v_rd))
            h_toks.append(hev)
            reg_free[ri] = [hev]
            wfc_prev[wb] = hev
        h_done = last(*h_toks)
        t2toks = []
        for h2 in range(2):
            deps = last(h_done, ld_apj, pcq_free)
            for fc in range(NF):
                tok = mm(APJT[:, fc * 16:(fc + 1) * 16],
                         H[:, fc * 1024 + h2 * 512:
                           fc * 1024 + (h2 + 1) * 512],
                         PC[0:16, :], fc == 0, fc == NF - 1,
                         inc=(fc == NF - 1), deps=deps if fc == 0 else ())
            t2toks.append(act(T2L[0:16, h2 * 512:(h2 + 1) * 512], PC[0:16, :],
                              AF.Identity, scale=QS2[0:16, 5:6],
                              deps=last(tok, q_pj["sinv2"])))
            pcq_free = t2toks[-1]
        t2_done = last(*t2toks)
        for half in range(4):
            ts_pair = (2 * half, 2 * half + 1)
            regs = (PBIG[:, 0:1024], PBIG[:, 1024:2048])
            hd = last(*(reg_free[0] + reg_free[1]))
            for fc in range(NF):
                wb = fc % 2
                stg8 = AMSCI8[:, 2048 + wb * 1024: 2048 + (wb + 1) * 1024]
                ld8 = dma("sync", stg8, spj3[:, fc, :],
                          deps=last(wpj_cst[wb], q_pj["done"], hl),
                          ctr_name="wpl")
                cst = vop(lambda e, stg8=stg8, wb=wb: e.tensor_copy(
                    out=WSLB[:, wb * 1024:(wb + 1) * 1024], in_=stg8),
                    deps=last(ld8, wpj_prev[wb]))
                wpj_cst[wb] = cst
                wdeps = last(cst, h_done, t2_done) + \
                    (hd if fc == 0 else [])
                for k in range(2):
                    ts_i = ts_pair[k]
                    for nh in range(2):
                        tok = mm(H[:, fc * 1024 + ts_i * 128:
                                   fc * 1024 + (ts_i + 1) * 128],
                                 WSLB[:, wb * 1024 + nh * 512:
                                      wb * 1024 + (nh + 1) * 512],
                                 regs[k][:, nh * 512:(nh + 1) * 512],
                                 fc == 0, False, inc=(k == 1 and nh == 1),
                                 deps=wdeps if (k == 0 and nh == 0) else ())
                wpj_prev[wb] = tok
            f2_last = []
            for k in range(2):
                ts_i = ts_pair[k]
                for nh in range(2):
                    mm(T2L[0:16, ts_i * 128:(ts_i + 1) * 128],
                       BPJT[0:16, nh * 512:(nh + 1) * 512],
                       regs[k][:, nh * 512:(nh + 1) * 512], False, False,
                       deps=last(ld_bpjT, t2_done))
                for nh in range(2):
                    tok = mm(ONES[:, :],
                             BRW[:, 1024 + nh * 512:1024 + (nh + 1) * 512],
                             regs[k][:, nh * 512:(nh + 1) * 512], False, True,
                             inc=(nh == 1), deps=last(bpr_tok))
                f2_last.append(tok)
            for k in range(2):
                ts_i = ts_pair[k]
                t_abs = tb + ts_i
                par = t_abs % 2
                reg = regs[k]
                ag = stats1024(reg, [f2_last[k]])
                vp = vop(lambda e: e.scalar_tensor_tensor(
                    out=SA[:, 2:3], in0=MV[:, 1:2], scalar=QS2[:, 4:5],
                    in1=EPSC[:, :], op0=OP.mult, op1=OP.add),
                    deps=last(ag, ms_eps, q_pj["s2"]))
                w2r = vop(lambda e: e.reciprocal(out=SA[:, 3:4],
                                                 in_=SA[:, 2:3]), deps=[vp])
                aa0 = act(SA[:, 4:5], SA[:, 3:4], AF.Sqrt, deps=[w2r])
                aa = vop(lambda e: e.tensor_tensor(
                    out=SA[:, 0:1], in0=SA[:, 4:5], in1=QS2[:, 3:4],
                    op=OP.mult), deps=[aa0])
                bb = vop(lambda e: e.scalar_tensor_tensor(
                    out=SA[:, 1:2], in0=MV[:, 0:1], scalar=-1.0,
                    in1=SA[:, 0:1], op0=OP.mult, op1=OP.mult), deps=[aa])
                olnt = act(OLN, reg, AF.Identity, bias=SA[:, 1:2],
                           scale=SA[:, 0:1], deps=last(bb, oln_prev))
                x2b = LKT[:, (2 + par) * 1024:(3 + par) * 1024]
                x2l = dma("sync", x2b,
                          x2_scr[t_abs * 128:(t_abs + 1) * 128, :],
                          deps=last(x2_sts[t_abs], x2ld2_prev[par]),
                          ctr_name="x2l")
                ob2 = FSCR[:, :] if par == 0 else FSC2[:, :]
                fin = vop(lambda e, x2b=x2b, ob2=ob2: e.tensor_add(
                    out=ob2, in0=x2b, in1=OLN),
                    deps=last(olnt, x2l, outt_prev[par]))
                oln_prev = fin
                x2ld2_prev[par] = fin
                stok = dma("sync", out_h[t_abs * 128:(t_abs + 1) * 128, :],
                           ob2, deps=[fin], ctr_name="st")
                outt_prev[par] = stok
                out_sts.append(stok)
                reg_free[k] = [fin]
        azq_rd = oln_prev

    waits("sync", last(*out_sts))

    with nc.Block() as block:
        @block.sync
        def _(eng):
            for nm, fn in ops:
                if nm == "sync":
                    fn(eng)

        @block.gpsimd
        def _(eng):
            for nm, fn in ops:
                if nm == "gpsimd":
                    fn(eng)

        @block.tensor
        def _(eng):
            for nm, fn in ops:
                if nm == "tensor":
                    fn(eng)

        @block.vector
        def _(eng):
            for nm, fn in ops:
                if nm == "vector":
                    fn(eng)

        @block.scalar
        def _(eng):
            for nm, fn in ops:
                if nm == "scalar":
                    fn(eng)

    stack.close()
    return nc, dbg_t


def _prep_inputs(inputs):
    import ml_dtypes
    bf = ml_dtypes.bfloat16
    f32 = np.float32
    ct = lambda a: np.ascontiguousarray(np.asarray(a), dtype=f32)

    common = {
        "wqkT": ct(np.asarray(inputs["L0_attn_Wqk"]).T),
        "wvT": ct(np.asarray(inputs["L0_attn_Wv"]).T),
        "wgT": ct(np.asarray(inputs["L0_attn_Wg"]).T),
        "woT": ct(np.asarray(inputs["L0_attn_Wo"]).T),
        "wfcT": ct(np.asarray(inputs["L0_ffn_W_fc"]).T),
        "wpjT": ct(np.asarray(inputs["L0_ffn_W_proj"]).T),
        "afcT": ct(np.asarray(inputs["L0_ffn_A_fc"]).T),
        "bfcT": ct(np.asarray(inputs["L0_ffn_B_fc"]).T),
        "apjT": ct(np.asarray(inputs["L0_ffn_A_proj"]).T),
        "bpjT": ct(np.asarray(inputs["L0_ffn_B_proj"]).T),
        "bqk": ct(np.asarray(inputs["L0_attn_bqk"]).reshape(NQ, 128).T),
        "bv": ct(np.asarray(inputs["L0_attn_bv"]).reshape(1, E)),
        "bg": ct(np.asarray(inputs["L0_attn_bg"]).reshape(NE, 128).T),
        "bo": ct(np.asarray(inputs["L0_attn_bo"]).reshape(1, D)),
        "bfc": ct(np.asarray(inputs["L0_ffn_b_fc"]).reshape(NF, 128).T),
        "bpj": ct(np.asarray(inputs["L0_ffn_b_proj"]).reshape(1, D)),
        "ln1w": ct(np.asarray(inputs["L0_attn_ln1_w"]).reshape(ND, 128).T),
        "ln1b": ct(np.asarray(inputs["L0_attn_ln1_b"]).reshape(ND, 128).T),
        "ln2wr": ct(np.broadcast_to(np.asarray(inputs["L0_attn_ln2_w"]),
                                    (128, D))),
        "ln2br": ct(np.broadcast_to(np.asarray(inputs["L0_attn_ln2_b"]),
                                    (128, D))),
        "gam": ct(np.asarray(inputs["L0_attn_gamma"]).reshape(4, 2, 128)
                  .transpose(2, 1, 0).reshape(128, 8)),
        "bet": ct(np.asarray(inputs["L0_attn_beta"]).reshape(4, 2, 128)
                  .transpose(2, 1, 0).reshape(128, 8)),
        "posemb": ct(np.asarray(inputs["L0_attn_pos_emb"]).reshape(65, 1)),
    }
    idx = np.arange(G)
    dist = np.clip(np.abs(idx[:, None] - idx[None, :]), 0, 32)
    onehot = np.zeros((65, G * G), dtype=bf)
    onehot[dist.reshape(-1), np.arange(G * G)] = 1
    common["onehot"] = onehot
    common["identb"] = np.eye(128, dtype=bf)
    common["identf"] = np.eye(128, dtype=f32)
    common["onesrow"] = np.ones((1, 128), dtype=bf)
    common["onesrowf"] = np.ones((1, 128), dtype=f32)
    x = np.ascontiguousarray(np.asarray(inputs["L0_attn_x"]), dtype=f32)
    return [dict(common, x=x[i]) for i in range(NCORES)]


def kernel(**inputs):
    in_maps = _prep_inputs(inputs)
    nc, _ = build_nc(dbg=False)
    res = run_bass_kernel_spmd(nc, in_maps, core_ids=list(range(NCORES)))
    out = np.stack([np.asarray(r["out"]) for r in res.results], axis=0)
    return out.astype(np.float32)

# version 16
# speedup vs baseline: 1.5806x; 1.0910x over previous
import math
import numpy as np
import concourse.bass as bass
import concourse.mybir as mybir
from concourse.bass_utils import run_bass_kernel_spmd

F32 = mybir.dt.float32
BF16 = mybir.dt.bfloat16
I32 = mybir.dt.int32
I8 = mybir.dt.int8
AX = mybir.AxisListType.X
AF = mybir.ActivationFunctionType
OP = mybir.AluOpType

NCORES = 8
T = 2048
D = 1024
DQK = 256
E = 2048
FF = 4096
G = 64
NT = T // 128
ND = D // 128
NE = E // 128
NQ = DQK // 128
NF = FF // 128
EPS = 1e-5
PI = math.pi
INV2PI = 1.0 / (2.0 * PI)
PI_LO = float(np.nextafter(np.float32(np.pi), np.float32(0.0)))
LORA_SCALING = 2.0


def build_nc(dbg=False):
    nc = bass.Bass("TRN2", target_bir_lowering=False)

    dr = lambda name, shape, dt, kind: nc.dram_tensor(name, shape, dt, kind=kind).ap()
    x_h = dr("x", [T, D], F32, "ExternalInput")
    wqkT = dr("wqkT", [D, DQK], F32, "ExternalInput")
    wvT = dr("wvT", [D, E], F32, "ExternalInput")
    wgT = dr("wgT", [D, E], F32, "ExternalInput")
    woT = dr("woT", [E, D], F32, "ExternalInput")
    wfcT = dr("wfcT", [D, FF], F32, "ExternalInput")
    wpjT = dr("wpjT", [FF, D], F32, "ExternalInput")
    afcT_h = dr("afcT", [D, 16], F32, "ExternalInput")
    bfcT_h = dr("bfcT", [16, FF], F32, "ExternalInput")
    apjT_h = dr("apjT", [FF, 16], F32, "ExternalInput")
    bpjT_h = dr("bpjT", [16, D], F32, "ExternalInput")
    bqk_h = dr("bqk", [128, NQ], F32, "ExternalInput")
    bv_h = dr("bv", [1, E], F32, "ExternalInput")
    bg_h = dr("bg", [128, NE], F32, "ExternalInput")
    bo_h = dr("bo", [1, D], F32, "ExternalInput")
    bfc_h = dr("bfc", [128, NF], F32, "ExternalInput")
    bpj_h = dr("bpj", [1, D], F32, "ExternalInput")
    ln1w_h = dr("ln1w", [128, ND], F32, "ExternalInput")
    ln1b_h = dr("ln1b", [128, ND], F32, "ExternalInput")
    ln2w_h = dr("ln2wr", [128, D], F32, "ExternalInput")
    ln2b_h = dr("ln2br", [128, D], F32, "ExternalInput")
    gam_h = dr("gam", [128, 8], F32, "ExternalInput")
    bet_h = dr("bet", [128, 8], F32, "ExternalInput")
    pos_h = dr("posemb", [65, 1], F32, "ExternalInput")
    oneh_h = dr("onehot", [65, G * G], BF16, "ExternalInput")
    ident_h = dr("identb", [128, 128], BF16, "ExternalInput")
    identf_h = dr("identf", [128, 128], F32, "ExternalInput")
    ones_h = dr("onesrow", [1, 128], BF16, "ExternalInput")
    onesf_h = dr("onesrowf", [1, 128], F32, "ExternalInput")
    out_h = dr("out", [T, D], F32, "ExternalOutput")
    sc_fc = nc.dram_tensor("scr_fc", [D, FF], I8, kind="Internal").ap()
    scr1 = nc.dram_tensor("scr_1", [1, 1], F32, kind="Internal").ap()
    sc_pj = nc.dram_tensor("scr_pj", [FF, D], I8, kind="Internal").ap()
    x2_scr = nc.dram_tensor("scr_x2", [T, D], BF16, kind="Internal").ap()

    dbg_t = {}
    if dbg:
        for nm, shape in (("d_qs", [128, 8]), ("d_qs2", [128, 8]),
                          ("d_mxp", [128, 32]),
                          ("d_y", [128, ND * T]), ("d_qk", [128, NQ * T]),
                          ("d_v", [128, NT * E]), ("d_lkv", [128, 2 * E]),
                          ("d_x2", [T, D]), ("d_att", [128, 512]),
                          ("d_cos", [128, T]), ("d_sin", [128, T]),
                          ("d_wfc", [D, FF])):
            dbg_t[nm] = dr(nm, shape, F32, "ExternalOutput")

    al = nc.alloc_sbuf_tensor
    GAMT = al("GAMT", [128, 8], F32)
    BETT = al("BETT", [128, 8], F32)
    BQK = al("BQK", [128, NQ], F32)
    BGC = al("BGC", [128, NE], F32)
    BFCC = al("BFCC", [128, NF], F32)
    LN1W = al("LN1W", [128, ND], F32)
    LN1B = al("LN1B", [128, ND], F32)
    LN2W = al("LN2W", [128, D], BF16)
    LN2B = al("LN2B", [128, D], BF16)
    BRW = al("BRW", [1, E], BF16)
    POSE = al("POSE", [65, 1], BF16)
    IDENT = al("IDENT", [128, 128], BF16)
    IDENTF = al("IDENTF", [128, 128], F32)
    ONES = al("ONES", [1, 128], BF16)
    ZROW = al("ZROW", [1, 128], BF16)
    ONESF = al("ONESF", [1, 128], F32)
    BIAS64 = al("BIAS64", [64, 64], F32)
    EPSC = al("EPSC", [128, 1], F32)
    ST6 = al("ST6", [128, 12], F32)
    MV = al("MV", [128, 2], F32)
    SA = al("SA", [128, 8], F32)
    MXP = al("MXP", [128, 32], F32)
    QS = al("QS", [128, 8], F32)
    QS2 = al("QS2", [128, 8], F32)
    COS = al("COS", [128, T], BF16)
    SIN = al("SIN", [128, T], BF16)
    XT = al("XT", [128, 2 * D], F32)
    FSCR = al("FSCR", [128, 1024], F32)
    FSC2 = al("FSC2", [128, 1024], F32)
    ISCR = al("ISCR", [128, 1024], I32)
    AV = al("AV", [128, NT * E], BF16)
    AY = al("AY", [128, ND * T], BF16)
    AGT = al("AGT", [128, NE * 512], BF16)
    AZQ = al("AZQ", [128, 4096], BF16)
    AMSC = al("AMSC", [128, 2048], BF16)
    QK = al("QK", [128, NQ * T], BF16)
    LINKV = al("LINKV", [128, 2 * E], BF16)
    AW = al("AW", [128, 4096], BF16)
    WSLB = al("WSLB", [128, 2048], BF16)
    LKT = al("LKT", [128, NT * DQK], BF16)
    QQ = al("QQ", [128, 1024], BF16)
    QKK = al("QKK", [128, 1024], BF16)
    LQ = al("LQ", [128, 1024], BF16)
    GATEE = al("GATEE", [128, 1024], BF16)
    ATT = al("ATT", [128, 512], BF16)

    ONEH = AV[0:65, 0:G * G]
    SQT = FSCR[:, 0:256]
    M1 = AMSC[:, 0:512]
    M2 = AMSC[:, 512:1024]
    UB0 = AMSC[:, 1024:1536]
    UB1 = AMSC[:, 1536:2048]
    Z2 = AZQ[:, 0:1024]
    T2E = AZQ[:, 1024:2048]
    OLN = AZQ[:, 2048:3072]
    BPJT = QQ[0:16, 0:1024]
    APJT = QKK[:, 0:512]
    AFCT = QKK[:, 512:640]
    T1 = LQ[0:16, 0:1024]
    T2L = GATEE[0:16, 0:1024]
    BFCTA = COS[0:16, :]
    BFCTB = SIN[0:16, :]
    AGTI8 = AGT.bitcast(I8)
    AMSCI8 = AMSC.bitcast(I8)

    PBIG = nc.alloc_psum_tensor("PBIG", [128, 2048], F32)
    PA = nc.alloc_psum_tensor("PA", [128, 1024], F32)
    PC = nc.alloc_psum_tensor("PC", [128, 512], F32)
    PD = nc.alloc_psum_tensor("PD", [128, 512], F32)
    PCD = [PC, PD]

    ops = []
    EMIT = {k: (lambda fn, k=k: ops.append((k, fn)))
            for k in ("tensor", "vector", "scalar", "sync", "gpsimd")}

    class Ctr:
        def __init__(self, h):
            self.h = h
            self.n = 0

    from contextlib import ExitStack
    stack = ExitStack()
    sem = {k: Ctr(stack.enter_context(nc.semaphore(k)))
           for k in ("ld", "gp", "gpc", "te", "ve", "se", "st",
                     "lps", "b64", "qsc",
                     "qld0", "qld1", "xs0", "xs1", "wgl0", "wgl1",
                     "wol0", "wol1", "xrl0", "xrl1", "wvl0", "wvl1",
                     "wfl0", "wfl1", "wpl0", "wpl1", "x2l0", "x2l1",
                     "qst0", "qst1")}
    pipetog = {}
    hw = {e: {} for e in EMIT}

    last_dma = {e: None for e in EMIT}

    def wait(engine, ctr, val):
        if val is None or val <= 0:
            return
        if val > hw[engine].get(id(ctr), 0):
            hw[engine][id(ctr)] = val
            last_dma[engine] = None
            EMIT[engine](lambda e, h=ctr.h, v=val: e.wait_ge(h, v))

    def waits(engine, deps):
        for d in deps:
            if d is None:
                continue
            if isinstance(d, list):
                waits(engine, d)
            else:
                wait(engine, d[0], d[1])

    def last(*tokens):
        out = {}
        for t in tokens:
            if t is None:
                continue
            if isinstance(t, list):
                for u in t:
                    if u is not None and out.get(id(u[0]), (None, 0))[1] < u[1]:
                        out[id(u[0])] = u
                continue
            if out.get(id(t[0]), (None, 0))[1] < t[1]:
                out[id(t[0])] = t
        return list(out.values())

    def dma(engine, out_ap, in_ap, deps=(), ctr_name=None):
        waits(engine, deps)
        if ctr_name is not None and (ctr_name + "0") in sem:
            t = pipetog.get(ctr_name, 0)
            pipetog[ctr_name] = 1 - t
            c = sem[ctr_name + str(t)]
        else:
            c = sem[ctr_name] if ctr_name else (
                sem["ld"] if engine == "sync" else sem["gp"])
        wait(engine, c, c.n)
        c.n += 16
        n = c.n
        EMIT[engine](lambda e, o=out_ap, i=in_ap, h=c.h:
                     e.dma_start(out=o, in_=i).then_inc(h, 16))
        return (c, n)

    def mm(lhsT, rhs, out, start, stop, inc=False, deps=()):
        waits("tensor", deps)
        if inc:
            sem["te"].n += 1
            n = sem["te"].n
            EMIT["tensor"](lambda e, a=lhsT, b=rhs, o=out, s1=start, s2=stop,
                           h=sem["te"].h:
                           e.matmul(o, a, b, start=s1, stop=s2).then_inc(h, 1))
            return (sem["te"], n)
        EMIT["tensor"](lambda e, a=lhsT, b=rhs, o=out, s1=start, s2=stop:
                       e.matmul(o, a, b, start=s1, stop=s2))
        return None

    def act(out_ap, in_ap, func, bias=0.0, scale=1.0, deps=()):
        waits("scalar", deps)
        sem["se"].n += 1
        n = sem["se"].n
        EMIT["scalar"](lambda e, o=out_ap, i=in_ap, f=func, b=bias, s=scale,
                       h=sem["se"].h:
                       e.activation(out=o, in_=i, func=f, bias=b, scale=s)
                       .then_inc(h, 1))
        return (sem["se"], n)

    def vop(emitfn, deps=()):
        waits("vector", deps)
        sem["ve"].n += 1
        n = sem["ve"].n
        EMIT["vector"](lambda e, f=emitfn, h=sem["ve"].h: f(e).then_inc(h, 1))
        return (sem["ve"], n)

    def gop(emitfn, deps=()):
        waits("gpsimd", deps)
        sem["gpc"].n += 1
        n = sem["gpc"].n
        EMIT["gpsimd"](lambda e, f=emitfn, h=sem["gpc"].h: f(e).then_inc(h, 1))
        return (sem["gpc"], n)

    def flush_mm(region, deps=(), m=128):
        return mm(ZROW[:, 0:m], ONES[:, :], region, False, True, inc=True,
                  deps=deps)

    ld_gam = dma("sync", GAMT[:, :], gam_h)
    ld_bet = dma("sync", BETT[:, :], bet_h)
    ld_bqk = dma("sync", BQK[:, :], bqk_h)
    ld_bg = dma("sync", BGC[:, :], bg_h)
    ld_bfc = dma("sync", BFCC[:, :], bfc_h)
    ld_l1w = dma("sync", LN1W[:, :], ln1w_h)
    ld_l1b = dma("sync", LN1B[:, :], ln1b_h)
    ld_onh = dma("sync", ONEH, oneh_h)
    ld_id = dma("sync", IDENT[:, :], ident_h)
    ld_on = dma("sync", ONES[:, :], ones_h)
    ld_onf = dma("sync", ONESF[:, :], onesf_h)
    ld_l2w = dma("gpsimd", LN2W[:, :], ln2w_h)
    ld_l2b = dma("gpsimd", LN2B[:, :], ln2b_h)
    ld_bvr = dma("gpsimd", BRW[:, :], bv_h)
    ld_pos = dma("gpsimd", POSE[:, :], pos_h)
    pre_ld = (sem["ld"], sem["ld"].n)
    ld_gam = ld_bet = ld_bqk = ld_bg = ld_bfc = pre_ld
    ld_l1w = ld_l1b = ld_onh = ld_id = ld_on = ld_onf = pre_ld
    pre_gp = (sem["gp"], sem["gp"].n)
    ld_l2w = ld_l2b = ld_bvr = ld_pos = pre_gp
    iot1 = gop(lambda e: e.iota(FSCR[0:1, 0:128], [[1, 128]],
                                channel_multiplier=0,
                                allow_small_or_imprecise_dtypes=True))
    ms_att = gop(lambda e: e.memset(ATT[:, :], 0.0))
    ms_zr = gop(lambda e: e.memset(ZROW[:, :], 0.0))
    ms_eps = gop(lambda e: e.memset(EPSC[:, :], EPS))
    a_invf = act(IDENTF[0:1, 0:128], FSCR[0:1, 0:128], AF.Exp,
                 scale=-math.log(10000.0) / 128.0, deps=[iot1])

    def sincos_half2(hh, shift, dst_half, pdeps):
        posr = FSC2[0:1, 0:1024]
        pio = gop(lambda e, hh=hh: e.iota(posr, [[1, 1024]], base=hh * 1024,
                                          channel_multiplier=0,
                                          allow_small_or_imprecise_dtypes=True),
                  deps=list(pdeps))
        mmdeps = last(a_invf, pio) + list(pdeps)
        for j in range(2):
            ttok = mm(IDENTF[0:1, 0:128], posr[:, j * 512:(j + 1) * 512],
                      PA[:, j * 512:(j + 1) * 512], True, True, inc=(j == 1),
                      deps=mmdeps if j == 0 else ())
        ts = vop(lambda e: e.tensor_scalar(
            out=FSCR[:, 0:1024], in0=PA[:, :], scalar1=INV2PI, scalar2=shift,
            op0=OP.mult, op1=OP.add), deps=last(ttok))
        k1 = vop(lambda e: e.tensor_single_scalar(
            out=ISCR[:, 0:1024], in_=FSCR[:, 0:1024], scalar=1.0,
            op=OP.mult), deps=[ts])
        c2 = vop(lambda e: e.scalar_tensor_tensor(
            out=XT[:, 0:1024], in0=FSCR[:, 0:1024], scalar=1.0,
            in1=ISCR[:, 0:1024], op0=OP.mult, op1=OP.subtract), deps=[k1])
        km = vop(lambda e: e.scalar_tensor_tensor(
            out=XT[:, 1024:2048], in0=XT[:, 0:1024], scalar=-0.5,
            in1=ISCR[:, 0:1024], op0=OP.is_lt, op1=OP.subtract), deps=[c2])
        kf = vop(lambda e: e.scalar_tensor_tensor(
            out=XT[:, 0:1024], in0=XT[:, 0:1024], scalar=0.5,
            in1=XT[:, 1024:2048], op0=OP.is_gt, op1=OP.subtract), deps=[km])
        dd = vop(lambda e: e.scalar_tensor_tensor(
            out=FSCR[:, 0:1024], in0=XT[:, 0:1024], scalar=-1.0,
            in1=FSCR[:, 0:1024], op0=OP.mult, op1=OP.add), deps=[kf])
        r1 = vop(lambda e: e.tensor_scalar(
            out=FSCR[:, 0:1024], in0=FSCR[:, 0:1024], scalar1=2.0 * PI,
            scalar2=PI_LO, op0=OP.mult, op1=OP.min), deps=[dd])
        r2 = vop(lambda e: e.tensor_single_scalar(
            out=FSCR[:, 0:1024], in_=FSCR[:, 0:1024], scalar=-PI_LO,
            op=OP.max), deps=[r1])
        return act(dst_half, FSCR[:, 0:1024], AF.Sin, deps=[r2]), ttok

    tbl_toks = []
    mm_last = None
    for hh in range(2):
        prev = [tbl_toks[-1]] if tbl_toks else []
        tk, mm_last = sincos_half2(hh, 0.0,
                                   SIN[:, hh * 1024:(hh + 1) * 1024], prev)
        tbl_toks.append(tk)
        tk, mm_last = sincos_half2(hh, 0.25,
                                   COS[:, hh * 1024:(hh + 1) * 1024],
                                   [tbl_toks[-1]])
        tbl_toks.append(tk)
    tables_done = tbl_toks[-1]
    s_sin = tables_done
    s_cos = tables_done

    ev_prev = tables_done
    b64_dmas = []
    for j in range(8):
        tok = mm(POSE[:, :], ONEH[:, j * 512:(j + 1) * 512], PC[0:1, :],
                 True, True, inc=True,
                 deps=last(ld_pos, ld_onh, ev_prev,
                           b64_dmas[-1] if b64_dmas else None))
        ev_prev = vop(lambda e: e.tensor_copy(
            out=FSCR[0:1, 0:512], in_=PC[0:1, :]), deps=[tok])
        b64_dmas.append(dma("sync", BIAS64[8 * j:8 * (j + 1), :],
                            FSCR[0:1, 0:512], deps=[ev_prev],
                            ctr_name="b64"))
    ld_b64 = last(*b64_dmas)
    pc_free = ev_prev
    pd_free = None

    def stats1024(src, deps):
        s1 = vop(lambda e: e.bn_stats(out=ST6[:, 0:6], in_=src[:, 0:512]),
                 deps=deps)
        s2 = vop(lambda e: e.bn_stats(out=ST6[:, 6:12], in_=src[:, 512:1024]),
                 deps=[s1])
        return vop(lambda e: e.bn_aggr(out=MV[:, :], in_=ST6[:, 0:12]),
                   deps=[s2])

    def ln_single(vcol, mcol, acol, bcol, deps):
        w2 = vop(lambda e, v=vcol: e.tensor_scalar(
            out=SA[:, 2:3], in0=v, scalar1=1.0, scalar2=EPS,
            op0=OP.mult, op1=OP.add), deps=deps)
        w2r = vop(lambda e: e.reciprocal(out=SA[:, 3:4], in_=SA[:, 2:3]),
                  deps=[w2])
        aa = act(acol, SA[:, 3:4], AF.Sqrt, deps=[w2r])
        return vop(lambda e, m=mcol, a=acol, b=bcol: e.scalar_tensor_tensor(
            out=b, in0=m, scalar=-1.0, in1=a, op0=OP.mult, op1=OP.mult),
            deps=[aa])

    def ln_double(vcol, mcol, acol, bcol, deps):
        w2 = vop(lambda e, v=vcol: e.tensor_scalar(
            out=SA[:, 2:3], in0=v, scalar1=1.0, scalar2=EPS,
            op0=OP.mult, op1=OP.add), deps=deps)
        w2r = vop(lambda e: e.reciprocal(out=SA[:, 3:4], in_=SA[:, 2:3]),
                  deps=[w2])
        v2 = vop(lambda e, v=vcol: e.tensor_tensor(
            out=SA[:, 4:5], in0=v, in1=SA[:, 3:4], op=OP.mult), deps=[w2r])
        w4 = vop(lambda e: e.tensor_scalar(
            out=SA[:, 5:6], in0=SA[:, 4:5], scalar1=1.0, scalar2=EPS,
            op0=OP.mult, op1=OP.add), deps=[v2])
        w4r = vop(lambda e: e.reciprocal(out=SA[:, 6:7], in_=SA[:, 5:6]),
                  deps=[w4])
        pr = vop(lambda e: e.tensor_tensor(
            out=SA[:, 7:8], in0=SA[:, 3:4], in1=SA[:, 6:7], op=OP.mult),
            deps=[w4r])
        aa = act(acol, SA[:, 7:8], AF.Sqrt, deps=[pr])
        return vop(lambda e, m=mcol, a=acol, b=bcol: e.scalar_tensor_tensor(
            out=b, in0=m, scalar=-1.0, in1=a, op0=OP.mult, op1=OP.mult),
            deps=[aa])

    ev_tok = None
    zg_rd = [None] * 4
    prev_z = None
    for g in range(4):
        z_toks = []
        for tt in range(4):
            t = g * 4 + tt
            buf = tt % 2
            xs = XT[:, buf * D:(buf + 1) * D]
            xdep = [prev_z] if t >= 2 else [tables_done, ld_b64]
            xld = dma("sync", xs, x_h[t * 128:(t + 1) * 128, :], deps=xdep,
                      ctr_name="xs")
            ag = stats1024(xs, [xld])
            bb = ln_double(MV[:, 1:2], MV[:, 0:1], SA[:, 0:1], SA[:, 1:2],
                           [ag])
            ztok = vop(lambda e, xs=xs, tt=tt: e.tensor_scalar(
                out=AZQ[:, tt * D:(tt + 1) * D], in0=xs,
                scalar1=SA[:, 0:1], scalar2=SA[:, 1:2],
                op0=OP.mult, op1=OP.add),
                deps=last(bb, zg_rd[g - 1] if g >= 1 else None))
            z_toks.append(ztok)
            prev_z = ztok
        for c in range(ND):
            q = c % 4
            reg = PBIG[:, q * 512:(q + 1) * 512]
            deps = last(*(z_toks + [ev_tok, ld_id]))
            for tt in range(4):
                tok = mm(AZQ[:, tt * D + c * 128: tt * D + (c + 1) * 128],
                         IDENT[:, :], reg[:, tt * 128:(tt + 1) * 128],
                         True, True, inc=(tt == 3),
                         deps=deps if tt == 0 else ())
            ev_tok = act(AY[:, c * T + g * 512: c * T + (g + 1) * 512],
                         reg, AF.Identity, bias=LN1B[:, c:c + 1],
                         scale=LN1W[:, c:c + 1],
                         deps=last(tok, ld_l1w, ld_l1b))
            if c == ND - 1:
                zg_rd[g] = tok
    y_done = ev_tok
    Y = AY

    pcd_free = [pc_free, pd_free]
    aw_free = [None, None]
    qk_evs = []
    for qc in range(NQ):
        wb = qc % 2
        wsl = AW[:, wb * 1024:(wb + 1) * 1024]
        wqk_ld = dma("gpsimd", wsl.rearrange("p (c k) -> p c k", c=ND),
                     wqkT[:, qc * 128:(qc + 1) * 128]
                     .rearrange("(c p) k -> p c k", p=128),
                     deps=last(aw_free[wb]), ctr_name="wvl")
        for tc in range(4):
            pi = (qc * 4 + tc) % 2
            ps = PCD[pi]
            deps = last(wqk_ld, y_done, pcd_free[pi])
            for c in range(ND):
                tok = mm(wsl[:, c * 128:(c + 1) * 128],
                         Y[:, c * T + tc * 512: c * T + (tc + 1) * 512],
                         ps[:, :], c == 0, c == ND - 1, inc=(c == ND - 1),
                         deps=deps if c == 0 else ())
            ev = act(QK[:, qc * T + tc * 512: qc * T + (tc + 1) * 512],
                     ps[:, :], AF.Silu, bias=BQK[:, qc:qc + 1],
                     deps=last(tok, ld_bqk))
            pcd_free[pi] = ev
            qk_evs.append(ev)
        aw_free[wb] = tok
    qk_done = last(*qk_evs)
    V = AV

    def quant_max(w_hbm, cols, qsl, start_deps, xt_dep):
        w_mx = min(2048, cols)
        npc = (D * FF) // (128 * w_mx)
        w2 = w_hbm.rearrange("(c p) (j k) -> p c j k", p=128, k=w_mx)
        ncw2 = cols // w_mx
        mx_toks = []
        prev = xt_dep
        for i in range(npc):
            stg = XT[:, 0:w_mx]
            c, j = divmod(i, ncw2)
            ldt = dma("sync", stg, w2[:, c, j, :],
                      deps=last(prev) + (list(start_deps) if i == 0 else []),
                      ctr_name="qld")
            mx = vop(lambda e, stg=stg, i=i: e.tensor_reduce(
                out=MXP[:, i:i + 1], in_=stg, axis=AX, op=OP.max,
                apply_absolute_value=True), deps=[ldt])
            prev = mx
            mx_toks.append(mx)
        m1 = vop(lambda e, q=qsl, n=npc: e.tensor_reduce(
            out=q[:, 0:1], in_=MXP[:, 0:n], axis=AX, op=OP.max),
            deps=last(*mx_toks))
        d1 = dma("sync", FSCR[0:1, 0:128], qsl[:, 0:1], deps=[m1],
                 ctr_name="qsc")
        m2 = vop(lambda e, q=qsl: e.tensor_reduce(
            out=q[0:1, 6:7], in_=FSCR[0:1, 0:128], axis=AX, op=OP.max),
            deps=[d1])
        d2a = dma("sync", scr1, qsl[0:1, 6:7], deps=[m2], ctr_name="qsc")
        d2 = dma("sync", qsl[:, 0:1], scr1.broadcast_to([128, 1]),
                 deps=[d2a], ctr_name="qsc")
        rr = vop(lambda e, q=qsl: e.reciprocal(out=q[:, 1:2], in_=q[:, 0:1]),
                 deps=[d2])
        sinv = vop(lambda e, q=qsl: e.tensor_single_scalar(
            out=q[:, 2:3], in_=q[:, 1:2], scalar=127.0, op=OP.mult),
            deps=[rr])
        s_ = vop(lambda e, q=qsl: e.tensor_single_scalar(
            out=q[:, 3:4], in_=q[:, 0:1], scalar=1.0 / 127.0, op=OP.mult),
            deps=[d2])
        s2 = vop(lambda e, q=qsl: e.tensor_tensor(
            out=q[:, 4:5], in0=q[:, 3:4], in1=q[:, 3:4], op=OP.mult),
            deps=[s_])
        sinv2 = vop(lambda e, q=qsl: e.tensor_single_scalar(
            out=q[:, 5:6], in_=q[:, 2:3], scalar=LORA_SCALING, op=OP.mult),
            deps=[sinv])
        return dict(sinv=sinv, s=s_, s2=s2, sinv2=sinv2,
                    st_prev=[last(*mx_toks), last(*mx_toks)],
                    qf_prev=[None, None], stores=[],
                    w4=w_hbm.rearrange("(c p) (j k) -> p c j k", p=128,
                                       k=1024),
                    ncw=cols // 1024, qsl=qsl)

    def quant_pieces(st, rng, wf_fn, extra_deps=()):
        scr = st["scr"]
        for i in rng:
            b = i % 2
            stg = XT[:, b * 1024:(b + 1) * 1024]
            wf = wf_fn(b)
            c, j = divmod(i, st["ncw"])
            ldt = dma("sync", stg, st["w4"][:, c, j, :],
                      deps=last(st["st_prev"][b]), ctr_name="qld")
            q1 = vop(lambda e, stg=stg, q=st["qsl"]: e.tensor_single_scalar(
                out=ISCR[:, :], in_=stg, scalar=q[:, 2:3], op=OP.mult),
                deps=last(ldt, st["sinv"]))
            cc = vop(lambda e, stg=stg, q=st["qsl"]: e.scalar_tensor_tensor(
                out=FSCR[:, :], in0=stg, scalar=q[:, 2:3], in1=ISCR[:, :],
                op0=OP.mult, op1=OP.subtract), deps=[q1])
            st["st_prev"][b] = cc
            qm = vop(lambda e: e.scalar_tensor_tensor(
                out=FSC2[:, :], in0=FSCR[:, :], scalar=-0.5, in1=ISCR[:, :],
                op0=OP.is_lt, op1=OP.subtract), deps=[cc])
            qf = vop(lambda e, wf=wf: e.scalar_tensor_tensor(
                out=wf, in0=FSCR[:, :], scalar=0.5, in1=FSC2[:, :],
                op0=OP.is_gt, op1=OP.subtract),
                deps=last(qm, st["qf_prev"][b]) + list(extra_deps))
            stt = dma("sync", st["s4"][:, c, j, :], wf, deps=[qf],
                      ctr_name="qst")
            st["qf_prev"][b] = stt
            st["stores"].append(stt)

    q_fc = quant_max(wfcT, FF, QS, last(y_done, tables_done, ld_b64),
                     last(prev_z, tables_done))
    q_fc["scr"] = None
    q_fc["s4"] = sc_fc.rearrange("(c p) (j k) -> p c j k", p=128, k=1024)
    quant_pieces(q_fc, range(32), lambda b: AGTI8[:, b * 1024:(b + 1) * 1024])
    q_fc["done"] = last(*q_fc["stores"])

    v_evs = []
    V3 = V[:, :].rearrange("p (t k) -> p t k", t=NT)
    for oc in range(8):
        wb = oc % 2
        wsl = AW[:, wb * 2048:(wb + 1) * 2048]
        wdep = last(aw_free[wb]) if oc < 2 else [v_evs[8 * (oc - 2) + 7]]
        wv_ld = dma("gpsimd", wsl.rearrange("p (c k) -> p c k", c=ND),
                    wvT[:, oc * 256:(oc + 1) * 256]
                    .rearrange("(c p) k -> p c k", p=128),
                    deps=wdep, ctr_name="wvl")
        for tg in range(8):
            pi = tg % 2
            ps = PCD[pi]
            deps = last(wv_ld, y_done, pcd_free[pi], ld_on, ld_bvr)
            for tt in range(2):
                t = 2 * tg + tt
                for c in range(ND):
                    mm(Y[:, c * T + t * 128: c * T + (t + 1) * 128],
                       wsl[:, c * 256:(c + 1) * 256],
                       ps[:, tt * 256:(tt + 1) * 256], c == 0, False,
                       deps=deps if (c == 0 and tt == 0) else ())
                tok = mm(ONES[:, :], BRW[:, oc * 256:(oc + 1) * 256],
                         ps[:, tt * 256:(tt + 1) * 256], False, True,
                         inc=(tt == 1))
            ev = act(V3[:, 2 * tg:2 * tg + 2, oc * 256:(oc + 1) * 256],
                     ps[:, :].rearrange("p (t k) -> p t k", t=2),
                     AF.Silu, deps=[tok])
            pcd_free[pi] = ev
            v_evs.append(ev)
    aw_free = [v_evs[-1], v_evs[-2]]
    v_done = last(*v_evs)

    def head_rope(hidx, tok_off, dst, d0_off, d1_off, deps):
        n = 512
        a0 = act(UB0, QK[:, tok_off:tok_off + n], AF.Identity,
                 bias=BETT[:, hidx:hidx + 1], scale=GAMT[:, hidx:hidx + 1],
                 deps=deps)
        a1 = act(UB1, QK[:, T + tok_off:T + tok_off + n], AF.Identity,
                 bias=BETT[:, 4 + hidx:4 + hidx + 1],
                 scale=GAMT[:, 4 + hidx:4 + hidx + 1], deps=deps)
        cs = COS[:, tok_off:tok_off + n]
        sn = SIN[:, tok_off:tok_off + n]
        m1 = vop(lambda e: e.tensor_mul(out=M1, in0=UB0, in1=cs),
                 deps=last(a0, s_cos))
        m2 = vop(lambda e: e.tensor_mul(out=M2, in0=UB1, in1=sn),
                 deps=last(a1, s_sin))
        o1 = vop(lambda e: e.tensor_sub(
            out=dst[:, d0_off:d0_off + n], in0=M1, in1=M2), deps=[m1, m2])
        m3 = vop(lambda e: e.tensor_mul(out=M1, in0=UB1, in1=cs), deps=[o1])
        m4 = vop(lambda e: e.tensor_mul(out=M2, in0=UB0, in1=sn), deps=[o1])
        o2 = vop(lambda e: e.tensor_add(
            out=dst[:, d1_off:d1_off + n], in0=M1, in1=M2), deps=[m3, m4])
        return (last(a0, a1), last(o1, o2))

    LKF = AZQ
    lk_toks = []
    for cc in range(4):
        dep = last(qk_done, ld_gam, ld_bet,
                   zg_rd[3] if cc == 0 else None,
                   lk_toks[-1] if lk_toks else None)
        _, o = head_rope(3, cc * 512, LKF, cc * 512, 2048 + cc * 512, dep)
        lk_toks.append(o)
    lk_done = last(*lk_toks)
    lkt_evs = []
    ev = None
    for c2 in range(2):
        for tg in range(4):
            deps = last(lk_done, ev, pcd_free[0])
            for tt in range(4):
                t = tg * 4 + tt
                tok = mm(LKF[:, c2 * 2048 + t * 128: c2 * 2048 + (t + 1) * 128],
                         IDENT[:, :], PC[:, tt * 128:(tt + 1) * 128],
                         True, True, inc=(tt == 3),
                         deps=deps if tt == 0 else ())
            ev = vop(lambda e, c2=c2, tg=tg: e.tensor_copy(
                out=LKT[:, :].rearrange("p (t k) -> p t k", t=NT)
                [:, tg * 4:(tg + 1) * 4, c2 * 128:(c2 + 1) * 128],
                in_=PC[:, :].rearrange("p (t k) -> p t k", t=4)),
                deps=[tok])
            lkt_evs.append(ev)
    lkt_done = last(*lkt_evs)
    pcd_free[0] = lkt_evs[-1]

    q_pj = quant_max(wpjT, D, QS2, last(q_fc["done"]),
                     last(q_fc["st_prev"][0], q_fc["st_prev"][1]))
    q_pj["scr"] = None
    q_pj["s4"] = sc_pj.rearrange("(c p) (j k) -> p c j k", p=128, k=1024)

    regions8 = [PBIG[:, 0:512], PBIG[:, 512:1024], PBIG[:, 1024:1536],
                PBIG[:, 1536:2048], PA[:, 0:512], PA[:, 512:1024],
                PC[:, :], PD[:, :]]
    lkv_evs = []
    for m in range(2):
        for n4 in range(4):
            r = regions8[m * 4 + n4]
            deps = last(lkt_done, v_done, pcd_free[0], pcd_free[1], ev_tok)
            for t in range(NT):
                tok = mm(LKT[:, t * DQK + m * 128: t * DQK + (m + 1) * 128],
                         V[:, t * E + n4 * 512: t * E + (n4 + 1) * 512],
                         r, t == 0, t == NT - 1, inc=(t == NT - 1),
                         deps=deps if t == 0 else ())
            lkv_evs.append(act(
                LINKV[:, m * E + n4 * 512: m * E + (n4 + 1) * 512], r,
                AF.Identity, scale=1.0 / T, deps=[tok]))
    lkv_done = last(*lkv_evs)

    ld_bor = dma("gpsimd", BRW[:, 0:1024], bo_h, deps=last(v_done),
                 ctr_name="lps")
    XB0 = LKT[:, 0:1024]
    XB1 = LKT[:, 1024:2048]
    X2S0 = LKT[:, 2048:3072]
    X2S1 = LKT[:, 3072:4096]

    x2_sts = [None] * NT
    pa_free = [lkv_evs[4], lkv_evs[5]]
    pcd_free = [lkv_evs[6], lkv_evs[7]]
    pbig_free = [last(lkv_evs[0], lkv_evs[1]), last(lkv_evs[2], lkv_evs[3])]
    gt_free = v_evs[-1]
    gatee_prev = [None, None]
    wo_prev = [None, None]
    z2_rd = lkt_done
    epi_z2 = lkt_done
    epi_ol = lkt_done
    xb_prev = [lkv_done, lkv_done]
    x2s_prev = [None, None]

    pj_batch_dep = None
    pj_done_batches = [range(0, 11), range(11, 22), range(22, 32), range(0)]
    for C in range(4):
        base = C * 512
        d0 = last(qk_done, ms_att, lkv_done, z2_rd, pj_batch_dep)
        aq, hq = head_rope(0, base, QQ, 0, 512, d0)
        ak, hk = head_rope(2, base, QKK, 0, 512, [hq])
        al_, hl = head_rope(1, base, LQ, 0, 512, [hk])
        sdeps = last(hq, hk, pcd_free[0])
        for p in range(4):
            for c2 in range(2):
                mm(QKK[:, c2 * 512 + p * 128: c2 * 512 + (p + 1) * 128],
                   QQ[:, c2 * 512 + p * 128: c2 * 512 + (p + 1) * 128],
                   PC[:, p * 128:(p + 1) * 128], c2 == 0,
                   c2 == 1 and p != 3,
                   deps=sdeps if (p == 0 and c2 == 0) else ())
        tok = flush_mm(PC[:, 384:512], deps=last(ms_zr, ld_on))
        pc3 = PC[:, :].rearrange("p (c k) -> p c k", c=4)
        sq3 = SQT[:, :].rearrange("p (c k) -> p c k", c=4)
        at3 = ATT[:, :].rearrange("p (c k) -> p c k", c=4)
        b3 = BIAS64[0:64, :].unsqueeze(1).broadcast_to([64, 4, 64])
        sq_t = vop(lambda e: e.scalar_tensor_tensor(
            out=sq3[0:64, :, :], in0=pc3[0:64, :, 0:64], scalar=1.0 / G,
            in1=b3, op0=OP.mult, op1=OP.add), deps=last(tok, ld_b64))
        sq_b = vop(lambda e: e.scalar_tensor_tensor(
            out=sq3[64:128, :, :], in0=pc3[64:128, :, 64:128],
            scalar=1.0 / G, in1=b3, op0=OP.mult, op1=OP.add), deps=[sq_t])
        at_t = vop(lambda e: e.scalar_tensor_tensor(
            out=at3[0:64, :, 0:64], in0=sq3[0:64, :, :], scalar=0.0,
            in1=sq3[0:64, :, :], op0=OP.max, op1=OP.mult), deps=[sq_t])
        at_b = vop(lambda e: e.scalar_tensor_tensor(
            out=at3[64:128, :, 64:128], in0=sq3[64:128, :, :], scalar=0.0,
            in1=sq3[64:128, :, :], op0=OP.max, op1=OP.mult), deps=[sq_b])
        att_done = last(at_t, at_b)
        pcd_free[0] = at_b
        if dbg and C == 0:
            dma("gpsimd", dbg_t["d_att"], ATT[:, :], deps=[att_done],
                ctr_name="lps")

        gt_toks = []
        for Ei in range(NE):
            wb = Ei % 2
            prev_wg = gt_toks[-2] if len(gt_toks) >= 2 else gt_free
            wg_ld = dma("gpsimd", AW[:, wb * 1024:(wb + 1) * 1024]
                        .rearrange("p (c k) -> p c k", c=ND),
                        wgT[:, Ei * 128:(Ei + 1) * 128]
                        .rearrange("(c p) k -> p c k", p=128),
                        deps=last(prev_wg), ctr_name="wgl")
            pi = Ei % 2
            gdeps = last(wg_ld, pcd_free[pi], y_done)
            for c in range(ND):
                tok = mm(AW[:, wb * 1024 + c * 128: wb * 1024 + (c + 1) * 128],
                         Y[:, c * T + base: c * T + base + 512],
                         PCD[pi][:, :], c == 0, c == ND - 1,
                         inc=(c == ND - 1), deps=gdeps if c == 0 else ())
            gprev = gt_toks[-2] if len(gt_toks) >= 2 else gatee_prev[pi]
            agt_dep = (q_fc["qf_prev"][Ei]
                       if (C == 0 and Ei < 2) else None)
            gev = act(GATEE[:, 0:512] if pi == 0 else GATEE[:, 512:1024],
                      PCD[pi][:, :], AF.Silu, bias=BGC[:, Ei:Ei + 1],
                      deps=last(tok, ld_bg, gprev))
            pcd_free[pi] = gev
            ph = Ei % 2
            adeps = last(pa_free[ph], lkv_done, hl, att_done, v_done)
            for m in range(2):
                mm(LINKV[:, m * E + Ei * 128: m * E + (Ei + 1) * 128],
                   LQ[:, m * 512:(m + 1) * 512],
                   PA[:, ph * 512:(ph + 1) * 512], m == 0, False,
                   deps=adeps if m == 0 else ())
            for p in range(4):
                mm(V[:, (4 * C + p) * E + Ei * 128:
                     (4 * C + p) * E + (Ei + 1) * 128],
                   ATT[:, p * 128:(p + 1) * 128],
                   PA[:, ph * 512 + p * 128: ph * 512 + (p + 1) * 128],
                   False, False)
            tok2 = flush_mm(PA[:, ph * 512: ph * 512 + 128])
            gt = vop(lambda e, Ei=Ei, ph=ph, pi=pi: e.tensor_tensor(
                out=AGT[:, Ei * 512:(Ei + 1) * 512],
                in0=PA[:, ph * 512:(ph + 1) * 512],
                in1=GATEE[:, pi * 512:(pi + 1) * 512], op=OP.mult),
                deps=last(tok2, gev, agt_dep))
            pa_free[ph] = gt
            gt_toks.append(gt)
        gts = last(*gt_toks)

        regs4 = (PBIG[:, 0:1024], PBIG[:, 1024:2048], PA[:, :], None)
        hd = last(gts, pbig_free[0], pbig_free[1], pa_free[0], pa_free[1],
                  pcd_free[0], pcd_free[1])
        for Ei in range(NE):
            wb = Ei % 2
            wo_ld = dma("gpsimd", WSLB[:, wb * 1024:(wb + 1) * 1024],
                        woT[Ei * 128:(Ei + 1) * 128, :],
                        deps=last(wo_prev[wb]), ctr_name="wol")
            wdeps = last(wo_ld, gts) + (hd if Ei == 0 else [])
            for k in range(4):
                for nh in range(2):
                    dst = (PCD[nh][:, :] if k == 3 else
                           regs4[k][:, nh * 512:(nh + 1) * 512])
                    tok = mm(AGT[:, Ei * 512 + k * 128:
                                 Ei * 512 + (k + 1) * 128],
                             WSLB[:, wb * 1024 + nh * 512:
                                  wb * 1024 + (nh + 1) * 512],
                             dst, Ei == 0, False,
                             inc=(k == 3 and nh == 1),
                             deps=wdeps if (k == 0 and nh == 0) else ())
            wo_prev[wb] = tok
        wo_last = []
        for k in range(4):
            for nh in range(2):
                dst = (PCD[nh][:, :] if k == 3 else
                       regs4[k][:, nh * 512:(nh + 1) * 512])
                tok = mm(ONES[:, :], BRW[:, nh * 512:(nh + 1) * 512],
                         dst, False, True, inc=(nh == 1),
                         deps=last(ld_bor, ld_on))
            wo_last.append(tok)
        for k in range(4):
            t_abs = 4 * C + k
            if k == 3:
                s1 = vop(lambda e: e.bn_stats(out=ST6[:, 0:6], in_=PC[:, :]),
                         deps=[wo_last[k]])
                s2 = vop(lambda e: e.bn_stats(out=ST6[:, 6:12], in_=PD[:, :]),
                         deps=[s1])
                ag = vop(lambda e: e.bn_aggr(out=MV[:, :], in_=ST6[:, 0:12]),
                         deps=[s2])
            else:
                ag = stats1024(regs4[k], [wo_last[k]])
            bb = ln_single(MV[:, 1:2], MV[:, 0:1], SA[:, 0:1],
                           SA[:, 1:2], [ag])
            if k == 3:
                z2t_a = act(Z2[:, 0:512], PC[:, :], AF.Identity,
                            bias=SA[:, 1:2], scale=SA[:, 0:1],
                            deps=last(bb, epi_z2))
                z2t = act(Z2[:, 512:1024], PD[:, :], AF.Identity,
                          bias=SA[:, 1:2], scale=SA[:, 0:1], deps=[z2t_a])
                pcd_free = [z2t, z2t]
            else:
                z2t = act(Z2, regs4[k], AF.Identity, bias=SA[:, 1:2],
                          scale=SA[:, 0:1], deps=last(bb, epi_z2))
            t2a = vop(lambda e: e.tensor_mul(out=T2E, in0=Z2,
                                             in1=LN2W[:, :]),
                      deps=last(z2t, ld_l2w))
            epi_z2 = t2a
            t2b = vop(lambda e: e.tensor_add(out=T2E, in0=T2E,
                                             in1=LN2B[:, :]),
                      deps=last(t2a, ld_l2b))
            ag2 = stats1024(T2E, [t2b])
            bb2 = ln_single(MV[:, 1:2], MV[:, 0:1], SA[:, 0:1],
                            SA[:, 1:2], [ag2])
            olnt = act(OLN, T2E, AF.Identity, bias=SA[:, 1:2],
                       scale=SA[:, 0:1], deps=last(bb2, epi_ol))
            par = t_abs % 2
            xb = XB0 if par == 0 else XB1
            x2s = X2S0 if par == 0 else X2S1
            xre = dma("gpsimd", xb, x_h[t_abs * 128:(t_abs + 1) * 128, :],
                      deps=last(xb_prev[par]), ctr_name="xrl")
            fin = vop(lambda e, xb=xb, x2s=x2s: e.tensor_add(
                out=x2s, in0=xb, in1=OLN),
                deps=last(olnt, xre, x2s_prev[par]))
            epi_ol = fin
            xb_prev[par] = fin
            stok = dma("sync", x2_scr[t_abs * 128:(t_abs + 1) * 128, :],
                       x2s, deps=[fin], ctr_name="st")
            x2s_prev[par] = stok
            x2_sts[t_abs] = stok
            if k < 2:
                pbig_free[k] = fin
            elif k == 2:
                pa_free = [fin, fin]
        gt_free = wo_last[-1]
        z2_rd = gts
        n0 = len(q_pj["stores"])
        quant_pieces(q_pj, pj_done_batches[C],
                     lambda b: AMSCI8[:, b * 1024:(b + 1) * 1024],
                     extra_deps=last(hl))
        if len(q_pj["stores"]) > n0:
            pj_batch_dep = last(*q_pj["stores"][n0:])

    q_pj["done"] = last(*q_pj["stores"])
    x2_done = last(*[x2_sts[t] for t in range(NT)])
    if dbg:
        for nm, src in (("d_y", Y), ("d_qk", QK), ("d_v", V),
                        ("d_lkv", LINKV), ("d_cos", COS), ("d_sin", SIN)):
            dma("gpsimd", dbg_t[nm], src[:, :],
                deps=last(y_done, qk_done, v_done, lkv_done, x2_done),
                ctr_name="lps")
        dma("gpsimd", dbg_t["d_x2"], x2_scr, deps=last(x2_done),
            ctr_name="lps")

    ld_bpr = dma("sync", XT[0:1, 0:1024], bpj_h,
                 deps=last(q_pj["st_prev"][0], q_pj["st_prev"][1]),
                 ctr_name="qsc")
    bpr_tok = act(BRW[:, 1024:2048], XT[0:1, 0:1024], AF.Identity,
                  scale=QS2[0:1, 2:3], deps=last(ld_bpr, q_pj["sinv"],
                                                 v_done))
    if dbg:
        dma("gpsimd", dbg_t["d_wfc"], sc_fc, deps=[q_fc["done"]],
            ctr_name="lps")
        dma("sync", dbg_t["d_qs"], QS[:, :],
            deps=last(q_fc["sinv2"], q_fc["s2"]), ctr_name="b64")
        dma("sync", dbg_t["d_qs2"], QS2[:, :],
            deps=last(q_pj["sinv2"], q_pj["s2"]), ctr_name="b64")
        dma("sync", dbg_t["d_mxp"], MXP[:, :], deps=last(q_fc["sinv2"]),
            ctr_name="b64")

    lkv_rd_done = gts
    late_dep = last(hl, att_done, gts, bpr_tok)
    ld_afc = dma("gpsimd", AFCT.rearrange("p (c k) -> p c k", c=ND),
                 afcT_h.rearrange("(c p) k -> p c k", p=128), deps=late_dep,
                 ctr_name="lps")
    ld_apj = dma("gpsimd", APJT.rearrange("p (c k) -> p c k", c=NF),
                 apjT_h.rearrange("(c p) k -> p c k", p=128), deps=late_dep,
                 ctr_name="lps")
    ld_bpjT = dma("gpsimd", BPJT, bpjT_h, deps=late_dep, ctr_name="lps")
    ld_bfcA = dma("gpsimd", BFCTA, bfcT_h[:, 0:2048], deps=late_dep,
                  ctr_name="lps")
    ld_bfcB = dma("gpsimd", BFCTB, bfcT_h[:, 2048:4096], deps=late_dep,
                  ctr_name="lps")

    Y2 = AY
    H = AV
    regs3 = [PBIG[:, 0:1024], PBIG[:, 1024:2048], PA[:, :]]
    reg_free = [last(pbig_free[0]), last(pbig_free[1]),
                last(pa_free[0], pa_free[1])]
    pcq_free = last(pcd_free[0], pcd_free[1])
    wfc_prev = [None, None]
    wpj_prev = [None, None]
    wfc_cst = [None, None]
    wpj_cst = [None, None]
    sfc4 = sc_fc.rearrange("(c p) (o k) -> p c o k", p=128, k=128)
    spj3 = sc_pj.rearrange("(c p) k -> p c k", p=128)
    out_sts = []
    azq_rd = epi_ol
    y_rd = gts
    v_rd = gts
    x2ld_prev = [None, None]
    x2ld2_prev = [None, None]
    outt_prev = [None, None]
    oln_prev = bpr_tok

    for Cf in range(2):
        tb = Cf * 8
        ztoks_all = []
        ev = None
        for g2 in range(2):
            z_toks = []
            for tt in range(4):
                t_abs = tb + g2 * 4 + tt
                par = tt % 2
                xbuf = LKT[:, par * 1024:(par + 1) * 1024]
                xld = dma("sync", xbuf,
                          x2_scr[t_abs * 128:(t_abs + 1) * 128, :],
                          deps=last(x2_sts[t_abs], x2ld_prev[par]),
                          ctr_name="x2l")
                ag = stats1024(xbuf, [xld])
                bb = ln_single(MV[:, 1:2], MV[:, 0:1], SA[:, 0:1],
                               SA[:, 1:2], [ag])
                ztok = vop(lambda e, xbuf=xbuf, tt=tt: e.tensor_scalar(
                    out=AZQ[:, tt * D:(tt + 1) * D], in0=xbuf,
                    scalar1=SA[:, 0:1], scalar2=SA[:, 1:2],
                    op0=OP.mult, op1=OP.add),
                    deps=last(bb, azq_rd))
                x2ld_prev[par] = ztok
                z_toks.append(ztok)
            for c in range(ND):
                q = c % 4
                reg = PBIG[:, q * 512:(q + 1) * 512]
                deps = last(*(z_toks + [ev, ld_id] + reg_free[q // 2]))
                for tt in range(4):
                    tok = mm(AZQ[:, tt * D + c * 128: tt * D + (c + 1) * 128],
                             IDENT[:, :], reg[:, tt * 128:(tt + 1) * 128],
                             True, True, inc=(tt == 3),
                             deps=deps if tt == 0 else ())
                ev = vop(lambda e, c=c, g2=g2, reg=reg: e.tensor_copy(
                    out=Y2[:, c * 1024 + g2 * 512:
                           c * 1024 + (g2 + 1) * 512],
                    in_=reg), deps=[tok])
                reg_free[q // 2] = [ev]
            azq_rd = tok
            ztoks_all += z_toks
        y2_done = ev

        t1toks = []
        for h2 in range(2):
            deps = last(y2_done, ld_afc, pcq_free, hl)
            for c in range(ND):
                tok = mm(AFCT[:, c * 16:(c + 1) * 16],
                         Y2[:, c * 1024 + h2 * 512: c * 1024 + (h2 + 1) * 512],
                         PC[0:16, :], c == 0, c == ND - 1,
                         inc=(c == ND - 1), deps=deps if c == 0 else ())
            t1toks.append(act(T1[0:16, h2 * 512:(h2 + 1) * 512], PC[0:16, :],
                              AF.Identity, scale=QS[0:16, 5:6],
                              deps=last(tok, q_fc["sinv2"])))
            pcq_free = t1toks[-1]
        t1_done = last(*t1toks)
        h_toks = []
        for o1 in range(NF):
            wb = o1 % 2
            stg8 = AMSCI8[:, wb * 1024:(wb + 1) * 1024]
            ld8 = dma("sync", stg8, sfc4[:, :, o1, :],
                      deps=last(wfc_cst[wb], q_fc["done"], gt_free,
                                q_pj["qf_prev"][wb]),
                      ctr_name="wfl")
            cst = vop(lambda e, stg8=stg8, wb=wb: e.tensor_copy(
                out=AW[:, wb * 1024:(wb + 1) * 1024], in_=stg8),
                deps=last(ld8, wfc_prev[wb]))
            wfc_cst[wb] = cst
            ri = o1 % 3
            reg = regs3[ri]
            deps = last(cst, y2_done, t1_done, ld_bfcA, ld_bfcB,
                        *reg_free[ri])
            for c in range(ND):
                for nh in range(2):
                    mm(AW[:, wb * 1024 + c * 128: wb * 1024 + (c + 1) * 128],
                       Y2[:, c * 1024 + nh * 512: c * 1024 + (nh + 1) * 512],
                       reg[:, nh * 512:(nh + 1) * 512], c == 0, False,
                       deps=deps if (c == 0 and nh == 0) else ())
            bsrc = BFCTA if o1 < 16 else BFCTB
            bof = (o1 % 16) * 128
            for nh in range(2):
                tok = mm(bsrc[:, bof:bof + 128],
                         T1[0:16, nh * 512:(nh + 1) * 512],
                         reg[:, nh * 512:(nh + 1) * 512], False, True,
                         inc=(nh == 1))
            hev = act(H[:, o1 * 1024:(o1 + 1) * 1024], reg, AF.Gelu,
                      bias=BFCC[:, o1:o1 + 1], scale=QS[:, 3:4],
                      deps=last(tok, q_fc["s"], ld_bfc, v_rd))
            h_toks.append(hev)
            reg_free[ri] = [hev]
            wfc_prev[wb] = hev
        h_done = last(*h_toks)
        t2toks = []
        for h2 in range(2):
            deps = last(h_done, ld_apj, pcq_free)
            for fc in range(NF):
                tok = mm(APJT[:, fc * 16:(fc + 1) * 16],
                         H[:, fc * 1024 + h2 * 512:
                           fc * 1024 + (h2 + 1) * 512],
                         PC[0:16, :], fc == 0, fc == NF - 1,
                         inc=(fc == NF - 1), deps=deps if fc == 0 else ())
            t2toks.append(act(T2L[0:16, h2 * 512:(h2 + 1) * 512], PC[0:16, :],
                              AF.Identity, scale=QS2[0:16, 5:6],
                              deps=last(tok, q_pj["sinv2"])))
            pcq_free = t2toks[-1]
        t2_done = last(*t2toks)
        for qp in range(2):
            regs4f = (PBIG[:, 0:1024], PBIG[:, 1024:2048], PA[:, :], None)

            def f2dst(k, nh):
                return (PCD[nh][:, :] if k == 3 else
                        regs4f[k][:, nh * 512:(nh + 1) * 512])
            hd = last(*(reg_free[0] + reg_free[1] + reg_free[2] +
                        [pcq_free]))
            for fc in range(NF):
                wb = fc % 2
                stg8 = AMSCI8[:, 2048 + wb * 1024: 2048 + (wb + 1) * 1024]
                ld8 = dma("sync", stg8, spj3[:, fc, :],
                          deps=last(wpj_cst[wb], q_pj["done"], hl),
                          ctr_name="wpl")
                cst = vop(lambda e, stg8=stg8, wb=wb: e.tensor_copy(
                    out=WSLB[:, wb * 1024:(wb + 1) * 1024], in_=stg8),
                    deps=last(ld8, wpj_prev[wb]))
                wpj_cst[wb] = cst
                wdeps = last(cst, h_done, t2_done) + \
                    (hd if fc == 0 else [])
                for k in range(4):
                    ts_i = qp * 4 + k
                    for nh in range(2):
                        tok = mm(H[:, fc * 1024 + ts_i * 128:
                                   fc * 1024 + (ts_i + 1) * 128],
                                 WSLB[:, wb * 1024 + nh * 512:
                                      wb * 1024 + (nh + 1) * 512],
                                 f2dst(k, nh),
                                 fc == 0, False, inc=(k == 3 and nh == 1),
                                 deps=wdeps if (k == 0 and nh == 0) else ())
                wpj_prev[wb] = tok
            f2_last = []
            for k in range(4):
                ts_i = qp * 4 + k
                for nh in range(2):
                    mm(T2L[0:16, ts_i * 128:(ts_i + 1) * 128],
                       BPJT[0:16, nh * 512:(nh + 1) * 512],
                       f2dst(k, nh), False, False,
                       deps=last(ld_bpjT, t2_done))
                for nh in range(2):
                    tok = mm(ONES[:, :],
                             BRW[:, 1024 + nh * 512:1024 + (nh + 1) * 512],
                             f2dst(k, nh), False, True,
                             inc=(nh == 1), deps=last(bpr_tok))
                f2_last.append(tok)
            for k in range(4):
                ts_i = qp * 4 + k
                t_abs = tb + ts_i
                par = t_abs % 2
                if k == 3:
                    s1f = vop(lambda e: e.bn_stats(out=ST6[:, 0:6],
                                                   in_=PC[:, :]),
                              deps=[f2_last[k]])
                    s2f = vop(lambda e: e.bn_stats(out=ST6[:, 6:12],
                                                   in_=PD[:, :]),
                              deps=[s1f])
                    ag = vop(lambda e: e.bn_aggr(out=MV[:, :],
                                                 in_=ST6[:, 0:12]),
                             deps=[s2f])
                else:
                    ag = stats1024(regs4f[k], [f2_last[k]])
                vp = vop(lambda e: e.scalar_tensor_tensor(
                    out=SA[:, 2:3], in0=MV[:, 1:2], scalar=QS2[:, 4:5],
                    in1=EPSC[:, :], op0=OP.mult, op1=OP.add),
                    deps=last(ag, ms_eps, q_pj["s2"]))
                w2r = vop(lambda e: e.reciprocal(out=SA[:, 3:4],
                                                 in_=SA[:, 2:3]), deps=[vp])
                aa0 = act(SA[:, 4:5], SA[:, 3:4], AF.Sqrt, deps=[w2r])
                aa = vop(lambda e: e.tensor_tensor(
                    out=SA[:, 0:1], in0=SA[:, 4:5], in1=QS2[:, 3:4],
                    op=OP.mult), deps=[aa0])
                bb = vop(lambda e: e.scalar_tensor_tensor(
                    out=SA[:, 1:2], in0=MV[:, 0:1], scalar=-1.0,
                    in1=SA[:, 0:1], op0=OP.mult, op1=OP.mult), deps=[aa])
                if k == 3:
                    ol_a = act(OLN[:, 0:512], PC[:, :], AF.Identity,
                               bias=SA[:, 1:2], scale=SA[:, 0:1],
                               deps=last(bb, oln_prev))
                    olnt = act(OLN[:, 512:1024], PD[:, :], AF.Identity,
                               bias=SA[:, 1:2], scale=SA[:, 0:1],
                               deps=[ol_a])
                else:
                    olnt = act(OLN, regs4f[k], AF.Identity, bias=SA[:, 1:2],
                               scale=SA[:, 0:1], deps=last(bb, oln_prev))
                x2b = LKT[:, (2 + par) * 1024:(3 + par) * 1024]
                x2l = dma("sync", x2b,
                          x2_scr[t_abs * 128:(t_abs + 1) * 128, :],
                          deps=last(x2_sts[t_abs], x2ld2_prev[par]),
                          ctr_name="x2l")
                ob2 = FSCR[:, :] if par == 0 else FSC2[:, :]
                fin = vop(lambda e, x2b=x2b, ob2=ob2: e.tensor_add(
                    out=ob2, in0=x2b, in1=OLN),
                    deps=last(olnt, x2l, outt_prev[par]))
                oln_prev = fin
                x2ld2_prev[par] = fin
                stok = dma("sync", out_h[t_abs * 128:(t_abs + 1) * 128, :],
                           ob2, deps=[fin], ctr_name="st")
                outt_prev[par] = stok
                out_sts.append(stok)
                if k < 3:
                    reg_free[k] = [fin]
                else:
                    pcq_free = fin
        azq_rd = oln_prev

    waits("sync", last(*out_sts))

    with nc.Block() as block:
        @block.sync
        def _(eng):
            for nm, fn in ops:
                if nm == "sync":
                    fn(eng)

        @block.gpsimd
        def _(eng):
            for nm, fn in ops:
                if nm == "gpsimd":
                    fn(eng)

        @block.tensor
        def _(eng):
            for nm, fn in ops:
                if nm == "tensor":
                    fn(eng)

        @block.vector
        def _(eng):
            for nm, fn in ops:
                if nm == "vector":
                    fn(eng)

        @block.scalar
        def _(eng):
            for nm, fn in ops:
                if nm == "scalar":
                    fn(eng)

    stack.close()
    return nc, dbg_t


def _prep_inputs(inputs):
    import ml_dtypes
    bf = ml_dtypes.bfloat16
    f32 = np.float32
    ct = lambda a: np.ascontiguousarray(np.asarray(a), dtype=f32)

    common = {
        "wqkT": ct(np.asarray(inputs["L0_attn_Wqk"]).T),
        "wvT": ct(np.asarray(inputs["L0_attn_Wv"]).T),
        "wgT": ct(np.asarray(inputs["L0_attn_Wg"]).T),
        "woT": ct(np.asarray(inputs["L0_attn_Wo"]).T),
        "wfcT": ct(np.asarray(inputs["L0_ffn_W_fc"]).T),
        "wpjT": ct(np.asarray(inputs["L0_ffn_W_proj"]).T),
        "afcT": ct(np.asarray(inputs["L0_ffn_A_fc"]).T),
        "bfcT": ct(np.asarray(inputs["L0_ffn_B_fc"]).T),
        "apjT": ct(np.asarray(inputs["L0_ffn_A_proj"]).T),
        "bpjT": ct(np.asarray(inputs["L0_ffn_B_proj"]).T),
        "bqk": ct(np.asarray(inputs["L0_attn_bqk"]).reshape(NQ, 128).T),
        "bv": ct(np.asarray(inputs["L0_attn_bv"]).reshape(1, E)),
        "bg": ct(np.asarray(inputs["L0_attn_bg"]).reshape(NE, 128).T),
        "bo": ct(np.asarray(inputs["L0_attn_bo"]).reshape(1, D)),
        "bfc": ct(np.asarray(inputs["L0_ffn_b_fc"]).reshape(NF, 128).T),
        "bpj": ct(np.asarray(inputs["L0_ffn_b_proj"]).reshape(1, D)),
        "ln1w": ct(np.asarray(inputs["L0_attn_ln1_w"]).reshape(ND, 128).T),
        "ln1b": ct(np.asarray(inputs["L0_attn_ln1_b"]).reshape(ND, 128).T),
        "ln2wr": ct(np.broadcast_to(np.asarray(inputs["L0_attn_ln2_w"]),
                                    (128, D))),
        "ln2br": ct(np.broadcast_to(np.asarray(inputs["L0_attn_ln2_b"]),
                                    (128, D))),
        "gam": ct(np.asarray(inputs["L0_attn_gamma"]).reshape(4, 2, 128)
                  .transpose(2, 1, 0).reshape(128, 8)),
        "bet": ct(np.asarray(inputs["L0_attn_beta"]).reshape(4, 2, 128)
                  .transpose(2, 1, 0).reshape(128, 8)),
        "posemb": ct(np.asarray(inputs["L0_attn_pos_emb"]).reshape(65, 1)),
    }
    idx = np.arange(G)
    dist = np.clip(np.abs(idx[:, None] - idx[None, :]), 0, 32)
    onehot = np.zeros((65, G * G), dtype=bf)
    onehot[dist.reshape(-1), np.arange(G * G)] = 1
    common["onehot"] = onehot
    common["identb"] = np.eye(128, dtype=bf)
    common["identf"] = np.eye(128, dtype=f32)
    common["onesrow"] = np.ones((1, 128), dtype=bf)
    common["onesrowf"] = np.ones((1, 128), dtype=f32)
    x = np.ascontiguousarray(np.asarray(inputs["L0_attn_x"]), dtype=f32)
    return [dict(common, x=x[i]) for i in range(NCORES)]


def kernel(**inputs):
    in_maps = _prep_inputs(inputs)
    nc, _ = build_nc(dbg=False)
    res = run_bass_kernel_spmd(nc, in_maps, core_ids=list(range(NCORES)))
    out = np.stack([np.asarray(r["out"]) for r in res.results], axis=0)
    return out.astype(np.float32)

# version 17
# speedup vs baseline: 1.7132x; 1.0839x over previous
import math
import numpy as np
import concourse.bass as bass
import concourse.mybir as mybir
from concourse.bass_utils import run_bass_kernel_spmd

F32 = mybir.dt.float32
BF16 = mybir.dt.bfloat16
I32 = mybir.dt.int32
I8 = mybir.dt.int8
AX = mybir.AxisListType.X
AF = mybir.ActivationFunctionType
OP = mybir.AluOpType

NCORES = 8
T = 2048
D = 1024
DQK = 256
E = 2048
FF = 4096
G = 64
NT = T // 128
ND = D // 128
NE = E // 128
NQ = DQK // 128
NF = FF // 128
EPS = 1e-5
PI = math.pi
INV2PI = 1.0 / (2.0 * PI)
PI_LO = float(np.nextafter(np.float32(np.pi), np.float32(0.0)))
LORA_SCALING = 2.0


def build_nc(dbg=False):
    nc = bass.Bass("TRN2", target_bir_lowering=False)

    dr = lambda name, shape, dt, kind: nc.dram_tensor(name, shape, dt, kind=kind).ap()
    x_h = dr("x", [T, D], F32, "ExternalInput")
    wqkT = dr("wqkT", [D, DQK], F32, "ExternalInput")
    wvT = dr("wvT", [D, E], F32, "ExternalInput")
    wgT = dr("wgT", [D, E], F32, "ExternalInput")
    woT = dr("woT", [E, D], F32, "ExternalInput")
    wfcT = dr("wfcT", [D, FF], F32, "ExternalInput")
    wpjT = dr("wpjT", [FF, D], F32, "ExternalInput")
    afcT_h = dr("afcT", [D, 16], F32, "ExternalInput")
    bfcT_h = dr("bfcT", [16, FF], F32, "ExternalInput")
    apjT_h = dr("apjT", [FF, 16], F32, "ExternalInput")
    bpjT_h = dr("bpjT", [16, D], F32, "ExternalInput")
    bqk_h = dr("bqk", [128, NQ], F32, "ExternalInput")
    bv_h = dr("bv", [1, E], F32, "ExternalInput")
    bg_h = dr("bg", [128, NE], F32, "ExternalInput")
    bo_h = dr("bo", [1, D], F32, "ExternalInput")
    bfc_h = dr("bfc", [128, NF], F32, "ExternalInput")
    bpj_h = dr("bpj", [1, D], F32, "ExternalInput")
    ln1w_h = dr("ln1w", [128, ND], F32, "ExternalInput")
    ln1b_h = dr("ln1b", [128, ND], F32, "ExternalInput")
    ln2w_h = dr("ln2wr", [128, D], F32, "ExternalInput")
    ln2b_h = dr("ln2br", [128, D], F32, "ExternalInput")
    gam_h = dr("gam", [128, 8], F32, "ExternalInput")
    bet_h = dr("bet", [128, 8], F32, "ExternalInput")
    pos_h = dr("posemb", [65, 1], F32, "ExternalInput")
    oneh_h = dr("onehot", [65, G * G], BF16, "ExternalInput")
    ident_h = dr("identb", [128, 128], BF16, "ExternalInput")
    identf_h = dr("identf", [128, 128], F32, "ExternalInput")
    ones_h = dr("onesrow", [1, 128], BF16, "ExternalInput")
    onesf_h = dr("onesrowf", [1, 128], F32, "ExternalInput")
    out_h = dr("out", [T, D], F32, "ExternalOutput")
    sc_fc = nc.dram_tensor("scr_fc", [FF, D], I8, kind="Internal").ap()
    scr1 = nc.dram_tensor("scr_1", [1, 1], F32, kind="Internal").ap()
    sc_pj = nc.dram_tensor("scr_pj", [FF, D], I8, kind="Internal").ap()
    x2_scr = nc.dram_tensor("scr_x2", [T, D], BF16, kind="Internal").ap()

    dbg_t = {}
    if dbg:
        for nm, shape in (("d_qs", [128, 8]), ("d_qs2", [128, 8]),
                          ("d_mxp", [128, 32]),
                          ("d_y", [128, ND * T]), ("d_qk", [128, NQ * T]),
                          ("d_v", [128, NT * E]), ("d_lkv", [128, 2 * E]),
                          ("d_x2", [T, D]), ("d_att", [128, 512]),
                          ("d_cos", [128, T]), ("d_sin", [128, T]),
                          ("d_wfc", [FF, D])):
            dbg_t[nm] = dr(nm, shape, F32, "ExternalOutput")

    al = nc.alloc_sbuf_tensor
    GAMT = al("GAMT", [128, 8], F32)
    BETT = al("BETT", [128, 8], F32)
    BQK = al("BQK", [128, NQ], F32)
    BGC = al("BGC", [128, NE], F32)
    BFCC = al("BFCC", [128, NF], F32)
    LN1W = al("LN1W", [128, ND], F32)
    LN1B = al("LN1B", [128, ND], F32)
    LN2W = al("LN2W", [128, D], BF16)
    LN2B = al("LN2B", [128, D], BF16)
    BRW = al("BRW", [1, E], BF16)
    POSE = al("POSE", [65, 1], BF16)
    IDENT = al("IDENT", [128, 128], BF16)
    IDENTF = al("IDENTF", [128, 128], F32)
    ONES = al("ONES", [1, 128], BF16)
    ZROW = al("ZROW", [1, 128], BF16)
    ONESF = al("ONESF", [1, 128], F32)
    BIAS64 = al("BIAS64", [64, 64], F32)
    EPSC = al("EPSC", [128, 1], F32)
    ST6 = al("ST6", [128, 12], F32)
    MV = al("MV", [128, 2], F32)
    SA = al("SA", [128, 8], F32)
    MXP = al("MXP", [128, 32], F32)
    QS = al("QS", [128, 8], F32)
    QS2 = al("QS2", [128, 8], F32)
    COS = al("COS", [128, T], BF16)
    SIN = al("SIN", [128, T], BF16)
    XT = al("XT", [128, 2 * D], F32)
    FSCR = al("FSCR", [128, 1024], F32)
    FSC2 = al("FSC2", [128, 1024], F32)
    ISCR = al("ISCR", [128, 1024], I32)
    AV = al("AV", [128, NT * E], BF16)
    AY = al("AY", [128, ND * T], BF16)
    AGT = al("AGT", [128, NE * 512], BF16)
    AZQ = al("AZQ", [128, 4096], BF16)
    AMSC = al("AMSC", [128, 2048], BF16)
    QK = al("QK", [128, NQ * T], BF16)
    LINKV = al("LINKV", [128, 2 * E], BF16)
    AW = al("AW", [128, 4096], BF16)
    WSLB = al("WSLB", [128, 2048], BF16)
    LKT = al("LKT", [128, NT * DQK], BF16)
    QQ = al("QQ", [128, 1024], BF16)
    QKK = al("QKK", [128, 1024], BF16)
    LQ = al("LQ", [128, 1024], BF16)
    GATEE = al("GATEE", [128, 1024], BF16)
    ATT = al("ATT", [128, 512], BF16)

    ONEH = AV[0:65, 0:G * G]
    SQT = FSCR[:, 0:256]
    M1 = AMSC[:, 0:512]
    M2 = AMSC[:, 512:1024]
    UB0 = AMSC[:, 1024:1536]
    UB1 = AMSC[:, 1536:2048]
    Z2 = AZQ[:, 0:1024]
    T2E = AZQ[:, 1024:2048]
    OLN = AZQ[:, 2048:3072]
    BPJT = QQ[0:16, 0:1024]
    APJT = QKK[:, 0:512]
    AFCT = QKK[:, 512:640]
    T1 = LQ[0:16, 0:1024]
    T2L = GATEE[0:16, 0:1024]
    BFCTA = COS[0:16, :]
    BFCTB = SIN[0:16, :]
    AMSCI8 = AMSC.bitcast(I8)

    PBIG = nc.alloc_psum_tensor("PBIG", [128, 2048], F32)
    PA = nc.alloc_psum_tensor("PA", [128, 1024], F32)
    PC = nc.alloc_psum_tensor("PC", [128, 512], F32)
    PD = nc.alloc_psum_tensor("PD", [128, 512], F32)
    PCD = [PC, PD]

    ops = []
    EMIT = {k: (lambda fn, k=k: ops.append((k, fn)))
            for k in ("tensor", "vector", "scalar", "sync", "gpsimd")}

    class Ctr:
        def __init__(self, h):
            self.h = h
            self.n = 0

    from contextlib import ExitStack
    stack = ExitStack()
    sem = {k: Ctr(stack.enter_context(nc.semaphore(k)))
           for k in ("ld", "gp", "gpc", "te", "ve", "se", "st",
                     "lps", "b64", "qsc",
                     "qld0", "qld1", "xs0", "xs1", "wgl0", "wgl1",
                     "wol0", "wol1", "xrl0", "xrl1", "wvl0", "wvl1",
                     "wfl0", "wfl1", "wpl0", "wpl1", "x2l0", "x2l1",
                     "qst0", "qst1")}
    pipetog = {}
    hw = {e: {} for e in EMIT}

    last_dma = {e: None for e in EMIT}

    def wait(engine, ctr, val):
        if val is None or val <= 0:
            return
        if val > hw[engine].get(id(ctr), 0):
            hw[engine][id(ctr)] = val
            last_dma[engine] = None
            EMIT[engine](lambda e, h=ctr.h, v=val: e.wait_ge(h, v))

    def waits(engine, deps):
        for d in deps:
            if d is None:
                continue
            if isinstance(d, list):
                waits(engine, d)
            else:
                wait(engine, d[0], d[1])

    def last(*tokens):
        out = {}
        for t in tokens:
            if t is None:
                continue
            if isinstance(t, list):
                for u in t:
                    if u is not None and out.get(id(u[0]), (None, 0))[1] < u[1]:
                        out[id(u[0])] = u
                continue
            if out.get(id(t[0]), (None, 0))[1] < t[1]:
                out[id(t[0])] = t
        return list(out.values())

    def dma(engine, out_ap, in_ap, deps=(), ctr_name=None):
        waits(engine, deps)
        if ctr_name is not None and (ctr_name + "0") in sem:
            t = pipetog.get(ctr_name, 0)
            pipetog[ctr_name] = 1 - t
            c = sem[ctr_name + str(t)]
        else:
            c = sem[ctr_name] if ctr_name else (
                sem["ld"] if engine == "sync" else sem["gp"])
        wait(engine, c, c.n)
        c.n += 16
        n = c.n
        EMIT[engine](lambda e, o=out_ap, i=in_ap, h=c.h:
                     e.dma_start(out=o, in_=i).then_inc(h, 16))
        return (c, n)

    def mm(lhsT, rhs, out, start, stop, inc=False, deps=()):
        waits("tensor", deps)
        if inc:
            sem["te"].n += 1
            n = sem["te"].n
            EMIT["tensor"](lambda e, a=lhsT, b=rhs, o=out, s1=start, s2=stop,
                           h=sem["te"].h:
                           e.matmul(o, a, b, start=s1, stop=s2).then_inc(h, 1))
            return (sem["te"], n)
        EMIT["tensor"](lambda e, a=lhsT, b=rhs, o=out, s1=start, s2=stop:
                       e.matmul(o, a, b, start=s1, stop=s2))
        return None

    def act(out_ap, in_ap, func, bias=0.0, scale=1.0, deps=()):
        waits("scalar", deps)
        sem["se"].n += 1
        n = sem["se"].n
        EMIT["scalar"](lambda e, o=out_ap, i=in_ap, f=func, b=bias, s=scale,
                       h=sem["se"].h:
                       e.activation(out=o, in_=i, func=f, bias=b, scale=s)
                       .then_inc(h, 1))
        return (sem["se"], n)

    def vop(emitfn, deps=()):
        waits("vector", deps)
        sem["ve"].n += 1
        n = sem["ve"].n
        EMIT["vector"](lambda e, f=emitfn, h=sem["ve"].h: f(e).then_inc(h, 1))
        return (sem["ve"], n)

    def gop(emitfn, deps=()):
        waits("gpsimd", deps)
        sem["gpc"].n += 1
        n = sem["gpc"].n
        EMIT["gpsimd"](lambda e, f=emitfn, h=sem["gpc"].h: f(e).then_inc(h, 1))
        return (sem["gpc"], n)

    def flush_mm(region, deps=(), m=128):
        return mm(ZROW[:, 0:m], ONES[:, :], region, False, True, inc=True,
                  deps=deps)

    ld_gam = dma("sync", GAMT[:, :], gam_h)
    ld_bet = dma("sync", BETT[:, :], bet_h)
    ld_bqk = dma("sync", BQK[:, :], bqk_h)
    ld_bg = dma("sync", BGC[:, :], bg_h)
    ld_bfc = dma("sync", BFCC[:, :], bfc_h)
    ld_l1w = dma("sync", LN1W[:, :], ln1w_h)
    ld_l1b = dma("sync", LN1B[:, :], ln1b_h)
    ld_onh = dma("sync", ONEH, oneh_h)
    ld_id = dma("sync", IDENT[:, :], ident_h)
    ld_on = dma("sync", ONES[:, :], ones_h)
    ld_onf = dma("sync", ONESF[:, :], onesf_h)
    ld_l2w = dma("gpsimd", LN2W[:, :], ln2w_h)
    ld_l2b = dma("gpsimd", LN2B[:, :], ln2b_h)
    ld_bvr = dma("gpsimd", BRW[:, :], bv_h)
    ld_pos = dma("gpsimd", POSE[:, :], pos_h)
    pre_ld = (sem["ld"], sem["ld"].n)
    ld_gam = ld_bet = ld_bqk = ld_bg = ld_bfc = pre_ld
    ld_l1w = ld_l1b = ld_onh = ld_id = ld_on = ld_onf = pre_ld
    pre_gp = (sem["gp"], sem["gp"].n)
    ld_l2w = ld_l2b = ld_bvr = ld_pos = pre_gp
    iot1 = gop(lambda e: e.iota(FSCR[0:1, 0:128], [[1, 128]],
                                channel_multiplier=0,
                                allow_small_or_imprecise_dtypes=True))
    ms_att = gop(lambda e: e.memset(ATT[:, :], 0.0))
    ms_zr = gop(lambda e: e.memset(ZROW[:, :], 0.0))
    ms_eps = gop(lambda e: e.memset(EPSC[:, :], EPS))
    a_invf = act(IDENTF[0:1, 0:128], FSCR[0:1, 0:128], AF.Exp,
                 scale=-math.log(10000.0) / 128.0, deps=[iot1])

    def sincos_half2(hh, shift, dst_half, pdeps):
        posr = FSC2[0:1, 0:1024]
        pio = gop(lambda e, hh=hh: e.iota(posr, [[1, 1024]], base=hh * 1024,
                                          channel_multiplier=0,
                                          allow_small_or_imprecise_dtypes=True),
                  deps=list(pdeps))
        mmdeps = last(a_invf, pio) + list(pdeps)
        for j in range(2):
            ttok = mm(IDENTF[0:1, 0:128], posr[:, j * 512:(j + 1) * 512],
                      PA[:, j * 512:(j + 1) * 512], True, True, inc=(j == 1),
                      deps=mmdeps if j == 0 else ())
        ts = vop(lambda e: e.tensor_scalar(
            out=FSCR[:, 0:1024], in0=PA[:, :], scalar1=INV2PI, scalar2=shift,
            op0=OP.mult, op1=OP.add), deps=last(ttok))
        k1 = vop(lambda e: e.tensor_single_scalar(
            out=ISCR[:, 0:1024], in_=FSCR[:, 0:1024], scalar=1.0,
            op=OP.mult), deps=[ts])
        c2 = vop(lambda e: e.scalar_tensor_tensor(
            out=XT[:, 0:1024], in0=FSCR[:, 0:1024], scalar=1.0,
            in1=ISCR[:, 0:1024], op0=OP.mult, op1=OP.subtract), deps=[k1])
        km = vop(lambda e: e.scalar_tensor_tensor(
            out=XT[:, 1024:2048], in0=XT[:, 0:1024], scalar=-0.5,
            in1=ISCR[:, 0:1024], op0=OP.is_lt, op1=OP.subtract), deps=[c2])
        kf = vop(lambda e: e.scalar_tensor_tensor(
            out=XT[:, 0:1024], in0=XT[:, 0:1024], scalar=0.5,
            in1=XT[:, 1024:2048], op0=OP.is_gt, op1=OP.subtract), deps=[km])
        dd = vop(lambda e: e.scalar_tensor_tensor(
            out=FSCR[:, 0:1024], in0=XT[:, 0:1024], scalar=-1.0,
            in1=FSCR[:, 0:1024], op0=OP.mult, op1=OP.add), deps=[kf])
        r1 = vop(lambda e: e.tensor_scalar(
            out=FSCR[:, 0:1024], in0=FSCR[:, 0:1024], scalar1=2.0 * PI,
            scalar2=PI_LO, op0=OP.mult, op1=OP.min), deps=[dd])
        r2 = vop(lambda e: e.tensor_single_scalar(
            out=FSCR[:, 0:1024], in_=FSCR[:, 0:1024], scalar=-PI_LO,
            op=OP.max), deps=[r1])
        return act(dst_half, FSCR[:, 0:1024], AF.Sin, deps=[r2]), ttok

    tbl_toks = []
    mm_last = None
    for hh in range(2):
        prev = [tbl_toks[-1]] if tbl_toks else []
        tk, mm_last = sincos_half2(hh, 0.0,
                                   SIN[:, hh * 1024:(hh + 1) * 1024], prev)
        tbl_toks.append(tk)
        tk, mm_last = sincos_half2(hh, 0.25,
                                   COS[:, hh * 1024:(hh + 1) * 1024],
                                   [tbl_toks[-1]])
        tbl_toks.append(tk)
    tables_done = tbl_toks[-1]
    s_sin = tables_done
    s_cos = tables_done

    ev_prev = tables_done
    b64_dmas = []
    for j in range(8):
        tok = mm(POSE[:, :], ONEH[:, j * 512:(j + 1) * 512], PC[0:1, :],
                 True, True, inc=True,
                 deps=last(ld_pos, ld_onh, ev_prev,
                           b64_dmas[-1] if b64_dmas else None))
        ev_prev = vop(lambda e: e.tensor_copy(
            out=FSCR[0:1, 0:512], in_=PC[0:1, :]), deps=[tok])
        b64_dmas.append(dma("sync", BIAS64[8 * j:8 * (j + 1), :],
                            FSCR[0:1, 0:512], deps=[ev_prev],
                            ctr_name="b64"))
    ld_b64 = last(*b64_dmas)
    pc_free = ev_prev
    pd_free = None

    def stats1024(src, deps):
        s1 = vop(lambda e: e.bn_stats(out=ST6[:, 0:6], in_=src[:, 0:512]),
                 deps=deps)
        s2 = vop(lambda e: e.bn_stats(out=ST6[:, 6:12], in_=src[:, 512:1024]),
                 deps=[s1])
        return vop(lambda e: e.bn_aggr(out=MV[:, :], in_=ST6[:, 0:12]),
                   deps=[s2])

    def ln_single(vcol, mcol, acol, bcol, deps):
        w2 = vop(lambda e, v=vcol: e.tensor_scalar(
            out=SA[:, 2:3], in0=v, scalar1=1.0, scalar2=EPS,
            op0=OP.mult, op1=OP.add), deps=deps)
        w2r = vop(lambda e: e.reciprocal(out=SA[:, 3:4], in_=SA[:, 2:3]),
                  deps=[w2])
        aa = act(acol, SA[:, 3:4], AF.Sqrt, deps=[w2r])
        return vop(lambda e, m=mcol, a=acol, b=bcol: e.scalar_tensor_tensor(
            out=b, in0=m, scalar=-1.0, in1=a, op0=OP.mult, op1=OP.mult),
            deps=[aa])

    def ln_double(vcol, mcol, acol, bcol, deps):
        w2 = vop(lambda e, v=vcol: e.tensor_scalar(
            out=SA[:, 2:3], in0=v, scalar1=1.0, scalar2=EPS,
            op0=OP.mult, op1=OP.add), deps=deps)
        w2r = vop(lambda e: e.reciprocal(out=SA[:, 3:4], in_=SA[:, 2:3]),
                  deps=[w2])
        v2 = vop(lambda e, v=vcol: e.tensor_tensor(
            out=SA[:, 4:5], in0=v, in1=SA[:, 3:4], op=OP.mult), deps=[w2r])
        w4 = vop(lambda e: e.tensor_scalar(
            out=SA[:, 5:6], in0=SA[:, 4:5], scalar1=1.0, scalar2=EPS,
            op0=OP.mult, op1=OP.add), deps=[v2])
        w4r = vop(lambda e: e.reciprocal(out=SA[:, 6:7], in_=SA[:, 5:6]),
                  deps=[w4])
        pr = vop(lambda e: e.tensor_tensor(
            out=SA[:, 7:8], in0=SA[:, 3:4], in1=SA[:, 6:7], op=OP.mult),
            deps=[w4r])
        aa = act(acol, SA[:, 7:8], AF.Sqrt, deps=[pr])
        return vop(lambda e, m=mcol, a=acol, b=bcol: e.scalar_tensor_tensor(
            out=b, in0=m, scalar=-1.0, in1=a, op0=OP.mult, op1=OP.mult),
            deps=[aa])

    ev_tok = None
    zg_rd = [None] * 4
    prev_z = None
    for g in range(4):
        z_toks = []
        for tt in range(4):
            t = g * 4 + tt
            buf = tt % 2
            xs = XT[:, buf * D:(buf + 1) * D]
            xdep = [prev_z] if t >= 2 else [tables_done, ld_b64]
            xld = dma("sync", xs, x_h[t * 128:(t + 1) * 128, :], deps=xdep,
                      ctr_name="xs")
            ag = stats1024(xs, [xld])
            bb = ln_double(MV[:, 1:2], MV[:, 0:1], SA[:, 0:1], SA[:, 1:2],
                           [ag])
            ztok = vop(lambda e, xs=xs, tt=tt: e.tensor_scalar(
                out=AZQ[:, tt * D:(tt + 1) * D], in0=xs,
                scalar1=SA[:, 0:1], scalar2=SA[:, 1:2],
                op0=OP.mult, op1=OP.add),
                deps=last(bb, zg_rd[g - 1] if g >= 1 else None))
            z_toks.append(ztok)
            prev_z = ztok
        for c in range(ND):
            q = c % 4
            reg = PBIG[:, q * 512:(q + 1) * 512]
            deps = last(*(z_toks + [ev_tok, ld_id]))
            for tt in range(4):
                tok = mm(AZQ[:, tt * D + c * 128: tt * D + (c + 1) * 128],
                         IDENT[:, :], reg[:, tt * 128:(tt + 1) * 128],
                         True, True, inc=(tt == 3),
                         deps=deps if tt == 0 else ())
            ev_tok = act(AY[:, c * T + g * 512: c * T + (g + 1) * 512],
                         reg, AF.Identity, bias=LN1B[:, c:c + 1],
                         scale=LN1W[:, c:c + 1],
                         deps=last(tok, ld_l1w, ld_l1b))
            if c == ND - 1:
                zg_rd[g] = tok
    y_done = ev_tok
    Y = AY

    pcd_free = [pc_free, pd_free]
    aw_free = [None, None]
    qk_evs = []
    for qc in range(NQ):
        wb = qc % 2
        wsl = AW[:, wb * 1024:(wb + 1) * 1024]
        wqk_ld = dma("gpsimd", wsl.rearrange("p (c k) -> p c k", c=ND),
                     wqkT[:, qc * 128:(qc + 1) * 128]
                     .rearrange("(c p) k -> p c k", p=128),
                     deps=last(aw_free[wb]), ctr_name="wvl")
        for tc in range(4):
            pi = (qc * 4 + tc) % 2
            ps = PCD[pi]
            deps = last(wqk_ld, y_done, pcd_free[pi])
            for c in range(ND):
                tok = mm(wsl[:, c * 128:(c + 1) * 128],
                         Y[:, c * T + tc * 512: c * T + (tc + 1) * 512],
                         ps[:, :], c == 0, c == ND - 1, inc=(c == ND - 1),
                         deps=deps if c == 0 else ())
            ev = act(QK[:, qc * T + tc * 512: qc * T + (tc + 1) * 512],
                     ps[:, :], AF.Silu, bias=BQK[:, qc:qc + 1],
                     deps=last(tok, ld_bqk))
            pcd_free[pi] = ev
            qk_evs.append(ev)
        aw_free[wb] = tok
    qk_done = last(*qk_evs)
    V = AV

    xtp = [last(prev_z, tables_done), last(prev_z, tables_done)]

    def quant_max(w_hbm, cols, qsl, start_deps):
        npc = (D * FF) // (128 * 1024)
        w2 = w_hbm.rearrange("(c p) (j k) -> p c j k", p=128, k=1024)
        ncw2 = cols // 1024
        mx_toks = []
        for i in range(npc):
            b = i % 2
            stg = XT[:, b * 1024:(b + 1) * 1024]
            c, j = divmod(i, ncw2)
            ldt = dma("sync", stg, w2[:, c, j, :],
                      deps=last(xtp[b]) + (list(start_deps) if i == 0 else []),
                      ctr_name="qld")
            mx = vop(lambda e, stg=stg, i=i: e.tensor_reduce(
                out=MXP[:, i:i + 1], in_=stg, axis=AX, op=OP.max,
                apply_absolute_value=True), deps=[ldt])
            xtp[b] = mx
            mx_toks.append(mx)
        m1 = vop(lambda e, q=qsl, n=npc: e.tensor_reduce(
            out=q[:, 0:1], in_=MXP[:, 0:n], axis=AX, op=OP.max),
            deps=last(*mx_toks))
        d1 = dma("sync", FSCR[0:1, 0:128], qsl[:, 0:1], deps=[m1],
                 ctr_name="qsc")
        m2 = vop(lambda e, q=qsl: e.tensor_reduce(
            out=q[0:1, 6:7], in_=FSCR[0:1, 0:128], axis=AX, op=OP.max),
            deps=[d1])
        d2a = dma("sync", scr1, qsl[0:1, 6:7], deps=[m2], ctr_name="qsc")
        d2 = dma("sync", qsl[:, 0:1], scr1.broadcast_to([128, 1]),
                 deps=[d2a], ctr_name="qsc")
        rr = vop(lambda e, q=qsl: e.reciprocal(out=q[:, 1:2], in_=q[:, 0:1]),
                 deps=[d2])
        sinv = vop(lambda e, q=qsl: e.tensor_single_scalar(
            out=q[:, 2:3], in_=q[:, 1:2], scalar=127.0, op=OP.mult),
            deps=[rr])
        s_ = vop(lambda e, q=qsl: e.tensor_single_scalar(
            out=q[:, 3:4], in_=q[:, 0:1], scalar=1.0 / 127.0, op=OP.mult),
            deps=[d2])
        s2 = vop(lambda e, q=qsl: e.tensor_tensor(
            out=q[:, 4:5], in0=q[:, 3:4], in1=q[:, 3:4], op=OP.mult),
            deps=[s_])
        sinv2 = vop(lambda e, q=qsl: e.tensor_single_scalar(
            out=q[:, 5:6], in_=q[:, 2:3], scalar=LORA_SCALING, op=OP.mult),
            deps=[sinv])
        return dict(sinv=sinv, s=s_, s2=s2, sinv2=sinv2, qsl=qsl, stores=[])

    def jit_quant(st, b, w_src, wf, wf_deps, stt_dst):
        stg = XT[:, b * 1024:(b + 1) * 1024]
        ldt = dma("sync", stg, w_src, deps=last(xtp[b]), ctr_name="qld")
        q1 = vop(lambda e, stg=stg, q=st["qsl"]: e.tensor_single_scalar(
            out=ISCR[:, :], in_=stg, scalar=q[:, 2:3], op=OP.mult),
            deps=last(ldt, st["sinv"]))
        cc = vop(lambda e, stg=stg, q=st["qsl"]: e.scalar_tensor_tensor(
            out=FSCR[:, :], in0=stg, scalar=q[:, 2:3], in1=ISCR[:, :],
            op0=OP.mult, op1=OP.subtract), deps=[q1])
        xtp[b] = cc
        qm = vop(lambda e: e.scalar_tensor_tensor(
            out=FSC2[:, :], in0=FSCR[:, :], scalar=-0.5, in1=ISCR[:, :],
            op0=OP.is_lt, op1=OP.subtract), deps=[cc])
        qf = vop(lambda e, wf=wf: e.scalar_tensor_tensor(
            out=wf, in0=FSCR[:, :], scalar=0.5, in1=FSC2[:, :],
            op0=OP.is_gt, op1=OP.subtract),
            deps=last(qm) + list(wf_deps))
        stt = dma("gpsimd", stt_dst, wf, deps=[qf], ctr_name="qst")
        st["stores"].append(stt)
        return qf, stt

    q_fc = quant_max(wfcT, FF, QS, last(y_done, tables_done, ld_b64))
    q_fc["oj"] = wfcT.rearrange("(c p) (o k) -> p c o k", p=128, k=128)

    v_evs = []
    V3 = V[:, :].rearrange("p (t k) -> p t k", t=NT)
    for oc in range(8):
        wb = oc % 2
        wsl = AW[:, wb * 2048:(wb + 1) * 2048]
        wdep = (last(aw_free[0], aw_free[1]) if oc < 2
                else [v_evs[8 * (oc - 2) + 7]])
        wv_ld = dma("gpsimd", wsl.rearrange("p (c k) -> p c k", c=ND),
                    wvT[:, oc * 256:(oc + 1) * 256]
                    .rearrange("(c p) k -> p c k", p=128),
                    deps=wdep, ctr_name="wvl")
        for tg in range(8):
            pi = tg % 2
            ps = PCD[pi]
            deps = last(wv_ld, y_done, pcd_free[pi], ld_on, ld_bvr)
            for tt in range(2):
                t = 2 * tg + tt
                for c in range(ND):
                    mm(Y[:, c * T + t * 128: c * T + (t + 1) * 128],
                       wsl[:, c * 256:(c + 1) * 256],
                       ps[:, tt * 256:(tt + 1) * 256], c == 0, False,
                       deps=deps if (c == 0 and tt == 0) else ())
                tok = mm(ONES[:, :], BRW[:, oc * 256:(oc + 1) * 256],
                         ps[:, tt * 256:(tt + 1) * 256], False, True,
                         inc=(tt == 1))
            ev = act(V3[:, 2 * tg:2 * tg + 2, oc * 256:(oc + 1) * 256],
                     ps[:, :].rearrange("p (t k) -> p t k", t=2),
                     AF.Silu, deps=[tok])
            pcd_free[pi] = ev
            v_evs.append(ev)
    aw_free = [v_evs[-1], v_evs[-2]]
    v_done = last(*v_evs)

    def head_rope(hidx, tok_off, dst, d0_off, d1_off, deps):
        n = 512
        a0 = act(UB0, QK[:, tok_off:tok_off + n], AF.Identity,
                 bias=BETT[:, hidx:hidx + 1], scale=GAMT[:, hidx:hidx + 1],
                 deps=deps)
        a1 = act(UB1, QK[:, T + tok_off:T + tok_off + n], AF.Identity,
                 bias=BETT[:, 4 + hidx:4 + hidx + 1],
                 scale=GAMT[:, 4 + hidx:4 + hidx + 1], deps=deps)
        cs = COS[:, tok_off:tok_off + n]
        sn = SIN[:, tok_off:tok_off + n]
        m1 = vop(lambda e: e.tensor_mul(out=M1, in0=UB0, in1=cs),
                 deps=last(a0, s_cos))
        m2 = vop(lambda e: e.tensor_mul(out=M2, in0=UB1, in1=sn),
                 deps=last(a1, s_sin))
        o1 = vop(lambda e: e.tensor_sub(
            out=dst[:, d0_off:d0_off + n], in0=M1, in1=M2), deps=[m1, m2])
        m3 = vop(lambda e: e.tensor_mul(out=M1, in0=UB1, in1=cs), deps=[o1])
        m4 = vop(lambda e: e.tensor_mul(out=M2, in0=UB0, in1=sn), deps=[o1])
        o2 = vop(lambda e: e.tensor_add(
            out=dst[:, d1_off:d1_off + n], in0=M1, in1=M2), deps=[m3, m4])
        return (last(a0, a1), last(o1, o2))

    LKF = AZQ
    lk_toks = []
    for cc in range(4):
        dep = last(qk_done, ld_gam, ld_bet,
                   zg_rd[3] if cc == 0 else None,
                   lk_toks[-1] if lk_toks else None)
        _, o = head_rope(3, cc * 512, LKF, cc * 512, 2048 + cc * 512, dep)
        lk_toks.append(o)
    lk_done = last(*lk_toks)
    lkt_evs = []
    ev = None
    for c2 in range(2):
        for tg in range(4):
            deps = last(lk_done, ev, pcd_free[0])
            for tt in range(4):
                t = tg * 4 + tt
                tok = mm(LKF[:, c2 * 2048 + t * 128: c2 * 2048 + (t + 1) * 128],
                         IDENT[:, :], PC[:, tt * 128:(tt + 1) * 128],
                         True, True, inc=(tt == 3),
                         deps=deps if tt == 0 else ())
            ev = vop(lambda e, c2=c2, tg=tg: e.tensor_copy(
                out=LKT[:, :].rearrange("p (t k) -> p t k", t=NT)
                [:, tg * 4:(tg + 1) * 4, c2 * 128:(c2 + 1) * 128],
                in_=PC[:, :].rearrange("p (t k) -> p t k", t=4)),
                deps=[tok])
            lkt_evs.append(ev)
    lkt_done = last(*lkt_evs)
    pcd_free[0] = lkt_evs[-1]

    q_pj = quant_max(wpjT, D, QS2, [])
    q_pj["w4"] = wpjT.rearrange("(c p) k -> p c k", p=128)

    regions8 = [PBIG[:, 0:512], PBIG[:, 512:1024], PBIG[:, 1024:1536],
                PBIG[:, 1536:2048], PA[:, 0:512], PA[:, 512:1024],
                PC[:, :], PD[:, :]]
    lkv_evs = []
    for m in range(2):
        for n4 in range(4):
            r = regions8[m * 4 + n4]
            deps = last(lkt_done, v_done, pcd_free[0], pcd_free[1], ev_tok)
            for t in range(NT):
                tok = mm(LKT[:, t * DQK + m * 128: t * DQK + (m + 1) * 128],
                         V[:, t * E + n4 * 512: t * E + (n4 + 1) * 512],
                         r, t == 0, t == NT - 1, inc=(t == NT - 1),
                         deps=deps if t == 0 else ())
            lkv_evs.append(act(
                LINKV[:, m * E + n4 * 512: m * E + (n4 + 1) * 512], r,
                AF.Identity, scale=1.0 / T, deps=[tok]))
    lkv_done = last(*lkv_evs)

    ld_bor = dma("gpsimd", BRW[:, 0:1024], bo_h, deps=last(v_done),
                 ctr_name="lps")
    XB0 = LKT[:, 0:1024]
    XB1 = LKT[:, 1024:2048]
    X2S0 = LKT[:, 2048:3072]
    X2S1 = LKT[:, 3072:4096]

    x2_sts = [None] * NT
    pa_free = [lkv_evs[4], lkv_evs[5]]
    pcd_free = [lkv_evs[6], lkv_evs[7]]
    pbig_free = [last(lkv_evs[0], lkv_evs[1]), last(lkv_evs[2], lkv_evs[3])]
    gt_free = v_evs[-1]
    gatee_prev = [None, None]
    wo_prev = [None, None]
    z2_rd = lkt_done
    epi_z2 = lkt_done
    epi_ol = lkt_done
    xb_prev = [lkv_done, lkv_done]
    x2s_prev = [None, None]

    d0 = last(qk_done, ms_att, lkv_done, z2_rd)
    aq, hq = head_rope(0, 0, QQ, 0, 512, d0)
    ak, hk = head_rope(2, 0, QKK, 0, 512, [hq])
    al_, hl = head_rope(1, 0, LQ, 0, 512, [hk])
    heads_nxt = (hq, hk, hl)
    for C in range(4):
        base = C * 512
        hq, hk, hl = heads_nxt
        sdeps = last(hq, hk, pcd_free[0])
        for p in range(4):
            for c2 in range(2):
                mm(QKK[:, c2 * 512 + p * 128: c2 * 512 + (p + 1) * 128],
                   QQ[:, c2 * 512 + p * 128: c2 * 512 + (p + 1) * 128],
                   PC[:, p * 128:(p + 1) * 128], c2 == 0,
                   c2 == 1 and p != 3,
                   deps=sdeps if (p == 0 and c2 == 0) else ())
        tok = flush_mm(PC[:, 384:512], deps=last(ms_zr, ld_on))
        sc_tok = tok
        pc3 = PC[:, :].rearrange("p (c k) -> p c k", c=4)
        sq3 = SQT[:, :].rearrange("p (c k) -> p c k", c=4)
        at3 = ATT[:, :].rearrange("p (c k) -> p c k", c=4)
        b3 = BIAS64[0:64, :].unsqueeze(1).broadcast_to([64, 4, 64])
        sq_t = vop(lambda e: e.scalar_tensor_tensor(
            out=sq3[0:64, :, :], in0=pc3[0:64, :, 0:64], scalar=1.0 / G,
            in1=b3, op0=OP.mult, op1=OP.add), deps=last(tok, ld_b64))
        sq_b = vop(lambda e: e.scalar_tensor_tensor(
            out=sq3[64:128, :, :], in0=pc3[64:128, :, 64:128],
            scalar=1.0 / G, in1=b3, op0=OP.mult, op1=OP.add), deps=[sq_t])
        at_t = vop(lambda e: e.scalar_tensor_tensor(
            out=at3[0:64, :, 0:64], in0=sq3[0:64, :, :], scalar=0.0,
            in1=sq3[0:64, :, :], op0=OP.max, op1=OP.mult), deps=[sq_t])
        at_b = vop(lambda e: e.scalar_tensor_tensor(
            out=at3[64:128, :, 64:128], in0=sq3[64:128, :, :], scalar=0.0,
            in1=sq3[64:128, :, :], op0=OP.max, op1=OP.mult), deps=[sq_b])
        att_done = last(at_t, at_b)
        pcd_free[0] = at_b
        if dbg and C == 0:
            dma("gpsimd", dbg_t["d_att"], ATT[:, :], deps=[att_done],
                ctr_name="lps")

        gt_toks = []
        for Ei in range(NE):
            wb = Ei % 2
            prev_wg = gt_toks[-2] if len(gt_toks) >= 2 else gt_free
            wg_ld = dma("gpsimd", AW[:, wb * 1024:(wb + 1) * 1024]
                        .rearrange("p (c k) -> p c k", c=ND),
                        wgT[:, Ei * 128:(Ei + 1) * 128]
                        .rearrange("(c p) k -> p c k", p=128),
                        deps=last(prev_wg), ctr_name="wgl")
            pi = Ei % 2
            gdeps = last(wg_ld, pcd_free[pi], y_done)
            for c in range(ND):
                tok = mm(AW[:, wb * 1024 + c * 128: wb * 1024 + (c + 1) * 128],
                         Y[:, c * T + base: c * T + base + 512],
                         PCD[pi][:, :], c == 0, c == ND - 1,
                         inc=(c == ND - 1), deps=gdeps if c == 0 else ())
            gprev = gt_toks[-2] if len(gt_toks) >= 2 else gatee_prev[pi]
            gev = act(GATEE[:, 0:512] if pi == 0 else GATEE[:, 512:1024],
                      PCD[pi][:, :], AF.Silu, bias=BGC[:, Ei:Ei + 1],
                      deps=last(tok, ld_bg, gprev))
            pcd_free[pi] = gev
            ph = Ei % 2
            adeps = last(pa_free[ph], lkv_done, hl, att_done, v_done)
            for m in range(2):
                mm(LINKV[:, m * E + Ei * 128: m * E + (Ei + 1) * 128],
                   LQ[:, m * 512:(m + 1) * 512],
                   PA[:, ph * 512:(ph + 1) * 512], m == 0, False,
                   deps=adeps if m == 0 else ())
            for p in range(4):
                mm(V[:, (4 * C + p) * E + Ei * 128:
                     (4 * C + p) * E + (Ei + 1) * 128],
                   ATT[:, p * 128:(p + 1) * 128],
                   PA[:, ph * 512 + p * 128: ph * 512 + (p + 1) * 128],
                   False, False)
            tok2 = flush_mm(PA[:, ph * 512: ph * 512 + 128])
            gt = vop(lambda e, Ei=Ei, ph=ph, pi=pi: e.tensor_tensor(
                out=AGT[:, Ei * 512:(Ei + 1) * 512],
                in0=PA[:, ph * 512:(ph + 1) * 512],
                in1=GATEE[:, pi * 512:(pi + 1) * 512], op=OP.mult),
                deps=last(tok2, gev))
            pa_free[ph] = gt
            gt_toks.append(gt)
        gts = last(*gt_toks)

        if C < 3:
            hdeps = last(sc_tok, gts, hl)
            _, hq2 = head_rope(0, base + 512, QQ, 0, 512, hdeps)
            _, hk2 = head_rope(2, base + 512, QKK, 0, 512, [hq2])
            _, hl2 = head_rope(1, base + 512, LQ, 0, 512, [hk2])
            heads_nxt = (hq2, hk2, hl2)

        regs4 = (PBIG[:, 0:1024], PBIG[:, 1024:2048], PA[:, :], None)
        hd = last(gts, pbig_free[0], pbig_free[1], pa_free[0], pa_free[1],
                  pcd_free[0], pcd_free[1])
        for Ei in range(NE):
            wb = Ei % 2
            wo_ld = dma("gpsimd", WSLB[:, wb * 1024:(wb + 1) * 1024],
                        woT[Ei * 128:(Ei + 1) * 128, :],
                        deps=last(wo_prev[wb]), ctr_name="wol")
            wdeps = last(wo_ld, gts) + (hd if Ei == 0 else [])
            for k in range(4):
                for nh in range(2):
                    dst = (PCD[nh][:, :] if k == 3 else
                           regs4[k][:, nh * 512:(nh + 1) * 512])
                    tok = mm(AGT[:, Ei * 512 + k * 128:
                                 Ei * 512 + (k + 1) * 128],
                             WSLB[:, wb * 1024 + nh * 512:
                                  wb * 1024 + (nh + 1) * 512],
                             dst, Ei == 0, False,
                             inc=(k == 3 and nh == 1),
                             deps=wdeps if (k == 0 and nh == 0) else ())
            wo_prev[wb] = tok
        wo_last = []
        for k in range(4):
            for nh in range(2):
                dst = (PCD[nh][:, :] if k == 3 else
                       regs4[k][:, nh * 512:(nh + 1) * 512])
                tok = mm(ONES[:, :], BRW[:, nh * 512:(nh + 1) * 512],
                         dst, False, True, inc=(nh == 1),
                         deps=last(ld_bor, ld_on))
            wo_last.append(tok)
        for k in range(4):
            t_abs = 4 * C + k
            if k == 3:
                s1 = vop(lambda e: e.bn_stats(out=ST6[:, 0:6], in_=PC[:, :]),
                         deps=[wo_last[k]])
                s2 = vop(lambda e: e.bn_stats(out=ST6[:, 6:12], in_=PD[:, :]),
                         deps=[s1])
                ag = vop(lambda e: e.bn_aggr(out=MV[:, :], in_=ST6[:, 0:12]),
                         deps=[s2])
            else:
                ag = stats1024(regs4[k], [wo_last[k]])
            bb = ln_single(MV[:, 1:2], MV[:, 0:1], SA[:, 0:1],
                           SA[:, 1:2], [ag])
            if k == 3:
                z2t_a = act(Z2[:, 0:512], PC[:, :], AF.Identity,
                            bias=SA[:, 1:2], scale=SA[:, 0:1],
                            deps=last(bb, epi_z2))
                z2t = act(Z2[:, 512:1024], PD[:, :], AF.Identity,
                          bias=SA[:, 1:2], scale=SA[:, 0:1], deps=[z2t_a])
                pcd_free = [z2t, z2t]
            else:
                z2t = act(Z2, regs4[k], AF.Identity, bias=SA[:, 1:2],
                          scale=SA[:, 0:1], deps=last(bb, epi_z2))
            t2a = vop(lambda e: e.tensor_mul(out=T2E, in0=Z2,
                                             in1=LN2W[:, :]),
                      deps=last(z2t, ld_l2w))
            epi_z2 = t2a
            t2b = vop(lambda e: e.tensor_add(out=T2E, in0=T2E,
                                             in1=LN2B[:, :]),
                      deps=last(t2a, ld_l2b))
            ag2 = stats1024(T2E, [t2b])
            bb2 = ln_single(MV[:, 1:2], MV[:, 0:1], SA[:, 0:1],
                            SA[:, 1:2], [ag2])
            olnt = act(OLN, T2E, AF.Identity, bias=SA[:, 1:2],
                       scale=SA[:, 0:1], deps=last(bb2, epi_ol))
            par = t_abs % 2
            xb = XB0 if par == 0 else XB1
            x2s = X2S0 if par == 0 else X2S1
            xre = dma("gpsimd", xb, x_h[t_abs * 128:(t_abs + 1) * 128, :],
                      deps=last(xb_prev[par]), ctr_name="xrl")
            fin = vop(lambda e, xb=xb, x2s=x2s: e.tensor_add(
                out=x2s, in0=xb, in1=OLN),
                deps=last(olnt, xre, x2s_prev[par]))
            epi_ol = fin
            xb_prev[par] = fin
            stok = dma("sync", x2_scr[t_abs * 128:(t_abs + 1) * 128, :],
                       x2s, deps=[fin], ctr_name="st")
            x2s_prev[par] = stok
            x2_sts[t_abs] = stok
            if k < 2:
                pbig_free[k] = fin
            elif k == 2:
                pa_free = [fin, fin]
        gt_free = wo_last[-1]
        z2_rd = gts

    x2_done = last(*[x2_sts[t] for t in range(NT)])
    if dbg:
        for nm, src in (("d_y", Y), ("d_qk", QK), ("d_v", V),
                        ("d_lkv", LINKV), ("d_cos", COS), ("d_sin", SIN)):
            dma("gpsimd", dbg_t[nm], src[:, :],
                deps=last(y_done, qk_done, v_done, lkv_done, x2_done),
                ctr_name="lps")
        dma("gpsimd", dbg_t["d_x2"], x2_scr, deps=last(x2_done),
            ctr_name="lps")

    lkv_rd_done = gts
    late_dep = last(hl, att_done, gts)
    ld_afc = dma("gpsimd", AFCT.rearrange("p (c k) -> p c k", c=ND),
                 afcT_h.rearrange("(c p) k -> p c k", p=128), deps=late_dep,
                 ctr_name="lps")
    ld_apj = dma("gpsimd", APJT.rearrange("p (c k) -> p c k", c=NF),
                 apjT_h.rearrange("(c p) k -> p c k", p=128), deps=late_dep,
                 ctr_name="lps")
    ld_bpjT = dma("gpsimd", BPJT, bpjT_h, deps=late_dep, ctr_name="lps")
    ld_bfcA = dma("gpsimd", BFCTA, bfcT_h[:, 0:2048], deps=late_dep,
                  ctr_name="lps")
    ld_bfcB = dma("gpsimd", BFCTB, bfcT_h[:, 2048:4096], deps=late_dep,
                  ctr_name="lps")

    Y2 = AY
    H = AV
    regs3 = [PBIG[:, 0:1024], PBIG[:, 1024:2048], PA[:, :]]
    reg_free = [last(pbig_free[0]), last(pbig_free[1]),
                last(pa_free[0], pa_free[1])]
    pcq_free = last(pcd_free[0], pcd_free[1])
    wfc_prev = [None, None]
    wpj_prev = [wo_prev[0], wo_prev[1]]
    wfc_st = [None, None]
    wpj_st = [None, None]
    wfc_cst = [None, None]
    wpj_cst = [None, None]
    scjA = sc_fc.rearrange("(o p) k -> o p k", p=128)
    spj3 = sc_pj.rearrange("(c p) k -> p c k", p=128)
    out_sts = []
    azq_rd = epi_ol
    y_rd = gts
    v_rd = gts
    x2ld_prev = [None, None]
    x2ld2_prev = [None, None]
    outt_prev = [None, None]
    oln_prev = epi_ol

    for Cf in range(2):
        tb = Cf * 8
        ztoks_all = []
        ev = None
        for g2 in range(2):
            z_toks = []
            for tt in range(4):
                t_abs = tb + g2 * 4 + tt
                par = tt % 2
                xbuf = LKT[:, par * 1024:(par + 1) * 1024]
                xld = dma("sync", xbuf,
                          x2_scr[t_abs * 128:(t_abs + 1) * 128, :],
                          deps=last(x2_sts[t_abs], x2ld_prev[par]),
                          ctr_name="x2l")
                ag = stats1024(xbuf, [xld])
                bb = ln_single(MV[:, 1:2], MV[:, 0:1], SA[:, 0:1],
                               SA[:, 1:2], [ag])
                ztok = vop(lambda e, xbuf=xbuf, tt=tt: e.tensor_scalar(
                    out=AZQ[:, tt * D:(tt + 1) * D], in0=xbuf,
                    scalar1=SA[:, 0:1], scalar2=SA[:, 1:2],
                    op0=OP.mult, op1=OP.add),
                    deps=last(bb, azq_rd))
                x2ld_prev[par] = ztok
                z_toks.append(ztok)
            for c in range(ND):
                q = c % 4
                reg = PBIG[:, q * 512:(q + 1) * 512]
                deps = last(*(z_toks + [ev, ld_id] + reg_free[q // 2]))
                for tt in range(4):
                    tok = mm(AZQ[:, tt * D + c * 128: tt * D + (c + 1) * 128],
                             IDENT[:, :], reg[:, tt * 128:(tt + 1) * 128],
                             True, True, inc=(tt == 3),
                             deps=deps if tt == 0 else ())
                ev = vop(lambda e, c=c, g2=g2, reg=reg: e.tensor_copy(
                    out=Y2[:, c * 1024 + g2 * 512:
                           c * 1024 + (g2 + 1) * 512],
                    in_=reg), deps=[tok])
                reg_free[q // 2] = [ev]
            azq_rd = tok
            ztoks_all += z_toks
        y2_done = ev

        t1toks = []
        for h2 in range(2):
            deps = last(y2_done, ld_afc, pcq_free, hl)
            for c in range(ND):
                tok = mm(AFCT[:, c * 16:(c + 1) * 16],
                         Y2[:, c * 1024 + h2 * 512: c * 1024 + (h2 + 1) * 512],
                         PC[0:16, :], c == 0, c == ND - 1,
                         inc=(c == ND - 1), deps=deps if c == 0 else ())
            t1toks.append(act(T1[0:16, h2 * 512:(h2 + 1) * 512], PC[0:16, :],
                              AF.Identity, scale=QS[0:16, 5:6],
                              deps=last(tok, q_fc["sinv2"])))
            pcq_free = t1toks[-1]
        t1_done = last(*t1toks)
        h_toks = []
        for o1 in range(NF):
            wb = o1 % 2
            if Cf == 0:
                wdep, stt = jit_quant(
                    q_fc, wb, q_fc["oj"][:, :, o1, :],
                    AW[:, wb * 1024:(wb + 1) * 1024],
                    last(wfc_prev[wb], wfc_st[wb], gt_free),
                    scjA[o1])
                wfc_st[wb] = stt
            else:
                stg8 = AMSCI8[:, wb * 1024:(wb + 1) * 1024]
                ld8 = dma("sync", stg8, scjA[o1],
                          deps=last(wfc_cst[wb], q_fc["done"], hl),
                          ctr_name="wfl")
                wdep = vop(lambda e, stg8=stg8, wb=wb: e.tensor_copy(
                    out=AW[:, wb * 1024:(wb + 1) * 1024], in_=stg8),
                    deps=last(ld8, wfc_prev[wb]))
                wfc_cst[wb] = wdep
            ri = o1 % 3
            reg = regs3[ri]
            deps = last(wdep, y2_done, t1_done, ld_bfcA, ld_bfcB,
                        *reg_free[ri])
            for c in range(ND):
                for nh in range(2):
                    mm(AW[:, wb * 1024 + c * 128: wb * 1024 + (c + 1) * 128],
                       Y2[:, c * 1024 + nh * 512: c * 1024 + (nh + 1) * 512],
                       reg[:, nh * 512:(nh + 1) * 512], c == 0, False,
                       deps=deps if (c == 0 and nh == 0) else ())
            bsrc = BFCTA if o1 < 16 else BFCTB
            bof = (o1 % 16) * 128
            for nh in range(2):
                tok = mm(bsrc[:, bof:bof + 128],
                         T1[0:16, nh * 512:(nh + 1) * 512],
                         reg[:, nh * 512:(nh + 1) * 512], False, True,
                         inc=(nh == 1))
            hev = act(H[:, o1 * 1024:(o1 + 1) * 1024], reg, AF.Gelu,
                      bias=BFCC[:, o1:o1 + 1], scale=QS[:, 3:4],
                      deps=last(tok, q_fc["s"], ld_bfc, v_rd))
            h_toks.append(hev)
            reg_free[ri] = [hev]
            wfc_prev[wb] = hev
        h_done = last(*h_toks)
        if Cf == 0:
            q_fc["done"] = last(*q_fc["stores"])
            ld_bpr = dma("sync", XT[0:1, 0:1024], bpj_h,
                         deps=last(xtp[0]), ctr_name="qsc")
            bpr_tok = act(BRW[:, 1024:2048], XT[0:1, 0:1024], AF.Identity,
                          scale=QS2[0:1, 2:3],
                          deps=last(ld_bpr, q_pj["sinv"], v_done))
            xtp[0] = last(xtp[0], bpr_tok)
        t2toks = []
        for h2 in range(2):
            deps = last(h_done, ld_apj, pcq_free)
            for fc in range(NF):
                tok = mm(APJT[:, fc * 16:(fc + 1) * 16],
                         H[:, fc * 1024 + h2 * 512:
                           fc * 1024 + (h2 + 1) * 512],
                         PC[0:16, :], fc == 0, fc == NF - 1,
                         inc=(fc == NF - 1), deps=deps if fc == 0 else ())
            t2toks.append(act(T2L[0:16, h2 * 512:(h2 + 1) * 512], PC[0:16, :],
                              AF.Identity, scale=QS2[0:16, 5:6],
                              deps=last(tok, q_pj["sinv2"])))
            pcq_free = t2toks[-1]
        t2_done = last(*t2toks)
        for qp in range(2):
            regs4f = (PBIG[:, 0:1024], PBIG[:, 1024:2048], PA[:, :], None)

            def f2dst(k, nh):
                return (PCD[nh][:, :] if k == 3 else
                        regs4f[k][:, nh * 512:(nh + 1) * 512])
            hd = last(*(reg_free[0] + reg_free[1] + reg_free[2] +
                        [pcq_free]))
            for fc in range(NF):
                wb = fc % 2
                if Cf == 0 and qp == 0:
                    wdep, stt = jit_quant(
                        q_pj, wb, q_pj["w4"][:, fc, :],
                        WSLB[:, wb * 1024:(wb + 1) * 1024],
                        last(wpj_prev[wb], wpj_st[wb]),
                        spj3[:, fc, :])
                    wpj_st[wb] = stt
                else:
                    stg8 = AMSCI8[:, 2048 + wb * 1024:
                                  2048 + (wb + 1) * 1024]
                    ld8 = dma("sync", stg8, spj3[:, fc, :],
                              deps=last(wpj_cst[wb], q_pj["done"], hl),
                              ctr_name="wpl")
                    wdep = vop(lambda e, stg8=stg8, wb=wb: e.tensor_copy(
                        out=WSLB[:, wb * 1024:(wb + 1) * 1024], in_=stg8),
                        deps=last(ld8, wpj_prev[wb]))
                    wpj_cst[wb] = wdep
                wdeps = last(wdep, h_done, t2_done) + \
                    (hd if fc == 0 else [])
                for k in range(4):
                    ts_i = qp * 4 + k
                    for nh in range(2):
                        tok = mm(H[:, fc * 1024 + ts_i * 128:
                                   fc * 1024 + (ts_i + 1) * 128],
                                 WSLB[:, wb * 1024 + nh * 512:
                                      wb * 1024 + (nh + 1) * 512],
                                 f2dst(k, nh),
                                 fc == 0, False, inc=(k == 3 and nh == 1),
                                 deps=wdeps if (k == 0 and nh == 0) else ())
                wpj_prev[wb] = tok
            if Cf == 0 and qp == 0:
                q_pj["done"] = last(*q_pj["stores"])
            f2_last = []
            for k in range(4):
                ts_i = qp * 4 + k
                for nh in range(2):
                    mm(T2L[0:16, ts_i * 128:(ts_i + 1) * 128],
                       BPJT[0:16, nh * 512:(nh + 1) * 512],
                       f2dst(k, nh), False, False,
                       deps=last(ld_bpjT, t2_done))
                for nh in range(2):
                    tok = mm(ONES[:, :],
                             BRW[:, 1024 + nh * 512:1024 + (nh + 1) * 512],
                             f2dst(k, nh), False, True,
                             inc=(nh == 1), deps=last(bpr_tok))
                f2_last.append(tok)
            for k in range(4):
                ts_i = qp * 4 + k
                t_abs = tb + ts_i
                par = t_abs % 2
                if k == 3:
                    s1f = vop(lambda e: e.bn_stats(out=ST6[:, 0:6],
                                                   in_=PC[:, :]),
                              deps=[f2_last[k]])
                    s2f = vop(lambda e: e.bn_stats(out=ST6[:, 6:12],
                                                   in_=PD[:, :]),
                              deps=[s1f])
                    ag = vop(lambda e: e.bn_aggr(out=MV[:, :],
                                                 in_=ST6[:, 0:12]),
                             deps=[s2f])
                else:
                    ag = stats1024(regs4f[k], [f2_last[k]])
                vp = vop(lambda e: e.scalar_tensor_tensor(
                    out=SA[:, 2:3], in0=MV[:, 1:2], scalar=QS2[:, 4:5],
                    in1=EPSC[:, :], op0=OP.mult, op1=OP.add),
                    deps=last(ag, ms_eps, q_pj["s2"]))
                w2r = vop(lambda e: e.reciprocal(out=SA[:, 3:4],
                                                 in_=SA[:, 2:3]), deps=[vp])
                aa0 = act(SA[:, 4:5], SA[:, 3:4], AF.Sqrt, deps=[w2r])
                aa = vop(lambda e: e.tensor_tensor(
                    out=SA[:, 0:1], in0=SA[:, 4:5], in1=QS2[:, 3:4],
                    op=OP.mult), deps=[aa0])
                bb = vop(lambda e: e.scalar_tensor_tensor(
                    out=SA[:, 1:2], in0=MV[:, 0:1], scalar=-1.0,
                    in1=SA[:, 0:1], op0=OP.mult, op1=OP.mult), deps=[aa])
                if k == 3:
                    ol_a = act(OLN[:, 0:512], PC[:, :], AF.Identity,
                               bias=SA[:, 1:2], scale=SA[:, 0:1],
                               deps=last(bb, oln_prev))
                    olnt = act(OLN[:, 512:1024], PD[:, :], AF.Identity,
                               bias=SA[:, 1:2], scale=SA[:, 0:1],
                               deps=[ol_a])
                else:
                    olnt = act(OLN, regs4f[k], AF.Identity, bias=SA[:, 1:2],
                               scale=SA[:, 0:1], deps=last(bb, oln_prev))
                x2b = LKT[:, (2 + par) * 1024:(3 + par) * 1024]
                x2l = dma("sync", x2b,
                          x2_scr[t_abs * 128:(t_abs + 1) * 128, :],
                          deps=last(x2_sts[t_abs], x2ld2_prev[par]),
                          ctr_name="x2l")
                ob2 = FSCR[:, :] if par == 0 else FSC2[:, :]
                fin = vop(lambda e, x2b=x2b, ob2=ob2: e.tensor_add(
                    out=ob2, in0=x2b, in1=OLN),
                    deps=last(olnt, x2l, outt_prev[par]))
                oln_prev = fin
                x2ld2_prev[par] = fin
                stok = dma("sync", out_h[t_abs * 128:(t_abs + 1) * 128, :],
                           ob2, deps=[fin], ctr_name="st")
                outt_prev[par] = stok
                out_sts.append(stok)
                if k < 3:
                    reg_free[k] = [fin]
                else:
                    pcq_free = fin
        azq_rd = oln_prev

    if dbg:
        dma("gpsimd", dbg_t["d_wfc"], sc_fc, deps=[q_fc["done"]],
            ctr_name="lps")
        dma("sync", dbg_t["d_qs"], QS[:, :],
            deps=last(q_fc["sinv2"], q_fc["s2"]), ctr_name="b64")
        dma("sync", dbg_t["d_qs2"], QS2[:, :],
            deps=last(q_pj["sinv2"], q_pj["s2"]), ctr_name="b64")
        dma("sync", dbg_t["d_mxp"], MXP[:, :], deps=last(q_fc["sinv2"]),
            ctr_name="b64")
    waits("sync", last(*out_sts))

    with nc.Block() as block:
        @block.sync
        def _(eng):
            for nm, fn in ops:
                if nm == "sync":
                    fn(eng)

        @block.gpsimd
        def _(eng):
            for nm, fn in ops:
                if nm == "gpsimd":
                    fn(eng)

        @block.tensor
        def _(eng):
            for nm, fn in ops:
                if nm == "tensor":
                    fn(eng)

        @block.vector
        def _(eng):
            for nm, fn in ops:
                if nm == "vector":
                    fn(eng)

        @block.scalar
        def _(eng):
            for nm, fn in ops:
                if nm == "scalar":
                    fn(eng)

    stack.close()
    return nc, dbg_t


def _prep_inputs(inputs):
    import ml_dtypes
    bf = ml_dtypes.bfloat16
    f32 = np.float32
    ct = lambda a: np.ascontiguousarray(np.asarray(a), dtype=f32)

    common = {
        "wqkT": ct(np.asarray(inputs["L0_attn_Wqk"]).T),
        "wvT": ct(np.asarray(inputs["L0_attn_Wv"]).T),
        "wgT": ct(np.asarray(inputs["L0_attn_Wg"]).T),
        "woT": ct(np.asarray(inputs["L0_attn_Wo"]).T),
        "wfcT": ct(np.asarray(inputs["L0_ffn_W_fc"]).T),
        "wpjT": ct(np.asarray(inputs["L0_ffn_W_proj"]).T),
        "afcT": ct(np.asarray(inputs["L0_ffn_A_fc"]).T),
        "bfcT": ct(np.asarray(inputs["L0_ffn_B_fc"]).T),
        "apjT": ct(np.asarray(inputs["L0_ffn_A_proj"]).T),
        "bpjT": ct(np.asarray(inputs["L0_ffn_B_proj"]).T),
        "bqk": ct(np.asarray(inputs["L0_attn_bqk"]).reshape(NQ, 128).T),
        "bv": ct(np.asarray(inputs["L0_attn_bv"]).reshape(1, E)),
        "bg": ct(np.asarray(inputs["L0_attn_bg"]).reshape(NE, 128).T),
        "bo": ct(np.asarray(inputs["L0_attn_bo"]).reshape(1, D)),
        "bfc": ct(np.asarray(inputs["L0_ffn_b_fc"]).reshape(NF, 128).T),
        "bpj": ct(np.asarray(inputs["L0_ffn_b_proj"]).reshape(1, D)),
        "ln1w": ct(np.asarray(inputs["L0_attn_ln1_w"]).reshape(ND, 128).T),
        "ln1b": ct(np.asarray(inputs["L0_attn_ln1_b"]).reshape(ND, 128).T),
        "ln2wr": ct(np.broadcast_to(np.asarray(inputs["L0_attn_ln2_w"]),
                                    (128, D))),
        "ln2br": ct(np.broadcast_to(np.asarray(inputs["L0_attn_ln2_b"]),
                                    (128, D))),
        "gam": ct(np.asarray(inputs["L0_attn_gamma"]).reshape(4, 2, 128)
                  .transpose(2, 1, 0).reshape(128, 8)),
        "bet": ct(np.asarray(inputs["L0_attn_beta"]).reshape(4, 2, 128)
                  .transpose(2, 1, 0).reshape(128, 8)),
        "posemb": ct(np.asarray(inputs["L0_attn_pos_emb"]).reshape(65, 1)),
    }
    idx = np.arange(G)
    dist = np.clip(np.abs(idx[:, None] - idx[None, :]), 0, 32)
    onehot = np.zeros((65, G * G), dtype=bf)
    onehot[dist.reshape(-1), np.arange(G * G)] = 1
    common["onehot"] = onehot
    common["identb"] = np.eye(128, dtype=bf)
    common["identf"] = np.eye(128, dtype=f32)
    common["onesrow"] = np.ones((1, 128), dtype=bf)
    common["onesrowf"] = np.ones((1, 128), dtype=f32)
    x = np.ascontiguousarray(np.asarray(inputs["L0_attn_x"]), dtype=f32)
    return [dict(common, x=x[i]) for i in range(NCORES)]


def kernel(**inputs):
    in_maps = _prep_inputs(inputs)
    nc, _ = build_nc(dbg=False)
    res = run_bass_kernel_spmd(nc, in_maps, core_ids=list(range(NCORES)))
    out = np.stack([np.asarray(r["out"]) for r in res.results], axis=0)
    return out.astype(np.float32)

# version 18
# speedup vs baseline: 2.0145x; 1.1758x over previous
import math
import numpy as np
import concourse.bass as bass
import concourse.mybir as mybir
from concourse.bass_utils import run_bass_kernel_spmd

F32 = mybir.dt.float32
BF16 = mybir.dt.bfloat16
I32 = mybir.dt.int32
I8 = mybir.dt.int8
AX = mybir.AxisListType.X
AF = mybir.ActivationFunctionType
OP = mybir.AluOpType

NCORES = 8
T = 2048
D = 1024
DQK = 256
E = 2048
FF = 4096
G = 64
NT = T // 128
ND = D // 128
NE = E // 128
NQ = DQK // 128
NF = FF // 128
EPS = 1e-5
PI = math.pi
INV2PI = 1.0 / (2.0 * PI)
PI_LO = float(np.nextafter(np.float32(np.pi), np.float32(0.0)))
LORA_SCALING = 2.0


def build_nc(dbg=False):
    nc = bass.Bass("TRN2", target_bir_lowering=False)

    dr = lambda name, shape, dt, kind: nc.dram_tensor(name, shape, dt, kind=kind).ap()
    x_h = dr("x", [T, D], F32, "ExternalInput")
    wqkT = dr("wqkT", [D, DQK], F32, "ExternalInput")
    wvT = dr("wvT", [D, E], F32, "ExternalInput")
    wgT = dr("wgT", [D, E], F32, "ExternalInput")
    woT = dr("woT", [E, D], F32, "ExternalInput")
    wfcT = dr("wfcT", [D, FF], F32, "ExternalInput")
    wpjT = dr("wpjT", [FF, D], F32, "ExternalInput")
    afcT_h = dr("afcT", [D, 16], F32, "ExternalInput")
    bfcT_h = dr("bfcT", [16, FF], F32, "ExternalInput")
    apjT_h = dr("apjT", [FF, 16], F32, "ExternalInput")
    bpjT_h = dr("bpjT", [16, D], F32, "ExternalInput")
    bqk_h = dr("bqk", [128, NQ], F32, "ExternalInput")
    bv_h = dr("bv", [1, E], F32, "ExternalInput")
    bg_h = dr("bg", [128, NE], F32, "ExternalInput")
    bo_h = dr("bo", [1, D], F32, "ExternalInput")
    bfc_h = dr("bfc", [128, NF], F32, "ExternalInput")
    bpj_h = dr("bpj", [1, D], F32, "ExternalInput")
    ln1w_h = dr("ln1w", [128, ND], F32, "ExternalInput")
    ln1b_h = dr("ln1b", [128, ND], F32, "ExternalInput")
    ln2w_h = dr("ln2wr", [128, D], F32, "ExternalInput")
    ln2b_h = dr("ln2br", [128, D], F32, "ExternalInput")
    gam_h = dr("gam", [128, 8], F32, "ExternalInput")
    bet_h = dr("bet", [128, 8], F32, "ExternalInput")
    pos_h = dr("posemb", [65, 1], F32, "ExternalInput")
    oneh_h = dr("onehot", [65, G * G], BF16, "ExternalInput")
    ident_h = dr("identb", [128, 128], BF16, "ExternalInput")
    identf_h = dr("identf", [128, 128], F32, "ExternalInput")
    ones_h = dr("onesrow", [1, 128], BF16, "ExternalInput")
    onesf_h = dr("onesrowf", [1, 128], F32, "ExternalInput")
    out_h = dr("out", [T, D], F32, "ExternalOutput")
    sc_fc = nc.dram_tensor("scr_fc", [FF, D], I8, kind="Internal").ap()
    scr1 = nc.dram_tensor("scr_1", [1, 1], F32, kind="Internal").ap()
    sc_pj = nc.dram_tensor("scr_pj", [FF, D], I8, kind="Internal").ap()
    x2_scr = nc.dram_tensor("scr_x2", [T, D], BF16, kind="Internal").ap()

    dbg_t = {}
    if dbg:
        for nm, shape in (("d_qs", [128, 8]), ("d_qs2", [128, 8]),
                          ("d_mxp", [128, 32]),
                          ("d_y", [128, ND * T]), ("d_qk", [128, NQ * T]),
                          ("d_v", [128, NT * E]), ("d_lkv", [128, 2 * E]),
                          ("d_x2", [T, D]), ("d_att", [128, 512]),
                          ("d_cos", [128, T]), ("d_sin", [128, T]),
                          ("d_wfc", [FF, D])):
            dbg_t[nm] = dr(nm, shape, F32, "ExternalOutput")

    al = nc.alloc_sbuf_tensor
    GAMT = al("GAMT", [128, 8], F32)
    BETT = al("BETT", [128, 8], F32)
    BQK = al("BQK", [128, NQ], F32)
    BGC = al("BGC", [128, NE], F32)
    BFCC = al("BFCC", [128, NF], F32)
    LN1W = al("LN1W", [128, ND], F32)
    LN1B = al("LN1B", [128, ND], F32)
    LN2W = al("LN2W", [128, D], BF16)
    LN2B = al("LN2B", [128, D], BF16)
    BRW = al("BRW", [1, E], BF16)
    POSE = al("POSE", [65, 1], BF16)
    IDENT = al("IDENT", [128, 128], BF16)
    IDENTF = al("IDENTF", [128, 128], F32)
    ONES = al("ONES", [1, 128], BF16)
    ZROW = al("ZROW", [1, 128], BF16)
    ONESF = al("ONESF", [1, 128], F32)
    BIAS64 = al("BIAS64", [64, 64], F32)
    EPSC = al("EPSC", [128, 1], F32)
    ST6 = al("ST6", [128, 12], F32)
    MV = al("MV", [128, 2], F32)
    SA = al("SA", [128, 8], F32)
    MXP = al("MXP", [128, 32], F32)
    QS = al("QS", [128, 8], F32)
    QS2 = al("QS2", [128, 8], F32)
    COS = al("COS", [128, T], BF16)
    SIN = al("SIN", [128, T], BF16)
    XT = al("XT", [128, 2 * D], F32)
    FSCR = al("FSCR", [128, 1024], F32)
    FSC2 = al("FSC2", [128, 1024], F32)
    ISCR = al("ISCR", [128, 1024], I32)
    AV = al("AV", [128, NT * E], BF16)
    AY = al("AY", [128, ND * T], BF16)
    AGT = al("AGT", [128, NE * 512], BF16)
    AZQ = al("AZQ", [128, 4096], BF16)
    AMSC = al("AMSC", [128, 2048], BF16)
    QK = al("QK", [128, NQ * T], BF16)
    LINKV = al("LINKV", [128, 2 * E], BF16)
    AW = al("AW", [128, 4096], BF16)
    WSLB = al("WSLB", [128, 2048], BF16)
    LKT = al("LKT", [128, NT * DQK], BF16)
    QQ = al("QQ", [128, 1024], BF16)
    QKK = al("QKK", [128, 1024], BF16)
    LQ = al("LQ", [128, 1024], BF16)
    GATEE = al("GATEE", [128, 1024], BF16)
    ATT = al("ATT", [128, 512], BF16)

    ONEH = AV[0:65, 0:G * G]
    SQT = FSCR[:, 0:256]
    M1 = AMSC[:, 0:512]
    M2 = AMSC[:, 512:1024]
    UB0 = AMSC[:, 1024:1536]
    UB1 = AMSC[:, 1536:2048]
    Z2 = AZQ[:, 0:1024]
    T2E = AZQ[:, 1024:2048]
    OLN = AZQ[:, 2048:3072]
    BPJT = QQ[0:16, 0:1024]
    APJT = QKK[:, 0:512]
    AFCT = QKK[:, 512:640]
    T1 = LQ[0:16, 0:1024]
    T2L = GATEE[0:16, 0:1024]
    BFCTA = COS[0:16, :]
    BFCTB = SIN[0:16, :]
    ISCRI8 = ISCR.bitcast(I8)

    PBIG = nc.alloc_psum_tensor("PBIG", [128, 2048], F32)
    PA = nc.alloc_psum_tensor("PA", [128, 1024], F32)
    PC = nc.alloc_psum_tensor("PC", [128, 512], F32)
    PD = nc.alloc_psum_tensor("PD", [128, 512], F32)
    PCD = [PC, PD]

    ops = []
    EMIT = {k: (lambda fn, k=k: ops.append((k, fn)))
            for k in ("tensor", "vector", "scalar", "sync", "gpsimd")}

    class Ctr:
        def __init__(self, h):
            self.h = h
            self.n = 0

    from contextlib import ExitStack
    stack = ExitStack()
    sem = {k: Ctr(stack.enter_context(nc.semaphore(k)))
           for k in ("ld", "gp", "gpc", "te", "ve", "se", "st",
                     "lps", "b64", "qsc",
                     "qld0", "qld1", "xs0", "xs1", "wgl0", "wgl1",
                     "wol0", "wol1", "xrl0", "xrl1", "wvl0", "wvl1",
                     "wfl0", "wfl1", "wpl0", "wpl1", "x2l0", "x2l1",
                     "qst0", "qst1")}
    pipetog = {}
    hw = {e: {} for e in EMIT}

    last_dma = {e: None for e in EMIT}

    def wait(engine, ctr, val):
        if val is None or val <= 0:
            return
        if val > hw[engine].get(id(ctr), 0):
            hw[engine][id(ctr)] = val
            last_dma[engine] = None
            EMIT[engine](lambda e, h=ctr.h, v=val: e.wait_ge(h, v))

    def waits(engine, deps):
        for d in deps:
            if d is None:
                continue
            if isinstance(d, list):
                waits(engine, d)
            else:
                wait(engine, d[0], d[1])

    def last(*tokens):
        out = {}
        for t in tokens:
            if t is None:
                continue
            if isinstance(t, list):
                for u in t:
                    if u is not None and out.get(id(u[0]), (None, 0))[1] < u[1]:
                        out[id(u[0])] = u
                continue
            if out.get(id(t[0]), (None, 0))[1] < t[1]:
                out[id(t[0])] = t
        return list(out.values())

    def dma(engine, out_ap, in_ap, deps=(), ctr_name=None):
        waits(engine, deps)
        if ctr_name is not None and (ctr_name + "0") in sem:
            t = pipetog.get(ctr_name, 0)
            pipetog[ctr_name] = 1 - t
            c = sem[ctr_name + str(t)]
        else:
            c = sem[ctr_name] if ctr_name else (
                sem["ld"] if engine == "sync" else sem["gp"])
        wait(engine, c, c.n)
        c.n += 16
        n = c.n
        EMIT[engine](lambda e, o=out_ap, i=in_ap, h=c.h:
                     e.dma_start(out=o, in_=i).then_inc(h, 16))
        return (c, n)

    def mm(lhsT, rhs, out, start, stop, inc=False, deps=()):
        waits("tensor", deps)
        if inc:
            sem["te"].n += 1
            n = sem["te"].n
            EMIT["tensor"](lambda e, a=lhsT, b=rhs, o=out, s1=start, s2=stop,
                           h=sem["te"].h:
                           e.matmul(o, a, b, start=s1, stop=s2).then_inc(h, 1))
            return (sem["te"], n)
        EMIT["tensor"](lambda e, a=lhsT, b=rhs, o=out, s1=start, s2=stop:
                       e.matmul(o, a, b, start=s1, stop=s2))
        return None

    def act(out_ap, in_ap, func, bias=0.0, scale=1.0, deps=()):
        waits("scalar", deps)
        sem["se"].n += 1
        n = sem["se"].n
        EMIT["scalar"](lambda e, o=out_ap, i=in_ap, f=func, b=bias, s=scale,
                       h=sem["se"].h:
                       e.activation(out=o, in_=i, func=f, bias=b, scale=s)
                       .then_inc(h, 1))
        return (sem["se"], n)

    def vop(emitfn, deps=()):
        waits("vector", deps)
        sem["ve"].n += 1
        n = sem["ve"].n
        EMIT["vector"](lambda e, f=emitfn, h=sem["ve"].h: f(e).then_inc(h, 1))
        return (sem["ve"], n)

    def gop(emitfn, deps=()):
        waits("gpsimd", deps)
        sem["gpc"].n += 1
        n = sem["gpc"].n
        EMIT["gpsimd"](lambda e, f=emitfn, h=sem["gpc"].h: f(e).then_inc(h, 1))
        return (sem["gpc"], n)

    def flush_mm(region, deps=(), m=128):
        return mm(ZROW[:, 0:m], ONES[:, :], region, False, True, inc=True,
                  deps=deps)

    ld_gam = dma("sync", GAMT[:, :], gam_h)
    ld_bet = dma("sync", BETT[:, :], bet_h)
    ld_bqk = dma("sync", BQK[:, :], bqk_h)
    ld_bg = dma("sync", BGC[:, :], bg_h)
    ld_bfc = dma("sync", BFCC[:, :], bfc_h)
    ld_l1w = dma("sync", LN1W[:, :], ln1w_h)
    ld_l1b = dma("sync", LN1B[:, :], ln1b_h)
    ld_onh = dma("sync", ONEH, oneh_h)
    ld_id = dma("sync", IDENT[:, :], ident_h)
    ld_on = dma("sync", ONES[:, :], ones_h)
    ld_onf = dma("sync", ONESF[:, :], onesf_h)
    ld_l2w = dma("gpsimd", LN2W[:, :], ln2w_h)
    ld_l2b = dma("gpsimd", LN2B[:, :], ln2b_h)
    ld_bvr = dma("gpsimd", BRW[:, :], bv_h)
    ld_pos = dma("gpsimd", POSE[:, :], pos_h)
    pre_ld = (sem["ld"], sem["ld"].n)
    ld_gam = ld_bet = ld_bqk = ld_bg = ld_bfc = pre_ld
    ld_l1w = ld_l1b = ld_onh = ld_id = ld_on = ld_onf = pre_ld
    pre_gp = (sem["gp"], sem["gp"].n)
    ld_l2w = ld_l2b = ld_bvr = ld_pos = pre_gp
    iot1 = gop(lambda e: e.iota(FSCR[0:1, 0:128], [[1, 128]],
                                channel_multiplier=0,
                                allow_small_or_imprecise_dtypes=True))
    ms_att = gop(lambda e: e.memset(ATT[:, :], 0.0))
    ms_zr = gop(lambda e: e.memset(ZROW[:, :], 0.0))
    ms_eps = gop(lambda e: e.memset(EPSC[:, :], EPS))
    a_invf = act(IDENTF[0:1, 0:128], FSCR[0:1, 0:128], AF.Exp,
                 scale=-math.log(10000.0) / 128.0, deps=[iot1])

    def sincos_half2(hh, shift, dst_half, pdeps):
        posr = FSC2[0:1, 0:1024]
        pio = gop(lambda e, hh=hh: e.iota(posr, [[1, 1024]], base=hh * 1024,
                                          channel_multiplier=0,
                                          allow_small_or_imprecise_dtypes=True),
                  deps=list(pdeps))
        mmdeps = last(a_invf, pio) + list(pdeps)
        for j in range(2):
            ttok = mm(IDENTF[0:1, 0:128], posr[:, j * 512:(j + 1) * 512],
                      PA[:, j * 512:(j + 1) * 512], True, True, inc=(j == 1),
                      deps=mmdeps if j == 0 else ())
        ts = vop(lambda e: e.tensor_scalar(
            out=FSCR[:, 0:1024], in0=PA[:, :], scalar1=INV2PI, scalar2=shift,
            op0=OP.mult, op1=OP.add), deps=last(ttok))
        k1 = vop(lambda e: e.tensor_single_scalar(
            out=ISCR[:, 0:1024], in_=FSCR[:, 0:1024], scalar=1.0,
            op=OP.mult), deps=[ts])
        c2 = vop(lambda e: e.scalar_tensor_tensor(
            out=XT[:, 0:1024], in0=FSCR[:, 0:1024], scalar=1.0,
            in1=ISCR[:, 0:1024], op0=OP.mult, op1=OP.subtract), deps=[k1])
        km = vop(lambda e: e.scalar_tensor_tensor(
            out=XT[:, 1024:2048], in0=XT[:, 0:1024], scalar=-0.5,
            in1=ISCR[:, 0:1024], op0=OP.is_lt, op1=OP.subtract), deps=[c2])
        kf = vop(lambda e: e.scalar_tensor_tensor(
            out=XT[:, 0:1024], in0=XT[:, 0:1024], scalar=0.5,
            in1=XT[:, 1024:2048], op0=OP.is_gt, op1=OP.subtract), deps=[km])
        dd = vop(lambda e: e.scalar_tensor_tensor(
            out=FSCR[:, 0:1024], in0=XT[:, 0:1024], scalar=-1.0,
            in1=FSCR[:, 0:1024], op0=OP.mult, op1=OP.add), deps=[kf])
        r1 = vop(lambda e: e.tensor_scalar(
            out=FSCR[:, 0:1024], in0=FSCR[:, 0:1024], scalar1=2.0 * PI,
            scalar2=PI_LO, op0=OP.mult, op1=OP.min), deps=[dd])
        r2 = vop(lambda e: e.tensor_single_scalar(
            out=FSCR[:, 0:1024], in_=FSCR[:, 0:1024], scalar=-PI_LO,
            op=OP.max), deps=[r1])
        return act(dst_half, FSCR[:, 0:1024], AF.Sin, deps=[r2]), ttok

    tbl_toks = []
    mm_last = None
    for hh in range(2):
        prev = [tbl_toks[-1]] if tbl_toks else []
        tk, mm_last = sincos_half2(hh, 0.0,
                                   SIN[:, hh * 1024:(hh + 1) * 1024], prev)
        tbl_toks.append(tk)
        tk, mm_last = sincos_half2(hh, 0.25,
                                   COS[:, hh * 1024:(hh + 1) * 1024],
                                   [tbl_toks[-1]])
        tbl_toks.append(tk)
    tables_done = tbl_toks[-1]
    s_sin = tables_done
    s_cos = tables_done

    ev_prev = tables_done
    b64_dmas = []
    for j in range(8):
        tok = mm(POSE[:, :], ONEH[:, j * 512:(j + 1) * 512], PC[0:1, :],
                 True, True, inc=True,
                 deps=last(ld_pos, ld_onh, ev_prev,
                           b64_dmas[-1] if b64_dmas else None))
        ev_prev = vop(lambda e: e.tensor_copy(
            out=FSCR[0:1, 0:512], in_=PC[0:1, :]), deps=[tok])
        b64_dmas.append(dma("sync", BIAS64[8 * j:8 * (j + 1), :],
                            FSCR[0:1, 0:512], deps=[ev_prev],
                            ctr_name="b64"))
    ld_b64 = last(*b64_dmas)
    pc_free = ev_prev
    pd_free = None

    def stats1024(src, deps):
        s1 = vop(lambda e: e.bn_stats(out=ST6[:, 0:6], in_=src[:, 0:512]),
                 deps=deps)
        s2 = vop(lambda e: e.bn_stats(out=ST6[:, 6:12], in_=src[:, 512:1024]),
                 deps=[s1])
        return vop(lambda e: e.bn_aggr(out=MV[:, :], in_=ST6[:, 0:12]),
                   deps=[s2])

    def ln_single(vcol, mcol, acol, bcol, deps):
        w2 = vop(lambda e, v=vcol: e.tensor_scalar(
            out=SA[:, 2:3], in0=v, scalar1=1.0, scalar2=EPS,
            op0=OP.mult, op1=OP.add), deps=deps)
        w2r = vop(lambda e: e.reciprocal(out=SA[:, 3:4], in_=SA[:, 2:3]),
                  deps=[w2])
        aa = act(acol, SA[:, 3:4], AF.Sqrt, deps=[w2r])
        return vop(lambda e, m=mcol, a=acol, b=bcol: e.scalar_tensor_tensor(
            out=b, in0=m, scalar=-1.0, in1=a, op0=OP.mult, op1=OP.mult),
            deps=[aa])

    def ln_double(vcol, mcol, acol, bcol, deps):
        w2 = vop(lambda e, v=vcol: e.tensor_scalar(
            out=SA[:, 2:3], in0=v, scalar1=1.0, scalar2=EPS,
            op0=OP.mult, op1=OP.add), deps=deps)
        w2r = vop(lambda e: e.reciprocal(out=SA[:, 3:4], in_=SA[:, 2:3]),
                  deps=[w2])
        v2 = vop(lambda e, v=vcol: e.tensor_tensor(
            out=SA[:, 4:5], in0=v, in1=SA[:, 3:4], op=OP.mult), deps=[w2r])
        w4 = vop(lambda e: e.tensor_scalar(
            out=SA[:, 5:6], in0=SA[:, 4:5], scalar1=1.0, scalar2=EPS,
            op0=OP.mult, op1=OP.add), deps=[v2])
        w4r = vop(lambda e: e.reciprocal(out=SA[:, 6:7], in_=SA[:, 5:6]),
                  deps=[w4])
        pr = vop(lambda e: e.tensor_tensor(
            out=SA[:, 7:8], in0=SA[:, 3:4], in1=SA[:, 6:7], op=OP.mult),
            deps=[w4r])
        aa = act(acol, SA[:, 7:8], AF.Sqrt, deps=[pr])
        return vop(lambda e, m=mcol, a=acol, b=bcol: e.scalar_tensor_tensor(
            out=b, in0=m, scalar=-1.0, in1=a, op0=OP.mult, op1=OP.mult),
            deps=[aa])

    ev_tok = None
    zg_rd = [None] * 4
    prev_z = None
    for g in range(4):
        z_toks = []
        for tt in range(4):
            t = g * 4 + tt
            buf = tt % 2
            xs = XT[:, buf * D:(buf + 1) * D]
            xdep = [prev_z] if t >= 2 else [tables_done, ld_b64]
            xld = dma("sync", xs, x_h[t * 128:(t + 1) * 128, :], deps=xdep,
                      ctr_name="xs")
            ag = stats1024(xs, [xld])
            bb = ln_double(MV[:, 1:2], MV[:, 0:1], SA[:, 0:1], SA[:, 1:2],
                           [ag])
            ztok = vop(lambda e, xs=xs, tt=tt: e.tensor_scalar(
                out=AZQ[:, tt * D:(tt + 1) * D], in0=xs,
                scalar1=SA[:, 0:1], scalar2=SA[:, 1:2],
                op0=OP.mult, op1=OP.add),
                deps=last(bb, zg_rd[g - 1] if g >= 1 else None))
            z_toks.append(ztok)
            prev_z = ztok
        for c in range(ND):
            q = c % 4
            reg = PBIG[:, q * 512:(q + 1) * 512]
            deps = last(*(z_toks + [ev_tok, ld_id]))
            for tt in range(4):
                tok = mm(AZQ[:, tt * D + c * 128: tt * D + (c + 1) * 128],
                         IDENT[:, :], reg[:, tt * 128:(tt + 1) * 128],
                         True, True, inc=(tt == 3),
                         deps=deps if tt == 0 else ())
            ev_tok = act(AY[:, c * T + g * 512: c * T + (g + 1) * 512],
                         reg, AF.Identity, bias=LN1B[:, c:c + 1],
                         scale=LN1W[:, c:c + 1],
                         deps=last(tok, ld_l1w, ld_l1b))
            if c == ND - 1:
                zg_rd[g] = tok
    y_done = ev_tok
    Y = AY

    pcd_free = [pc_free, pd_free]
    aw_free = [None, None]
    qk_evs = []
    for qc in range(NQ):
        wb = qc % 2
        wsl = AW[:, wb * 1024:(wb + 1) * 1024]
        wqk_ld = dma("gpsimd", wsl.rearrange("p (c k) -> p c k", c=ND),
                     wqkT[:, qc * 128:(qc + 1) * 128]
                     .rearrange("(c p) k -> p c k", p=128),
                     deps=last(aw_free[wb]), ctr_name="wvl")
        for tc in range(4):
            pi = (qc * 4 + tc) % 2
            ps = PCD[pi]
            deps = last(wqk_ld, y_done, pcd_free[pi])
            for c in range(ND):
                tok = mm(wsl[:, c * 128:(c + 1) * 128],
                         Y[:, c * T + tc * 512: c * T + (tc + 1) * 512],
                         ps[:, :], c == 0, c == ND - 1, inc=(c == ND - 1),
                         deps=deps if c == 0 else ())
            ev = act(QK[:, qc * T + tc * 512: qc * T + (tc + 1) * 512],
                     ps[:, :], AF.Silu, bias=BQK[:, qc:qc + 1],
                     deps=last(tok, ld_bqk))
            pcd_free[pi] = ev
            qk_evs.append(ev)
        aw_free[wb] = tok
    qk_done = last(*qk_evs)
    V = AV

    xtp = [last(prev_z, tables_done), last(prev_z, tables_done)]

    def quant_max(w_hbm, cols, qsl, start_deps):
        npc = (D * FF) // (128 * 1024)
        w2 = w_hbm.rearrange("(c p) (j k) -> p c j k", p=128, k=1024)
        ncw2 = cols // 1024
        mx_toks = []
        for i in range(npc):
            b = i % 2
            stg = XT[:, b * 1024:(b + 1) * 1024]
            c, j = divmod(i, ncw2)
            ldt = dma("sync", stg, w2[:, c, j, :],
                      deps=last(xtp[b]) + (list(start_deps) if i == 0 else []),
                      ctr_name="qld")
            mx = vop(lambda e, stg=stg, i=i: e.tensor_reduce(
                out=MXP[:, i:i + 1], in_=stg, axis=AX, op=OP.max,
                apply_absolute_value=True), deps=[ldt])
            xtp[b] = mx
            mx_toks.append(mx)
        m1 = vop(lambda e, q=qsl, n=npc: e.tensor_reduce(
            out=q[:, 0:1], in_=MXP[:, 0:n], axis=AX, op=OP.max),
            deps=last(*mx_toks))
        d1 = dma("sync", FSCR[0:1, 0:128], qsl[:, 0:1], deps=[m1],
                 ctr_name="qsc")
        m2 = vop(lambda e, q=qsl: e.tensor_reduce(
            out=q[0:1, 6:7], in_=FSCR[0:1, 0:128], axis=AX, op=OP.max),
            deps=[d1])
        d2a = dma("sync", scr1, qsl[0:1, 6:7], deps=[m2], ctr_name="qsc")
        d2 = dma("sync", qsl[:, 0:1], scr1.broadcast_to([128, 1]),
                 deps=[d2a], ctr_name="qsc")
        rr = vop(lambda e, q=qsl: e.reciprocal(out=q[:, 1:2], in_=q[:, 0:1]),
                 deps=[d2])
        sinv = vop(lambda e, q=qsl: e.tensor_single_scalar(
            out=q[:, 2:3], in_=q[:, 1:2], scalar=127.0, op=OP.mult),
            deps=[rr])
        s_ = vop(lambda e, q=qsl: e.tensor_single_scalar(
            out=q[:, 3:4], in_=q[:, 0:1], scalar=1.0 / 127.0, op=OP.mult),
            deps=[d2])
        s2 = vop(lambda e, q=qsl: e.tensor_tensor(
            out=q[:, 4:5], in0=q[:, 3:4], in1=q[:, 3:4], op=OP.mult),
            deps=[s_])
        sinv2 = vop(lambda e, q=qsl: e.tensor_single_scalar(
            out=q[:, 5:6], in_=q[:, 2:3], scalar=LORA_SCALING, op=OP.mult),
            deps=[sinv])
        return dict(sinv=sinv, s=s_, s2=s2, sinv2=sinv2, qsl=qsl, stores=[])

    def jit_quant(st, b, w_src, wf, wf_deps, stt_dst):
        stg = XT[:, b * 1024:(b + 1) * 1024]
        ldt = dma("sync", stg, w_src, deps=last(xtp[b]), ctr_name="qld")
        q1 = vop(lambda e, stg=stg, q=st["qsl"]: e.tensor_single_scalar(
            out=ISCR[:, :], in_=stg, scalar=q[:, 2:3], op=OP.mult),
            deps=last(ldt, st["sinv"]))
        cc = vop(lambda e, stg=stg, q=st["qsl"]: e.scalar_tensor_tensor(
            out=FSCR[:, :], in0=stg, scalar=q[:, 2:3], in1=ISCR[:, :],
            op0=OP.mult, op1=OP.subtract), deps=[q1])
        xtp[b] = cc
        qm = vop(lambda e: e.scalar_tensor_tensor(
            out=FSC2[:, :], in0=FSCR[:, :], scalar=-0.5, in1=ISCR[:, :],
            op0=OP.is_lt, op1=OP.subtract), deps=[cc])
        qf = vop(lambda e, wf=wf: e.scalar_tensor_tensor(
            out=wf, in0=FSCR[:, :], scalar=0.5, in1=FSC2[:, :],
            op0=OP.is_gt, op1=OP.subtract),
            deps=last(qm) + list(wf_deps))
        stt = dma("gpsimd", stt_dst, wf, deps=[qf], ctr_name="qst")
        st["stores"].append(stt)
        return qf, stt

    q_fc = quant_max(wfcT, FF, QS, last(y_done, tables_done, ld_b64))
    q_fc["oj"] = wfcT.rearrange("(c p) (o k) -> p c o k", p=128, k=128)

    v_evs = []
    V3 = V[:, :].rearrange("p (t k) -> p t k", t=NT)
    for oc in range(8):
        wb = oc % 2
        wsl = AW[:, wb * 2048:(wb + 1) * 2048]
        wdep = (last(aw_free[0], aw_free[1]) if oc < 2
                else [v_evs[8 * (oc - 2) + 7]])
        wv_ld = dma("gpsimd", wsl.rearrange("p (c k) -> p c k", c=ND),
                    wvT[:, oc * 256:(oc + 1) * 256]
                    .rearrange("(c p) k -> p c k", p=128),
                    deps=wdep, ctr_name="wvl")
        for tg in range(8):
            pi = tg % 2
            ps = PCD[pi]
            deps = last(wv_ld, y_done, pcd_free[pi], ld_on, ld_bvr)
            for tt in range(2):
                t = 2 * tg + tt
                for c in range(ND):
                    mm(Y[:, c * T + t * 128: c * T + (t + 1) * 128],
                       wsl[:, c * 256:(c + 1) * 256],
                       ps[:, tt * 256:(tt + 1) * 256], c == 0, False,
                       deps=deps if (c == 0 and tt == 0) else ())
                tok = mm(ONES[:, :], BRW[:, oc * 256:(oc + 1) * 256],
                         ps[:, tt * 256:(tt + 1) * 256], False, True,
                         inc=(tt == 1))
            ev = act(V3[:, 2 * tg:2 * tg + 2, oc * 256:(oc + 1) * 256],
                     ps[:, :].rearrange("p (t k) -> p t k", t=2),
                     AF.Silu, deps=[tok])
            pcd_free[pi] = ev
            v_evs.append(ev)
    aw_free = [v_evs[-1], v_evs[-2]]
    v_done = last(*v_evs)

    def head_rope(hidx, tok_off, dst, d0_off, d1_off, deps):
        n = 512
        a0 = act(UB0, QK[:, tok_off:tok_off + n], AF.Identity,
                 bias=BETT[:, hidx:hidx + 1], scale=GAMT[:, hidx:hidx + 1],
                 deps=deps)
        a1 = act(UB1, QK[:, T + tok_off:T + tok_off + n], AF.Identity,
                 bias=BETT[:, 4 + hidx:4 + hidx + 1],
                 scale=GAMT[:, 4 + hidx:4 + hidx + 1], deps=deps)
        cs = COS[:, tok_off:tok_off + n]
        sn = SIN[:, tok_off:tok_off + n]
        m1 = vop(lambda e: e.tensor_mul(out=M1, in0=UB0, in1=cs),
                 deps=last(a0, s_cos))
        m2 = vop(lambda e: e.tensor_mul(out=M2, in0=UB1, in1=sn),
                 deps=last(a1, s_sin))
        o1 = vop(lambda e: e.tensor_sub(
            out=dst[:, d0_off:d0_off + n], in0=M1, in1=M2), deps=[m1, m2])
        m3 = vop(lambda e: e.tensor_mul(out=M1, in0=UB1, in1=cs), deps=[o1])
        m4 = vop(lambda e: e.tensor_mul(out=M2, in0=UB0, in1=sn), deps=[o1])
        o2 = vop(lambda e: e.tensor_add(
            out=dst[:, d1_off:d1_off + n], in0=M1, in1=M2), deps=[m3, m4])
        return (last(a0, a1), last(o1, o2))

    LKF = AZQ
    lk_toks = []
    for cc in range(4):
        dep = last(qk_done, ld_gam, ld_bet,
                   zg_rd[3] if cc == 0 else None,
                   lk_toks[-1] if lk_toks else None)
        _, o = head_rope(3, cc * 512, LKF, cc * 512, 2048 + cc * 512, dep)
        lk_toks.append(o)
    lk_done = last(*lk_toks)
    lkt_evs = []
    ev = None
    for c2 in range(2):
        for tg in range(4):
            deps = last(lk_done, ev, pcd_free[0])
            for tt in range(4):
                t = tg * 4 + tt
                tok = mm(LKF[:, c2 * 2048 + t * 128: c2 * 2048 + (t + 1) * 128],
                         IDENT[:, :], PC[:, tt * 128:(tt + 1) * 128],
                         True, True, inc=(tt == 3),
                         deps=deps if tt == 0 else ())
            ev = vop(lambda e, c2=c2, tg=tg: e.tensor_copy(
                out=LKT[:, :].rearrange("p (t k) -> p t k", t=NT)
                [:, tg * 4:(tg + 1) * 4, c2 * 128:(c2 + 1) * 128],
                in_=PC[:, :].rearrange("p (t k) -> p t k", t=4)),
                deps=[tok])
            lkt_evs.append(ev)
    lkt_done = last(*lkt_evs)
    pcd_free[0] = lkt_evs[-1]

    q_pj = quant_max(wpjT, D, QS2, [])
    q_pj["w4"] = wpjT.rearrange("(c p) k -> p c k", p=128)

    regions8 = [PBIG[:, 0:512], PBIG[:, 512:1024], PBIG[:, 1024:1536],
                PBIG[:, 1536:2048], PA[:, 0:512], PA[:, 512:1024],
                PC[:, :], PD[:, :]]
    lkv_evs = []
    for m in range(2):
        for n4 in range(4):
            r = regions8[m * 4 + n4]
            deps = last(lkt_done, v_done, pcd_free[0], pcd_free[1], ev_tok)
            for t in range(NT):
                tok = mm(LKT[:, t * DQK + m * 128: t * DQK + (m + 1) * 128],
                         V[:, t * E + n4 * 512: t * E + (n4 + 1) * 512],
                         r, t == 0, t == NT - 1, inc=(t == NT - 1),
                         deps=deps if t == 0 else ())
            lkv_evs.append(act(
                LINKV[:, m * E + n4 * 512: m * E + (n4 + 1) * 512], r,
                AF.Identity, scale=1.0 / T, deps=[tok]))
    lkv_done = last(*lkv_evs)

    ld_bor = dma("gpsimd", BRW[:, 0:1024], bo_h, deps=last(v_done),
                 ctr_name="lps")
    XB0 = LKT[:, 0:1024]
    XB1 = LKT[:, 1024:2048]
    X2S0 = LKT[:, 2048:3072]
    X2S1 = LKT[:, 3072:4096]

    x2_sts = [None] * NT
    pa_free = [lkv_evs[4], lkv_evs[5]]
    pcd_free = [lkv_evs[6], lkv_evs[7]]
    pbig_free = [last(lkv_evs[0], lkv_evs[1]), last(lkv_evs[2], lkv_evs[3])]
    gt_free = v_evs[-1]
    gatee_prev = [None, None]
    wo_prev = [None, None]
    z2_rd = lkt_done
    epi_z2 = lkt_done
    epi_ol = lkt_done
    xb_prev = [lkv_done, lkv_done]
    x2s_prev = [None, None]

    d0 = last(qk_done, ms_att, lkv_done, z2_rd)
    aq, hq = head_rope(0, 0, QQ, 0, 512, d0)
    ak, hk = head_rope(2, 0, QKK, 0, 512, [hq])
    al_, hl = head_rope(1, 0, LQ, 0, 512, [hk])
    heads_nxt = (hq, hk, hl)
    for C in range(4):
        base = C * 512
        hq, hk, hl = heads_nxt
        sdeps = last(hq, hk, pcd_free[0])
        for p in range(4):
            for c2 in range(2):
                mm(QKK[:, c2 * 512 + p * 128: c2 * 512 + (p + 1) * 128],
                   QQ[:, c2 * 512 + p * 128: c2 * 512 + (p + 1) * 128],
                   PC[:, p * 128:(p + 1) * 128], c2 == 0,
                   c2 == 1 and p != 3,
                   deps=sdeps if (p == 0 and c2 == 0) else ())
        tok = flush_mm(PC[:, 384:512], deps=last(ms_zr, ld_on))
        sc_tok = tok
        pc3 = PC[:, :].rearrange("p (c k) -> p c k", c=4)
        sq3 = SQT[:, :].rearrange("p (c k) -> p c k", c=4)
        at3 = ATT[:, :].rearrange("p (c k) -> p c k", c=4)
        b3 = BIAS64[0:64, :].unsqueeze(1).broadcast_to([64, 4, 64])
        sq_t = vop(lambda e: e.scalar_tensor_tensor(
            out=sq3[0:64, :, :], in0=pc3[0:64, :, 0:64], scalar=1.0 / G,
            in1=b3, op0=OP.mult, op1=OP.add), deps=last(tok, ld_b64))
        sq_b = vop(lambda e: e.scalar_tensor_tensor(
            out=sq3[64:128, :, :], in0=pc3[64:128, :, 64:128],
            scalar=1.0 / G, in1=b3, op0=OP.mult, op1=OP.add), deps=[sq_t])
        at_t = vop(lambda e: e.scalar_tensor_tensor(
            out=at3[0:64, :, 0:64], in0=sq3[0:64, :, :], scalar=0.0,
            in1=sq3[0:64, :, :], op0=OP.max, op1=OP.mult), deps=[sq_t])
        at_b = vop(lambda e: e.scalar_tensor_tensor(
            out=at3[64:128, :, 64:128], in0=sq3[64:128, :, :], scalar=0.0,
            in1=sq3[64:128, :, :], op0=OP.max, op1=OP.mult), deps=[sq_b])
        att_done = last(at_t, at_b)
        pcd_free[0] = at_b
        if dbg and C == 0:
            dma("gpsimd", dbg_t["d_att"], ATT[:, :], deps=[att_done],
                ctr_name="lps")

        gt_toks = []
        for Ei in range(NE):
            wb = Ei % 2
            prev_wg = gt_toks[-2] if len(gt_toks) >= 2 else gt_free
            wg_ld = dma("gpsimd", AW[:, wb * 1024:(wb + 1) * 1024]
                        .rearrange("p (c k) -> p c k", c=ND),
                        wgT[:, Ei * 128:(Ei + 1) * 128]
                        .rearrange("(c p) k -> p c k", p=128),
                        deps=last(prev_wg), ctr_name="wgl")
            pi = Ei % 2
            gdeps = last(wg_ld, pcd_free[pi], y_done)
            for c in range(ND):
                tok = mm(AW[:, wb * 1024 + c * 128: wb * 1024 + (c + 1) * 128],
                         Y[:, c * T + base: c * T + base + 512],
                         PCD[pi][:, :], c == 0, c == ND - 1,
                         inc=(c == ND - 1), deps=gdeps if c == 0 else ())
            gprev = gt_toks[-2] if len(gt_toks) >= 2 else gatee_prev[pi]
            gev = act(GATEE[:, 0:512] if pi == 0 else GATEE[:, 512:1024],
                      PCD[pi][:, :], AF.Silu, bias=BGC[:, Ei:Ei + 1],
                      deps=last(tok, ld_bg, gprev))
            pcd_free[pi] = gev
            ph = Ei % 2
            adeps = last(pa_free[ph], lkv_done, hl, att_done, v_done)
            for m in range(2):
                mm(LINKV[:, m * E + Ei * 128: m * E + (Ei + 1) * 128],
                   LQ[:, m * 512:(m + 1) * 512],
                   PA[:, ph * 512:(ph + 1) * 512], m == 0, False,
                   deps=adeps if m == 0 else ())
            for p in range(4):
                mm(V[:, (4 * C + p) * E + Ei * 128:
                     (4 * C + p) * E + (Ei + 1) * 128],
                   ATT[:, p * 128:(p + 1) * 128],
                   PA[:, ph * 512 + p * 128: ph * 512 + (p + 1) * 128],
                   False, False)
            tok2 = flush_mm(PA[:, ph * 512: ph * 512 + 128])
            gt = vop(lambda e, Ei=Ei, ph=ph, pi=pi: e.tensor_tensor(
                out=AGT[:, Ei * 512:(Ei + 1) * 512],
                in0=PA[:, ph * 512:(ph + 1) * 512],
                in1=GATEE[:, pi * 512:(pi + 1) * 512], op=OP.mult),
                deps=last(tok2, gev))
            pa_free[ph] = gt
            gt_toks.append(gt)
        gts = last(*gt_toks)

        if C < 3:
            hdeps = last(sc_tok, gts, hl)
            _, hq2 = head_rope(0, base + 512, QQ, 0, 512, hdeps)
            _, hk2 = head_rope(2, base + 512, QKK, 0, 512, [hq2])
            _, hl2 = head_rope(1, base + 512, LQ, 0, 512, [hk2])
            heads_nxt = (hq2, hk2, hl2)

        regs4 = (PBIG[:, 0:1024], PBIG[:, 1024:2048], PA[:, :], None)
        hd = last(gts, pbig_free[0], pbig_free[1], pa_free[0], pa_free[1],
                  pcd_free[0], pcd_free[1])
        for Ei in range(NE):
            wb = Ei % 2
            wo_ld = dma("gpsimd", WSLB[:, wb * 1024:(wb + 1) * 1024],
                        woT[Ei * 128:(Ei + 1) * 128, :],
                        deps=last(wo_prev[wb]), ctr_name="wol")
            wdeps = last(wo_ld, gts) + (hd if Ei == 0 else [])
            for k in range(4):
                for nh in range(2):
                    dst = (PCD[nh][:, :] if k == 3 else
                           regs4[k][:, nh * 512:(nh + 1) * 512])
                    tok = mm(AGT[:, Ei * 512 + k * 128:
                                 Ei * 512 + (k + 1) * 128],
                             WSLB[:, wb * 1024 + nh * 512:
                                  wb * 1024 + (nh + 1) * 512],
                             dst, Ei == 0, False,
                             inc=(k == 3 and nh == 1),
                             deps=wdeps if (k == 0 and nh == 0) else ())
            wo_prev[wb] = tok
        wo_last = []
        for k in range(4):
            for nh in range(2):
                dst = (PCD[nh][:, :] if k == 3 else
                       regs4[k][:, nh * 512:(nh + 1) * 512])
                tok = mm(ONES[:, :], BRW[:, nh * 512:(nh + 1) * 512],
                         dst, False, True, inc=(nh == 1),
                         deps=last(ld_bor, ld_on))
            wo_last.append(tok)
        for k in range(4):
            t_abs = 4 * C + k
            if k == 3:
                s1 = vop(lambda e: e.bn_stats(out=ST6[:, 0:6], in_=PC[:, :]),
                         deps=[wo_last[k]])
                s2 = vop(lambda e: e.bn_stats(out=ST6[:, 6:12], in_=PD[:, :]),
                         deps=[s1])
                ag = vop(lambda e: e.bn_aggr(out=MV[:, :], in_=ST6[:, 0:12]),
                         deps=[s2])
            else:
                ag = stats1024(regs4[k], [wo_last[k]])
            bb = ln_single(MV[:, 1:2], MV[:, 0:1], SA[:, 0:1],
                           SA[:, 1:2], [ag])
            if k == 3:
                z2t_a = act(Z2[:, 0:512], PC[:, :], AF.Identity,
                            bias=SA[:, 1:2], scale=SA[:, 0:1],
                            deps=last(bb, epi_z2))
                z2t = act(Z2[:, 512:1024], PD[:, :], AF.Identity,
                          bias=SA[:, 1:2], scale=SA[:, 0:1], deps=[z2t_a])
                pcd_free = [z2t, z2t]
            else:
                z2t = act(Z2, regs4[k], AF.Identity, bias=SA[:, 1:2],
                          scale=SA[:, 0:1], deps=last(bb, epi_z2))
            t2a = vop(lambda e: e.tensor_mul(out=T2E, in0=Z2,
                                             in1=LN2W[:, :]),
                      deps=last(z2t, ld_l2w))
            epi_z2 = t2a
            t2b = vop(lambda e: e.tensor_add(out=T2E, in0=T2E,
                                             in1=LN2B[:, :]),
                      deps=last(t2a, ld_l2b))
            ag2 = stats1024(T2E, [t2b])
            bb2 = ln_single(MV[:, 1:2], MV[:, 0:1], SA[:, 0:1],
                            SA[:, 1:2], [ag2])
            olnt = act(OLN, T2E, AF.Identity, bias=SA[:, 1:2],
                       scale=SA[:, 0:1], deps=last(bb2, epi_ol))
            par = t_abs % 2
            xb = XB0 if par == 0 else XB1
            x2s = X2S0 if par == 0 else X2S1
            xre = dma("gpsimd", xb, x_h[t_abs * 128:(t_abs + 1) * 128, :],
                      deps=last(xb_prev[par]), ctr_name="xrl")
            fin = vop(lambda e, xb=xb, x2s=x2s: e.tensor_add(
                out=x2s, in0=xb, in1=OLN),
                deps=last(olnt, xre, x2s_prev[par]))
            epi_ol = fin
            xb_prev[par] = fin
            stok = dma("sync", x2_scr[t_abs * 128:(t_abs + 1) * 128, :],
                       x2s, deps=[fin], ctr_name="st")
            x2s_prev[par] = stok
            x2_sts[t_abs] = stok
            if k < 2:
                pbig_free[k] = fin
            elif k == 2:
                pa_free = [fin, fin]
        gt_free = wo_last[-1]
        z2_rd = gts

    x2_done = last(*[x2_sts[t] for t in range(NT)])
    if dbg:
        for nm, src in (("d_y", Y), ("d_qk", QK), ("d_v", V),
                        ("d_lkv", LINKV), ("d_cos", COS), ("d_sin", SIN)):
            dma("gpsimd", dbg_t[nm], src[:, :],
                deps=last(y_done, qk_done, v_done, lkv_done, x2_done),
                ctr_name="lps")
        dma("gpsimd", dbg_t["d_x2"], x2_scr, deps=last(x2_done),
            ctr_name="lps")

    lkv_rd_done = gts
    late_dep = last(hl, att_done, gts)
    ld_afc = dma("gpsimd", AFCT.rearrange("p (c k) -> p c k", c=ND),
                 afcT_h.rearrange("(c p) k -> p c k", p=128), deps=late_dep,
                 ctr_name="lps")
    ld_apj = dma("gpsimd", APJT.rearrange("p (c k) -> p c k", c=NF),
                 apjT_h.rearrange("(c p) k -> p c k", p=128), deps=late_dep,
                 ctr_name="lps")
    ld_bpjT = dma("gpsimd", BPJT, bpjT_h, deps=late_dep, ctr_name="lps")
    ld_bfcA = dma("gpsimd", BFCTA, bfcT_h[:, 0:2048], deps=late_dep,
                  ctr_name="lps")
    ld_bfcB = dma("gpsimd", BFCTB, bfcT_h[:, 2048:4096], deps=late_dep,
                  ctr_name="lps")

    Y2 = AY
    H = AV
    regs3 = [PBIG[:, 0:1024], PBIG[:, 1024:2048], PA[:, :]]
    reg_free = [last(pbig_free[0]), last(pbig_free[1]),
                last(pa_free[0], pa_free[1])]
    pcq_free = last(pcd_free[0], pcd_free[1])
    wfc_prev = [None, None]
    wpj_prev = [wo_prev[0], wo_prev[1]]
    wfc_st = [None, None]
    wpj_st = [None, None]
    wfc_cst = [None, None]
    wpj_cst = [None, None]
    scjA = sc_fc.rearrange("(o p) k -> o p k", p=128)
    spj3 = sc_pj.rearrange("(c p) k -> p c k", p=128)
    out_sts = []
    azq_rd = epi_ol
    y_rd = gts
    v_rd = gts
    x2ld_prev = [None, None]
    x2ld2_prev = [None, None]
    outt_prev = [None, None]
    oln_prev = epi_ol

    for Cf in range(2):
        tb = Cf * 8
        ztoks_all = []
        ev = None
        for g2 in range(2):
            z_toks = []
            for tt in range(4):
                t_abs = tb + g2 * 4 + tt
                par = tt % 2
                xbuf = LKT[:, par * 1024:(par + 1) * 1024]
                xld = dma("sync", xbuf,
                          x2_scr[t_abs * 128:(t_abs + 1) * 128, :],
                          deps=last(x2_sts[t_abs], x2ld_prev[par]),
                          ctr_name="x2l")
                ag = stats1024(xbuf, [xld])
                bb = ln_single(MV[:, 1:2], MV[:, 0:1], SA[:, 0:1],
                               SA[:, 1:2], [ag])
                ztok = vop(lambda e, xbuf=xbuf, tt=tt: e.tensor_scalar(
                    out=AZQ[:, tt * D:(tt + 1) * D], in0=xbuf,
                    scalar1=SA[:, 0:1], scalar2=SA[:, 1:2],
                    op0=OP.mult, op1=OP.add),
                    deps=last(bb, azq_rd))
                x2ld_prev[par] = ztok
                z_toks.append(ztok)
            for c in range(ND):
                q = c % 4
                reg = PBIG[:, q * 512:(q + 1) * 512]
                deps = last(*(z_toks + [ev, ld_id] + reg_free[q // 2]))
                for tt in range(4):
                    tok = mm(AZQ[:, tt * D + c * 128: tt * D + (c + 1) * 128],
                             IDENT[:, :], reg[:, tt * 128:(tt + 1) * 128],
                             True, True, inc=(tt == 3),
                             deps=deps if tt == 0 else ())
                ev = vop(lambda e, c=c, g2=g2, reg=reg: e.tensor_copy(
                    out=Y2[:, c * 1024 + g2 * 512:
                           c * 1024 + (g2 + 1) * 512],
                    in_=reg), deps=[tok])
                reg_free[q // 2] = [ev]
            azq_rd = tok
            ztoks_all += z_toks
        y2_done = ev

        t1toks = []
        for h2 in range(2):
            deps = last(y2_done, ld_afc, pcq_free, hl)
            for c in range(ND):
                tok = mm(AFCT[:, c * 16:(c + 1) * 16],
                         Y2[:, c * 1024 + h2 * 512: c * 1024 + (h2 + 1) * 512],
                         PC[0:16, :], c == 0, c == ND - 1,
                         inc=(c == ND - 1), deps=deps if c == 0 else ())
            t1toks.append(act(T1[0:16, h2 * 512:(h2 + 1) * 512], PC[0:16, :],
                              AF.Identity, scale=QS[0:16, 5:6],
                              deps=last(tok, q_fc["sinv2"])))
            pcq_free = t1toks[-1]
        t1_done = last(*t1toks)
        h_toks = []
        for o1 in range(NF):
            wb = o1 % 2
            if Cf == 0:
                wdep, stt = jit_quant(
                    q_fc, wb, q_fc["oj"][:, :, o1, :],
                    AW[:, wb * 1024:(wb + 1) * 1024],
                    last(wfc_prev[wb], wfc_st[wb], gt_free),
                    scjA[o1])
                wfc_st[wb] = stt
            else:
                stg8 = ISCRI8[:, wb * 1024:(wb + 1) * 1024]
                ld8 = dma("sync", stg8, scjA[o1],
                          deps=last(wfc_cst[wb], q_fc["done"],
                                    q_pj["done"], hl),
                          ctr_name="wfl")
                wdep = vop(lambda e, stg8=stg8, wb=wb: e.tensor_copy(
                    out=AW[:, wb * 1024:(wb + 1) * 1024], in_=stg8),
                    deps=last(ld8, wfc_prev[wb]))
                wfc_cst[wb] = wdep
            ri = o1 % 3
            reg = regs3[ri]
            deps = last(wdep, y2_done, t1_done, ld_bfcA, ld_bfcB,
                        *reg_free[ri])
            for c in range(ND):
                for nh in range(2):
                    mm(AW[:, wb * 1024 + c * 128: wb * 1024 + (c + 1) * 128],
                       Y2[:, c * 1024 + nh * 512: c * 1024 + (nh + 1) * 512],
                       reg[:, nh * 512:(nh + 1) * 512], c == 0, False,
                       deps=deps if (c == 0 and nh == 0) else ())
            bsrc = BFCTA if o1 < 16 else BFCTB
            bof = (o1 % 16) * 128
            for nh in range(2):
                tok = mm(bsrc[:, bof:bof + 128],
                         T1[0:16, nh * 512:(nh + 1) * 512],
                         reg[:, nh * 512:(nh + 1) * 512], False, True,
                         inc=(nh == 1))
            hev = act(H[:, o1 * 1024:(o1 + 1) * 1024], reg, AF.Gelu,
                      bias=BFCC[:, o1:o1 + 1], scale=QS[:, 3:4],
                      deps=last(tok, q_fc["s"], ld_bfc, v_rd))
            h_toks.append(hev)
            reg_free[ri] = [hev]
            wfc_prev[wb] = hev
        h_done = last(*h_toks)
        if Cf == 0:
            q_fc["done"] = last(*q_fc["stores"])
            ld_bpr = dma("sync", XT[0:1, 0:1024], bpj_h,
                         deps=last(xtp[0]), ctr_name="qsc")
            bpr_tok = act(BRW[:, 1024:2048], XT[0:1, 0:1024], AF.Identity,
                          scale=QS2[0:1, 2:3],
                          deps=last(ld_bpr, q_pj["sinv"], v_done))
            xtp[0] = last(xtp[0], bpr_tok)
        t2toks = []
        for h2 in range(2):
            deps = last(h_done, ld_apj, pcq_free)
            for fc in range(NF):
                tok = mm(APJT[:, fc * 16:(fc + 1) * 16],
                         H[:, fc * 1024 + h2 * 512:
                           fc * 1024 + (h2 + 1) * 512],
                         PC[0:16, :], fc == 0, fc == NF - 1,
                         inc=(fc == NF - 1), deps=deps if fc == 0 else ())
            t2toks.append(act(T2L[0:16, h2 * 512:(h2 + 1) * 512], PC[0:16, :],
                              AF.Identity, scale=QS2[0:16, 5:6],
                              deps=last(tok, q_pj["sinv2"])))
            pcq_free = t2toks[-1]
        t2_done = last(*t2toks)
        for qp in range(2):
            regs4f = (PBIG[:, 0:1024], PBIG[:, 1024:2048], PA[:, :], None)

            def f2dst(k, nh):
                return (PCD[nh][:, :] if k == 3 else
                        regs4f[k][:, nh * 512:(nh + 1) * 512])
            hd = last(*(reg_free[0] + reg_free[1] + reg_free[2] +
                        [pcq_free]))
            for fc in range(NF):
                wb = fc % 2
                if Cf == 0 and qp == 0:
                    wdep, stt = jit_quant(
                        q_pj, wb, q_pj["w4"][:, fc, :],
                        WSLB[:, wb * 1024:(wb + 1) * 1024],
                        last(wpj_prev[wb], wpj_st[wb]),
                        spj3[:, fc, :])
                    wpj_st[wb] = stt
                else:
                    stg8 = ISCRI8[:, 2048 + wb * 1024:
                                  2048 + (wb + 1) * 1024]
                    ld8 = dma("sync", stg8, spj3[:, fc, :],
                              deps=last(wpj_cst[wb], q_pj["done"], hl),
                              ctr_name="wpl")
                    wdep = vop(lambda e, stg8=stg8, wb=wb: e.tensor_copy(
                        out=WSLB[:, wb * 1024:(wb + 1) * 1024], in_=stg8),
                        deps=last(ld8, wpj_prev[wb]))
                    wpj_cst[wb] = wdep
                wdeps = last(wdep, h_done, t2_done) + \
                    (hd if fc == 0 else [])
                for k in range(4):
                    ts_i = qp * 4 + k
                    for nh in range(2):
                        tok = mm(H[:, fc * 1024 + ts_i * 128:
                                   fc * 1024 + (ts_i + 1) * 128],
                                 WSLB[:, wb * 1024 + nh * 512:
                                      wb * 1024 + (nh + 1) * 512],
                                 f2dst(k, nh),
                                 fc == 0, False, inc=(k == 3 and nh == 1),
                                 deps=wdeps if (k == 0 and nh == 0) else ())
                wpj_prev[wb] = tok
            if Cf == 0 and qp == 0:
                q_pj["done"] = last(*q_pj["stores"])
            f2_last = []
            for k in range(4):
                ts_i = qp * 4 + k
                for nh in range(2):
                    mm(T2L[0:16, ts_i * 128:(ts_i + 1) * 128],
                       BPJT[0:16, nh * 512:(nh + 1) * 512],
                       f2dst(k, nh), False, False,
                       deps=last(ld_bpjT, t2_done))
                for nh in range(2):
                    tok = mm(ONES[:, :],
                             BRW[:, 1024 + nh * 512:1024 + (nh + 1) * 512],
                             f2dst(k, nh), False, True,
                             inc=(nh == 1), deps=last(bpr_tok))
                f2_last.append(tok)
            for k in range(4):
                ts_i = qp * 4 + k
                t_abs = tb + ts_i
                par = t_abs % 2
                if k == 3:
                    s1f = vop(lambda e: e.bn_stats(out=ST6[:, 0:6],
                                                   in_=PC[:, :]),
                              deps=[f2_last[k]])
                    s2f = vop(lambda e: e.bn_stats(out=ST6[:, 6:12],
                                                   in_=PD[:, :]),
                              deps=[s1f])
                    ag = vop(lambda e: e.bn_aggr(out=MV[:, :],
                                                 in_=ST6[:, 0:12]),
                             deps=[s2f])
                else:
                    ag = stats1024(regs4f[k], [f2_last[k]])
                vp = vop(lambda e: e.scalar_tensor_tensor(
                    out=SA[:, 2:3], in0=MV[:, 1:2], scalar=QS2[:, 4:5],
                    in1=EPSC[:, :], op0=OP.mult, op1=OP.add),
                    deps=last(ag, ms_eps, q_pj["s2"]))
                w2r = vop(lambda e: e.reciprocal(out=SA[:, 3:4],
                                                 in_=SA[:, 2:3]), deps=[vp])
                aa0 = act(SA[:, 4:5], SA[:, 3:4], AF.Sqrt, deps=[w2r])
                aa = vop(lambda e: e.tensor_tensor(
                    out=SA[:, 0:1], in0=SA[:, 4:5], in1=QS2[:, 3:4],
                    op=OP.mult), deps=[aa0])
                bb = vop(lambda e: e.scalar_tensor_tensor(
                    out=SA[:, 1:2], in0=MV[:, 0:1], scalar=-1.0,
                    in1=SA[:, 0:1], op0=OP.mult, op1=OP.mult), deps=[aa])
                if k == 3:
                    ol_a = act(OLN[:, 0:512], PC[:, :], AF.Identity,
                               bias=SA[:, 1:2], scale=SA[:, 0:1],
                               deps=last(bb, oln_prev))
                    olnt = act(OLN[:, 512:1024], PD[:, :], AF.Identity,
                               bias=SA[:, 1:2], scale=SA[:, 0:1],
                               deps=[ol_a])
                else:
                    olnt = act(OLN, regs4f[k], AF.Identity, bias=SA[:, 1:2],
                               scale=SA[:, 0:1], deps=last(bb, oln_prev))
                x2b = LKT[:, (2 + par) * 1024:(3 + par) * 1024]
                x2l = dma("sync", x2b,
                          x2_scr[t_abs * 128:(t_abs + 1) * 128, :],
                          deps=last(x2_sts[t_abs], x2ld2_prev[par]),
                          ctr_name="x2l")
                ob2 = FSCR[:, :] if par == 0 else FSC2[:, :]
                fin = vop(lambda e, x2b=x2b, ob2=ob2: e.tensor_add(
                    out=ob2, in0=x2b, in1=OLN),
                    deps=last(olnt, x2l, outt_prev[par]))
                oln_prev = fin
                x2ld2_prev[par] = fin
                stok = dma("sync", out_h[t_abs * 128:(t_abs + 1) * 128, :],
                           ob2, deps=[fin], ctr_name="st")
                outt_prev[par] = stok
                out_sts.append(stok)
                if k < 3:
                    reg_free[k] = [fin]
                else:
                    pcq_free = fin
        azq_rd = oln_prev

    if dbg:
        dma("gpsimd", dbg_t["d_wfc"], sc_fc, deps=[q_fc["done"]],
            ctr_name="lps")
        dma("sync", dbg_t["d_qs"], QS[:, :],
            deps=last(q_fc["sinv2"], q_fc["s2"]), ctr_name="b64")
        dma("sync", dbg_t["d_qs2"], QS2[:, :],
            deps=last(q_pj["sinv2"], q_pj["s2"]), ctr_name="b64")
        dma("sync", dbg_t["d_mxp"], MXP[:, :], deps=last(q_fc["sinv2"]),
            ctr_name="b64")
    waits("sync", last(*out_sts))

    with nc.Block() as block:
        @block.sync
        def _(eng):
            for nm, fn in ops:
                if nm == "sync":
                    fn(eng)

        @block.gpsimd
        def _(eng):
            for nm, fn in ops:
                if nm == "gpsimd":
                    fn(eng)

        @block.tensor
        def _(eng):
            for nm, fn in ops:
                if nm == "tensor":
                    fn(eng)

        @block.vector
        def _(eng):
            for nm, fn in ops:
                if nm == "vector":
                    fn(eng)

        @block.scalar
        def _(eng):
            for nm, fn in ops:
                if nm == "scalar":
                    fn(eng)

    stack.close()
    return nc, dbg_t


def _prep_inputs(inputs):
    import ml_dtypes
    bf = ml_dtypes.bfloat16
    f32 = np.float32
    ct = lambda a: np.ascontiguousarray(np.asarray(a), dtype=f32)

    common = {
        "wqkT": ct(np.asarray(inputs["L0_attn_Wqk"]).T),
        "wvT": ct(np.asarray(inputs["L0_attn_Wv"]).T),
        "wgT": ct(np.asarray(inputs["L0_attn_Wg"]).T),
        "woT": ct(np.asarray(inputs["L0_attn_Wo"]).T),
        "wfcT": ct(np.asarray(inputs["L0_ffn_W_fc"]).T),
        "wpjT": ct(np.asarray(inputs["L0_ffn_W_proj"]).T),
        "afcT": ct(np.asarray(inputs["L0_ffn_A_fc"]).T),
        "bfcT": ct(np.asarray(inputs["L0_ffn_B_fc"]).T),
        "apjT": ct(np.asarray(inputs["L0_ffn_A_proj"]).T),
        "bpjT": ct(np.asarray(inputs["L0_ffn_B_proj"]).T),
        "bqk": ct(np.asarray(inputs["L0_attn_bqk"]).reshape(NQ, 128).T),
        "bv": ct(np.asarray(inputs["L0_attn_bv"]).reshape(1, E)),
        "bg": ct(np.asarray(inputs["L0_attn_bg"]).reshape(NE, 128).T),
        "bo": ct(np.asarray(inputs["L0_attn_bo"]).reshape(1, D)),
        "bfc": ct(np.asarray(inputs["L0_ffn_b_fc"]).reshape(NF, 128).T),
        "bpj": ct(np.asarray(inputs["L0_ffn_b_proj"]).reshape(1, D)),
        "ln1w": ct(np.asarray(inputs["L0_attn_ln1_w"]).reshape(ND, 128).T),
        "ln1b": ct(np.asarray(inputs["L0_attn_ln1_b"]).reshape(ND, 128).T),
        "ln2wr": ct(np.broadcast_to(np.asarray(inputs["L0_attn_ln2_w"]),
                                    (128, D))),
        "ln2br": ct(np.broadcast_to(np.asarray(inputs["L0_attn_ln2_b"]),
                                    (128, D))),
        "gam": ct(np.asarray(inputs["L0_attn_gamma"]).reshape(4, 2, 128)
                  .transpose(2, 1, 0).reshape(128, 8)),
        "bet": ct(np.asarray(inputs["L0_attn_beta"]).reshape(4, 2, 128)
                  .transpose(2, 1, 0).reshape(128, 8)),
        "posemb": ct(np.asarray(inputs["L0_attn_pos_emb"]).reshape(65, 1)),
    }
    idx = np.arange(G)
    dist = np.clip(np.abs(idx[:, None] - idx[None, :]), 0, 32)
    onehot = np.zeros((65, G * G), dtype=bf)
    onehot[dist.reshape(-1), np.arange(G * G)] = 1
    common["onehot"] = onehot
    common["identb"] = np.eye(128, dtype=bf)
    common["identf"] = np.eye(128, dtype=f32)
    common["onesrow"] = np.ones((1, 128), dtype=bf)
    common["onesrowf"] = np.ones((1, 128), dtype=f32)
    x = np.ascontiguousarray(np.asarray(inputs["L0_attn_x"]), dtype=f32)
    return [dict(common, x=x[i]) for i in range(NCORES)]


def kernel(**inputs):
    in_maps = _prep_inputs(inputs)
    nc, _ = build_nc(dbg=False)
    res = run_bass_kernel_spmd(nc, in_maps, core_ids=list(range(NCORES)))
    out = np.stack([np.asarray(r["out"]) for r in res.results], axis=0)
    return out.astype(np.float32)